# version 02
# baseline (speedup 1.0000x reference)
import contextlib
import numpy as np

import concourse.bass as bass
import concourse.mybir as mybir
from concourse.bass_utils import run_bass_kernel_spmd

F32 = mybir.dt.float32
AF = mybir.ActivationFunctionType
ALU = mybir.AluOpType

T_HALF = 2048
CH = 256
NCH = T_HALF // CH
NSUB = 16
EPS = 1e-5
F32EPS = float(np.finfo(np.float32).eps)
STAGES = 4
DBG = None
PAIRS = [[0, 1], [2, 3], [4, 5], [6, 7]]

ENGS = ("tensor", "vector", "scalar", "gpsimd", "sync")
SEM_OWNER = {"t": "tensor", "v": "vector", "a": "scalar", "g": "gpsimd"}
SEM_NAMES = ("t", "v", "a", "g", "cc", "di", "dw", "dx0", "dx1", "do", "de", "d4")
DMA_SEMS = ("di", "dw", "dx0", "dx1", "do", "de", "d4")


class Bld:
    def __init__(self, nc, sems):
        self.nc = nc
        self.sems = sems
        self.cnt = {k: 0 for k in sems}
        self.stream = {e: [] for e in ENGS}

    def emit(self, eng, fn, waits=(), pub=None, amt=1):
        toks = []
        for tk in waits:
            if tk is None:
                continue
            s, v = tk
            if s in DMA_SEMS:
                v = self.cnt[s]
            toks.append((s, v))
        tok = None
        if pub is not None:
            self.cnt[pub] += amt
            tok = (pub, self.cnt[pub])
        self.stream[eng].append((toks, fn, pub, amt))
        return tok

    def dma(self, eng, out, in_, waits=(), pub="di"):
        waits = list(waits) + [(pub, self.cnt[pub])]
        return self.emit(eng, lambda h, o=out, i=in_: h.dma_start(out=o, in_=i),
                         waits, pub, 16)

    def barrier_wait(self, eng, waits):
        self.emit(eng, lambda h: None, waits)

    def realize(self, block):
        for eng in ENGS:
            items = self.stream[eng]
            sems = self.sems

            def body(h, items=items):
                hw = {}
                for toks, fn, pub, amt in items:
                    need = {}
                    for s, v in toks:
                        if hw.get(s, 0) < v and need.get(s, 0) < v:
                            need[s] = v
                    for s, v in need.items():
                        h.wait_ge(sems[s], v)
                        hw[s] = v
                    inst = fn(h)
                    if pub is not None:
                        inst.then_inc(sems[pub], amt)

            getattr(block, eng)(body)


class Pool:
    def __init__(self, tiles):
        self.tiles = tiles
        self.rd = [None] * len(tiles)
        self.i = 0

    def get(self):
        i = self.i % len(self.tiles)
        self.i += 1
        w = self.rd[i]
        self.rd[i] = None

        def done(tok, i=i):
            self.rd[i] = tok

        return self.tiles[i], ([w] if w is not None else []), done


def _f32(a):
    return np.ascontiguousarray(a, dtype=np.float32)


def _col_pack(v):
    return _f32(np.transpose(np.asarray(v).reshape(-1, 128)))


def _colmat_pack(m, inner):
    m = np.asarray(m)
    nblk = m.shape[0] // 128
    return _f32(np.transpose(m.reshape(nblk, 128, inner), (1, 0, 2)).reshape(128, nblk * inner))


def host_prepare(inputs):
    x = np.asarray(inputs["L0_attn_hidden_states"])

    def wT(name):
        return _f32(np.asarray(inputs[name]).T)

    def arr(name):
        return _f32(np.asarray(inputs[name]))

    shared = {
        "l0_ipw": wT("L0_attn_in_proj_w"),
        "l0_opw": wT("L0_attn_out_proj_w"),
        "l0_fcw": wT("L0_attn_fc_w"),
        "l0_fcb": _col_pack(inputs["L0_attn_fc_b"]),
        "l0_lng": _col_pack(inputs["L0_attn_ln_g"]),
        "l0_lnb": _col_pack(inputs["L0_attn_ln_b"]),
        "f0_gw": arr("L0_ffn_w_gate"),
        "f0_uw": arr("L0_ffn_w_up"),
        "f0_dw": arr("L0_ffn_w_down"),
        "l1_lng": _col_pack(inputs["L1_attn_ln_g"]),
        "l1_lnb": _col_pack(inputs["L1_attn_ln_b"]),
        "f1_w1": _f32(np.transpose(np.asarray(inputs["L1_ffn_w1"]), (1, 0, 2)).reshape(1024, 16384)),
        "f1_w2": _f32(np.asarray(inputs["L1_ffn_w2"]).reshape(16384, 1024)),
        "f1_b1": _f32(np.transpose(np.asarray(inputs["L1_ffn_b1"]).reshape(128, 128))),
        "f1_b2": arr("L1_ffn_b2"),
        "f1_gw": wT("L1_ffn_gate_w"),
        "f1_gb": _f32(np.asarray(inputs["L1_ffn_gate_b"]).reshape(1, 8)),
        "f1_rms": _f32(np.asarray(inputs["L1_ffn_rms_w"]).reshape(1, 8)),
        "c_ident": _f32(np.eye(128)),
        "c_scal": _f32(np.tile(np.array([1e-5, 1.0, 2.0, float(np.finfo(np.float32).eps),
                                         1e-12, 1.0 / 6.0, 1e-6, 0.5], dtype=np.float32),
                               (128, 1))),
        "c_ones1": _f32(np.ones((1, 128))),
        "c_w1024": _f32(np.full((128, 1), 1.0 / 1024.0)),
    }
    oh16 = np.zeros((16, 2048), dtype=np.float32)
    for n in range(16):
        oh16[n, n * 128:(n + 1) * 128] = 1.0
    shared["c_oh16"] = oh16

    l0_br = {}
    for tag, sfx in (("f", ""), ("b", "_b")):
        l0_br[tag] = {
            "convw": _colmat_pack(inputs["L0_attn_conv_w" + sfx], 4),
            "convb": _col_pack(inputs["L0_attn_conv_b" + sfx]),
            "xpw": wT("L0_attn_x_proj_w" + sfx),
            "dtw": wT("L0_attn_dt_proj_w" + sfx),
            "dtb": _col_pack(inputs["L0_attn_dt_proj_b" + sfx]),
            "alog": _colmat_pack(inputs["L0_attn_A_log" + sfx], 16),
            "dp": _col_pack(inputs["L0_attn_Dp" + sfx]),
        }
    l1_br = {}
    for tag in ("m1", "m2"):
        p = "L1_attn_" + tag + "_"
        l1_br[tag] = {
            "win": arr(p + "Win"),
            "convw": _colmat_pack(inputs[p + "Wconv"], 4),
            "convb": _col_pack(inputs[p + "bconv"]),
            "xpw": arr(p + "Wx"),
            "dtw": arr(p + "Wdt"),
            "dtb": _col_pack(inputs[p + "bdt"]),
            "alog": _colmat_pack(inputs[p + "Alog"], 16),
            "dp": _col_pack(inputs[p + "Dp"]),
            "wout": arr(p + "Wout"),
        }

    in_maps = []
    for c in range(8):
        bi, par = c // 2, c % 2
        m = dict(shared)
        if par == 0:
            sl = x[bi, 0:T_HALF]
            halo = x[bi, T_HALF + 2:T_HALF - 1:-1]
        else:
            sl = x[bi, T_HALF:2 * T_HALF][::-1]
            halo = x[bi, T_HALF - 3:T_HALF]
        xin = np.zeros((2176, 1024), dtype=np.float32)
        xin[:T_HALF] = sl
        xin[T_HALF:T_HALF + 3] = halo
        m["xin"] = xin
        pe = np.full((128, 1), 1.0 if par == 0 else 0.0, dtype=np.float32)
        m["c_pe"] = pe
        m["c_po"] = _f32(1.0 - pe)
        A, B = (l0_br["f"], l0_br["b"]) if par == 0 else (l0_br["b"], l0_br["f"])
        for k, v in A.items():
            m["l0a_" + k] = v
        for k, v in B.items():
            m["l0b_" + k] = v
        A, B = (l1_br["m1"], l1_br["m2"]) if par == 0 else (l1_br["m2"], l1_br["m1"])
        for k, v in A.items():
            m["l1a_" + k] = v
        for k, v in B.items():
            m["l1b_" + k] = v
        in_maps.append(m)
    return in_maps


def host_gather(results):
    out = np.empty((4, 4096, 1024), dtype=np.float32)
    for c in range(8):
        bi, par = c // 2, c % 2
        y = results[c]["yout"]
        if par == 0:
            out[bi, 0:T_HALF] = y
        else:
            out[bi, T_HALF:2 * T_HALF] = y[::-1]
    return out


def build_nc():
    nc = bass.Bass("TRN2", target_bir_lowering=False)
    ctx = contextlib.ExitStack()

    IN = {}
    in_specs = [
        ("xin", [2176, 1024]),
        ("l0_ipw", [1024, 2048]), ("l0_opw", [1024, 1024]), ("l0_fcw", [1024, 1024]),
        ("l0_fcb", [128, 8]), ("l0_lng", [128, 8]), ("l0_lnb", [128, 8]),
        ("f0_gw", [2816, 1024]), ("f0_uw", [2816, 1024]), ("f0_dw", [1024, 2816]),
        ("l1_lng", [128, 8]), ("l1_lnb", [128, 8]),
        ("f1_w1", [1024, 16384]), ("f1_w2", [16384, 1024]),
        ("f1_b1", [128, 128]), ("f1_b2", [8, 1024]),
        ("f1_gw", [1024, 8]), ("f1_gb", [1, 8]), ("f1_rms", [1, 8]),
        ("c_ident", [128, 128]), ("c_ones1", [1, 128]), ("c_w1024", [128, 1]),
        ("c_scal", [128, 8]),
        ("c_oh16", [16, 2048]), ("c_pe", [128, 1]), ("c_po", [128, 1]),
    ]
    for ph in ("a", "b"):
        in_specs += [
            ("l0%s_convw" % ph, [128, 32]), ("l0%s_convb" % ph, [128, 8]),
            ("l0%s_xpw" % ph, [1024, 96]), ("l0%s_dtw" % ph, [64, 1024]),
            ("l0%s_dtb" % ph, [128, 8]), ("l0%s_alog" % ph, [128, 128]),
            ("l0%s_dp" % ph, [128, 8]),
            ("l1%s_win" % ph, [1024, 2048]), ("l1%s_convw" % ph, [128, 32]),
            ("l1%s_convb" % ph, [128, 8]), ("l1%s_xpw" % ph, [1024, 64]),
            ("l1%s_dtw" % ph, [32, 1024]), ("l1%s_dtb" % ph, [128, 8]),
            ("l1%s_alog" % ph, [128, 128]), ("l1%s_dp" % ph, [128, 8]),
            ("l1%s_wout" % ph, [1024, 512]),
        ]
    for name, shape in in_specs:
        IN[name] = nc.dram_tensor(name, shape, F32, kind="ExternalInput").ap()
    yout = nc.dram_tensor("yout", [T_HALF, 1024], F32, kind="ExternalOutput").ap()

    qgw = nc.dram_tensor("qgw", [1024, 2816], F32).ap()
    quw = nc.dram_tensor("quw", [1024, 2816], F32).ap()
    qdw = nc.dram_tensor("qdw", [2816, 1024], F32).ap()
    AR = {}
    for nm, shape in (("ar1", [128, 256]), ("ar2", [128, 48]),
                      ("ar3", [128, 256]), ("ar4", [128, 16384])):
        AR[nm] = (nc.dram_tensor(nm + "_in", shape, F32).ap(),
                  nc.dram_tensor(nm + "_out", shape, F32).ap())

    def sbuf(name, shape, dt=F32):
        return ctx.enter_context(nc.sbuf_tensor(name, shape, dt))

    xstr_t = sbuf("xstr", [128, 16384])
    acc_t = sbuf("acc", [128, 16384])
    wsa_t = sbuf("wsa", [128, 10776])
    s512_t = [sbuf("s5_%d" % i, [128, 512]) for i in range(3)]
    scn_t = sbuf("scn", [128, 768])
    coh_t = sbuf("coh", [16, 2048])
    strip_t = sbuf("strip", [128, 1024])
    bcB_t = sbuf("bcB", [16, 256])
    bcC_t = sbuf("bcC", [16, 256])
    xdbl_t = sbuf("xdbl", [64, 256])
    statc_t = sbuf("statc", [128, 64])
    mub_t = sbuf("mub", [128, 256])
    r12b_t = sbuf("r12b", [128, 256])
    rowb_t = sbuf("rowb", [1, 1792])
    hrow_t = sbuf("hrow", [1, 32])
    ident_t = sbuf("ident", [128, 128])
    ones1_t = sbuf("ones1", [1, 128])
    w1024_t = sbuf("w1024", [128, 1])
    cs_t = sbuf("cscal", [128, 8])
    pe_t = sbuf("pe", [128, 1])
    po_t = sbuf("po", [128, 1])
    smalls_t = sbuf("smalls", [128, 448])
    stash_t = sbuf("stash", [128, 32])
    stA_t = sbuf("stA", [128, 128])
    stB_t = sbuf("stB", [128, 128])
    xh_t = sbuf("xhalo", [128, 24])

    psum_t = [ctx.enter_context(nc.psum_tensor("ps%d" % i, [128, 512], F32))
              for i in range(8)]
    sems = {k: ctx.enter_context(nc.semaphore("sem_" + k)) for k in SEM_NAMES}
    block = ctx.enter_context(nc.Block())
    b = Bld(nc, sems)
    pp = Pool(psum_t)
    sp = Pool([t[:, :] for t in s512_t])

    def wv(off, w):
        return wsa_t[:, off:off + w]

    HS = [wv(0, 256), wv(256, 256)]
    XS = [wv(512 + i * 259, 259) for i in range(8)]
    UU = [wv(2584 + i * 256, 256) for i in range(8)]
    DU = [wv(4632 + i * 256, 256) for i in range(8)]
    YY = [wv(6680 + i * 256, 256) for i in range(8)]
    DL = [wv(8728 + i * 256, 256) for i in range(8)]
    HTM = wv(0, 2816)
    QHTM = wv(2816, 2816)
    QHF = [wv(5632 + i * 128, 128) for i in range(22)]
    QXF = [wv(8448 + i * 128, 128) for i in range(8)]
    TMT = wv(5632, 1024)
    QAT = wv(6656, 1024)
    FS1 = wv(9472, 1024)
    FS2 = wv(10496 - 1024 + 1024, 1024) if False else None
    FS2 = wv(2816, 1024)
    XF = [wv(i * 512, 512) for i in range(8)]
    HSEG = [wv(4096 + i * 512, 512) for i in range(2)]
    W2S = [wv(5120 + i * 1024, 1024) for i in range(2)]
    ROWG = wsa_t[0:1, 7168:9216]
    NGB = wv(9216, 512)
    GWS = wv(9728, 64)

    exch_t = scn_t
    SCA = scn_t[:, 0:256]
    SCB = scn_t[:, 256:512]
    SCH = scn_t[:, 512:768]

    xstr = [xstr_t[:, i * 2048:(i + 1) * 2048] for i in range(8)]
    accv = [acc_t[:, i * 2048:(i + 1) * 2048] for i in range(8)]

    def mm(psum_ap, lhsT, rhs, start, stop, waits=(), pub="t"):
        return b.emit("tensor",
                      lambda h, p=psum_ap, l=lhsT, r=rhs, s=start, t=stop:
                      h.matmul(p, l, r, start=s, stop=t),
                      waits, "t", 1)

    def tr(psum_ap, in_ap, waits, idn=None):
        if idn is None:
            idn = ident_t[:, :]
        return b.emit("tensor",
                      lambda h, p=psum_ap, i=in_ap, d=idn: h.transpose(p, i, d),
                      waits, "t", 1)

    def act(out, in_, func, bias=0.0, scale=1.0, waits=()):
        return b.emit("scalar",
                      lambda h, o=out, i=in_, f=func, bi=bias, sc=scale:
                      h.activation(o, i, f, bias=bi, scale=sc),
                      waits, "a", 1)

    def vcopy(out, in_, waits=()):
        return b.emit("vector", lambda h, o=out, i=in_: h.tensor_copy(o, i),
                      waits, "v", 1)

    def vtt(out, a0, a1, op, waits=()):
        return b.emit("vector", lambda h, o=out, x=a0, y=a1, p=op:
                      h.tensor_tensor(o, x, y, p), waits, "v", 1)

    def gtt(out, a0, a1, op, waits=()):
        return b.emit("gpsimd", lambda h, o=out, x=a0, y=a1, p=op:
                      h.tensor_tensor(o, x, y, p), waits, "g", 1)

    def vts1(out, in0, s1, op, waits=()):
        return b.emit("vector", lambda h, o=out, i=in0, a=s1, p=op:
                      h.tensor_single_scalar(o, i, a, p), waits, "v", 1)

    def vts2(out, in0, s1, s2, op0, op1, waits=()):
        return b.emit("vector", lambda h, o=out, i=in0, a=s1, c=s2, p0=op0, p1=op1:
                      h.tensor_scalar(o, i, a, c, p0, p1), waits, "v", 1)

    def vstt(out, in0, s, in1, op0, op1, waits=()):
        return b.emit("vector", lambda h, o=out, i=in0, a=s, j=in1, p0=op0, p1=op1:
                      h.scalar_tensor_tensor(o, i, a, j, p0, p1), waits, "v", 1)

    def vscan(out, d0, d1, init, waits=()):
        return b.emit("vector", lambda h, o=out, x=d0, y=d1, i=init:
                      h.tensor_tensor_scan(o, x, y, i, ALU.mult, ALU.add),
                      waits, "v", 1)

    def vrecip(out, in_, waits=()):
        return b.emit("vector", lambda h, o=out, i=in_: h.reciprocal(o, i),
                      waits, "v", 1)

    def vmemset(ap, val, waits=()):
        return b.emit("vector", lambda h, a=ap, v=val: h.memset(a, v),
                      waits, "v", 1)

    def vreduce(out, in_, op, absval=False, waits=()):
        return b.emit("vector",
                      lambda h, o=out, i=in_, p=op, av=absval:
                      h.tensor_reduce(o, i, mybir.AxisListType.X, p,
                                      apply_absolute_value=av),
                      waits, "v", 1)

    ld0 = []
    for src, dst in ((IN["c_ident"], ident_t[:, :]), (IN["c_ones1"], ones1_t[:, :]),
                     (IN["c_w1024"], w1024_t[:, :]), (IN["c_oh16"], coh_t[:, :]),
                     (IN["c_pe"], pe_t[:, :]), (IN["c_po"], po_t[:, :]),
                     (IN["c_scal"], cs_t[:, :])):
        ld0.append(b.dma("sync", dst, src))
    for i, v in enumerate([1e-5, 1.0, 2.0, F32EPS, 1e-12, 1.0 / 6.0, 1e-6, 0.5]):
        nc.const_aps.register(F32, float(v), cs_t[:, i:i + 1])
    sm = smalls_t

    def lsm(c0, src, w):
        return b.dma("sync", sm[:, c0:c0 + w], src)

    ld0 += [lsm(0, IN["l0_fcb"], 8), lsm(8, IN["l0_lng"], 8), lsm(16, IN["l0_lnb"], 8),
            lsm(24, IN["l1_lng"], 8), lsm(32, IN["l1_lnb"], 8)]
    SM_FCB = sm[:, 0:8]
    SM_LNG = {0: sm[:, 8:16], 1: sm[:, 24:32]}
    SM_LNB = {0: sm[:, 16:24], 1: sm[:, 32:40]}
    SM_B2S = sm[:, 40:48]

    def SM_CONVB(ph):
        return sm[:, 48 + 24 * ph:56 + 24 * ph]

    def SM_DTB(ph):
        return sm[:, 56 + 24 * ph:64 + 24 * ph]

    def SM_DP(ph):
        return sm[:, 64 + 24 * ph:72 + 24 * ph]

    def SM_CONVW(ph):
        return sm[:, 96 + 32 * ph:128 + 32 * ph]

    def SM_AMAT(ph):
        return sm[:, 160 + 128 * ph:288 + 128 * ph]

    def load_branch_smalls(layer, deps):
        pre = "l0" if layer == 0 else "l1"
        toks = list(deps)
        for ph, pch in ((0, "a"), (1, "b")):
            toks.append(lsm(48 + 24 * ph, IN[pre + pch + "_convb"], 8))
            toks.append(lsm(56 + 24 * ph, IN[pre + pch + "_dtb"], 8))
            toks.append(lsm(64 + 24 * ph, IN[pre + pch + "_dp"], 8))
            toks.append(lsm(96 + 32 * ph, IN[pre + pch + "_convw"], 32))
            toks.append(lsm(160 + 128 * ph, IN[pre + pch + "_alog"], 128))
        out = []
        for ph in (0, 1):
            a0 = 160 + 128 * ph
            t1 = act(sm[:, a0:a0 + 128], sm[:, a0:a0 + 128], AF.Exp, waits=toks)
            out.append(vts1(sm[:, a0:a0 + 128], sm[:, a0:a0 + 128], -1.0, ALU.mult,
                            waits=[t1]))
        return out

    xload_toks = []
    xld_last = [[], []]
    for tt in range(17):
        tmt = TMT if tt % 2 == 0 else QAT
        wts = xld_last[tt % 2]
        xld_last[tt % 2] = []
        dtok = b.dma("sync", tmt[:, 0:1024], IN["xin"][tt * 128:(tt + 1) * 128, :],
                     waits=wts + (ld0 if tt == 0 else []), pub="dx0" if tt % 2 == 0 else "dx1")
        last = None
        for db in range(8):
            ps, pw, pdone = pp.get()
            ttk = tr(ps[:, 0:128], tmt[:, db * 128:(db + 1) * 128], [dtok] + pw)
            if tt < 16:
                etk = act(xstr[db][:, tt * 128:(tt + 1) * 128], ps[:, 0:128], AF.Copy,
                          waits=[ttk])
            else:
                etk = act(xh_t[:, db * 3:db * 3 + 3], ps[:, 0:3], AF.Copy, waits=[ttk])
            pdone(etk)
            last = etk
        xld_last[tt % 2] = [last]
        xload_toks.append(last)
    XTOK = ("a", b.cnt["a"])

    def stats_pass(src_tiles, deps):
        sq1, w1, d1 = sp.get()
        sq2, w2, d2 = sp.get()
        toks = []
        prev_q = w1 + w2
        for s in range(NSUB):
            ps, pw, pdone = pp.get()
            mtk = None
            sqtoks = []
            for db in range(8):
                src = src_tiles[db][:, s * 128:(s + 1) * 128]
                w0 = list(deps) + (pw if db == 0 else [])
                mtk = mm(ps[:, 0:1], src, w1024_t[:, 0:1], db == 0, db == 7,
                         w0, "t" if db == 7 else None)
                sqt = (sq1 if db < 4 else sq2)[:, (db % 4) * 128:(db % 4 + 1) * 128]
                sqtoks.append(act(sqt, src, AF.Square, waits=w0 + prev_q))
            e1 = act(statc_t[:, s:s + 1], ps[:, 0:1], AF.Copy, waits=[mtk])
            pdone(e1)
            ps2, pw2, pdone2 = pp.get()
            qtk = None
            for db in range(8):
                sqt = (sq1 if db < 4 else sq2)[:, (db % 4) * 128:(db % 4 + 1) * 128]
                qtk = mm(ps2[:, 0:1], sqt, w1024_t[:, 0:1], db == 0, db == 7,
                         [sqtoks[db]] + (pw2 if db == 0 else []),
                         "t" if db == 7 else None)
            prev_q = [qtk]
            e2 = act(statc_t[:, 16 + s:16 + s + 1], ps2[:, 0:1], AF.Copy, waits=[qtk])
            pdone2(e2)
            toks += [e1, e2]
        d1(prev_q[0])
        d2(prev_q[0])
        return toks

    def ln_rows(s0, ns, deps, rev, want_r2):
        w = ns * 128
        tts = []
        for j, stat in ((0, 0), (1, 16)):
            for s in range(ns):
                ps, pw, pdone = pp.get()
                tk = tr(ps[0:1, 0:128], statc_t[:, stat + s0 + s:stat + s0 + s + 1],
                        list(deps) + pw)
                ek = act(rowb_t[0:1, j * 256 + s * 128:j * 256 + (s + 1) * 128],
                         ps[0:1, 0:128], AF.Copy, waits=[tk])
                pdone(ek)
                tts.append(ek)
        mu = rowb_t[0:1, 0:w]
        r1 = rowb_t[0:1, 256:256 + w]
        r12 = rowb_t[0:1, 512:512 + w]
        tmp = rowb_t[0:1, 768:768 + w]
        t1 = vtt(tmp, mu, mu, ALU.mult, waits=tts)
        t2 = vtt(r1, r1, tmp, ALU.subtract, waits=[t1])
        tok = t2
        if want_r2:
            t3 = vcopy(tmp, r1, waits=[t2])
        t4 = act(r1, r1, AF.Sqrt, bias=EPS, waits=[tok if not want_r2 else t3])
        t5 = vrecip(r1, r1, waits=[t4])
        tok = t5
        if want_r2:
            t6 = vtt(r12, r1, r1, ALU.mult, waits=[t5])
            t7 = vtt(r12, r12, tmp, ALU.mult, waits=[t6])
            t8 = act(r12, r12, AF.Sqrt, bias=EPS, waits=[t7])
            t9 = vrecip(r12, r12, waits=[t8])
            t10 = vtt(r12, r12, r1, ALU.mult, waits=[t9])
            tok = t10
        if not rev:
            return mu, r1, (r12 if want_r2 else None), tok
        mur = rowb_t[0:1, 1024:1024 + w]
        r1r = rowb_t[0:1, 1280:1280 + w]
        c1 = vcopy(mur, mu[0:1, w - 1::-1], waits=[tok])
        c2 = vcopy(r1r, r1[0:1, w - 1::-1], waits=[c1])
        tok = c2
        if want_r2:
            r12r = rowb_t[0:1, 1536:1536 + w]
            tok = vcopy(r12r, r12[0:1, w - 1::-1], waits=[c2])
            return mur, r1r, r12r, tok
        return mur, r1r, None, tok

    bc_last = {"mub": [], "r12b": []}

    def bcast_row(row_ap, w, deps, dst=None):
        ps, pw, pdone = pp.get()
        t1 = mm(ps[:, 0:w], ones1_t[0:1, :], row_ap, True, True, list(deps) + pw, "t")
        if dst is None:
            tile, tw, tdone = sp.get()
            t2 = act(tile[:, 0:w], ps[:, 0:w], AF.Copy, waits=[t1] + tw)
            pdone(t2)
            return tile[:, 0:w], t2, tdone
        tile = {"mub": mub_t, "r12b": r12b_t}[dst]
        t2 = act(tile[:, 0:w], ps[:, 0:w], AF.Copy, waits=[t1] + bc_last[dst])
        pdone(t2)

        def tdone(tok, dst=dst):
            bc_last[dst] = [tok]

        return tile[:, 0:w], t2, tdone

    def strip_dma(buf, src_ap, deps):
        return b.dma("sync", buf, src_ap, waits=deps, pub="dw")

    last = {"strip": [], "bcB": [], "bcC": [], "xdbl": [],
            "XS": [[] for _ in range(8)], "UU": [[] for _ in range(8)],
            "DU": [[] for _ in range(8)], "YY": [[] for _ in range(8)],
            "DL": [[] for _ in range(8)], "HS": [[], []], "stash": [],
            "exch": [], "state": [[], []], "sca": [], "scb": [], "sch": [], "htm": [], "fs2": [], "fs2w": [],
            "scn": [], "qxf": [], "qhf": [], "tmrd": [], "xf": [],
            "hseg": [[], []], "w2s": [[], []], "ngb": [],
            "acc1": {i: [] for i in range(8)}}

    def hs_build(layer, out_ap, src_ap, mub, r12b, row_tok, w, deps):
        t1 = vstt(out_ap, src_ap, 1.0, mub, ALU.mult, ALU.subtract,
                  waits=list(deps) + [row_tok])
        t2 = vtt(out_ap, out_ap, r12b, ALU.mult, waits=[t1])
        t3 = vts2(out_ap, out_ap, SM_LNG[layer][:, 0:1].broadcast_to([128, 1]),
                  SM_LNB[layer][:, 0:1], ALU.mult, ALU.add, waits=[t2])
        return t3

    def in_sweep(layer, ph, colrange, w, rev, mub, r12b, row_tok, zpart, out_cb):
        ipw = IN["l0_ipw"] if layer == 0 else IN["l1a_win" if ph == 0 else "l1b_win"]
        c0 = 1024 if zpart else 0
        banks = [pp.get() for _ in range(8)]
        lnga = SM_LNG[layer]
        lnba = SM_LNB[layer]
        mmtoks = [None] * 8
        for k in range(8):
            sd = strip_dma(strip_t[:, 0:1024], ipw[k * 128:(k + 1) * 128, c0:c0 + 1024],
                           last["strip"])
            last["strip"] = []
            hs = HS[k % 2]
            src = xstr[k][:, colrange[0]:colrange[0] + w]
            if rev:
                src = src[:, w - 1::-1]
            hw = list(last["HS"][k % 2])
            t1 = vstt(hs[:, 0:w], src, 1.0, mub, ALU.mult, ALU.subtract,
                      waits=hw + [row_tok])
            t2 = vtt(hs[:, 0:w], hs[:, 0:w], r12b, ALU.mult, waits=[t1])
            t3 = vts2(hs[:, 0:w], hs[:, 0:w], lnga[:, k:k + 1], lnba[:, k:k + 1],
                      ALU.mult, ALU.add, waits=[t2])
            if DBG is not None and DBG[3] == "hs2" and zpart and k < 2:
                t3 = b.dma("gpsimd", yout[k * 128:(k + 1) * 128, 0:w], hs[:, 0:w],
                           waits=[t3], pub="do")
            for m in range(8):
                tile, pw, pdone = banks[m]
                wts = [t3, sd] + (pw if k == 0 else [])
                tok = mm(tile[:, 0:w], strip_t[:, m * 128:(m + 1) * 128],
                         hs[:, 0:w], k == 0, k == 7, wts,
                         "t" if (k == 7 or m == 7) else None)
                if k == 7:
                    mmtoks[m] = tok
                if m == 7:
                    last["HS"][k % 2] = [tok]
                    last["strip"] = [tok]
        for m in range(8):
            tile, pw, pdone = banks[m]
            out_cb(m, tile, mmtoks[m], pdone)

    def attn_chunk(layer, ph, ci, stok):
        rev = ph == 1
        sc0 = ci * CH
        dtk = 64 if layer == 0 else 32
        xw = 96 if layer == 0 else 64
        st_t = stA_t if ph == 0 else stB_t
        mu, r1, r12, rtok = ln_rows(ci * 2, 2, [stok], rev, True)
        mub, mtk, mubd = bcast_row(mu, CH, [rtok], dst="mub")
        r12b, rtk, r12bd = bcast_row(r12, CH, [rtok], dst="r12b")
        row_tok = rtk if rtk[1] > mtk[1] else mtk

        def xcb(m, tile, mmtok, pdone):
            e = act(XS[m][:, 3:3 + CH], tile[:, 0:CH], AF.Copy,
                    waits=[mmtok] + last["XS"][m])
            last["XS"][m] = []
            pdone(e)
            h1 = vcopy(XS[m][:, 0:3], stash_t[:, m * 4:m * 4 + 3],
                       waits=last["stash"] + [e])
            h2 = vcopy(stash_t[:, m * 4:m * 4 + 3], XS[m][:, CH:CH + 3], waits=[e, h1])
            cw = SM_CONVW(ph)
            c = vts1(UU[m][:, 0:CH], XS[m][:, 0:CH], cw[:, m * 4:m * 4 + 1], ALU.mult,
                     waits=[h1] + last["UU"][m])
            last["UU"][m] = []
            for kk in range(1, 4):
                c = vstt(UU[m][:, 0:CH], XS[m][:, kk:kk + CH], cw[:, m * 4 + kk:m * 4 + kk + 1],
                         UU[m][:, 0:CH], ALU.mult, ALU.add, waits=[c])
            s = act(UU[m][:, 0:CH], UU[m][:, 0:CH], AF.Silu,
                    bias=SM_CONVB(ph)[:, m:m + 1], waits=[c])
            last["XS"][m] = [c]
            last["stash"] = last["stash"] + [h2]
            last["UU"][m] = [s]

        in_sweep(layer, ph, (sc0,), CH, rev, mub, r12b, row_tok, False, xcb)
        u_toks = [last["UU"][m][0] for m in range(8)]

        xpw = IN[("l0" if layer == 0 else "l1") + ("a" if ph == 0 else "b") + "_xpw"]
        sd = strip_dma(strip_t[:, 0:8 * xw].rearrange("p (kb j) -> p kb j", j=xw),
                       xpw.rearrange("(kb p) j -> p kb j", p=128), last["strip"])
        last["strip"] = []
        psd, pwd, pdd = pp.get()
        psb, pwb, pdb = pp.get()
        psc, pwc, pdc = pp.get()
        tD = tB = tC = None
        for kb in range(8):
            base = kb * xw
            wts = [sd] + u_toks
            tD = mm(psd[0:dtk, 0:CH], strip_t[:, base:base + dtk], UU[kb][:, 0:CH],
                    kb == 0, kb == 7, wts + (pwd if kb == 0 else []),
                    "t" if kb == 7 else None)
            tB = mm(psb[0:16, 0:CH], strip_t[:, base + dtk:base + dtk + 16],
                    UU[kb][:, 0:CH], kb == 0, kb == 7, pwb if kb == 0 else (),
                    "t" if kb == 7 else None)
            tC = mm(psc[0:16, 0:CH], strip_t[:, base + dtk + 16:base + dtk + 32],
                    UU[kb][:, 0:CH], kb == 0, kb == 7, pwc if kb == 0 else (),
                    "t" if kb == 7 else None)
        last["strip"] = [tD, tB, tC]
        eD = act(xdbl_t[0:dtk, 0:CH], psd[0:dtk, 0:CH], AF.Copy,
                 waits=[tD] + last["xdbl"])
        last["xdbl"] = []
        eB = act(bcB_t[0:16, 0:CH], psb[0:16, 0:CH], AF.Copy, waits=[tB] + last["bcB"])
        last["bcB"] = []
        eC = act(bcC_t[0:16, 0:CH], psc[0:16, 0:CH], AF.Copy, waits=[tC] + last["bcC"])
        last["bcC"] = []
        pdd(eD)
        pdb(eB)
        pdc(eC)

        dtw = IN[("l0" if layer == 0 else "l1") + ("a" if ph == 0 else "b") + "_dtw"]
        sd2 = strip_dma(strip_t[0:dtk, 0:1024], dtw, last["strip"])
        last["strip"] = []
        dtmm = []
        for m in range(8):
            tile, pw, pdone = pp.get()
            tmk = mm(tile[0:128, 0:CH], strip_t[0:dtk, m * 128:(m + 1) * 128],
                     xdbl_t[0:dtk, 0:CH], True, True, [sd2, eD] + pw, "t")
            tmp, tw, tdone = sp.get()
            a1 = act(tmp[:, 0:CH], tile[:, 0:CH], AF.Exp,
                     bias=SM_DTB(ph)[:, m:m + 1], waits=[tmk] + tw)
            pdone(a1)
            a2 = act(DL[m][:, 0:CH], tmp[:, 0:CH], AF.Ln, bias=1.0,
                     waits=[a1] + last["DL"][m])
            last["DL"][m] = []
            tdone(a2)
            dtmm.append(tmk)
            d1 = vtt(DU[m][:, 0:CH], DL[m][:, 0:CH], UU[m][:, 0:CH], ALU.mult,
                     waits=[a2] + last["DU"][m])
            y1 = vts1(YY[m][:, 0:CH], UU[m][:, 0:CH], SM_DP(ph)[:, m:m + 1], ALU.mult,
                      waits=last["YY"][m])
            last["DU"][m] = [d1]
            last["YY"][m] = [y1]
            last["DL"][m] = [d1]
        last["strip"] = last["strip"] + [dtmm[-1]]
        last["xdbl"] = [dtmm[-1]]

        amat = SM_AMAT(ph)
        for n in range(16):
            pB, pwB, pdB = pp.get()
            t1 = mm(pB[:, 0:CH], coh_t[0:16, n * 128:(n + 1) * 128], bcB_t[0:16, 0:CH],
                    True, True, [eB] + pwB, "t")
            Bt, Bw, Bd = sp.get()
            e1 = act(Bt[:, 0:CH], pB[:, 0:CH], AF.Copy, waits=[t1] + Bw)
            pdB(e1)
            pC, pwC, pdC = pp.get()
            t2 = mm(pC[:, 0:CH], coh_t[0:16, n * 128:(n + 1) * 128], bcC_t[0:16, 0:CH],
                    True, True, [eC] + pwC, "t")
            Ct, Cw, Cd = sp.get()
            e2 = act(Ct[:, 0:CH], pC[:, 0:CH], AF.Copy, waits=[t2] + Cw)
            pdC(e2)
            lb = lc = None
            for m in range(8):
                col = m * 16 + n
                ea = act(SCA, DL[m][:, 0:CH], AF.Exp, scale=amat[:, col:col + 1],
                         waits=last["DL"][m] + last["sca"])
                bb = gtt(SCB, DU[m][:, 0:CH], Bt[:, 0:CH], ALU.mult,
                         waits=[e1] + last["DU"][m] + last["scb"])
                sc = vscan(SCH, SCA, SCB, st_t[:, col:col + 1],
                           waits=[ea, bb] + last["state"][ph] + last["sch"])
                last["state"][ph] = []
                stc = vcopy(st_t[:, col:col + 1], SCH[:, CH - 1:CH], waits=[sc])
                tm2 = vtt(SCA, SCH, Ct[:, 0:CH], ALU.mult, waits=[sc, e2])
                ya = vtt(YY[m][:, 0:CH], YY[m][:, 0:CH], SCA, ALU.add,
                         waits=[tm2] + last["YY"][m])
                last["YY"][m] = [ya]
                last["DL"][m] = [ea]
                last["DU"][m] = [bb]
                last["sca"] = [ya]
                last["scb"] = [sc]
                last["sch"] = [tm2, stc]
                lb, lc = ya, stc
            Bd(lb)
            Cd(lb)
            last["state"][ph].append(lc)

        if DBG is not None and DBG[3] == "y0" and layer == DBG[0] and ph == DBG[1] and ci == DBG[2]:
            for m in range(8):
                dt_ = b.dma("gpsimd", yout[m * 128:(m + 1) * 128, 0:CH], YY[m][:, 0:CH],
                            waits=last["YY"][m], pub="do")
                last["YY"][m] = last["YY"][m] + [dt_]
            b.barrier_wait("gpsimd", [("do", b.cnt["do"])])

        def zcb(m, tile, mmtok, pdone):
            szt, zw, zd = sp.get()
            s = act(szt[:, 0:CH], tile[:, 0:CH], AF.Silu, waits=[mmtok] + zw)
            pdone(s)
            if DBG is not None and DBG[3] in ("sz", "szx") and (layer, ph, ci) == DBG[:3]:
                s = b.dma("gpsimd", yout[m * 128:(m + 1) * 128, 0:CH], szt[:, 0:CH],
                          waits=[s], pub="do")
            g = vtt(YY[m][:, 0:CH], YY[m][:, 0:CH], szt[:, 0:CH], ALU.mult,
                    waits=[s] + last["YY"][m])
            zd(g)
            last["YY"][m] = [g]

        zp_flag = not (DBG is not None and DBG[3] == "szx")
        in_sweep(layer, ph, (sc0,), CH, rev, mub, r12b, row_tok, zp_flag, zcb)
        y_toks = [last["YY"][m][0] for m in range(8)]
        mubd(y_toks[7])
        r12bd(y_toks[7])

        if DBG is not None and DBG[3] in ("u", "dl", "du", "y", "xs") and layer == DBG[0] and ph == DBG[1] and ci == DBG[2]:
            srcs = {"u": UU, "dl": DL, "du": DU, "y": YY, "xs": None}[DBG[3]]
            for m in range(8):
                if DBG[3] == "xs":
                    ap = XS[m][:, 3:3 + CH]
                else:
                    ap = srcs[m][:, 0:CH]
                dt_ = b.dma("gpsimd", yout[m * 128:(m + 1) * 128, 0:CH], ap,
                            waits=[y_toks[m]], pub="do")
                key = "UU" if DBG[3] == "u" else ("DL" if DBG[3] == "dl" else
                      ("DU" if DBG[3] == "du" else ("YY" if DBG[3] == "y" else "XS")))
                last[key][m] = last[key][m] + [dt_]
            b.barrier_wait("gpsimd", [("do", b.cnt["do"])])
        return y_toks

    def l0_outproj(ph, ci, y_toks):
        sc0 = ci * CH
        opm = []
        for m in range(8):
            sdw = strip_dma(strip_t[:, 0:1024].rearrange("p (kb j) -> p kb j", j=128),
                            IN["l0_opw"][:, m * 128:(m + 1) * 128]
                            .rearrange("(kb p) j -> p kb j", p=128), last["strip"])
            last["strip"] = []
            tile, pw, pdone = pp.get()
            tk = None
            for kb in range(8):
                tk = mm(tile[:, 0:CH], strip_t[:, kb * 128:(kb + 1) * 128],
                        YY[kb][:, 0:CH], kb == 0, kb == 7,
                        ([sdw] + y_toks + pw) if kb == 0 else (),
                        "t" if kb == 7 else None)
            e = act(UU[m][:, 0:CH], tile[:, 0:CH], AF.Copy, waits=[tk] + last["UU"][m])
            pdone(e)
            last["strip"] = [tk]
            last["UU"][m] = [e]
            opm.append(e)
        for m in range(8):
            sdw = strip_dma(strip_t[:, 0:1024].rearrange("p (kb j) -> p kb j", j=128),
                            IN["l0_fcw"][:, m * 128:(m + 1) * 128]
                            .rearrange("(kb p) j -> p kb j", p=128), last["strip"])
            last["strip"] = []
            tile, pw, pdone = pp.get()
            tk = None
            for kb in range(8):
                tk = mm(tile[:, 0:CH], strip_t[:, kb * 128:(kb + 1) * 128],
                        UU[kb][:, 0:CH], kb == 0, kb == 7,
                        ([sdw] + opm + pw) if kb == 0 else (),
                        "t" if kb == 7 else None)
            acc_sl = accv[m][:, sc0:sc0 + CH]
            if ph == 0:
                e = act(acc_sl, tile[:, 0:CH], AF.Copy, waits=[tk])
                pdone(e)
            else:
                tmp, tw, tdone = sp.get()
                e0 = act(tmp[:, 0:CH], tile[:, 0:CH], AF.Copy, waits=[tk] + tw)
                pdone(e0)
                e = vtt(acc_sl, acc_sl, tmp[:, CH - 1::-1], ALU.add, waits=[e0])
                tdone(e)
            last["strip"] = [tk]
            last["UU"][m].append(tk)

    def halo_mini(layer, src24, src_tok):
        ps, pw, pdone = pp.get()
        mt = None
        sq, sw, sqd = sp.get()
        sq_toks = []
        for db in range(8):
            src = src24[:, db * 3:db * 3 + 3]
            mt = mm(ps[0:128, 0:1][0:3, 0:1] if False else ps[0:3, 0:1],
                    src, w1024_t[:, 0:1], db == 0, db == 7,
                    ([src_tok] + pw) if db == 0 else (), "t" if db == 7 else None)
            sq_toks.append(act(sq[:, db * 4:db * 4 + 3], src, AF.Square,
                               waits=[src_tok] + (sw if db == 0 else [])))
        ps2, pw2, pdone2 = pp.get()
        qt = None
        for db in range(8):
            qt = mm(ps2[0:3, 0:1], sq[:, db * 4:db * 4 + 3], w1024_t[:, 0:1],
                    db == 0, db == 7, [sq_toks[db]] + (pw2 if db == 0 else []),
                    "t" if db == 7 else None)
        sqd(qt)
        ps3, pw3, pdone3 = pp.get()
        t1 = tr(ps3[0:1, 0:3], ps[0:3, 0:1], [mt] + pw3, idn=ident_t[0:3, 0:3])
        return None

    def halo_mini2(layer, ph, src24, src_tok):
        mcol, mw, mdone = sp.get()
        ps, pw, pdone = pp.get()
        mt = None
        sq_toks = []
        sq, sw, sqd = sp.get()
        for db in range(8):
            src = src24[:, db * 3:db * 3 + 3]
            mt = mm(ps[0:3, 0:1], src, w1024_t[:, 0:1], db == 0, db == 7,
                    ([src_tok] + pw) if db == 0 else (), "t" if db == 7 else None)
            sq_toks.append(act(sq[:, db * 4:db * 4 + 3], src, AF.Square,
                               waits=[src_tok] + (sw if db == 0 else [])))
        e1 = act(mcol[0:3, 0:1], ps[0:3, 0:1], AF.Copy, waits=[mt] + mw)
        pdone(e1)
        ps2, pw2, pdone2 = pp.get()
        qt = None
        for db in range(8):
            qt = mm(ps2[0:3, 0:1], sq[:, db * 4:db * 4 + 3], w1024_t[:, 0:1],
                    db == 0, db == 7, [sq_toks[db]] + (pw2 if db == 0 else []),
                    "t" if db == 7 else None)
        e2 = act(mcol[0:3, 1:2], ps2[0:3, 0:1], AF.Copy, waits=[qt])
        pdone2(e2)
        sqd(qt)
        ps3, pw3, pdone3 = pp.get()
        t1 = tr(ps3[0:1, 0:3], mcol[0:3, 0:1], [e1] + pw3, idn=ident_t[0:3, 0:3])
        r1 = act(hrow_t[0:1, 0:3], ps3[0:1, 0:3], AF.Copy, waits=[t1])
        pdone3(r1)
        ps4, pw4, pdone4 = pp.get()
        t2 = tr(ps4[0:1, 0:3], mcol[0:3, 1:2], [e2] + pw4, idn=ident_t[0:3, 0:3])
        r2 = act(hrow_t[0:1, 4:7], ps4[0:1, 0:3], AF.Copy, waits=[t2])
        pdone4(r2)
        mdone(t2)
        mu3 = hrow_t[0:1, 0:3]
        q3 = hrow_t[0:1, 4:7]
        v3 = hrow_t[0:1, 8:11]
        r12_3 = hrow_t[0:1, 12:15]
        x1 = vtt(v3, mu3, mu3, ALU.mult, waits=[r1, r2])
        x2 = vtt(q3, q3, v3, ALU.subtract, waits=[x1])
        x3 = vcopy(v3, q3, waits=[x2])
        x4 = act(q3, q3, AF.Sqrt, bias=EPS, waits=[x3])
        x5 = vrecip(q3, q3, waits=[x4])
        x6 = vtt(r12_3, q3, q3, ALU.mult, waits=[x5])
        x7 = vtt(r12_3, r12_3, v3, ALU.mult, waits=[x6])
        x8 = act(r12_3, r12_3, AF.Sqrt, bias=EPS, waits=[x7])
        x9 = vrecip(r12_3, r12_3, waits=[x8])
        x10 = vtt(r12_3, r12_3, q3, ALU.mult, waits=[x9])
        mb3, mb3t, mb3d = bcast_row(mu3, 3, [x10], dst="mub")
        rb3, rb3t, rb3d = bcast_row(r12_3, 3, [x10], dst="r12b")
        ipw = IN["l0_ipw"] if layer == 0 else IN["l1b_win"]
        hst, hw2, hsd = sp.get()
        banks = [pp.get() for _ in range(8)]
        mmt = [None] * 8
        hst_readers = list(hw2)
        for k in range(8):
            sd = strip_dma(strip_t[:, 0:1024], ipw[k * 128:(k + 1) * 128, 0:1024],
                           last["strip"])
            last["strip"] = []
            h1 = vstt(hst[:, 0:3],
                      src24[:, k * 3:k * 3 + 3], 1.0, mb3, ALU.mult, ALU.subtract,
                      waits=[mb3t, src_tok] + hst_readers)
            hst_readers = []
            h2 = vtt(hst[:, 0:3], hst[:, 0:3], rb3, ALU.mult, waits=[h1, rb3t])
            h3 = vts2(hst[:, 0:3], hst[:, 0:3], SM_LNG[layer][:, k:k + 1],
                      SM_LNB[layer][:, k:k + 1], ALU.mult, ALU.add, waits=[h2])
            for m in range(8):
                tile, pw, pdone = banks[m]
                tok = mm(tile[:, 0:3], strip_t[:, m * 128:(m + 1) * 128],
                         hst[:, 0:3], k == 0, k == 7,
                         ([h3, sd] + pw) if k == 0 else [h3, sd],
                         "t" if (k == 7 or m == 7) else None)
                if k == 7:
                    mmt[m] = tok
                if m == 7:
                    last["strip"] = [tok]
                    hst_readers = [tok]
        hsd(mmt[7])
        mb3d(mmt[7])
        rb3d(mmt[7])
        sttoks = []
        for m in range(8):
            tile, pw, pdone = banks[m]
            e = act(stash_t[:, m * 4:m * 4 + 3], tile[:, 0:3], AF.Copy,
                    waits=[mmt[m]] + last["stash"])
            pdone(e)
            sttoks.append(e)
        last["stash"] = sttoks

    def exchange(nm, payload_ap, pay_w, pay_tok, out_ap):
        ain, aout = AR[nm]
        t0 = vts1(exch_t[:, 0:pay_w], payload_ap, pe_t[:, 0:1], ALU.mult,
                  waits=[pay_tok] + last["exch"])
        t1 = vts1(exch_t[:, 256:256 + pay_w], payload_ap, po_t[:, 0:1], ALU.mult,
                  waits=[pay_tok])
        d0 = b.dma("gpsimd", ain[:, 0:pay_w], exch_t[:, 0:pay_w], waits=[t0], pub="de")
        d1 = b.dma("gpsimd", ain[:, pay_w:2 * pay_w], exch_t[:, 256:256 + pay_w],
                   waits=[t1], pub="de")
        cc = b.emit("gpsimd",
                    lambda h, i=ain, o=aout: h.collective_compute(
                        "AllReduce", ALU.add, replica_groups=PAIRS,
                        ins=[i.opt()], outs=[o.opt()]),
                    [d0, d1], "cc", 1)
        d2 = b.dma("gpsimd", exch_t[:, 0:pay_w], aout[:, 0:pay_w], waits=[cc], pub="de")
        d3 = b.dma("gpsimd", exch_t[:, 256:256 + pay_w], aout[:, pay_w:2 * pay_w],
                   waits=[cc], pub="de")
        c0 = vts1(out_ap, exch_t[:, 0:pay_w], po_t[:, 0:1], ALU.mult, waits=[d2])
        c1 = vstt(out_ap, exch_t[:, 256:256 + pay_w], pe_t[:, 0:1], out_ap,
                  ALU.mult, ALU.add, waits=[d3, c0])
        last["exch"] = [c1]
        return c1

    def stage_barrier():
        toks = [(s, b.cnt[s]) for s in SEM_NAMES if b.cnt[s] > 0 and s != "cc"]
        toks += [(s, b.cnt[s]) for s in ("cc",) if b.cnt[s] > 0]
        for e in ENGS:
            b.barrier_wait(e, toks)
        return toks

    def quant128(tm_ap, w, deps):
        nb = w // 16
        t_abs = act(FS1[:, 0:w], tm_ap, AF.Abs, waits=deps)
        scl = scn_t[:, 0:nb]
        scl2 = scn_t[:, 256:256 + nb]
        rscl = scn_t[:, 512:512 + nb]
        t1 = vreduce(scl.rearrange("p (nb one) -> p nb one", one=1),
                     tm_ap.rearrange("p (nb w2) -> p nb w2", w2=16),
                     ALU.max, absval=True, waits=deps + last["scn"])
        t2 = vts2(scl2, scl, 1.0 / 6.0, 1e-12, ALU.mult, ALU.max, waits=[t1])
        t3 = vrecip(rscl, scl2, waits=[t2])
        rexp = rscl.unsqueeze(2).broadcast_to([128, nb, 16])
        sexp = scl2.unsqueeze(2).broadcast_to([128, nb, 16])
        f3 = FS1[:, 0:w].rearrange("p (nb w2) -> p nb w2", w2=16)
        t4 = vtt(f3, f3, rexp, ALU.mult, waits=[t3, t_abs])
        xn = FS1[:, 0:w]
        a0 = vts1(FS2[:, 0:w], xn, 2.5, ALU.is_gt, waits=[t4] + last["fs2"])
        a1 = vstt(FS2[:, 0:w], xn, 3.5, FS2[:, 0:w], ALU.is_gt, ALU.add, waits=[a0])
        a2 = vstt(FS2[:, 0:w], xn, 5.0, FS2[:, 0:w], ALU.is_gt, ALU.add, waits=[a1])
        a3 = vstt(FS2[:, 0:w], xn, 5.0, FS2[:, 0:w], ALU.is_gt, ALU.add, waits=[a2])
        a4 = vts1(FS2[:, 0:w], FS2[:, 0:w], 2.0, ALU.mult, waits=[a3])
        a5 = vstt(FS2[:, 0:w], xn, 0.25, FS2[:, 0:w], ALU.is_gt, ALU.add, waits=[a4])
        a6 = vstt(FS2[:, 0:w], xn, 0.75, FS2[:, 0:w], ALU.is_gt, ALU.add, waits=[a5])
        a7 = vstt(FS2[:, 0:w], xn, 1.25, FS2[:, 0:w], ALU.is_gt, ALU.add, waits=[a6])
        a8 = vstt(FS2[:, 0:w], xn, 1.75, FS2[:, 0:w], ALU.is_gt, ALU.add, waits=[a7])
        a9 = vts1(FS2[:, 0:w], FS2[:, 0:w], 0.5, ALU.mult, waits=[a8])
        sg = act(FS1[:, 0:w], tm_ap, AF.Sign, waits=[a8])
        q1 = vtt(FS2[:, 0:w], FS2[:, 0:w], FS1[:, 0:w], ALU.mult, waits=[a9, sg])
        f2 = FS2[:, 0:w].rearrange("p (nb w2) -> p nb w2", w2=16)
        q2 = vtt(f2, f2, sexp, ALU.mult, waits=[q1])
        last["scn"] = [q2]
        last["fs2"] = []
        last["tmrd"] = [q2, sg]
        return q2

    def wq_pass():
        specs = [(IN["f0_gw"], qgw, 22, 1024), (IN["f0_uw"], quw, 22, 1024),
                 (IN["f0_dw"], qdw, 8, 2816)]
        stage = FS1[:, 0:128]
        for wap, qap, nrt, C in specs:
            for r in range(nrt):
                d0 = b.dma("sync", HTM[:, 0:C], wap[r * 128:(r + 1) * 128, :],
                           waits=last["htm"], pub="dw")
                last["htm"] = []
                qtoks = []
                for s0 in range(0, C, 1024):
                    w = min(1024, C - s0)
                    qt = quant128(HTM[:, s0:s0 + w], w, [d0] + last["fs2w"])
                    for c in range(w // 128):
                        ps, pw, pdone = pp.get()
                        tt_ = tr(ps[:, 0:128], FS2[:, c * 128:(c + 1) * 128], [qt] + pw)
                        ek = act(r12b_t[:, 0:128], ps[:, 0:128], AF.Copy,
                                 waits=[tt_] + bc_last["r12b"])
                        pdone(ek)
                        bc_last["r12b"] = [ek]
                        dk = b.dma("gpsimd", qap[s0 + c * 128:s0 + (c + 1) * 128,
                                               r * 128:(r + 1) * 128],
                                   r12b_t[:, 0:128], waits=[ek], pub="d4")
                        bc_last["r12b"] = [ek, dk]
                        qtoks.append(dk)
                    last["fs2w"] = [qtoks[-1]] if qtoks else []
                last["htm"] = qtoks
        last["fs2w"] = []

    def f0_ffn(bar):
        sttoks = stats_pass(xstr, bar)
        stok = max(sttoks, key=lambda t: t[1])
        for ci in range(16):
            mu, r1, _, rtok = ln_rows(ci, 1, [stok], False, False)
            mub, mtk, mubd = bcast_row(mu, 128, [rtok], dst="mub")
            r1b, rtk, r1bd = bcast_row(r1, 128, [rtok], dst="r12b")
            rowt = rtk if rtk[1] > mtk[1] else mtk
            c0 = ci * 128
            etoks = []
            for k in range(8):
                lt, lw, ld_ = sp.get()
                t1 = vstt(lt[:, 0:128], xstr[k][:, c0:c0 + 128], 1.0, mub,
                          ALU.mult, ALU.subtract, waits=[rowt] + lw)
                t2 = vtt(lt[:, 0:128], lt[:, 0:128], r1b, ALU.mult, waits=[t1])
                ps, pw, pdone = pp.get()
                t3 = tr(ps[:, 0:128], lt[:, 0:128], [t2] + pw)
                ld_(t3)
                e = act(HTM[:, k * 128:(k + 1) * 128], ps[:, 0:128], AF.Copy,
                        waits=[t3] + last["htm"])
                last["htm"] = []
                pdone(e)
                etoks.append(e)
            qt = quant128(HTM[:, 0:1024], 1024, etoks)
            if DBG is not None and DBG[3] == "f0x" and ci == 0:
                d1 = b.dma("gpsimd", yout[0:128, 0:1024], HTM[:, 0:1024],
                           waits=etoks, pub="do")
                d2 = b.dma("gpsimd", yout[128:256, 0:1024], FS2[:, 0:1024],
                           waits=[qt], pub="do")
                last["htm"] = last["htm"] + [d1]
                last["tmrd"] = last["tmrd"] + [d1]
                last["fs2"] = last["fs2"] + [d2]
            qxtoks = []
            for k in range(8):
                ps, pw, pdone = pp.get()
                t3 = tr(ps[:, 0:128], FS2[:, k * 128:(k + 1) * 128], [qt] + pw)
                e = act(QXF[k][:, 0:128], ps[:, 0:128], AF.Copy,
                        waits=[t3] + last["qxf"])
                pdone(e)
                qxtoks.append(e)
            last["qxf"] = []
            last["fs2"] = qxtoks
            htoks = []
            for m in range(22):
                sd1 = strip_dma(strip_t[:, 0:1024].rearrange("p (kb j) -> p kb j", j=128),
                                qgw[:, m * 128:(m + 1) * 128]
                                .rearrange("(kb p) j -> p kb j", p=128), last["strip"])
                pg, pwg, pdg = pp.get()
                tg = None
                for k in range(8):
                    tg = mm(pg[:, 0:128], strip_t[:, k * 128:(k + 1) * 128],
                            QXF[k][:, 0:128], k == 0, k == 7,
                            ([sd1] + qxtoks + pwg) if k == 0 else ())
                sd2 = strip_dma(strip_t[:, 0:1024].rearrange("p (kb j) -> p kb j", j=128),
                                quw[:, m * 128:(m + 1) * 128]
                                .rearrange("(kb p) j -> p kb j", p=128), [tg])
                pu, pwu, pdu = pp.get()
                tu = None
                for k in range(8):
                    tu = mm(pu[:, 0:128], strip_t[:, k * 128:(k + 1) * 128],
                            QXF[k][:, 0:128], k == 0, k == 7,
                            ([sd2] + pwu) if k == 0 else ())
                last["strip"] = [tu]
                g1, gw1, gd1 = sp.get()
                sg_ = act(g1[:, 0:128], pg[:, 0:128], AF.Silu, waits=[tg] + gw1)
                pdg(sg_)
                g2, gw2, gd2 = sp.get()
                su_ = act(g2[:, 0:128], pu[:, 0:128], AF.Copy, waits=[tu] + gw2)
                pdu(su_)
                hv = vtt(g1[:, 0:128], g1[:, 0:128], g2[:, 0:128], ALU.mult,
                         waits=[sg_, su_])
                gd2(hv)
                ps, pw, pdone = pp.get()
                t3 = tr(ps[:, 0:128], g1[:, 0:128], [hv] + pw)
                gd1(t3)
                e = act(HTM[:, m * 128:(m + 1) * 128], ps[:, 0:128], AF.Copy,
                        waits=[t3] + last["tmrd"])
                pdone(e)
                htoks.append(e)
            if DBG is not None and DBG[3] == "f0h" and ci == 0:
                for j in range(3):
                    w = 1024 if j < 2 else 768
                    dj = b.dma("gpsimd", yout[j * 128:(j + 1) * 128, 0:w],
                               HTM[:, j * 1024:j * 1024 + w], waits=htoks, pub="do")
                    htoks = htoks + [dj]
            qhtoks = []
            for s0, w in ((0, 1024), (1024, 1024), (2048, 768)):
                qt = quant128(HTM[:, s0:s0 + w], w, htoks)
                for c in range(w // 128):
                    mi = (s0 // 128) + c
                    ps, pw, pdone = pp.get()
                    t3 = tr(ps[:, 0:128], FS2[:, c * 128:(c + 1) * 128], [qt] + pw)
                    e = act(QHF[mi][:, 0:128], ps[:, 0:128], AF.Copy,
                            waits=[t3] + last["qhf"])
                    pdone(e)
                    qhtoks.append(e)
                last["fs2"] = qhtoks[-8:]
            last["qhf"] = []
            dtoks = []
            for og in range(8):
                po_, pwo, pdo = pp.get()
                to = None
                ki = 0
                for kg0, kgn in ((0, 8), (8, 8), (16, 6)):
                    sd = strip_dma(
                        strip_t[:, 0:kgn * 128].rearrange("p (kb j) -> p kb j", j=128),
                        qdw[kg0 * 128:(kg0 + kgn) * 128, og * 128:(og + 1) * 128]
                        .rearrange("(kb p) j -> p kb j", p=128), last["strip"])
                    for kk in range(kgn):
                        mi = kg0 + kk
                        to = mm(po_[:, 0:128], strip_t[:, kk * 128:(kk + 1) * 128],
                                QHF[mi][:, 0:128], mi == 0, mi == 21,
                                ([sd] + qhtoks + pwo) if mi == 0 else [sd])
                    last["strip"] = [to]
                x_sl = xstr[og][:, c0:c0 + 128]
                e = vstt(x_sl, po_[:, 0:128], 1.0, x_sl, ALU.mult, ALU.add,
                         waits=[to])
                pdo(e)
                dtoks.append(e)
            last["qhf"] = qhtoks
            last["qxf"] = qxtoks
            last["htm"] = htoks + dtoks
            mubd(dtoks[-1])
            r1bd(dtoks[-1])
        return ("v", b.cnt["v"])

    def l0_attn():
        smtok = load_branch_smalls(0, [XTOK])
        sttoks = stats_pass(xstr, [XTOK] + smtok)
        stok = max(sttoks, key=lambda t: t[1])
        z0 = vmemset(stash_t[:, :], 0.0, waits=last["stash"])
        z1 = vmemset(stA_t[:, :], 0.0, waits=last["state"][0])
        last["stash"] = [z0]
        last["state"][0] = [z1]
        for ci in range(NCH):
            yt = attn_chunk(0, 0, ci, stok)
            l0_outproj(0, ci, yt)
        exch_tok = exchange("ar1", stA_t[:, :], 128,
                            last["state"][0][-1] if last["state"][0] else z1,
                            stB_t[:, :])
        last["state"][1] = [exch_tok]
        halo_mini2(0, 1, xh_t[:, :], XTOK)
        for ci in reversed(range(NCH)):
            yt = attn_chunk(0, 1, ci, stok)
            l0_outproj(1, ci, yt)
        for ci in range(NCH):
            sc0 = ci * CH
            mu, r1, _, rtok = ln_rows(ci * 2, 2, [stok], False, False)
            mub, mtk, mubd = bcast_row(mu, CH, [rtok], dst="mub")
            r1b, rtk, r1bd = bcast_row(r1, CH, [rtok], dst="r12b")
            lt, lw, ld = sp.get()
            e = None
            for m in range(8):
                x_sl = xstr[m][:, sc0:sc0 + CH]
                acc_sl = accv[m][:, sc0:sc0 + CH]
                t1 = vstt(lt[:, 0:CH], x_sl, 1.0, mub, ALU.mult, ALU.subtract,
                          waits=[mtk, rtk] + (lw if m == 0 else []))
                t2 = vtt(lt[:, 0:CH], lt[:, 0:CH], r1b, ALU.mult, waits=[t1])
                t3 = vstt(acc_sl, acc_sl, SM_FCB[:, m:m + 1], lt[:, 0:CH],
                          ALU.add, ALU.add, waits=[t2])
                e = vtt(x_sl, x_sl, acc_sl, ALU.add, waits=[t3])
            ld(e)
            mubd(e)
            r1bd(e)
        return ("v", b.cnt["v"])

    def l1_wout(ph, ci, y_toks):
        sc0 = ci * CH
        wout = IN["l1a_wout" if ph == 0 else "l1b_wout"]
        for mi in range(4):
            sdw = strip_dma(strip_t[:, 0:1024].rearrange("p (kb j) -> p kb j", j=128),
                            wout[:, mi * 128:(mi + 1) * 128]
                            .rearrange("(kb p) j -> p kb j", p=128), last["strip"])
            last["strip"] = []
            tile, pw, pdone = pp.get()
            tk = None
            for kb in range(8):
                tk = mm(tile[:, 0:CH], strip_t[:, kb * 128:(kb + 1) * 128],
                        YY[kb][:, 0:CH], kb == 0, kb == 7,
                        ([sdw] + y_toks + pw) if kb == 0 else ())
            last["strip"] = [tk]
            tmp, tw, tdone = sp.get()
            e0 = act(tmp[:, 0:CH], tile[:, 0:CH], AF.Copy, waits=[tk] + tw)
            pdone(e0)
            acc_sl = accv[mi][:, sc0:sc0 + CH]
            stg, sw2, sd2 = sp.get()
            if ph == 0:
                t1 = vts1(acc_sl, tmp[:, 0:CH], pe_t[:, 0:1], ALU.mult, waits=[e0])
                t2 = vts1(stg[:, 0:CH], tmp[:, 0:CH], po_t[:, 0:1], ALU.mult,
                          waits=[e0] + sw2)
                dk = b.dma("gpsimd", AR["ar4"][0][:, mi * 2048 + sc0:mi * 2048 + sc0 + CH],
                           stg[:, 0:CH], waits=[t2], pub="d4")
            else:
                t1 = vstt(acc_sl, tmp[:, CH - 1::-1], po_t[:, 0:1], acc_sl,
                          ALU.mult, ALU.add, waits=[e0])
                t2 = vts1(stg[:, 0:CH], tmp[:, CH - 1::-1], pe_t[:, 0:1], ALU.mult,
                          waits=[e0] + sw2)
                dk = b.dma("gpsimd",
                           AR["ar4"][0][:, 8192 + mi * 2048 + sc0:8192 + mi * 2048 + sc0 + CH],
                           stg[:, 0:CH], waits=[t2], pub="d4")
            tdone(t2)
            sd2(dk)

    def l1_attn(bar):
        smtok = load_branch_smalls(1, bar)
        sttoks = stats_pass(xstr, bar + smtok)
        stok = max(sttoks, key=lambda t: t[1])
        z0 = vmemset(stash_t[:, :], 0.0, waits=last["stash"] + bar)
        z1 = vmemset(stA_t[:, :], 0.0, waits=last["state"][0] + bar)
        last["stash"] = [z0]
        last["state"][0] = [z1]
        for ci in range(NCH):
            yt = attn_chunk(1, 0, ci, stok)
            l1_wout(0, ci, yt)
        exch_tok = exchange("ar3", stA_t[:, :], 128, z1 if not last["state"][0]
                            else last["state"][0][-1], stB_t[:, :])
        last["state"][1] = [exch_tok]
        htoks = []
        for db in range(8):
            htoks.append(vcopy(xh_t[:, db * 3:db * 3 + 3],
                               xstr[db][:, T_HALF - 3:T_HALF], waits=bar))
        h_ex = exchange("ar2", xh_t[:, 0:24], 24, htoks[-1], xh_t[:, 0:24])
        halo_mini2(1, 1, xh_t[:, :], h_ex)
        for ci in reversed(range(NCH)):
            yt = attn_chunk(1, 1, ci, stok)
            l1_wout(1, ci, yt)
        bar2 = stage_barrier()
        cc4 = b.emit("gpsimd",
                     lambda h: h.collective_compute(
                         "AllReduce", ALU.add, replica_groups=PAIRS,
                         ins=[AR["ar4"][0].opt()], outs=[AR["ar4"][1].opt()]),
                     bar2, "cc", 1)
        for mi in range(4):
            for g in range(4):
                c0 = g * 512
                t0t, w0, d0_ = sp.get()
                dk0 = b.dma("gpsimd", t0t[:, 0:512],
                            AR["ar4"][1][:, mi * 2048 + c0:mi * 2048 + c0 + 512],
                            waits=[cc4] + w0, pub="de")
                acc_sl = accv[4 + mi][:, c0:c0 + 512]
                e1 = vts1(acc_sl, t0t[:, 0:512], pe_t[:, 0:1], ALU.mult, waits=[dk0])
                d0_(e1)
                t1t, w1, d1_ = sp.get()
                dk1 = b.dma("gpsimd", t1t[:, 0:512],
                            AR["ar4"][1][:, 8192 + mi * 2048 + c0:8192 + mi * 2048 + c0 + 512],
                            waits=[cc4] + w1, pub="de")
                e2 = vstt(acc_sl, t1t[:, 0:512], po_t[:, 0:1], acc_sl,
                          ALU.mult, ALU.add, waits=[dk1, e1])
                d1_(e2)
        for ci in range(NCH):
            sc0 = ci * CH
            mu, r1, _, rtok = ln_rows(ci * 2, 2, [stok], False, False)
            mub, mtk, mubd = bcast_row(mu, CH, [rtok], dst="mub")
            r1b, rtk, r1bd = bcast_row(r1, CH, [rtok], dst="r12b")
            e = None
            for m in range(8):
                x_sl = xstr[m][:, sc0:sc0 + CH]
                acc_sl = accv[m][:, sc0:sc0 + CH]
                lt, lw, ld_ = sp.get()
                t1 = vstt(lt[:, 0:CH], x_sl, 1.0, mub, ALU.mult, ALU.subtract,
                          waits=[mtk, rtk, ("v", b.cnt["v"])] + lw)
                t2 = vtt(lt[:, 0:CH], lt[:, 0:CH], r1b, ALU.mult, waits=[t1])
                t3 = vtt(lt[:, 0:CH], lt[:, 0:CH], acc_sl, ALU.add, waits=[t2])
                t4 = act(lt[:, 0:CH], lt[:, 0:CH], AF.Gelu, waits=[t3])
                e = vtt(x_sl, x_sl, lt[:, 0:CH], ALU.add, waits=[t4])
                ld_(e)
            mubd(e)
            r1bd(e)
        return ("v", b.cnt["v"])

    b1c_t = wv(9856, 128)

    def IN_B1_COL(m1):
        return b1c_t[:, m1:m1 + 1]

    def f1_ffn(bar):
        sttoks = stats_pass(xstr, bar)
        stok = max(sttoks, key=lambda t: t[1])
        lg = [b.dma("sync", b1c_t, IN["f1_b1"], waits=bar, pub="dw"),
              b.dma("sync", GWS[:, 0:64].rearrange("p (kb j) -> p kb j", j=8),
                    IN["f1_gw"].rearrange("(kb p) j -> p kb j", p=128),
                    waits=bar, pub="dw"),
              b.dma("sync", hrow_t[0:1, 16:24], IN["f1_gb"], waits=bar, pub="dw"),
              b.dma("sync", hrow_t[0:1, 24:32], IN["f1_rms"], waits=bar, pub="dw"),
              b.dma("sync", strip_t[0:8, 0:1024], IN["f1_b2"], waits=bar, pub="dw")]
        b2toks = []
        for m in range(8):
            ps, pw, pdone = pp.get()
            t1 = mm(ps[0:128, 0:1], strip_t[0:8, m * 128:(m + 1) * 128],
                    cs_t[0:8, 1:2], True, True, lg + pw)
            e = act(SM_B2S[:, m:m + 1], ps[:, 0:1], AF.Copy, waits=[t1])
            pdone(e)
            b2toks.append(e)
            last["strip"] = last["strip"] + [t1]
        gbb = sm[:, 416:424]
        rmb = sm[:, 424:432]
        ps, pw, pdone = pp.get()
        t1 = mm(ps[:, 0:8], ones1_t[0:1, :], hrow_t[0:1, 16:24], True, True, lg + pw)
        e1 = act(gbb, ps[:, 0:8], AF.Copy, waits=[t1])
        pdone(e1)
        ps, pw, pdone = pp.get()
        t1 = mm(ps[:, 0:8], ones1_t[0:1, :], hrow_t[0:1, 24:32], True, True, lg + pw)
        e2 = act(rmb, ps[:, 0:8], AF.Copy, waits=[t1])
        pdone(e2)
        ng_done = []
        for g in range(16):
            c0 = g * 128
            mu, r1, _, rtok = ln_rows(g, 1, [stok], False, False)
            mub, mtk, mubd = bcast_row(mu, 128, [rtok], dst="mub")
            r1b, rtk, r1bd = bcast_row(r1, 128, [rtok], dst="r12b")
            ps, pw, pdone = pp.get()
            tk = None
            lts = []
            for k in range(8):
                lt, lw, ld_ = sp.get()
                t1 = vstt(lt[:, 0:128], xstr[k][:, c0:c0 + 128], 1.0, mub,
                          ALU.mult, ALU.subtract, waits=[mtk, rtk] + lw)
                t2 = vtt(lt[:, 0:128], lt[:, 0:128], r1b, ALU.mult, waits=[t1])
                tk = mm(ps[:, 0:8], lt[:, 0:128],
                        GWS[:, 0:64].rearrange("p (kb j) -> p kb j", j=8)[:, k, :],
                        k == 0, k == 7, ([t2] + lg + pw) if k == 0 else [t2])
                ld_(tk)
            gt, gw_, gd = sp.get()
            q0 = b.emit("vector", lambda h, o=gt[:, 0:8], p=ps[:, 0:8], bb=gbb:
                        h.tensor_tensor(o, p, bb, ALU.add), [tk, e1] + gw_, "v", 1)
            pdone(q0)
            q1 = vtt(gt[:, 8:16], gt[:, 0:8], gt[:, 0:8], ALU.mult, waits=[q0])
            q2 = vreduce(gt[:, 16:17].rearrange("p (a b) -> p a b", b=1),
                         gt[:, 8:16].rearrange("p (a b) -> p a b", b=8),
                         ALU.add, waits=[q1])
            q3 = act(gt[:, 17:18], gt[:, 16:17], AF.Sqrt, bias=F32EPS, scale=0.125,
                     waits=[q2])
            q4 = vrecip(gt[:, 18:19], gt[:, 17:18], waits=[q3])
            q5 = b.emit("vector", lambda h, o=gt[:, 0:8], i=gt[:, 0:8], s=gt[:, 18:19]:
                        h.tensor_single_scalar(o, i, s, ALU.mult), [q4], "v", 1)
            q6 = vtt(gt[:, 0:8], gt[:, 0:8], rmb, ALU.mult, waits=[q5, e2])
            q7 = vreduce(gt[:, 16:17].rearrange("p (a b) -> p a b", b=1),
                         gt[:, 0:8].rearrange("p (a b) -> p a b", b=8),
                         ALU.max, waits=[q6])
            q8 = b.emit("vector", lambda h, o=gt[:, 0:8], i=gt[:, 0:8], s=gt[:, 16:17]:
                        h.tensor_single_scalar(o, i, s, ALU.subtract), [q7], "v", 1)
            q9 = act(gt[:, 0:8], gt[:, 0:8], AF.Exp, scale=2.0, waits=[q8])
            q10 = vreduce(gt[:, 16:17].rearrange("p (a b) -> p a b", b=1),
                          gt[:, 0:8].rearrange("p (a b) -> p a b", b=8),
                          ALU.add, waits=[q9])
            q11 = vrecip(gt[:, 17:18], gt[:, 16:17], waits=[q10])
            q12 = vts1(gt[:, 18:19], gt[:, 17:18], 1e-6, ALU.add, waits=[q11])
            q13 = vrecip(gt[:, 19:20], gt[:, 18:19], waits=[q12])
            q14 = vtt(gt[:, 20:21], gt[:, 17:18], gt[:, 19:20], ALU.mult, waits=[q13])
            ps2, pw2, pdone2 = pp.get()
            tt_ = tr(ps2[0:1, 0:128], gt[:, 20:21], [q14] + pw2)
            ek = act(ROWG[0:1, c0:c0 + 128], ps2[0:1, 0:128], AF.Copy,
                     waits=[tt_] + ng_done)
            pdone2(ek)
            gd(ek)
            ng_done = ng_done + [ek]
            mubd(ek)
            r1bd(ek)
        ngtok = ng_done[-1]
        for tc in range(4):
            sc0 = tc * 512
            xftoks = []
            for half in range(2):
                c0 = sc0 + half * 256
                mu, r1, _, rtok = ln_rows(tc * 4 + half * 2, 2, [stok], False, False)
                mub, mtk, mubd = bcast_row(mu, 256, [rtok], dst="mub")
                r1b, rtk, r1bd = bcast_row(r1, 256, [rtok], dst="r12b")
                e = None
                for k in range(8):
                    dst = XF[k][:, half * 256:half * 256 + 256]
                    t1 = vstt(dst, xstr[k][:, c0:c0 + 256], 1.0, mub,
                              ALU.mult, ALU.subtract,
                              waits=[mtk, rtk] + last["xf"] if k == 0 else [mtk, rtk])
                    e = vtt(dst, dst, r1b, ALU.mult, waits=[t1])
                    xftoks.append(e)
                last["xf"] = []
                mubd(e)
                r1bd(e)
            for seg in range(64):
                htoks = []
                w2toks = []
                for j in range(2):
                    m1 = seg * 2 + j
                    sd1 = strip_dma(strip_t[:, 0:1024]
                                    .rearrange("p (kb j2) -> p kb j2", j2=128),
                                    IN["f1_w1"][:, m1 * 128:(m1 + 1) * 128]
                                    .rearrange("(kb p) j2 -> p kb j2", p=128),
                                    last["strip"])
                    ph_, pwh, pdh = pp.get()
                    tk = None
                    for k in range(8):
                        tk = mm(ph_[:, 0:512], strip_t[:, k * 128:(k + 1) * 128],
                                XF[k][:, 0:512], k == 0, k == 7,
                                ([sd1] + xftoks + pwh) if k == 0 else ())
                    last["strip"] = [tk]
                    e = act(HSEG[j][:, 0:512], ph_[:, 0:512], AF.Gelu,
                            bias=IN_B1_COL(m1), waits=[tk] + last["hseg"][j])
                    last["hseg"][j] = []
                    pdh(e)
                    htoks.append(e)
                    w2toks.append(b.dma("sync", W2S[j][:, 0:1024],
                                        IN["f1_w2"][m1 * 128:(m1 + 1) * 128, :],
                                        waits=last["w2s"][j], pub="dw"))
                    last["w2s"][j] = []
                for og in range(8):
                    po_, pwo, pdo = pp.get()
                    tk = None
                    for j in range(2):
                        tk = mm(po_[:, 0:512], W2S[j][:, og * 128:(og + 1) * 128],
                                HSEG[j][:, 0:512], j == 0, j == 1,
                                ([w2toks[0], w2toks[1]] + htoks + pwo) if j == 0 else ())
                        last["w2s"][j] = last["w2s"][j] + [tk]
                        last["hseg"][j] = last["hseg"][j] + [tk]
                    acc_sl = accv[og][:, sc0:sc0 + 512]
                    if seg == 0:
                        e = act(acc_sl, po_[:, 0:512], AF.Copy, waits=[tk])
                    else:
                        e = vstt(acc_sl, po_[:, 0:512], 1.0, acc_sl, ALU.mult, ALU.add,
                                 waits=[tk])
                    pdo(e)
                    last["acc1"][og] = [e]
            ps, pw, pdone = pp.get()
            t1 = mm(ps[:, 0:512], ones1_t[0:1, :], ROWG[0:1, sc0:sc0 + 512], True, True,
                    [ngtok] + pw)
            en = act(NGB[:, 0:512], ps[:, 0:512], AF.Copy, waits=[t1] + last["ngb"])
            pdone(en)
            ftoks = []
            for m in range(8):
                x_sl = xstr[m][:, sc0:sc0 + 512]
                acc_sl = accv[m][:, sc0:sc0 + 512]
                lt, lw, ld_ = sp.get()
                t1 = vts1(lt[:, 0:512], acc_sl, SM_B2S[:, m:m + 1], ALU.add,
                          waits=last["acc1"][m] + b2toks + lw)
                t2 = vtt(lt[:, 0:512], lt[:, 0:512], NGB[:, 0:512], ALU.mult,
                         waits=[t1, en])
                e = vtt(x_sl, x_sl, lt[:, 0:512], ALU.add, waits=[t2])
                ld_(e)
                ftoks.append(e)
            last["ngb"] = ftoks
            last["xf"] = ftoks
        return ("v", b.cnt["v"])

    def store_out(final_tok):
        st_last = [[], []]
        for tt in range(16):
            tmt = TMT if tt % 2 == 0 else QAT
            wts = st_last[tt % 2]
            st_last[tt % 2] = []
            etk = []
            for db in range(8):
                ps, pw, pdone = pp.get()
                ttk = tr(ps[:, 0:128], xstr[db][:, tt * 128:(tt + 1) * 128],
                         [final_tok] + pw + (wts if db == 0 else []))
                ek = act(tmt[:, db * 128:(db + 1) * 128], ps[:, 0:128], AF.Copy,
                         waits=[ttk] + (wts if db == 0 else []))
                pdone(ek)
                etk.append(ek)
            dtok = b.dma("gpsimd", yout[tt * 128:(tt + 1) * 128, :], tmt[:, 0:1024],
                         waits=etk, pub="do")
            st_last[tt % 2] = [dtok]
        b.barrier_wait("gpsimd", [("do", b.cnt["do"])])

    if STAGES >= 2:
        stage_barrier()
        wq_pass()
        stage_barrier()
        if DBG is not None and DBG[3] == "wq":
            b.dma("gpsimd", yout[0:1024, :], qgw[0:1024, 1024:2048], pub="do")
            b.dma("gpsimd", yout[1024:2048, :], qgw[0:1024, 1792:2816], pub="do")
            b.barrier_wait("gpsimd", [("do", b.cnt["do"])])
    ftok = l0_attn()
    if STAGES >= 2:
        bar = stage_barrier()
        ftok = f0_ffn(bar)
    if STAGES >= 3:
        bar = stage_barrier()
        ftok = l1_attn(bar)
    if STAGES >= 4:
        bar = stage_barrier()
        ftok = f1_ffn(bar)
    stage_barrier()
    if DBG is None:
        store_out(ftok)
    else:
        b.barrier_wait("gpsimd", [("do", b.cnt["do"]), ftok])

    b.realize(block)
    nc._ctx_holder = ctx
    return nc


_NC_CACHE = {}


def kernel(**inputs):
    in_maps = host_prepare(inputs)
    nc = build_nc()
    res = run_bass_kernel_spmd(nc, in_maps, core_ids=list(range(8)))
    return host_gather(res.results)


# version 03
# speedup vs baseline: 1.2632x; 1.2632x over previous
import contextlib
import numpy as np

import concourse.bass as bass
import concourse.mybir as mybir
from concourse.bass_utils import run_bass_kernel_spmd

F32 = mybir.dt.float32
AF = mybir.ActivationFunctionType
ALU = mybir.AluOpType

T_HALF = 2048
CH = 256
NCH = T_HALF // CH
NSUB = 16
EPS = 1e-5
F32EPS = float(np.finfo(np.float32).eps)
STAGES = 4
DBG = None
PAIRS = [[0, 1], [2, 3], [4, 5], [6, 7]]

ENGS = ("tensor", "vector", "scalar", "gpsimd", "sync")
SEM_OWNER = {"t": "tensor", "v": "vector", "a": "scalar", "g": "gpsimd"}
SEM_NAMES = ("t", "v", "a", "g", "cc", "di", "dw", "dz", "dx0", "dx1", "do", "de", "d4")
DMA_SEMS = ("di", "dw", "dz", "dx0", "dx1", "do", "de", "d4")


class Bld:
    def __init__(self, nc, sems):
        self.nc = nc
        self.sems = sems
        self.cnt = {k: 0 for k in sems}
        self.stream = {e: [] for e in ENGS}

    def emit(self, eng, fn, waits=(), pub=None, amt=1):
        toks = []
        for tk in waits:
            if tk is None:
                continue
            s, v = tk
            if s in DMA_SEMS:
                v = self.cnt[s]
            toks.append((s, v))
        tok = None
        if pub is not None:
            self.cnt[pub] += amt
            tok = (pub, self.cnt[pub])
        self.stream[eng].append((toks, fn, pub, amt))
        return tok

    def dma(self, eng, out, in_, waits=(), pub="di"):
        waits = list(waits) + [(pub, self.cnt[pub])]
        return self.emit(eng, lambda h, o=out, i=in_: h.dma_start(out=o, in_=i),
                         waits, pub, 16)

    def barrier_wait(self, eng, waits):
        self.emit(eng, lambda h: None, waits)

    def realize(self, block):
        for eng in ENGS:
            items = self.stream[eng]
            sems = self.sems

            def body(h, items=items):
                hw = {}
                for toks, fn, pub, amt in items:
                    need = {}
                    for s, v in toks:
                        if hw.get(s, 0) < v and need.get(s, 0) < v:
                            need[s] = v
                    for s, v in need.items():
                        h.wait_ge(sems[s], v)
                        hw[s] = v
                    inst = fn(h)
                    if pub is not None:
                        inst.then_inc(sems[pub], amt)

            getattr(block, eng)(body)


class Pool:
    def __init__(self, tiles):
        self.tiles = tiles
        self.rd = [None] * len(tiles)
        self.i = 0

    def get(self):
        i = self.i % len(self.tiles)
        self.i += 1
        w = self.rd[i]
        self.rd[i] = None

        def done(tok, i=i):
            self.rd[i] = tok

        return self.tiles[i], ([w] if w is not None else []), done


def _f32(a):
    return np.ascontiguousarray(a, dtype=np.float32)


def _col_pack(v):
    return _f32(np.transpose(np.asarray(v).reshape(-1, 128)))


def _colmat_pack(m, inner):
    m = np.asarray(m)
    nblk = m.shape[0] // 128
    return _f32(np.transpose(m.reshape(nblk, 128, inner), (1, 0, 2)).reshape(128, nblk * inner))


def host_prepare(inputs):
    x = np.asarray(inputs["L0_attn_hidden_states"])

    def wT(name):
        return _f32(np.asarray(inputs[name]).T)

    def arr(name):
        return _f32(np.asarray(inputs[name]))

    shared = {
        "l0_ipw": wT("L0_attn_in_proj_w"),
        "l0_opw": wT("L0_attn_out_proj_w"),
        "l0_fcw": wT("L0_attn_fc_w"),
        "l0_fcb": _col_pack(inputs["L0_attn_fc_b"]),
        "l0_lng": _col_pack(inputs["L0_attn_ln_g"]),
        "l0_lnb": _col_pack(inputs["L0_attn_ln_b"]),
        "f0_gw": arr("L0_ffn_w_gate"),
        "f0_uw": arr("L0_ffn_w_up"),
        "f0_dw": arr("L0_ffn_w_down"),
        "l1_lng": _col_pack(inputs["L1_attn_ln_g"]),
        "l1_lnb": _col_pack(inputs["L1_attn_ln_b"]),
        "f1_w1": _f32(np.transpose(np.asarray(inputs["L1_ffn_w1"]), (1, 0, 2)).reshape(1024, 16384)),
        "f1_w2": _f32(np.asarray(inputs["L1_ffn_w2"]).reshape(16384, 1024)),
        "f1_b1": _f32(np.transpose(np.asarray(inputs["L1_ffn_b1"]).reshape(128, 128))),
        "f1_b2": arr("L1_ffn_b2"),
        "f1_gw": wT("L1_ffn_gate_w"),
        "f1_gb": _f32(np.asarray(inputs["L1_ffn_gate_b"]).reshape(1, 8)),
        "f1_rms": _f32(np.asarray(inputs["L1_ffn_rms_w"]).reshape(1, 8)),
        "c_ident": _f32(np.eye(128)),
        "c_scal": _f32(np.tile(np.array([1e-5, 1.0, 2.0, float(np.finfo(np.float32).eps),
                                         1e-12, 1.0 / 6.0, 1e-6, 0.5], dtype=np.float32),
                               (128, 1))),
        "c_ones1": _f32(np.ones((1, 128))),
        "c_w1024": _f32(np.full((128, 1), 1.0 / 1024.0)),
    }
    shared["c_oh16"] = _f32(np.eye(16))

    l0_br = {}
    for tag, sfx in (("f", ""), ("b", "_b")):
        l0_br[tag] = {
            "convw": _colmat_pack(inputs["L0_attn_conv_w" + sfx], 4),
            "convb": _col_pack(inputs["L0_attn_conv_b" + sfx]),
            "xpw": wT("L0_attn_x_proj_w" + sfx),
            "dtw": wT("L0_attn_dt_proj_w" + sfx),
            "dtb": _col_pack(inputs["L0_attn_dt_proj_b" + sfx]),
            "alog": _colmat_pack(inputs["L0_attn_A_log" + sfx], 16),
            "dp": _col_pack(inputs["L0_attn_Dp" + sfx]),
        }
    l1_br = {}
    for tag in ("m1", "m2"):
        p = "L1_attn_" + tag + "_"
        l1_br[tag] = {
            "win": arr(p + "Win"),
            "convw": _colmat_pack(inputs[p + "Wconv"], 4),
            "convb": _col_pack(inputs[p + "bconv"]),
            "xpw": arr(p + "Wx"),
            "dtw": arr(p + "Wdt"),
            "dtb": _col_pack(inputs[p + "bdt"]),
            "alog": _colmat_pack(inputs[p + "Alog"], 16),
            "dp": _col_pack(inputs[p + "Dp"]),
            "wout": arr(p + "Wout"),
        }

    in_maps = []
    for c in range(8):
        bi, par = c // 2, c % 2
        m = dict(shared)
        if par == 0:
            sl = x[bi, 0:T_HALF]
            halo = x[bi, T_HALF + 2:T_HALF - 1:-1]
        else:
            sl = x[bi, T_HALF:2 * T_HALF][::-1]
            halo = x[bi, T_HALF - 3:T_HALF]
        xin = np.zeros((2176, 1024), dtype=np.float32)
        xin[:T_HALF] = sl
        xin[T_HALF:T_HALF + 3] = halo
        m["xin"] = xin
        pe = np.full((128, 1), 1.0 if par == 0 else 0.0, dtype=np.float32)
        m["c_pe"] = pe
        m["c_po"] = _f32(1.0 - pe)
        A, B = (l0_br["f"], l0_br["b"]) if par == 0 else (l0_br["b"], l0_br["f"])
        for k, v in A.items():
            m["l0a_" + k] = v
        for k, v in B.items():
            m["l0b_" + k] = v
        A, B = (l1_br["m1"], l1_br["m2"]) if par == 0 else (l1_br["m2"], l1_br["m1"])
        for k, v in A.items():
            m["l1a_" + k] = v
        for k, v in B.items():
            m["l1b_" + k] = v
        in_maps.append(m)
    return in_maps


def host_gather(results):
    out = np.empty((4, 4096, 1024), dtype=np.float32)
    for c in range(8):
        bi, par = c // 2, c % 2
        y = results[c]["yout"]
        if par == 0:
            out[bi, 0:T_HALF] = y
        else:
            out[bi, T_HALF:2 * T_HALF] = y[::-1]
    return out


def build_nc():
    nc = bass.Bass("TRN2", target_bir_lowering=False)
    ctx = contextlib.ExitStack()

    IN = {}
    in_specs = [
        ("xin", [2176, 1024]),
        ("l0_ipw", [1024, 2048]), ("l0_opw", [1024, 1024]), ("l0_fcw", [1024, 1024]),
        ("l0_fcb", [128, 8]), ("l0_lng", [128, 8]), ("l0_lnb", [128, 8]),
        ("f0_gw", [2816, 1024]), ("f0_uw", [2816, 1024]), ("f0_dw", [1024, 2816]),
        ("l1_lng", [128, 8]), ("l1_lnb", [128, 8]),
        ("f1_w1", [1024, 16384]), ("f1_w2", [16384, 1024]),
        ("f1_b1", [128, 128]), ("f1_b2", [8, 1024]),
        ("f1_gw", [1024, 8]), ("f1_gb", [1, 8]), ("f1_rms", [1, 8]),
        ("c_ident", [128, 128]), ("c_ones1", [1, 128]), ("c_w1024", [128, 1]),
        ("c_scal", [128, 8]),
        ("c_oh16", [16, 16]), ("c_pe", [128, 1]), ("c_po", [128, 1]),
    ]
    for ph in ("a", "b"):
        in_specs += [
            ("l0%s_convw" % ph, [128, 32]), ("l0%s_convb" % ph, [128, 8]),
            ("l0%s_xpw" % ph, [1024, 96]), ("l0%s_dtw" % ph, [64, 1024]),
            ("l0%s_dtb" % ph, [128, 8]), ("l0%s_alog" % ph, [128, 128]),
            ("l0%s_dp" % ph, [128, 8]),
            ("l1%s_win" % ph, [1024, 2048]), ("l1%s_convw" % ph, [128, 32]),
            ("l1%s_convb" % ph, [128, 8]), ("l1%s_xpw" % ph, [1024, 64]),
            ("l1%s_dtw" % ph, [32, 1024]), ("l1%s_dtb" % ph, [128, 8]),
            ("l1%s_alog" % ph, [128, 128]), ("l1%s_dp" % ph, [128, 8]),
            ("l1%s_wout" % ph, [1024, 512]),
        ]
    for name, shape in in_specs:
        IN[name] = nc.dram_tensor(name, shape, F32, kind="ExternalInput").ap()
    yout = nc.dram_tensor("yout", [T_HALF, 1024], F32, kind="ExternalOutput").ap()

    qgw = nc.dram_tensor("qgw", [1024, 2816], F32).ap()
    quw = nc.dram_tensor("quw", [1024, 2816], F32).ap()
    qdw = nc.dram_tensor("qdw", [2816, 1024], F32).ap()
    AR = {}
    for nm, shape in (("ar1", [128, 256]), ("ar2", [128, 48]),
                      ("ar3", [128, 256]), ("ar4", [128, 16384])):
        AR[nm] = (nc.dram_tensor(nm + "_in", shape, F32).ap(),
                  nc.dram_tensor(nm + "_out", shape, F32).ap())

    def sbuf(name, shape, dt=F32):
        return ctx.enter_context(nc.sbuf_tensor(name, shape, dt))

    xstr_t = sbuf("xstr", [128, 16384])
    acc_t = sbuf("acc", [128, 16384])
    wsa_t = sbuf("wsa", [128, 10776])
    s512_t = [sbuf("s5_%d" % i, [128, 512]) for i in range(3)]
    scn_t = sbuf("scn", [128, 768])
    coh_t = sbuf("coh", [16, 16])
    strip_t = sbuf("strip", [128, 1024])
    stripb_t = sbuf("stripb", [128, 1024])
    scn2_t = sbuf("scn2", [128, 768])
    bcB_t = sbuf("bcB", [16, 256])
    bcC_t = sbuf("bcC", [16, 256])
    xdbl_t = sbuf("xdbl", [64, 256])
    statc_t = sbuf("statc", [128, 64])
    mub_t = sbuf("mub", [128, 256])
    r12b_t = sbuf("r12b", [128, 256])
    rowb_t = sbuf("rowb", [1, 1792])
    hrow_t = sbuf("hrow", [1, 32])
    ident_t = sbuf("ident", [128, 128])
    ones1_t = sbuf("ones1", [1, 128])
    w1024_t = sbuf("w1024", [128, 1])
    cs_t = sbuf("cscal", [128, 8])
    pe_t = sbuf("pe", [128, 1])
    po_t = sbuf("po", [128, 1])
    smalls_t = sbuf("smalls", [128, 448])
    stash_t = sbuf("stash", [128, 32])
    stA_t = sbuf("stA", [128, 128])
    stB_t = sbuf("stB", [128, 128])
    xh_t = sbuf("xhalo", [128, 24])

    psum_t = [ctx.enter_context(nc.psum_tensor("ps%d" % i, [128, 512], F32))
              for i in range(8)]
    sems = {k: ctx.enter_context(nc.semaphore("sem_" + k)) for k in SEM_NAMES}
    block = ctx.enter_context(nc.Block())
    b = Bld(nc, sems)
    pp = Pool(psum_t)
    sp = Pool([t[:, :] for t in s512_t])

    def wv(off, w):
        return wsa_t[:, off:off + w]

    HS = [wv(0, 256), wv(256, 256)]
    XS = [wv(512 + i * 259, 259) for i in range(8)]
    UU = [wv(2584 + i * 256, 256) for i in range(8)]
    DU = [wv(4632 + i * 256, 256) for i in range(8)]
    YY = [wv(6680 + i * 256, 256) for i in range(8)]
    DL = [wv(8728 + i * 256, 256) for i in range(8)]
    HTM = wv(0, 2816)
    QHTM = wv(2816, 2816)
    QHF = [wv(5632 + i * 128, 128) for i in range(22)]
    QXF = [wv(8448 + i * 128, 128) for i in range(8)]
    TMT = wv(5632, 1024)
    QAT = wv(6656, 1024)
    FS1 = wv(9472, 1024)
    FS2 = wv(10496 - 1024 + 1024, 1024) if False else None
    FS2 = wv(2816, 1024)
    XF = [wv(i * 512, 512) for i in range(8)]
    HSEG = [wv(4096 + i * 512, 512) for i in range(2)]
    W2S = [wv(5120 + i * 1024, 1024) for i in range(2)]
    ROWG = wsa_t[0:1, 7168:9216]
    NGB = wv(9216, 512)
    GWS = wv(9728, 64)

    exch_t = scn_t
    SCA = scn_t[:, 0:256]
    SCB = scn_t[:, 256:512]
    SCH = scn_t[:, 512:768]

    xstr = [xstr_t[:, i * 2048:(i + 1) * 2048] for i in range(8)]
    accv = [acc_t[:, i * 2048:(i + 1) * 2048] for i in range(8)]

    def mm(psum_ap, lhsT, rhs, start, stop, waits=(), pub="t"):
        return b.emit("tensor",
                      lambda h, p=psum_ap, l=lhsT, r=rhs, s=start, t=stop:
                      h.matmul(p, l, r, start=s, stop=t),
                      waits, "t", 1)

    def tr(psum_ap, in_ap, waits, idn=None):
        if idn is None:
            idn = ident_t[:, :]
        return b.emit("tensor",
                      lambda h, p=psum_ap, i=in_ap, d=idn: h.transpose(p, i, d),
                      waits, "t", 1)

    def act(out, in_, func, bias=0.0, scale=1.0, waits=()):
        return b.emit("scalar",
                      lambda h, o=out, i=in_, f=func, bi=bias, sc=scale:
                      h.activation(o, i, f, bias=bi, scale=sc),
                      waits, "a", 1)

    def vcopy(out, in_, waits=()):
        return b.emit("vector", lambda h, o=out, i=in_: h.tensor_copy(o, i),
                      waits, "v", 1)

    def vtt(out, a0, a1, op, waits=()):
        return b.emit("vector", lambda h, o=out, x=a0, y=a1, p=op:
                      h.tensor_tensor(o, x, y, p), waits, "v", 1)

    def gtt(out, a0, a1, op, waits=()):
        return b.emit("gpsimd", lambda h, o=out, x=a0, y=a1, p=op:
                      h.tensor_tensor(o, x, y, p), waits, "g", 1)

    def vts1(out, in0, s1, op, waits=()):
        return b.emit("vector", lambda h, o=out, i=in0, a=s1, p=op:
                      h.tensor_single_scalar(o, i, a, p), waits, "v", 1)

    def vts2(out, in0, s1, s2, op0, op1, waits=()):
        return b.emit("vector", lambda h, o=out, i=in0, a=s1, c=s2, p0=op0, p1=op1:
                      h.tensor_scalar(o, i, a, c, p0, p1), waits, "v", 1)

    def vstt(out, in0, s, in1, op0, op1, waits=()):
        return b.emit("vector", lambda h, o=out, i=in0, a=s, j=in1, p0=op0, p1=op1:
                      h.scalar_tensor_tensor(o, i, a, j, p0, p1), waits, "v", 1)

    def vscan(out, d0, d1, init, waits=()):
        return b.emit("vector", lambda h, o=out, x=d0, y=d1, i=init:
                      h.tensor_tensor_scan(o, x, y, i, ALU.mult, ALU.add),
                      waits, "v", 1)

    def vrecip(out, in_, waits=()):
        return b.emit("vector", lambda h, o=out, i=in_: h.reciprocal(o, i),
                      waits, "v", 1)

    def vmemset(ap, val, waits=()):
        return b.emit("vector", lambda h, a=ap, v=val: h.memset(a, v),
                      waits, "v", 1)

    def vreduce(out, in_, op, absval=False, waits=()):
        return b.emit("vector",
                      lambda h, o=out, i=in_, p=op, av=absval:
                      h.tensor_reduce(o, i, mybir.AxisListType.X, p,
                                      apply_absolute_value=av),
                      waits, "v", 1)

    ld0 = []
    for src, dst in ((IN["c_ident"], ident_t[:, :]), (IN["c_ones1"], ones1_t[:, :]),
                     (IN["c_w1024"], w1024_t[:, :]), (IN["c_oh16"], coh_t[:, :]),
                     (IN["c_pe"], pe_t[:, :]), (IN["c_po"], po_t[:, :]),
                     (IN["c_scal"], cs_t[:, :])):
        ld0.append(b.dma("sync", dst, src))
    for i, v in enumerate([1e-5, 1.0, 2.0, F32EPS, 1e-12, 1.0 / 6.0, 1e-6, 0.5]):
        nc.const_aps.register(F32, float(v), cs_t[:, i:i + 1])
    sm = smalls_t

    def lsm(c0, src, w):
        return b.dma("sync", sm[:, c0:c0 + w], src)

    ld0 += [lsm(0, IN["l0_fcb"], 8), lsm(8, IN["l0_lng"], 8), lsm(16, IN["l0_lnb"], 8),
            lsm(24, IN["l1_lng"], 8), lsm(32, IN["l1_lnb"], 8)]
    SM_FCB = sm[:, 0:8]
    SM_LNG = {0: sm[:, 8:16], 1: sm[:, 24:32]}
    SM_LNB = {0: sm[:, 16:24], 1: sm[:, 32:40]}
    SM_B2S = sm[:, 40:48]

    def SM_CONVB(ph):
        return sm[:, 48 + 24 * ph:56 + 24 * ph]

    def SM_DTB(ph):
        return sm[:, 56 + 24 * ph:64 + 24 * ph]

    def SM_DP(ph):
        return sm[:, 64 + 24 * ph:72 + 24 * ph]

    def SM_CONVW(ph):
        return sm[:, 96 + 32 * ph:128 + 32 * ph]

    def SM_AMAT(ph):
        return sm[:, 160 + 128 * ph:288 + 128 * ph]

    def load_branch_smalls(layer, deps):
        pre = "l0" if layer == 0 else "l1"
        toks = list(deps)
        for ph, pch in ((0, "a"), (1, "b")):
            toks.append(lsm(48 + 24 * ph, IN[pre + pch + "_convb"], 8))
            toks.append(lsm(56 + 24 * ph, IN[pre + pch + "_dtb"], 8))
            toks.append(lsm(64 + 24 * ph, IN[pre + pch + "_dp"], 8))
            toks.append(lsm(96 + 32 * ph, IN[pre + pch + "_convw"], 32))
            toks.append(lsm(160 + 128 * ph, IN[pre + pch + "_alog"], 128))
        out = []
        for ph in (0, 1):
            a0 = 160 + 128 * ph
            t1 = act(sm[:, a0:a0 + 128], sm[:, a0:a0 + 128], AF.Exp, waits=toks)
            out.append(vts1(sm[:, a0:a0 + 128], sm[:, a0:a0 + 128], -1.0, ALU.mult,
                            waits=[t1]))
        return out

    xload_toks = []
    xld_last = [[], []]
    for tt in range(17):
        tmt = TMT if tt % 2 == 0 else QAT
        wts = xld_last[tt % 2]
        xld_last[tt % 2] = []
        dtok = b.dma("sync", tmt[:, 0:1024], IN["xin"][tt * 128:(tt + 1) * 128, :],
                     waits=wts + (ld0 if tt == 0 else []), pub="dx0" if tt % 2 == 0 else "dx1")
        last = None
        for db in range(8):
            ps, pw, pdone = pp.get()
            ttk = tr(ps[:, 0:128], tmt[:, db * 128:(db + 1) * 128], [dtok] + pw)
            if tt < 16:
                etk = act(xstr[db][:, tt * 128:(tt + 1) * 128], ps[:, 0:128], AF.Copy,
                          waits=[ttk])
            else:
                etk = act(xh_t[:, db * 3:db * 3 + 3], ps[:, 0:3], AF.Copy, waits=[ttk])
            pdone(etk)
            last = etk
        xld_last[tt % 2] = [last]
        xload_toks.append(last)
    XTOK = ("a", b.cnt["a"])

    def stats_pass(src_tiles, deps):
        sq1, w1, d1 = sp.get()
        sq2, w2, d2 = sp.get()
        toks = []
        prev_q = w1 + w2
        for s in range(NSUB):
            ps, pw, pdone = pp.get()
            mtk = None
            sqtoks = []
            for db in range(8):
                src = src_tiles[db][:, s * 128:(s + 1) * 128]
                w0 = list(deps) + (pw if db == 0 else [])
                mtk = mm(ps[:, 0:1], src, w1024_t[:, 0:1], db == 0, db == 7,
                         w0, "t" if db == 7 else None)
                sqt = (sq1 if db < 4 else sq2)[:, (db % 4) * 128:(db % 4 + 1) * 128]
                sqtoks.append(act(sqt, src, AF.Square, waits=w0 + prev_q))
            e1 = act(statc_t[:, s:s + 1], ps[:, 0:1], AF.Copy, waits=[mtk])
            pdone(e1)
            ps2, pw2, pdone2 = pp.get()
            qtk = None
            for db in range(8):
                sqt = (sq1 if db < 4 else sq2)[:, (db % 4) * 128:(db % 4 + 1) * 128]
                qtk = mm(ps2[:, 0:1], sqt, w1024_t[:, 0:1], db == 0, db == 7,
                         [sqtoks[db]] + (pw2 if db == 0 else []),
                         "t" if db == 7 else None)
            prev_q = [qtk]
            e2 = act(statc_t[:, 16 + s:16 + s + 1], ps2[:, 0:1], AF.Copy, waits=[qtk])
            pdone2(e2)
            toks += [e1, e2]
        d1(prev_q[0])
        d2(prev_q[0])
        return toks

    def ln_rows(s0, ns, deps, rev, want_r2):
        w = ns * 128
        tts = []
        for j, stat in ((0, 0), (1, 16)):
            for s in range(ns):
                ps, pw, pdone = pp.get()
                tk = tr(ps[0:1, 0:128], statc_t[:, stat + s0 + s:stat + s0 + s + 1],
                        list(deps) + pw)
                ek = act(rowb_t[0:1, j * 256 + s * 128:j * 256 + (s + 1) * 128],
                         ps[0:1, 0:128], AF.Copy, waits=[tk])
                pdone(ek)
                tts.append(ek)
        mu = rowb_t[0:1, 0:w]
        r1 = rowb_t[0:1, 256:256 + w]
        r12 = rowb_t[0:1, 512:512 + w]
        tmp = rowb_t[0:1, 768:768 + w]
        t1 = vtt(tmp, mu, mu, ALU.mult, waits=tts)
        t2 = vtt(r1, r1, tmp, ALU.subtract, waits=[t1])
        tok = t2
        if want_r2:
            t3 = vcopy(tmp, r1, waits=[t2])
        t4 = act(r1, r1, AF.Sqrt, bias=EPS, waits=[tok if not want_r2 else t3])
        t5 = vrecip(r1, r1, waits=[t4])
        tok = t5
        if want_r2:
            t6 = vtt(r12, r1, r1, ALU.mult, waits=[t5])
            t7 = vtt(r12, r12, tmp, ALU.mult, waits=[t6])
            t8 = act(r12, r12, AF.Sqrt, bias=EPS, waits=[t7])
            t9 = vrecip(r12, r12, waits=[t8])
            t10 = vtt(r12, r12, r1, ALU.mult, waits=[t9])
            tok = t10
        if not rev:
            return mu, r1, (r12 if want_r2 else None), tok
        mur = rowb_t[0:1, 1024:1024 + w]
        r1r = rowb_t[0:1, 1280:1280 + w]
        c1 = vcopy(mur, mu[0:1, w - 1::-1], waits=[tok])
        c2 = vcopy(r1r, r1[0:1, w - 1::-1], waits=[c1])
        tok = c2
        if want_r2:
            r12r = rowb_t[0:1, 1536:1536 + w]
            tok = vcopy(r12r, r12[0:1, w - 1::-1], waits=[c2])
            return mur, r1r, r12r, tok
        return mur, r1r, None, tok

    bc_last = {"mub": [], "r12b": []}

    def bcast_row(row_ap, w, deps, dst=None):
        ps, pw, pdone = pp.get()
        t1 = mm(ps[:, 0:w], ones1_t[0:1, :], row_ap, True, True, list(deps) + pw, "t")
        if dst is None:
            tile, tw, tdone = sp.get()
            t2 = act(tile[:, 0:w], ps[:, 0:w], AF.Copy, waits=[t1] + tw)
            pdone(t2)
            return tile[:, 0:w], t2, tdone
        tile = {"mub": mub_t, "r12b": r12b_t}[dst]
        t2 = act(tile[:, 0:w], ps[:, 0:w], AF.Copy, waits=[t1] + bc_last[dst])
        pdone(t2)

        def tdone(tok, dst=dst):
            bc_last[dst] = [tok]

        return tile[:, 0:w], t2, tdone

    class StripMgr:
        def __init__(self):
            self.bufs = [strip_t, stripb_t]
            self.cons = [[], []]
            self.i = 0

        def load(self, src_ap, cols, extra=(), eng="sync", rows=128):
            i = self.i % 2
            self.i += 1
            buf = self.bufs[i]
            dst = buf[0:rows, 0:cols]
            if src_ap.ndim == 3:
                dst = dst.rearrange("p (kb j) -> p kb j", j=src_ap.shape[2])
            tok = b.dma(eng, dst, src_ap,
                        waits=list(extra) + self.cons[i],
                        pub="dw" if i == 0 else "dz")
            self.cons[i] = []

            def use(t, i=i):
                self.cons[i].append(t)

            return buf, tok, use

    smgr = StripMgr()

    def strip_dma(buf, src_ap, deps):
        return b.dma("sync", buf, src_ap, waits=deps, pub="dw")

    last = {"strip": [], "bcB": [], "bcC": [], "xdbl": [],
            "XS": [[] for _ in range(8)], "UU": [[] for _ in range(8)],
            "DU": [[] for _ in range(8)], "YY": [[] for _ in range(8)],
            "DL": [[] for _ in range(8)], "HS": [[], []], "stash": [],
            "exch": [], "state": [[], []], "sca": [], "scb": [], "sch": [], "htm": [], "fs2": [], "fs2w": [],
            "scn": [], "qxf": [], "qhf": [], "tmrd": [], "xf": [],
            "hseg": [[], []], "w2s": [[], []], "ngb": [],
            "acc1": {i: [] for i in range(8)}}

    def hs_build(layer, out_ap, src_ap, mub, r12b, row_tok, w, deps):
        t1 = vstt(out_ap, src_ap, 1.0, mub, ALU.mult, ALU.subtract,
                  waits=list(deps) + [row_tok])
        t2 = vtt(out_ap, out_ap, r12b, ALU.mult, waits=[t1])
        t3 = vts2(out_ap, out_ap, SM_LNG[layer][:, 0:1].broadcast_to([128, 1]),
                  SM_LNB[layer][:, 0:1], ALU.mult, ALU.add, waits=[t2])
        return t3

    def in_sweep(layer, ph, colrange, w, rev, mub, r12b, row_tok, zpart, out_cb):
        ipw = IN["l0_ipw"] if layer == 0 else IN["l1a_win" if ph == 0 else "l1b_win"]
        c0 = 1024 if zpart else 0
        banks = [pp.get() for _ in range(8)]
        lnga = SM_LNG[layer]
        lnba = SM_LNB[layer]
        mmtoks = [None] * 8
        for k in range(8):
            sbuf_w, sd, suse = smgr.load(ipw[k * 128:(k + 1) * 128, c0:c0 + 1024], 1024)
            hs = HS[k % 2]
            src = xstr[k][:, colrange[0]:colrange[0] + w]
            if rev:
                src = src[:, w - 1::-1]
            hw = list(last["HS"][k % 2])
            t1 = vstt(hs[:, 0:w], src, 1.0, mub, ALU.mult, ALU.subtract,
                      waits=hw + [row_tok])
            t2 = vtt(hs[:, 0:w], hs[:, 0:w], r12b, ALU.mult, waits=[t1])
            t3 = vts2(hs[:, 0:w], hs[:, 0:w], lnga[:, k:k + 1], lnba[:, k:k + 1],
                      ALU.mult, ALU.add, waits=[t2])
            if DBG is not None and DBG[3] == "hs2" and zpart and k < 2:
                t3 = b.dma("gpsimd", yout[k * 128:(k + 1) * 128, 0:w], hs[:, 0:w],
                           waits=[t3], pub="do")
            for m in range(8):
                tile, pw, pdone = banks[m]
                wts = [t3, sd] + (pw if k == 0 else [])
                tok = mm(tile[:, 0:w], sbuf_w[:, m * 128:(m + 1) * 128],
                         hs[:, 0:w], k == 0, k == 7, wts)
                if k == 7:
                    mmtoks[m] = tok
                if m == 7:
                    last["HS"][k % 2] = [tok]
                    suse(tok)
        for m in range(8):
            tile, pw, pdone = banks[m]
            out_cb(m, tile, mmtoks[m], pdone)

    def attn_chunk(layer, ph, ci, stok):
        rev = ph == 1
        sc0 = ci * CH
        dtk = 64 if layer == 0 else 32
        xw = 96 if layer == 0 else 64
        st_t = stA_t if ph == 0 else stB_t
        mu, r1, r12, rtok = ln_rows(ci * 2, 2, [stok], rev, True)
        mub, mtk, mubd = bcast_row(mu, CH, [rtok], dst="mub")
        r12b, rtk, r12bd = bcast_row(r12, CH, [rtok], dst="r12b")
        row_tok = rtk if rtk[1] > mtk[1] else mtk

        def xcb(m, tile, mmtok, pdone):
            e = act(XS[m][:, 3:3 + CH], tile[:, 0:CH], AF.Copy,
                    waits=[mmtok] + last["XS"][m])
            last["XS"][m] = []
            pdone(e)
            h1 = vcopy(XS[m][:, 0:3], stash_t[:, m * 4:m * 4 + 3],
                       waits=last["stash"] + [e])
            h2 = vcopy(stash_t[:, m * 4:m * 4 + 3], XS[m][:, CH:CH + 3], waits=[e, h1])
            cw = SM_CONVW(ph)
            c = vts1(UU[m][:, 0:CH], XS[m][:, 0:CH], cw[:, m * 4:m * 4 + 1], ALU.mult,
                     waits=[h1] + last["UU"][m])
            last["UU"][m] = []
            for kk in range(1, 4):
                c = vstt(UU[m][:, 0:CH], XS[m][:, kk:kk + CH], cw[:, m * 4 + kk:m * 4 + kk + 1],
                         UU[m][:, 0:CH], ALU.mult, ALU.add, waits=[c])
            s = act(UU[m][:, 0:CH], UU[m][:, 0:CH], AF.Silu,
                    bias=SM_CONVB(ph)[:, m:m + 1], waits=[c])
            last["XS"][m] = [c]
            last["stash"] = last["stash"] + [h2]
            last["UU"][m] = [s]

        in_sweep(layer, ph, (sc0,), CH, rev, mub, r12b, row_tok, False, xcb)
        u_toks = [last["UU"][m][0] for m in range(8)]

        xpw = IN[("l0" if layer == 0 else "l1") + ("a" if ph == 0 else "b") + "_xpw"]
        sbw, sd, suse = smgr.load(xpw.rearrange("(kb p) j -> p kb j", p=128), 8 * xw)
        psd, pwd, pdd = pp.get()
        psb, pwb, pdb = pp.get()
        psc, pwc, pdc = pp.get()
        tD = tB = tC = None
        for kb in range(8):
            base = kb * xw
            wts = [sd] + u_toks
            tD = mm(psd[0:dtk, 0:CH], sbw[:, base:base + dtk], UU[kb][:, 0:CH],
                    kb == 0, kb == 7, wts + (pwd if kb == 0 else []))
            tB = mm(psb[0:16, 0:CH], sbw[:, base + dtk:base + dtk + 16],
                    UU[kb][:, 0:CH], kb == 0, kb == 7, pwb if kb == 0 else ())
            tC = mm(psc[0:16, 0:CH], sbw[:, base + dtk + 16:base + dtk + 32],
                    UU[kb][:, 0:CH], kb == 0, kb == 7, pwc if kb == 0 else ())
        suse(tD)
        suse(tB)
        suse(tC)
        eD = act(xdbl_t[0:dtk, 0:CH], psd[0:dtk, 0:CH], AF.Copy,
                 waits=[tD] + last["xdbl"])
        last["xdbl"] = []
        eB = act(bcB_t[0:16, 0:CH], psb[0:16, 0:CH], AF.Copy, waits=[tB] + last["bcB"])
        last["bcB"] = []
        eC = act(bcC_t[0:16, 0:CH], psc[0:16, 0:CH], AF.Copy, waits=[tC] + last["bcC"])
        last["bcC"] = []
        pdd(eD)
        pdb(eB)
        pdc(eC)

        dtw = IN[("l0" if layer == 0 else "l1") + ("a" if ph == 0 else "b") + "_dtw"]
        sbw2, sd2, suse2 = smgr.load(dtw, 1024, rows=dtk)
        dtmm = []
        for m in range(8):
            tile, pw, pdone = pp.get()
            tmk = mm(tile[0:128, 0:CH], sbw2[0:dtk, m * 128:(m + 1) * 128],
                     xdbl_t[0:dtk, 0:CH], True, True, [sd2, eD] + pw, "t")
            tmp, tw, tdone = sp.get()
            a1 = act(tmp[:, 0:CH], tile[:, 0:CH], AF.Exp,
                     bias=SM_DTB(ph)[:, m:m + 1], waits=[tmk] + tw)
            pdone(a1)
            a2 = act(DL[m][:, 0:CH], tmp[:, 0:CH], AF.Ln, bias=1.0,
                     waits=[a1] + last["DL"][m])
            last["DL"][m] = []
            tdone(a2)
            dtmm.append(tmk)
            d1 = vtt(DU[m][:, 0:CH], DL[m][:, 0:CH], UU[m][:, 0:CH], ALU.mult,
                     waits=[a2] + last["DU"][m])
            y1 = vts1(YY[m][:, 0:CH], UU[m][:, 0:CH], SM_DP(ph)[:, m:m + 1], ALU.mult,
                      waits=last["YY"][m])
            last["DU"][m] = [d1]
            last["YY"][m] = [y1]
            last["DL"][m] = [d1]
        suse2(dtmm[-1])
        last["xdbl"] = [dtmm[-1]]

        amat = SM_AMAT(ph)
        exrow = rowb_t[0:1, 768:768 + CH]
        exrow2 = rowb_t[0:1, 1024 + 768 - 256:1024 + 768] if False else None
        exlast = []
        for n in range(16):
            pB, pwB, pdB = pp.get()
            tx = mm(pB[0:1, 0:CH], coh_t[0:16, n:n + 1], bcB_t[0:16, 0:CH],
                    True, True, [eB] + pwB, "t")
            ex1 = act(exrow, pB[0:1, 0:CH], AF.Copy, waits=[tx] + exlast)
            pdB(ex1)
            pB2, pwB2, pdB2 = pp.get()
            t1 = mm(pB2[:, 0:CH], ones1_t[0:1, :], exrow, True, True, [ex1] + pwB2, "t")
            Bt, Bw, Bd = sp.get()
            e1 = act(Bt[:, 0:CH], pB2[:, 0:CH], AF.Copy, waits=[t1] + Bw)
            pdB2(e1)
            exlast = [t1]
            pC, pwC, pdC = pp.get()
            tx = mm(pC[0:1, 0:CH], coh_t[0:16, n:n + 1], bcC_t[0:16, 0:CH],
                    True, True, [eC] + pwC, "t")
            ex2 = act(exrow, pC[0:1, 0:CH], AF.Copy, waits=[tx] + exlast)
            pdC(ex2)
            pC2, pwC2, pdC2 = pp.get()
            t2 = mm(pC2[:, 0:CH], ones1_t[0:1, :], exrow, True, True, [ex2] + pwC2, "t")
            Ct, Cw, Cd = sp.get()
            e2 = act(Ct[:, 0:CH], pC2[:, 0:CH], AF.Copy, waits=[t2] + Cw)
            pdC2(e2)
            exlast = [t2]
            lb = lc = None
            for m in range(8):
                col = m * 16 + n
                ea = act(SCA, DL[m][:, 0:CH], AF.Exp, scale=amat[:, col:col + 1],
                         waits=last["DL"][m] + last["sca"])
                bb = gtt(SCB, DU[m][:, 0:CH], Bt[:, 0:CH], ALU.mult,
                         waits=[e1] + last["DU"][m] + last["scb"])
                sc = vscan(SCH, SCA, SCB, st_t[:, col:col + 1],
                           waits=[ea, bb] + last["state"][ph] + last["sch"])
                last["state"][ph] = []
                stc = vcopy(st_t[:, col:col + 1], SCH[:, CH - 1:CH], waits=[sc])
                tm2 = vtt(SCA, SCH, Ct[:, 0:CH], ALU.mult, waits=[sc, e2])
                ya = vtt(YY[m][:, 0:CH], YY[m][:, 0:CH], SCA, ALU.add,
                         waits=[tm2] + last["YY"][m])
                last["YY"][m] = [ya]
                last["DL"][m] = [ea]
                last["DU"][m] = [bb]
                last["sca"] = [ya]
                last["scb"] = [sc]
                last["sch"] = [tm2, stc]
                lb, lc = ya, stc
            Bd(lb)
            Cd(lb)
            last["state"][ph].append(lc)

        if DBG is not None and DBG[3] == "y0" and layer == DBG[0] and ph == DBG[1] and ci == DBG[2]:
            for m in range(8):
                dt_ = b.dma("gpsimd", yout[m * 128:(m + 1) * 128, 0:CH], YY[m][:, 0:CH],
                            waits=last["YY"][m], pub="do")
                last["YY"][m] = last["YY"][m] + [dt_]
            b.barrier_wait("gpsimd", [("do", b.cnt["do"])])

        def zcb(m, tile, mmtok, pdone):
            szt, zw, zd = sp.get()
            s = act(szt[:, 0:CH], tile[:, 0:CH], AF.Silu, waits=[mmtok] + zw)
            pdone(s)
            if DBG is not None and DBG[3] in ("sz", "szx") and (layer, ph, ci) == DBG[:3]:
                s = b.dma("gpsimd", yout[m * 128:(m + 1) * 128, 0:CH], szt[:, 0:CH],
                          waits=[s], pub="do")
            g = vtt(YY[m][:, 0:CH], YY[m][:, 0:CH], szt[:, 0:CH], ALU.mult,
                    waits=[s] + last["YY"][m])
            zd(g)
            last["YY"][m] = [g]

        zp_flag = not (DBG is not None and DBG[3] == "szx")
        in_sweep(layer, ph, (sc0,), CH, rev, mub, r12b, row_tok, zp_flag, zcb)
        y_toks = [last["YY"][m][0] for m in range(8)]
        mubd(y_toks[7])
        r12bd(y_toks[7])

        if DBG is not None and DBG[3] in ("u", "dl", "du", "y", "xs") and layer == DBG[0] and ph == DBG[1] and ci == DBG[2]:
            srcs = {"u": UU, "dl": DL, "du": DU, "y": YY, "xs": None}[DBG[3]]
            for m in range(8):
                if DBG[3] == "xs":
                    ap = XS[m][:, 3:3 + CH]
                else:
                    ap = srcs[m][:, 0:CH]
                dt_ = b.dma("gpsimd", yout[m * 128:(m + 1) * 128, 0:CH], ap,
                            waits=[y_toks[m]], pub="do")
                key = "UU" if DBG[3] == "u" else ("DL" if DBG[3] == "dl" else
                      ("DU" if DBG[3] == "du" else ("YY" if DBG[3] == "y" else "XS")))
                last[key][m] = last[key][m] + [dt_]
            b.barrier_wait("gpsimd", [("do", b.cnt["do"])])
        return y_toks

    def l0_outproj(ph, ci, y_toks):
        sc0 = ci * CH
        opm = []
        for m in range(8):
            sbw, sdw, suse = smgr.load(IN["l0_opw"][:, m * 128:(m + 1) * 128]
                                       .rearrange("(kb p) j -> p kb j", p=128), 1024)
            tile, pw, pdone = pp.get()
            tk = None
            for kb in range(8):
                tk = mm(tile[:, 0:CH], sbw[:, kb * 128:(kb + 1) * 128],
                        YY[kb][:, 0:CH], kb == 0, kb == 7,
                        ([sdw] + y_toks + pw) if kb == 0 else ())
            e = act(UU[m][:, 0:CH], tile[:, 0:CH], AF.Copy, waits=[tk] + last["UU"][m])
            pdone(e)
            suse(tk)
            last["UU"][m] = [e]
            opm.append(e)
        for m in range(8):
            sbw, sdw, suse = smgr.load(IN["l0_fcw"][:, m * 128:(m + 1) * 128]
                                       .rearrange("(kb p) j -> p kb j", p=128), 1024)
            tile, pw, pdone = pp.get()
            tk = None
            for kb in range(8):
                tk = mm(tile[:, 0:CH], sbw[:, kb * 128:(kb + 1) * 128],
                        UU[kb][:, 0:CH], kb == 0, kb == 7,
                        ([sdw] + opm + pw) if kb == 0 else ())
            acc_sl = accv[m][:, sc0:sc0 + CH]
            if ph == 0:
                e = act(acc_sl, tile[:, 0:CH], AF.Copy, waits=[tk])
                pdone(e)
            else:
                tmp, tw, tdone = sp.get()
                e0 = act(tmp[:, 0:CH], tile[:, 0:CH], AF.Copy, waits=[tk] + tw)
                pdone(e0)
                e = vtt(acc_sl, acc_sl, tmp[:, CH - 1::-1], ALU.add, waits=[e0])
                tdone(e)
            suse(tk)
            last["UU"][m].append(tk)

    def halo_mini(layer, src24, src_tok):
        ps, pw, pdone = pp.get()
        mt = None
        sq, sw, sqd = sp.get()
        sq_toks = []
        for db in range(8):
            src = src24[:, db * 3:db * 3 + 3]
            mt = mm(ps[0:128, 0:1][0:3, 0:1] if False else ps[0:3, 0:1],
                    src, w1024_t[:, 0:1], db == 0, db == 7,
                    ([src_tok] + pw) if db == 0 else (), "t" if db == 7 else None)
            sq_toks.append(act(sq[:, db * 4:db * 4 + 3], src, AF.Square,
                               waits=[src_tok] + (sw if db == 0 else [])))
        ps2, pw2, pdone2 = pp.get()
        qt = None
        for db in range(8):
            qt = mm(ps2[0:3, 0:1], sq[:, db * 4:db * 4 + 3], w1024_t[:, 0:1],
                    db == 0, db == 7, [sq_toks[db]] + (pw2 if db == 0 else []),
                    "t" if db == 7 else None)
        sqd(qt)
        ps3, pw3, pdone3 = pp.get()
        t1 = tr(ps3[0:1, 0:3], ps[0:3, 0:1], [mt] + pw3, idn=ident_t[0:3, 0:3])
        return None

    def halo_mini2(layer, ph, src24, src_tok):
        mcol, mw, mdone = sp.get()
        ps, pw, pdone = pp.get()
        mt = None
        sq_toks = []
        sq, sw, sqd = sp.get()
        for db in range(8):
            src = src24[:, db * 3:db * 3 + 3]
            mt = mm(ps[0:3, 0:1], src, w1024_t[:, 0:1], db == 0, db == 7,
                    ([src_tok] + pw) if db == 0 else (), "t" if db == 7 else None)
            sq_toks.append(act(sq[:, db * 4:db * 4 + 3], src, AF.Square,
                               waits=[src_tok] + (sw if db == 0 else [])))
        e1 = act(mcol[0:3, 0:1], ps[0:3, 0:1], AF.Copy, waits=[mt] + mw)
        pdone(e1)
        ps2, pw2, pdone2 = pp.get()
        qt = None
        for db in range(8):
            qt = mm(ps2[0:3, 0:1], sq[:, db * 4:db * 4 + 3], w1024_t[:, 0:1],
                    db == 0, db == 7, [sq_toks[db]] + (pw2 if db == 0 else []),
                    "t" if db == 7 else None)
        e2 = act(mcol[0:3, 1:2], ps2[0:3, 0:1], AF.Copy, waits=[qt])
        pdone2(e2)
        sqd(qt)
        ps3, pw3, pdone3 = pp.get()
        t1 = tr(ps3[0:1, 0:3], mcol[0:3, 0:1], [e1] + pw3, idn=ident_t[0:3, 0:3])
        r1 = act(hrow_t[0:1, 0:3], ps3[0:1, 0:3], AF.Copy, waits=[t1])
        pdone3(r1)
        ps4, pw4, pdone4 = pp.get()
        t2 = tr(ps4[0:1, 0:3], mcol[0:3, 1:2], [e2] + pw4, idn=ident_t[0:3, 0:3])
        r2 = act(hrow_t[0:1, 4:7], ps4[0:1, 0:3], AF.Copy, waits=[t2])
        pdone4(r2)
        mdone(t2)
        mu3 = hrow_t[0:1, 0:3]
        q3 = hrow_t[0:1, 4:7]
        v3 = hrow_t[0:1, 8:11]
        r12_3 = hrow_t[0:1, 12:15]
        x1 = vtt(v3, mu3, mu3, ALU.mult, waits=[r1, r2])
        x2 = vtt(q3, q3, v3, ALU.subtract, waits=[x1])
        x3 = vcopy(v3, q3, waits=[x2])
        x4 = act(q3, q3, AF.Sqrt, bias=EPS, waits=[x3])
        x5 = vrecip(q3, q3, waits=[x4])
        x6 = vtt(r12_3, q3, q3, ALU.mult, waits=[x5])
        x7 = vtt(r12_3, r12_3, v3, ALU.mult, waits=[x6])
        x8 = act(r12_3, r12_3, AF.Sqrt, bias=EPS, waits=[x7])
        x9 = vrecip(r12_3, r12_3, waits=[x8])
        x10 = vtt(r12_3, r12_3, q3, ALU.mult, waits=[x9])
        mb3, mb3t, mb3d = bcast_row(mu3, 3, [x10], dst="mub")
        rb3, rb3t, rb3d = bcast_row(r12_3, 3, [x10], dst="r12b")
        ipw = IN["l0_ipw"] if layer == 0 else IN["l1b_win"]
        hst, hw2, hsd = sp.get()
        banks = [pp.get() for _ in range(8)]
        mmt = [None] * 8
        hst_readers = list(hw2)
        for k in range(8):
            sbw, sd, suse = smgr.load(ipw[k * 128:(k + 1) * 128, 0:1024], 1024)
            h1 = vstt(hst[:, 0:3],
                      src24[:, k * 3:k * 3 + 3], 1.0, mb3, ALU.mult, ALU.subtract,
                      waits=[mb3t, src_tok] + hst_readers)
            hst_readers = []
            h2 = vtt(hst[:, 0:3], hst[:, 0:3], rb3, ALU.mult, waits=[h1, rb3t])
            h3 = vts2(hst[:, 0:3], hst[:, 0:3], SM_LNG[layer][:, k:k + 1],
                      SM_LNB[layer][:, k:k + 1], ALU.mult, ALU.add, waits=[h2])
            for m in range(8):
                tile, pw, pdone = banks[m]
                tok = mm(tile[:, 0:3], sbw[:, m * 128:(m + 1) * 128],
                         hst[:, 0:3], k == 0, k == 7,
                         ([h3, sd] + pw) if k == 0 else [h3, sd])
                if k == 7:
                    mmt[m] = tok
                if m == 7:
                    suse(tok)
                    hst_readers = [tok]
        hsd(mmt[7])
        mb3d(mmt[7])
        rb3d(mmt[7])
        sttoks = []
        for m in range(8):
            tile, pw, pdone = banks[m]
            e = act(stash_t[:, m * 4:m * 4 + 3], tile[:, 0:3], AF.Copy,
                    waits=[mmt[m]] + last["stash"])
            pdone(e)
            sttoks.append(e)
        last["stash"] = sttoks

    def exchange(nm, payload_ap, pay_w, pay_tok, out_ap):
        ain, aout = AR[nm]
        t0 = vts1(exch_t[:, 0:pay_w], payload_ap, pe_t[:, 0:1], ALU.mult,
                  waits=[pay_tok] + last["exch"])
        t1 = vts1(exch_t[:, 256:256 + pay_w], payload_ap, po_t[:, 0:1], ALU.mult,
                  waits=[pay_tok])
        d0 = b.dma("gpsimd", ain[:, 0:pay_w], exch_t[:, 0:pay_w], waits=[t0], pub="de")
        d1 = b.dma("gpsimd", ain[:, pay_w:2 * pay_w], exch_t[:, 256:256 + pay_w],
                   waits=[t1], pub="de")
        cc = b.emit("gpsimd",
                    lambda h, i=ain, o=aout: h.collective_compute(
                        "AllReduce", ALU.add, replica_groups=PAIRS,
                        ins=[i.opt()], outs=[o.opt()]),
                    [d0, d1], "cc", 1)
        d2 = b.dma("gpsimd", exch_t[:, 0:pay_w], aout[:, 0:pay_w], waits=[cc], pub="de")
        d3 = b.dma("gpsimd", exch_t[:, 256:256 + pay_w], aout[:, pay_w:2 * pay_w],
                   waits=[cc], pub="de")
        c0 = vts1(out_ap, exch_t[:, 0:pay_w], po_t[:, 0:1], ALU.mult, waits=[d2])
        c1 = vstt(out_ap, exch_t[:, 256:256 + pay_w], pe_t[:, 0:1], out_ap,
                  ALU.mult, ALU.add, waits=[d3, c0])
        last["exch"] = [c1]
        return c1

    def stage_barrier():
        toks = [(s, b.cnt[s]) for s in SEM_NAMES if b.cnt[s] > 0 and s != "cc"]
        toks += [(s, b.cnt[s]) for s in ("cc",) if b.cnt[s] > 0]
        for e in ENGS:
            b.barrier_wait(e, toks)
        return toks

    def quant128(tm_ap, w, deps):
        nb = w // 16
        t_abs = act(FS1[:, 0:w], tm_ap, AF.Abs, waits=deps)
        scl = scn_t[:, 0:nb]
        scl2 = scn_t[:, 256:256 + nb]
        rscl = scn_t[:, 512:512 + nb]
        t1 = vreduce(scl.rearrange("p (nb one) -> p nb one", one=1),
                     tm_ap.rearrange("p (nb w2) -> p nb w2", w2=16),
                     ALU.max, absval=True, waits=deps + last["scn"])
        t2 = vts2(scl2, scl, 1.0 / 6.0, 1e-12, ALU.mult, ALU.max, waits=[t1])
        t3 = vrecip(rscl, scl2, waits=[t2])
        rexp = rscl.unsqueeze(2).broadcast_to([128, nb, 16])
        sexp = scl2.unsqueeze(2).broadcast_to([128, nb, 16])
        f3 = FS1[:, 0:w].rearrange("p (nb w2) -> p nb w2", w2=16)
        t4 = vtt(f3, f3, rexp, ALU.mult, waits=[t3, t_abs])
        xn = FS1[:, 0:w]
        a0 = vts1(FS2[:, 0:w], xn, 2.5, ALU.is_gt, waits=[t4] + last["fs2"])
        a1 = vstt(FS2[:, 0:w], xn, 3.5, FS2[:, 0:w], ALU.is_gt, ALU.add, waits=[a0])
        a2 = vstt(FS2[:, 0:w], xn, 5.0, FS2[:, 0:w], ALU.is_gt, ALU.add, waits=[a1])
        a3 = vstt(FS2[:, 0:w], xn, 5.0, FS2[:, 0:w], ALU.is_gt, ALU.add, waits=[a2])
        a4 = vts1(FS2[:, 0:w], FS2[:, 0:w], 2.0, ALU.mult, waits=[a3])
        a5 = vstt(FS2[:, 0:w], xn, 0.25, FS2[:, 0:w], ALU.is_gt, ALU.add, waits=[a4])
        a6 = vstt(FS2[:, 0:w], xn, 0.75, FS2[:, 0:w], ALU.is_gt, ALU.add, waits=[a5])
        a7 = vstt(FS2[:, 0:w], xn, 1.25, FS2[:, 0:w], ALU.is_gt, ALU.add, waits=[a6])
        a8 = vstt(FS2[:, 0:w], xn, 1.75, FS2[:, 0:w], ALU.is_gt, ALU.add, waits=[a7])
        a9 = vts1(FS2[:, 0:w], FS2[:, 0:w], 0.5, ALU.mult, waits=[a8])
        sg = act(FS1[:, 0:w], tm_ap, AF.Sign, waits=[a8])
        q1 = vtt(FS2[:, 0:w], FS2[:, 0:w], FS1[:, 0:w], ALU.mult, waits=[a9, sg])
        f2 = FS2[:, 0:w].rearrange("p (nb w2) -> p nb w2", w2=16)
        q2 = vtt(f2, f2, sexp, ALU.mult, waits=[q1])
        last["scn"] = [q2]
        last["fs2"] = []
        last["tmrd"] = [q2, sg]
        return q2

    def wq_pass():
        specs = [(IN["f0_gw"], qgw, 22, 1024), (IN["f0_uw"], quw, 22, 1024),
                 (IN["f0_dw"], qdw, 8, 2816)]
        stage = FS1[:, 0:128]
        for wap, qap, nrt, C in specs:
            for r in range(nrt):
                d0 = b.dma("sync", HTM[:, 0:C], wap[r * 128:(r + 1) * 128, :],
                           waits=last["htm"], pub="dw")
                last["htm"] = []
                qtoks = []
                for s0 in range(0, C, 1024):
                    w = min(1024, C - s0)
                    qt = quant128(HTM[:, s0:s0 + w], w, [d0] + last["fs2w"])
                    for c in range(w // 128):
                        ps, pw, pdone = pp.get()
                        tt_ = tr(ps[:, 0:128], FS2[:, c * 128:(c + 1) * 128], [qt] + pw)
                        ek = act(r12b_t[:, 0:128], ps[:, 0:128], AF.Copy,
                                 waits=[tt_] + bc_last["r12b"])
                        pdone(ek)
                        bc_last["r12b"] = [ek]
                        dk = b.dma("gpsimd", qap[s0 + c * 128:s0 + (c + 1) * 128,
                                               r * 128:(r + 1) * 128],
                                   r12b_t[:, 0:128], waits=[ek], pub="d4")
                        bc_last["r12b"] = [ek, dk]
                        qtoks.append(dk)
                    last["fs2w"] = [qtoks[-1]] if qtoks else []
                last["htm"] = qtoks
        last["fs2w"] = []

    def f0_ffn(bar):
        sttoks = stats_pass(xstr, bar)
        stok = max(sttoks, key=lambda t: t[1])
        for ci in range(16):
            mu, r1, _, rtok = ln_rows(ci, 1, [stok], False, False)
            mub, mtk, mubd = bcast_row(mu, 128, [rtok], dst="mub")
            r1b, rtk, r1bd = bcast_row(r1, 128, [rtok], dst="r12b")
            rowt = rtk if rtk[1] > mtk[1] else mtk
            c0 = ci * 128
            etoks = []
            for k in range(8):
                lt, lw, ld_ = sp.get()
                t1 = vstt(lt[:, 0:128], xstr[k][:, c0:c0 + 128], 1.0, mub,
                          ALU.mult, ALU.subtract, waits=[rowt] + lw)
                t2 = vtt(lt[:, 0:128], lt[:, 0:128], r1b, ALU.mult, waits=[t1])
                ps, pw, pdone = pp.get()
                t3 = tr(ps[:, 0:128], lt[:, 0:128], [t2] + pw)
                ld_(t3)
                e = act(HTM[:, k * 128:(k + 1) * 128], ps[:, 0:128], AF.Copy,
                        waits=[t3] + last["htm"])
                last["htm"] = []
                pdone(e)
                etoks.append(e)
            qt = quant128(HTM[:, 0:1024], 1024, etoks)
            if DBG is not None and DBG[3] == "f0x" and ci == 0:
                d1 = b.dma("gpsimd", yout[0:128, 0:1024], HTM[:, 0:1024],
                           waits=etoks, pub="do")
                d2 = b.dma("gpsimd", yout[128:256, 0:1024], FS2[:, 0:1024],
                           waits=[qt], pub="do")
                last["htm"] = last["htm"] + [d1]
                last["tmrd"] = last["tmrd"] + [d1]
                last["fs2"] = last["fs2"] + [d2]
            qxtoks = []
            for k in range(8):
                ps, pw, pdone = pp.get()
                t3 = tr(ps[:, 0:128], FS2[:, k * 128:(k + 1) * 128], [qt] + pw)
                e = act(QXF[k][:, 0:128], ps[:, 0:128], AF.Copy,
                        waits=[t3] + last["qxf"])
                pdone(e)
                qxtoks.append(e)
            last["qxf"] = []
            last["fs2"] = qxtoks
            htoks = []
            for m in range(22):
                sb1, sd1, su1 = smgr.load(qgw[:, m * 128:(m + 1) * 128]
                                          .rearrange("(kb p) j -> p kb j", p=128), 1024)
                pg, pwg, pdg = pp.get()
                tg = None
                for k in range(8):
                    tg = mm(pg[:, 0:128], sb1[:, k * 128:(k + 1) * 128],
                            QXF[k][:, 0:128], k == 0, k == 7,
                            ([sd1] + qxtoks + pwg) if k == 0 else ())
                su1(tg)
                sb2, sd2, su2 = smgr.load(quw[:, m * 128:(m + 1) * 128]
                                          .rearrange("(kb p) j -> p kb j", p=128), 1024)
                pu, pwu, pdu = pp.get()
                tu = None
                for k in range(8):
                    tu = mm(pu[:, 0:128], sb2[:, k * 128:(k + 1) * 128],
                            QXF[k][:, 0:128], k == 0, k == 7,
                            ([sd2] + pwu) if k == 0 else ())
                su2(tu)
                g1, gw1, gd1 = sp.get()
                sg_ = act(g1[:, 0:128], pg[:, 0:128], AF.Silu, waits=[tg] + gw1)
                pdg(sg_)
                g2, gw2, gd2 = sp.get()
                su_ = act(g2[:, 0:128], pu[:, 0:128], AF.Copy, waits=[tu] + gw2)
                pdu(su_)
                hv = vtt(g1[:, 0:128], g1[:, 0:128], g2[:, 0:128], ALU.mult,
                         waits=[sg_, su_])
                gd2(hv)
                ps, pw, pdone = pp.get()
                t3 = tr(ps[:, 0:128], g1[:, 0:128], [hv] + pw)
                gd1(t3)
                e = act(HTM[:, m * 128:(m + 1) * 128], ps[:, 0:128], AF.Copy,
                        waits=[t3] + last["tmrd"])
                pdone(e)
                htoks.append(e)
            if DBG is not None and DBG[3] == "f0h" and ci == 0:
                for j in range(3):
                    w = 1024 if j < 2 else 768
                    dj = b.dma("gpsimd", yout[j * 128:(j + 1) * 128, 0:w],
                               HTM[:, j * 1024:j * 1024 + w], waits=htoks, pub="do")
                    htoks = htoks + [dj]
            qhtoks = []
            for s0, w in ((0, 1024), (1024, 1024), (2048, 768)):
                qt = quant128(HTM[:, s0:s0 + w], w, htoks)
                for c in range(w // 128):
                    mi = (s0 // 128) + c
                    ps, pw, pdone = pp.get()
                    t3 = tr(ps[:, 0:128], FS2[:, c * 128:(c + 1) * 128], [qt] + pw)
                    e = act(QHF[mi][:, 0:128], ps[:, 0:128], AF.Copy,
                            waits=[t3] + last["qhf"])
                    pdone(e)
                    qhtoks.append(e)
                last["fs2"] = qhtoks[-8:]
            last["qhf"] = []
            dtoks = []
            for og in range(8):
                po_, pwo, pdo = pp.get()
                to = None
                for kg0, kgn in ((0, 8), (8, 8), (16, 6)):
                    sbd, sd, sud = smgr.load(
                        qdw[kg0 * 128:(kg0 + kgn) * 128, og * 128:(og + 1) * 128]
                        .rearrange("(kb p) j -> p kb j", p=128), kgn * 128)
                    for kk in range(kgn):
                        mi = kg0 + kk
                        to = mm(po_[:, 0:128], sbd[:, kk * 128:(kk + 1) * 128],
                                QHF[mi][:, 0:128], mi == 0, mi == 21,
                                ([sd] + qhtoks + pwo) if mi == 0 else [sd])
                    sud(to)
                x_sl = xstr[og][:, c0:c0 + 128]
                e = vstt(x_sl, po_[:, 0:128], 1.0, x_sl, ALU.mult, ALU.add,
                         waits=[to])
                pdo(e)
                dtoks.append(e)
            last["qhf"] = qhtoks
            last["qxf"] = qxtoks
            last["htm"] = htoks + dtoks
            mubd(dtoks[-1])
            r1bd(dtoks[-1])
        return ("v", b.cnt["v"])

    def l0_attn():
        smtok = load_branch_smalls(0, [XTOK])
        sttoks = stats_pass(xstr, [XTOK] + smtok)
        stok = max(sttoks, key=lambda t: t[1])
        z0 = vmemset(stash_t[:, :], 0.0, waits=last["stash"])
        z1 = vmemset(stA_t[:, :], 0.0, waits=last["state"][0])
        last["stash"] = [z0]
        last["state"][0] = [z1]
        for ci in range(NCH):
            yt = attn_chunk(0, 0, ci, stok)
            l0_outproj(0, ci, yt)
        exch_tok = exchange("ar1", stA_t[:, :], 128,
                            last["state"][0][-1] if last["state"][0] else z1,
                            stB_t[:, :])
        last["state"][1] = [exch_tok]
        halo_mini2(0, 1, xh_t[:, :], XTOK)
        for ci in reversed(range(NCH)):
            yt = attn_chunk(0, 1, ci, stok)
            l0_outproj(1, ci, yt)
        for ci in range(NCH):
            sc0 = ci * CH
            mu, r1, _, rtok = ln_rows(ci * 2, 2, [stok], False, False)
            mub, mtk, mubd = bcast_row(mu, CH, [rtok], dst="mub")
            r1b, rtk, r1bd = bcast_row(r1, CH, [rtok], dst="r12b")
            lt, lw, ld = sp.get()
            e = None
            for m in range(8):
                x_sl = xstr[m][:, sc0:sc0 + CH]
                acc_sl = accv[m][:, sc0:sc0 + CH]
                t1 = vstt(lt[:, 0:CH], x_sl, 1.0, mub, ALU.mult, ALU.subtract,
                          waits=[mtk, rtk] + (lw if m == 0 else []))
                t2 = vtt(lt[:, 0:CH], lt[:, 0:CH], r1b, ALU.mult, waits=[t1])
                t3 = vstt(acc_sl, acc_sl, SM_FCB[:, m:m + 1], lt[:, 0:CH],
                          ALU.add, ALU.add, waits=[t2])
                e = vtt(x_sl, x_sl, acc_sl, ALU.add, waits=[t3])
            ld(e)
            mubd(e)
            r1bd(e)
        return ("v", b.cnt["v"])

    def l1_wout(ph, ci, y_toks):
        sc0 = ci * CH
        wout = IN["l1a_wout" if ph == 0 else "l1b_wout"]
        for mi in range(4):
            sbw, sdw, suse = smgr.load(wout[:, mi * 128:(mi + 1) * 128]
                                       .rearrange("(kb p) j -> p kb j", p=128), 1024)
            tile, pw, pdone = pp.get()
            tk = None
            for kb in range(8):
                tk = mm(tile[:, 0:CH], sbw[:, kb * 128:(kb + 1) * 128],
                        YY[kb][:, 0:CH], kb == 0, kb == 7,
                        ([sdw] + y_toks + pw) if kb == 0 else ())
            suse(tk)
            tmp, tw, tdone = sp.get()
            e0 = act(tmp[:, 0:CH], tile[:, 0:CH], AF.Copy, waits=[tk] + tw)
            pdone(e0)
            acc_sl = accv[mi][:, sc0:sc0 + CH]
            stg, sw2, sd2 = sp.get()
            if ph == 0:
                t1 = vts1(acc_sl, tmp[:, 0:CH], pe_t[:, 0:1], ALU.mult, waits=[e0])
                t2 = vts1(stg[:, 0:CH], tmp[:, 0:CH], po_t[:, 0:1], ALU.mult,
                          waits=[e0] + sw2)
                dk = b.dma("gpsimd", AR["ar4"][0][:, mi * 2048 + sc0:mi * 2048 + sc0 + CH],
                           stg[:, 0:CH], waits=[t2], pub="d4")
            else:
                t1 = vstt(acc_sl, tmp[:, CH - 1::-1], po_t[:, 0:1], acc_sl,
                          ALU.mult, ALU.add, waits=[e0])
                t2 = vts1(stg[:, 0:CH], tmp[:, CH - 1::-1], pe_t[:, 0:1], ALU.mult,
                          waits=[e0] + sw2)
                dk = b.dma("gpsimd",
                           AR["ar4"][0][:, 8192 + mi * 2048 + sc0:8192 + mi * 2048 + sc0 + CH],
                           stg[:, 0:CH], waits=[t2], pub="d4")
            tdone(t2)
            sd2(dk)

    def l1_attn(bar):
        smtok = load_branch_smalls(1, bar)
        sttoks = stats_pass(xstr, bar + smtok)
        stok = max(sttoks, key=lambda t: t[1])
        z0 = vmemset(stash_t[:, :], 0.0, waits=last["stash"] + bar)
        z1 = vmemset(stA_t[:, :], 0.0, waits=last["state"][0] + bar)
        last["stash"] = [z0]
        last["state"][0] = [z1]
        for ci in range(NCH):
            yt = attn_chunk(1, 0, ci, stok)
            l1_wout(0, ci, yt)
        exch_tok = exchange("ar3", stA_t[:, :], 128, z1 if not last["state"][0]
                            else last["state"][0][-1], stB_t[:, :])
        last["state"][1] = [exch_tok]
        htoks = []
        for db in range(8):
            htoks.append(vcopy(xh_t[:, db * 3:db * 3 + 3],
                               xstr[db][:, T_HALF - 3:T_HALF], waits=bar))
        h_ex = exchange("ar2", xh_t[:, 0:24], 24, htoks[-1], xh_t[:, 0:24])
        halo_mini2(1, 1, xh_t[:, :], h_ex)
        for ci in reversed(range(NCH)):
            yt = attn_chunk(1, 1, ci, stok)
            l1_wout(1, ci, yt)
        bar2 = stage_barrier()
        cc4 = b.emit("gpsimd",
                     lambda h: h.collective_compute(
                         "AllReduce", ALU.add, replica_groups=PAIRS,
                         ins=[AR["ar4"][0].opt()], outs=[AR["ar4"][1].opt()]),
                     bar2, "cc", 1)
        for mi in range(4):
            for g in range(4):
                c0 = g * 512
                t0t, w0, d0_ = sp.get()
                dk0 = b.dma("gpsimd", t0t[:, 0:512],
                            AR["ar4"][1][:, mi * 2048 + c0:mi * 2048 + c0 + 512],
                            waits=[cc4] + w0, pub="de")
                acc_sl = accv[4 + mi][:, c0:c0 + 512]
                e1 = vts1(acc_sl, t0t[:, 0:512], pe_t[:, 0:1], ALU.mult, waits=[dk0])
                d0_(e1)
                t1t, w1, d1_ = sp.get()
                dk1 = b.dma("gpsimd", t1t[:, 0:512],
                            AR["ar4"][1][:, 8192 + mi * 2048 + c0:8192 + mi * 2048 + c0 + 512],
                            waits=[cc4] + w1, pub="de")
                e2 = vstt(acc_sl, t1t[:, 0:512], po_t[:, 0:1], acc_sl,
                          ALU.mult, ALU.add, waits=[dk1, e1])
                d1_(e2)
        for ci in range(NCH):
            sc0 = ci * CH
            mu, r1, _, rtok = ln_rows(ci * 2, 2, [stok], False, False)
            mub, mtk, mubd = bcast_row(mu, CH, [rtok], dst="mub")
            r1b, rtk, r1bd = bcast_row(r1, CH, [rtok], dst="r12b")
            e = None
            for m in range(8):
                x_sl = xstr[m][:, sc0:sc0 + CH]
                acc_sl = accv[m][:, sc0:sc0 + CH]
                lt, lw, ld_ = sp.get()
                t1 = vstt(lt[:, 0:CH], x_sl, 1.0, mub, ALU.mult, ALU.subtract,
                          waits=[mtk, rtk, ("v", b.cnt["v"])] + lw)
                t2 = vtt(lt[:, 0:CH], lt[:, 0:CH], r1b, ALU.mult, waits=[t1])
                t3 = vtt(lt[:, 0:CH], lt[:, 0:CH], acc_sl, ALU.add, waits=[t2])
                t4 = act(lt[:, 0:CH], lt[:, 0:CH], AF.Gelu, waits=[t3])
                e = vtt(x_sl, x_sl, lt[:, 0:CH], ALU.add, waits=[t4])
                ld_(e)
            mubd(e)
            r1bd(e)
        return ("v", b.cnt["v"])

    b1c_t = wv(9856, 128)

    def IN_B1_COL(m1):
        return b1c_t[:, m1:m1 + 1]

    def f1_ffn(bar):
        sttoks = stats_pass(xstr, bar)
        stok = max(sttoks, key=lambda t: t[1])
        b2buf, b2d, b2use = smgr.load(IN["f1_b2"], 1024, extra=bar, rows=8)
        lg = [b2d,
              b.dma("sync", b1c_t, IN["f1_b1"], waits=bar, pub="dw"),
              b.dma("sync", GWS[:, 0:64].rearrange("p (kb j) -> p kb j", j=8),
                    IN["f1_gw"].rearrange("(kb p) j -> p kb j", p=128),
                    waits=bar, pub="dw"),
              b.dma("sync", hrow_t[0:1, 16:24], IN["f1_gb"], waits=bar, pub="dw"),
              b.dma("sync", hrow_t[0:1, 24:32], IN["f1_rms"], waits=bar, pub="dw")]
        b2toks = []
        for m in range(8):
            ps, pw, pdone = pp.get()
            t1 = mm(ps[0:128, 0:1], b2buf[0:8, m * 128:(m + 1) * 128],
                    cs_t[0:8, 1:2], True, True, lg + pw)
            e = act(SM_B2S[:, m:m + 1], ps[:, 0:1], AF.Copy, waits=[t1])
            pdone(e)
            b2toks.append(e)
            b2use(t1)
        gbb = sm[:, 416:424]
        rmb = sm[:, 424:432]
        ps, pw, pdone = pp.get()
        t1 = mm(ps[:, 0:8], ones1_t[0:1, :], hrow_t[0:1, 16:24], True, True, lg + pw)
        e1 = act(gbb, ps[:, 0:8], AF.Copy, waits=[t1])
        pdone(e1)
        ps, pw, pdone = pp.get()
        t1 = mm(ps[:, 0:8], ones1_t[0:1, :], hrow_t[0:1, 24:32], True, True, lg + pw)
        e2 = act(rmb, ps[:, 0:8], AF.Copy, waits=[t1])
        pdone(e2)
        ng_done = []
        for g in range(16):
            c0 = g * 128
            mu, r1, _, rtok = ln_rows(g, 1, [stok], False, False)
            mub, mtk, mubd = bcast_row(mu, 128, [rtok], dst="mub")
            r1b, rtk, r1bd = bcast_row(r1, 128, [rtok], dst="r12b")
            ps, pw, pdone = pp.get()
            tk = None
            lts = []
            for k in range(8):
                lt, lw, ld_ = sp.get()
                t1 = vstt(lt[:, 0:128], xstr[k][:, c0:c0 + 128], 1.0, mub,
                          ALU.mult, ALU.subtract, waits=[mtk, rtk] + lw)
                t2 = vtt(lt[:, 0:128], lt[:, 0:128], r1b, ALU.mult, waits=[t1])
                tk = mm(ps[:, 0:8], lt[:, 0:128],
                        GWS[:, 0:64].rearrange("p (kb j) -> p kb j", j=8)[:, k, :],
                        k == 0, k == 7, ([t2] + lg + pw) if k == 0 else [t2])
                ld_(tk)
            gt, gw_, gd = sp.get()
            q0 = b.emit("vector", lambda h, o=gt[:, 0:8], p=ps[:, 0:8], bb=gbb:
                        h.tensor_tensor(o, p, bb, ALU.add), [tk, e1] + gw_, "v", 1)
            pdone(q0)
            q1 = vtt(gt[:, 8:16], gt[:, 0:8], gt[:, 0:8], ALU.mult, waits=[q0])
            q2 = vreduce(gt[:, 16:17].rearrange("p (a b) -> p a b", b=1),
                         gt[:, 8:16].rearrange("p (a b) -> p a b", b=8),
                         ALU.add, waits=[q1])
            q3 = act(gt[:, 17:18], gt[:, 16:17], AF.Sqrt, bias=F32EPS, scale=0.125,
                     waits=[q2])
            q4 = vrecip(gt[:, 18:19], gt[:, 17:18], waits=[q3])
            q5 = b.emit("vector", lambda h, o=gt[:, 0:8], i=gt[:, 0:8], s=gt[:, 18:19]:
                        h.tensor_single_scalar(o, i, s, ALU.mult), [q4], "v", 1)
            q6 = vtt(gt[:, 0:8], gt[:, 0:8], rmb, ALU.mult, waits=[q5, e2])
            q7 = vreduce(gt[:, 16:17].rearrange("p (a b) -> p a b", b=1),
                         gt[:, 0:8].rearrange("p (a b) -> p a b", b=8),
                         ALU.max, waits=[q6])
            q8 = b.emit("vector", lambda h, o=gt[:, 0:8], i=gt[:, 0:8], s=gt[:, 16:17]:
                        h.tensor_single_scalar(o, i, s, ALU.subtract), [q7], "v", 1)
            q9 = act(gt[:, 0:8], gt[:, 0:8], AF.Exp, scale=2.0, waits=[q8])
            q10 = vreduce(gt[:, 16:17].rearrange("p (a b) -> p a b", b=1),
                          gt[:, 0:8].rearrange("p (a b) -> p a b", b=8),
                          ALU.add, waits=[q9])
            q11 = vrecip(gt[:, 17:18], gt[:, 16:17], waits=[q10])
            q12 = vts1(gt[:, 18:19], gt[:, 17:18], 1e-6, ALU.add, waits=[q11])
            q13 = vrecip(gt[:, 19:20], gt[:, 18:19], waits=[q12])
            q14 = vtt(gt[:, 20:21], gt[:, 17:18], gt[:, 19:20], ALU.mult, waits=[q13])
            ps2, pw2, pdone2 = pp.get()
            tt_ = tr(ps2[0:1, 0:128], gt[:, 20:21], [q14] + pw2)
            ek = act(ROWG[0:1, c0:c0 + 128], ps2[0:1, 0:128], AF.Copy,
                     waits=[tt_] + ng_done)
            pdone2(ek)
            gd(ek)
            ng_done = ng_done + [ek]
            mubd(ek)
            r1bd(ek)
        ngtok = ng_done[-1]
        for tc in range(4):
            sc0 = tc * 512
            xftoks = []
            for half in range(2):
                c0 = sc0 + half * 256
                mu, r1, _, rtok = ln_rows(tc * 4 + half * 2, 2, [stok], False, False)
                mub, mtk, mubd = bcast_row(mu, 256, [rtok], dst="mub")
                r1b, rtk, r1bd = bcast_row(r1, 256, [rtok], dst="r12b")
                e = None
                for k in range(8):
                    dst = XF[k][:, half * 256:half * 256 + 256]
                    t1 = vstt(dst, xstr[k][:, c0:c0 + 256], 1.0, mub,
                              ALU.mult, ALU.subtract,
                              waits=[mtk, rtk] + last["xf"] if k == 0 else [mtk, rtk])
                    e = vtt(dst, dst, r1b, ALU.mult, waits=[t1])
                    xftoks.append(e)
                last["xf"] = []
                mubd(e)
                r1bd(e)
            for seg in range(64):
                htoks = []
                w2toks = []
                for j in range(2):
                    m1 = seg * 2 + j
                    sb1, sd1, su1 = smgr.load(IN["f1_w1"][:, m1 * 128:(m1 + 1) * 128]
                                              .rearrange("(kb p) j2 -> p kb j2", p=128),
                                              1024)
                    ph_, pwh, pdh = pp.get()
                    tk = None
                    for k in range(8):
                        tk = mm(ph_[:, 0:512], sb1[:, k * 128:(k + 1) * 128],
                                XF[k][:, 0:512], k == 0, k == 7,
                                ([sd1] + xftoks + pwh) if k == 0 else ())
                    su1(tk)
                    e = act(HSEG[j][:, 0:512], ph_[:, 0:512], AF.Gelu,
                            bias=IN_B1_COL(m1), waits=[tk] + last["hseg"][j])
                    last["hseg"][j] = []
                    pdh(e)
                    htoks.append(e)
                    w2toks.append(b.dma("sync", W2S[j][:, 0:1024],
                                        IN["f1_w2"][m1 * 128:(m1 + 1) * 128, :],
                                        waits=last["w2s"][j],
                                        pub="dw" if j == 0 else "dz"))
                    last["w2s"][j] = []
                for og in range(8):
                    po_, pwo, pdo = pp.get()
                    tk = None
                    for j in range(2):
                        tk = mm(po_[:, 0:512], W2S[j][:, og * 128:(og + 1) * 128],
                                HSEG[j][:, 0:512], j == 0, j == 1,
                                ([w2toks[0], w2toks[1]] + htoks + pwo) if j == 0 else ())
                        last["w2s"][j] = last["w2s"][j] + [tk]
                        last["hseg"][j] = last["hseg"][j] + [tk]
                    acc_sl = accv[og][:, sc0:sc0 + 512]
                    if seg == 0:
                        e = act(acc_sl, po_[:, 0:512], AF.Copy, waits=[tk])
                    else:
                        e = vstt(acc_sl, po_[:, 0:512], 1.0, acc_sl, ALU.mult, ALU.add,
                                 waits=[tk])
                    pdo(e)
                    last["acc1"][og] = [e]
            ps, pw, pdone = pp.get()
            t1 = mm(ps[:, 0:512], ones1_t[0:1, :], ROWG[0:1, sc0:sc0 + 512], True, True,
                    [ngtok] + pw)
            en = act(NGB[:, 0:512], ps[:, 0:512], AF.Copy, waits=[t1] + last["ngb"])
            pdone(en)
            ftoks = []
            for m in range(8):
                x_sl = xstr[m][:, sc0:sc0 + 512]
                acc_sl = accv[m][:, sc0:sc0 + 512]
                lt, lw, ld_ = sp.get()
                t1 = vts1(lt[:, 0:512], acc_sl, SM_B2S[:, m:m + 1], ALU.add,
                          waits=last["acc1"][m] + b2toks + lw)
                t2 = vtt(lt[:, 0:512], lt[:, 0:512], NGB[:, 0:512], ALU.mult,
                         waits=[t1, en])
                e = vtt(x_sl, x_sl, lt[:, 0:512], ALU.add, waits=[t2])
                ld_(e)
                ftoks.append(e)
            last["ngb"] = ftoks
            last["xf"] = ftoks
        return ("v", b.cnt["v"])

    def store_out(final_tok):
        st_last = [[], []]
        for tt in range(16):
            tmt = TMT if tt % 2 == 0 else QAT
            wts = st_last[tt % 2]
            st_last[tt % 2] = []
            etk = []
            for db in range(8):
                ps, pw, pdone = pp.get()
                ttk = tr(ps[:, 0:128], xstr[db][:, tt * 128:(tt + 1) * 128],
                         [final_tok] + pw + (wts if db == 0 else []))
                ek = act(tmt[:, db * 128:(db + 1) * 128], ps[:, 0:128], AF.Copy,
                         waits=[ttk] + (wts if db == 0 else []))
                pdone(ek)
                etk.append(ek)
            dtok = b.dma("gpsimd", yout[tt * 128:(tt + 1) * 128, :], tmt[:, 0:1024],
                         waits=etk, pub="do")
            st_last[tt % 2] = [dtok]
        b.barrier_wait("gpsimd", [("do", b.cnt["do"])])

    if STAGES >= 2:
        stage_barrier()
        wq_pass()
        stage_barrier()
        if DBG is not None and DBG[3] == "wq":
            b.dma("gpsimd", yout[0:1024, :], qgw[0:1024, 1024:2048], pub="do")
            b.dma("gpsimd", yout[1024:2048, :], qgw[0:1024, 1792:2816], pub="do")
            b.barrier_wait("gpsimd", [("do", b.cnt["do"])])
    ftok = l0_attn()
    if STAGES >= 2:
        bar = stage_barrier()
        ftok = f0_ffn(bar)
    if STAGES >= 3:
        bar = stage_barrier()
        ftok = l1_attn(bar)
    if STAGES >= 4:
        bar = stage_barrier()
        ftok = f1_ffn(bar)
    stage_barrier()
    if DBG is None:
        store_out(ftok)
    else:
        b.barrier_wait("gpsimd", [("do", b.cnt["do"]), ftok])

    b.realize(block)
    nc._ctx_holder = ctx
    return nc


_NC_CACHE = {}


def kernel(**inputs):
    in_maps = host_prepare(inputs)
    nc = build_nc()
    res = run_bass_kernel_spmd(nc, in_maps, core_ids=list(range(8)))
    return host_gather(res.results)


# version 05
# speedup vs baseline: 1.3043x; 1.0325x over previous
import contextlib
import numpy as np

import concourse.bass as bass
import concourse.mybir as mybir
from concourse.bass_utils import run_bass_kernel_spmd

F32 = mybir.dt.float32
AF = mybir.ActivationFunctionType
ALU = mybir.AluOpType

T_HALF = 2048
CH = 256
NCH = T_HALF // CH
NSUB = 16
EPS = 1e-5
F32EPS = float(np.finfo(np.float32).eps)
STAGES = 4
DBG = None
PAIRS = [[0, 1], [2, 3], [4, 5], [6, 7]]

ENGS = ("tensor", "vector", "scalar", "gpsimd", "sync")
SEM_OWNER = {"t": "tensor", "v": "vector", "a": "scalar", "g": "gpsimd"}
SEM_NAMES = ("t", "v", "a", "g", "cc", "di", "dw", "dz", "dx0", "dx1", "do", "de", "d4")
DMA_SEMS = ("di", "dw", "dz", "dx0", "dx1", "do", "de", "d4")


class Bld:
    def __init__(self, nc, sems):
        self.nc = nc
        self.sems = sems
        self.cnt = {k: 0 for k in sems}
        self.stream = {e: [] for e in ENGS}

    def emit(self, eng, fn, waits=(), pub=None, amt=1):
        toks = []
        for tk in waits:
            if tk is None:
                continue
            s, v = tk
            if s in DMA_SEMS:
                v = self.cnt[s]
            toks.append((s, v))
        tok = None
        if pub is not None:
            self.cnt[pub] += amt
            tok = (pub, self.cnt[pub])
        self.stream[eng].append((toks, fn, pub, amt))
        return tok

    def dma(self, eng, out, in_, waits=(), pub="di"):
        waits = list(waits) + [(pub, self.cnt[pub])]
        return self.emit(eng, lambda h, o=out, i=in_: h.dma_start(out=o, in_=i),
                         waits, pub, 16)

    def barrier_wait(self, eng, waits):
        self.emit(eng, lambda h: None, waits)

    def realize(self, block):
        for eng in ENGS:
            items = self.stream[eng]
            sems = self.sems

            def body(h, items=items):
                hw = {}
                for toks, fn, pub, amt in items:
                    need = {}
                    for s, v in toks:
                        if hw.get(s, 0) < v and need.get(s, 0) < v:
                            need[s] = v
                    for s, v in need.items():
                        h.wait_ge(sems[s], v)
                        hw[s] = v
                    inst = fn(h)
                    if pub is not None:
                        inst.then_inc(sems[pub], amt)

            getattr(block, eng)(body)


class Pool:
    def __init__(self, tiles):
        self.tiles = tiles
        self.rd = [None] * len(tiles)
        self.i = 0

    def get(self):
        i = self.i % len(self.tiles)
        self.i += 1
        w = self.rd[i]
        self.rd[i] = None

        def done(tok, i=i):
            self.rd[i] = tok

        return self.tiles[i], ([w] if w is not None else []), done


def _f32(a):
    return np.ascontiguousarray(a, dtype=np.float32)


def _col_pack(v):
    return _f32(np.transpose(np.asarray(v).reshape(-1, 128)))


def _colmat_pack(m, inner):
    m = np.asarray(m)
    nblk = m.shape[0] // 128
    return _f32(np.transpose(m.reshape(nblk, 128, inner), (1, 0, 2)).reshape(128, nblk * inner))


def host_prepare(inputs):
    x = np.asarray(inputs["L0_attn_hidden_states"])

    def wT(name):
        return _f32(np.asarray(inputs[name]).T)

    def arr(name):
        return _f32(np.asarray(inputs[name]))

    shared = {
        "l0_ipw": wT("L0_attn_in_proj_w"),
        "l0_opw": wT("L0_attn_out_proj_w"),
        "l0_fcw": wT("L0_attn_fc_w"),
        "l0_fcb": _col_pack(inputs["L0_attn_fc_b"]),
        "l0_lng": _col_pack(inputs["L0_attn_ln_g"]),
        "l0_lnb": _col_pack(inputs["L0_attn_ln_b"]),
        "f0_gw": arr("L0_ffn_w_gate"),
        "f0_uw": arr("L0_ffn_w_up"),
        "f0_dw": arr("L0_ffn_w_down"),
        "l1_lng": _col_pack(inputs["L1_attn_ln_g"]),
        "l1_lnb": _col_pack(inputs["L1_attn_ln_b"]),
        "f1_w1": _f32(np.transpose(np.asarray(inputs["L1_ffn_w1"]), (1, 0, 2)).reshape(1024, 16384)),
        "f1_w2": _f32(np.asarray(inputs["L1_ffn_w2"]).reshape(16384, 1024)),
        "f1_b1": _f32(np.transpose(np.asarray(inputs["L1_ffn_b1"]).reshape(128, 128))),
        "f1_b2": arr("L1_ffn_b2"),
        "f1_gw": wT("L1_ffn_gate_w"),
        "f1_gb": _f32(np.asarray(inputs["L1_ffn_gate_b"]).reshape(1, 8)),
        "f1_rms": _f32(np.asarray(inputs["L1_ffn_rms_w"]).reshape(1, 8)),
        "c_ident": _f32(np.eye(128)),
        "c_scal": _f32(np.tile(np.array([1e-5, 1.0, 2.0, float(np.finfo(np.float32).eps),
                                         1e-12, 1.0 / 6.0, 1e-6, 0.5], dtype=np.float32),
                               (128, 1))),
        "c_ones1": _f32(np.ones((1, 128))),
        "c_w1024": _f32(np.full((128, 1), 1.0 / 1024.0)),
    }
    shared["c_oh16"] = _f32(np.eye(16))

    l0_br = {}
    for tag, sfx in (("f", ""), ("b", "_b")):
        l0_br[tag] = {
            "convw": _colmat_pack(inputs["L0_attn_conv_w" + sfx], 4),
            "convb": _col_pack(inputs["L0_attn_conv_b" + sfx]),
            "xpw": wT("L0_attn_x_proj_w" + sfx),
            "dtw": wT("L0_attn_dt_proj_w" + sfx),
            "dtb": _col_pack(inputs["L0_attn_dt_proj_b" + sfx]),
            "alog": _colmat_pack(inputs["L0_attn_A_log" + sfx], 16),
            "dp": _col_pack(inputs["L0_attn_Dp" + sfx]),
        }
    l1_br = {}
    for tag in ("m1", "m2"):
        p = "L1_attn_" + tag + "_"
        l1_br[tag] = {
            "win": arr(p + "Win"),
            "convw": _colmat_pack(inputs[p + "Wconv"], 4),
            "convb": _col_pack(inputs[p + "bconv"]),
            "xpw": arr(p + "Wx"),
            "dtw": arr(p + "Wdt"),
            "dtb": _col_pack(inputs[p + "bdt"]),
            "alog": _colmat_pack(inputs[p + "Alog"], 16),
            "dp": _col_pack(inputs[p + "Dp"]),
            "wout": arr(p + "Wout"),
        }

    in_maps = []
    for c in range(8):
        bi, par = c // 2, c % 2
        m = dict(shared)
        if par == 0:
            sl = x[bi, 0:T_HALF]
            halo = x[bi, T_HALF + 2:T_HALF - 1:-1]
        else:
            sl = x[bi, T_HALF:2 * T_HALF][::-1]
            halo = x[bi, T_HALF - 3:T_HALF]
        xin = np.zeros((2176, 1024), dtype=np.float32)
        xin[:T_HALF] = sl
        xin[T_HALF:T_HALF + 3] = halo
        m["xin"] = xin
        pe = np.full((128, 1), 1.0 if par == 0 else 0.0, dtype=np.float32)
        m["c_pe"] = pe
        m["c_po"] = _f32(1.0 - pe)
        A, B = (l0_br["f"], l0_br["b"]) if par == 0 else (l0_br["b"], l0_br["f"])
        for k, v in A.items():
            m["l0a_" + k] = v
        for k, v in B.items():
            m["l0b_" + k] = v
        A, B = (l1_br["m1"], l1_br["m2"]) if par == 0 else (l1_br["m2"], l1_br["m1"])
        for k, v in A.items():
            m["l1a_" + k] = v
        for k, v in B.items():
            m["l1b_" + k] = v
        in_maps.append(m)
    return in_maps


def host_gather(results):
    out = np.empty((4, 4096, 1024), dtype=np.float32)
    for c in range(8):
        bi, par = c // 2, c % 2
        y = results[c]["yout"]
        if par == 0:
            out[bi, 0:T_HALF] = y
        else:
            out[bi, T_HALF:2 * T_HALF] = y[::-1]
    return out


def build_nc():
    nc = bass.Bass("TRN2", target_bir_lowering=False)
    ctx = contextlib.ExitStack()

    IN = {}
    in_specs = [
        ("xin", [2176, 1024]),
        ("l0_ipw", [1024, 2048]), ("l0_opw", [1024, 1024]), ("l0_fcw", [1024, 1024]),
        ("l0_fcb", [128, 8]), ("l0_lng", [128, 8]), ("l0_lnb", [128, 8]),
        ("f0_gw", [2816, 1024]), ("f0_uw", [2816, 1024]), ("f0_dw", [1024, 2816]),
        ("l1_lng", [128, 8]), ("l1_lnb", [128, 8]),
        ("f1_w1", [1024, 16384]), ("f1_w2", [16384, 1024]),
        ("f1_b1", [128, 128]), ("f1_b2", [8, 1024]),
        ("f1_gw", [1024, 8]), ("f1_gb", [1, 8]), ("f1_rms", [1, 8]),
        ("c_ident", [128, 128]), ("c_ones1", [1, 128]), ("c_w1024", [128, 1]),
        ("c_scal", [128, 8]),
        ("c_oh16", [16, 16]), ("c_pe", [128, 1]), ("c_po", [128, 1]),
    ]
    for ph in ("a", "b"):
        in_specs += [
            ("l0%s_convw" % ph, [128, 32]), ("l0%s_convb" % ph, [128, 8]),
            ("l0%s_xpw" % ph, [1024, 96]), ("l0%s_dtw" % ph, [64, 1024]),
            ("l0%s_dtb" % ph, [128, 8]), ("l0%s_alog" % ph, [128, 128]),
            ("l0%s_dp" % ph, [128, 8]),
            ("l1%s_win" % ph, [1024, 2048]), ("l1%s_convw" % ph, [128, 32]),
            ("l1%s_convb" % ph, [128, 8]), ("l1%s_xpw" % ph, [1024, 64]),
            ("l1%s_dtw" % ph, [32, 1024]), ("l1%s_dtb" % ph, [128, 8]),
            ("l1%s_alog" % ph, [128, 128]), ("l1%s_dp" % ph, [128, 8]),
            ("l1%s_wout" % ph, [1024, 512]),
        ]
    for name, shape in in_specs:
        IN[name] = nc.dram_tensor(name, shape, F32, kind="ExternalInput").ap()
    yout = nc.dram_tensor("yout", [T_HALF, 1024], F32, kind="ExternalOutput").ap()

    qgw = nc.dram_tensor("qgw", [1024, 2816], F32).ap()
    quw = nc.dram_tensor("quw", [1024, 2816], F32).ap()
    qdw = nc.dram_tensor("qdw", [2816, 1024], F32).ap()
    AR = {}
    for nm, shape in (("ar1", [128, 256]), ("ar2", [128, 48]),
                      ("ar3", [128, 256]), ("ar4", [128, 16384])):
        AR[nm] = (nc.dram_tensor(nm + "_in", shape, F32).ap(),
                  nc.dram_tensor(nm + "_out", shape, F32).ap())

    def sbuf(name, shape, dt=F32):
        return ctx.enter_context(nc.sbuf_tensor(name, shape, dt))

    xstr_t = sbuf("xstr", [128, 16384])
    acc_t = sbuf("acc", [128, 16384])
    wsa_t = sbuf("wsa", [128, 10776])
    s512_t = [sbuf("s5_%d" % i, [128, 512]) for i in range(3)]
    scn_t = sbuf("scn", [128, 768])
    coh_t = sbuf("coh", [16, 16])
    strip_t = sbuf("strip", [128, 1024])
    stripb_t = sbuf("stripb", [128, 1024])
    scn2_t = sbuf("scn2", [128, 768])
    bcB_t = sbuf("bcB", [16, 256])
    bcC_t = sbuf("bcC", [16, 256])
    xdbl_t = sbuf("xdbl", [64, 256])
    statc_t = sbuf("statc", [128, 64])
    mub_t = sbuf("mub", [128, 256])
    r12b_t = sbuf("r12b", [128, 256])
    rowb_t = sbuf("rowb", [1, 1792])
    hrow_t = sbuf("hrow", [1, 32])
    ident_t = sbuf("ident", [128, 128])
    ones1_t = sbuf("ones1", [1, 128])
    w1024_t = sbuf("w1024", [128, 1])
    cs_t = sbuf("cscal", [128, 8])
    pe_t = sbuf("pe", [128, 1])
    po_t = sbuf("po", [128, 1])
    smalls_t = sbuf("smalls", [128, 448])
    stash_t = sbuf("stash", [128, 32])
    stA_t = sbuf("stA", [128, 128])
    stB_t = sbuf("stB", [128, 128])
    xh_t = sbuf("xhalo", [128, 24])

    psum_t = [ctx.enter_context(nc.psum_tensor("ps%d" % i, [128, 512], F32))
              for i in range(8)]
    sems = {k: ctx.enter_context(nc.semaphore("sem_" + k)) for k in SEM_NAMES}
    block = ctx.enter_context(nc.Block())
    b = Bld(nc, sems)
    pp = Pool(psum_t)
    sp = Pool([t[:, :] for t in s512_t])

    def wv(off, w):
        return wsa_t[:, off:off + w]

    HS = [wv(0, 256), wv(256, 256)]
    XS = [wv(512 + i * 259, 259) for i in range(8)]
    UU = [wv(2584 + i * 256, 256) for i in range(8)]
    DU = [wv(4632 + i * 256, 256) for i in range(8)]
    YY = [wv(6680 + i * 256, 256) for i in range(8)]
    DL = [wv(8728 + i * 256, 256) for i in range(8)]
    HTM = wv(0, 2816)
    QHTM = wv(2816, 2816)
    QHF = [wv(5632 + i * 128, 128) for i in range(22)]
    QXF = [wv(8448 + i * 128, 128) for i in range(8)]
    TMT = wv(5632, 1024)
    QAT = wv(6656, 1024)
    FS1 = wv(9472, 1024)
    FS2 = wv(10496 - 1024 + 1024, 1024) if False else None
    FS2 = wv(2816, 1024)
    XF = [wv(i * 512, 512) for i in range(8)]
    HSEG = [wv(4096 + i * 512, 512) for i in range(2)]
    W2S = [wv(5120 + i * 1024, 1024) for i in range(2)]
    ROWG = wsa_t[0:1, 7168:9216]
    NGB = wv(9216, 512)
    GWS = wv(9728, 64)

    exch_t = scn_t
    SCA = [scn_t[:, 0:256], scn2_t[:, 0:256]]
    SCB = [scn_t[:, 256:512], scn2_t[:, 256:512]]
    SCH = [scn_t[:, 512:768], scn2_t[:, 512:768]]

    xstr = [xstr_t[:, i * 2048:(i + 1) * 2048] for i in range(8)]
    accv = [acc_t[:, i * 2048:(i + 1) * 2048] for i in range(8)]

    def mm(psum_ap, lhsT, rhs, start, stop, waits=(), pub="t"):
        return b.emit("tensor",
                      lambda h, p=psum_ap, l=lhsT, r=rhs, s=start, t=stop:
                      h.matmul(p, l, r, start=s, stop=t),
                      waits, "t", 1)

    def tr(psum_ap, in_ap, waits, idn=None):
        if idn is None:
            idn = ident_t[:, :]
        return b.emit("tensor",
                      lambda h, p=psum_ap, i=in_ap, d=idn: h.transpose(p, i, d),
                      waits, "t", 1)

    def act(out, in_, func, bias=0.0, scale=1.0, waits=()):
        return b.emit("scalar",
                      lambda h, o=out, i=in_, f=func, bi=bias, sc=scale:
                      h.activation(o, i, f, bias=bi, scale=sc),
                      waits, "a", 1)

    def vcopy(out, in_, waits=()):
        return b.emit("vector", lambda h, o=out, i=in_: h.tensor_copy(o, i),
                      waits, "v", 1)

    def vtt(out, a0, a1, op, waits=()):
        return b.emit("vector", lambda h, o=out, x=a0, y=a1, p=op:
                      h.tensor_tensor(o, x, y, p), waits, "v", 1)

    def gtt(out, a0, a1, op, waits=()):
        return b.emit("gpsimd", lambda h, o=out, x=a0, y=a1, p=op:
                      h.tensor_tensor(o, x, y, p), waits, "g", 1)

    def vts1(out, in0, s1, op, waits=()):
        return b.emit("vector", lambda h, o=out, i=in0, a=s1, p=op:
                      h.tensor_single_scalar(o, i, a, p), waits, "v", 1)

    def vts2(out, in0, s1, s2, op0, op1, waits=()):
        return b.emit("vector", lambda h, o=out, i=in0, a=s1, c=s2, p0=op0, p1=op1:
                      h.tensor_scalar(o, i, a, c, p0, p1), waits, "v", 1)

    def vstt(out, in0, s, in1, op0, op1, waits=()):
        return b.emit("vector", lambda h, o=out, i=in0, a=s, j=in1, p0=op0, p1=op1:
                      h.scalar_tensor_tensor(o, i, a, j, p0, p1), waits, "v", 1)

    def vscan(out, d0, d1, init, waits=()):
        return b.emit("vector", lambda h, o=out, x=d0, y=d1, i=init:
                      h.tensor_tensor_scan(o, x, y, i, ALU.mult, ALU.add),
                      waits, "v", 1)

    def vrecip(out, in_, waits=()):
        return b.emit("vector", lambda h, o=out, i=in_: h.reciprocal(o, i),
                      waits, "v", 1)

    def vmemset(ap, val, waits=()):
        return b.emit("vector", lambda h, a=ap, v=val: h.memset(a, v),
                      waits, "v", 1)

    def vreduce(out, in_, op, absval=False, waits=()):
        return b.emit("vector",
                      lambda h, o=out, i=in_, p=op, av=absval:
                      h.tensor_reduce(o, i, mybir.AxisListType.X, p,
                                      apply_absolute_value=av),
                      waits, "v", 1)

    ld0 = []
    for src, dst in ((IN["c_ident"], ident_t[:, :]), (IN["c_ones1"], ones1_t[:, :]),
                     (IN["c_w1024"], w1024_t[:, :]), (IN["c_oh16"], coh_t[:, :]),
                     (IN["c_pe"], pe_t[:, :]), (IN["c_po"], po_t[:, :]),
                     (IN["c_scal"], cs_t[:, :])):
        ld0.append(b.dma("sync", dst, src))
    for i, v in enumerate([1e-5, 1.0, 2.0, F32EPS, 1e-12, 1.0 / 6.0, 1e-6, 0.5]):
        nc.const_aps.register(F32, float(v), cs_t[:, i:i + 1])
    sm = smalls_t

    def lsm(c0, src, w):
        return b.dma("sync", sm[:, c0:c0 + w], src)

    ld0 += [lsm(0, IN["l0_fcb"], 8), lsm(8, IN["l0_lng"], 8), lsm(16, IN["l0_lnb"], 8),
            lsm(24, IN["l1_lng"], 8), lsm(32, IN["l1_lnb"], 8)]
    SM_FCB = sm[:, 0:8]
    SM_LNG = {0: sm[:, 8:16], 1: sm[:, 24:32]}
    SM_LNB = {0: sm[:, 16:24], 1: sm[:, 32:40]}
    SM_B2S = sm[:, 40:48]

    def SM_CONVB(ph):
        return sm[:, 48 + 24 * ph:56 + 24 * ph]

    def SM_DTB(ph):
        return sm[:, 56 + 24 * ph:64 + 24 * ph]

    def SM_DP(ph):
        return sm[:, 64 + 24 * ph:72 + 24 * ph]

    def SM_CONVW(ph):
        return sm[:, 96 + 32 * ph:128 + 32 * ph]

    def SM_AMAT(ph):
        return sm[:, 160 + 128 * ph:288 + 128 * ph]

    def load_branch_smalls(layer, deps):
        pre = "l0" if layer == 0 else "l1"
        toks = list(deps)
        for ph, pch in ((0, "a"), (1, "b")):
            toks.append(lsm(48 + 24 * ph, IN[pre + pch + "_convb"], 8))
            toks.append(lsm(56 + 24 * ph, IN[pre + pch + "_dtb"], 8))
            toks.append(lsm(64 + 24 * ph, IN[pre + pch + "_dp"], 8))
            toks.append(lsm(96 + 32 * ph, IN[pre + pch + "_convw"], 32))
            toks.append(lsm(160 + 128 * ph, IN[pre + pch + "_alog"], 128))
        out = []
        for ph in (0, 1):
            a0 = 160 + 128 * ph
            t1 = act(sm[:, a0:a0 + 128], sm[:, a0:a0 + 128], AF.Exp, waits=toks)
            out.append(vts1(sm[:, a0:a0 + 128], sm[:, a0:a0 + 128], -1.0, ALU.mult,
                            waits=[t1]))
        return out

    xload_toks = []
    xld_last = [[], []]
    for tt in range(17):
        tmt = TMT if tt % 2 == 0 else QAT
        wts = xld_last[tt % 2]
        xld_last[tt % 2] = []
        dtok = b.dma("sync", tmt[:, 0:1024], IN["xin"][tt * 128:(tt + 1) * 128, :],
                     waits=wts + (ld0 if tt == 0 else []), pub="dx0" if tt % 2 == 0 else "dx1")
        last = None
        for db in range(8):
            ps, pw, pdone = pp.get()
            ttk = tr(ps[:, 0:128], tmt[:, db * 128:(db + 1) * 128], [dtok] + pw)
            if tt < 16:
                etk = act(xstr[db][:, tt * 128:(tt + 1) * 128], ps[:, 0:128], AF.Copy,
                          waits=[ttk])
            else:
                etk = act(xh_t[:, db * 3:db * 3 + 3], ps[:, 0:3], AF.Copy, waits=[ttk])
            pdone(etk)
            last = etk
        xld_last[tt % 2] = [last]
        xload_toks.append(last)
    XTOK = ("a", b.cnt["a"])

    def stats_pass(src_tiles, deps):
        sq1, w1, d1 = sp.get()
        sq2, w2, d2 = sp.get()
        toks = []
        prev_q = w1 + w2
        for s in range(NSUB):
            ps, pw, pdone = pp.get()
            mtk = None
            sqtoks = []
            for db in range(8):
                src = src_tiles[db][:, s * 128:(s + 1) * 128]
                w0 = list(deps) + (pw if db == 0 else [])
                mtk = mm(ps[:, 0:1], src, w1024_t[:, 0:1], db == 0, db == 7,
                         w0, "t" if db == 7 else None)
                sqt = (sq1 if db < 4 else sq2)[:, (db % 4) * 128:(db % 4 + 1) * 128]
                sqtoks.append(act(sqt, src, AF.Square, waits=w0 + prev_q))
            e1 = act(statc_t[:, s:s + 1], ps[:, 0:1], AF.Copy, waits=[mtk])
            pdone(e1)
            ps2, pw2, pdone2 = pp.get()
            qtk = None
            for db in range(8):
                sqt = (sq1 if db < 4 else sq2)[:, (db % 4) * 128:(db % 4 + 1) * 128]
                qtk = mm(ps2[:, 0:1], sqt, w1024_t[:, 0:1], db == 0, db == 7,
                         [sqtoks[db]] + (pw2 if db == 0 else []),
                         "t" if db == 7 else None)
            prev_q = [qtk]
            e2 = act(statc_t[:, 16 + s:16 + s + 1], ps2[:, 0:1], AF.Copy, waits=[qtk])
            pdone2(e2)
            toks += [e1, e2]
        d1(prev_q[0])
        d2(prev_q[0])
        return toks

    def ln_rows(s0, ns, deps, rev, want_r2):
        w = ns * 128
        tts = []
        for j, stat in ((0, 0), (1, 16)):
            for s in range(ns):
                ps, pw, pdone = pp.get()
                tk = tr(ps[0:1, 0:128], statc_t[:, stat + s0 + s:stat + s0 + s + 1],
                        list(deps) + pw)
                ek = act(rowb_t[0:1, j * 256 + s * 128:j * 256 + (s + 1) * 128],
                         ps[0:1, 0:128], AF.Copy, waits=[tk])
                pdone(ek)
                tts.append(ek)
        mu = rowb_t[0:1, 0:w]
        r1 = rowb_t[0:1, 256:256 + w]
        r12 = rowb_t[0:1, 512:512 + w]
        tmp = rowb_t[0:1, 768:768 + w]
        t1 = vtt(tmp, mu, mu, ALU.mult, waits=tts)
        t2 = vtt(r1, r1, tmp, ALU.subtract, waits=[t1])
        tok = t2
        if want_r2:
            t3 = vcopy(tmp, r1, waits=[t2])
        t4 = act(r1, r1, AF.Sqrt, bias=EPS, waits=[tok if not want_r2 else t3])
        t5 = vrecip(r1, r1, waits=[t4])
        tok = t5
        if want_r2:
            t6 = vtt(r12, r1, r1, ALU.mult, waits=[t5])
            t7 = vtt(r12, r12, tmp, ALU.mult, waits=[t6])
            t8 = act(r12, r12, AF.Sqrt, bias=EPS, waits=[t7])
            t9 = vrecip(r12, r12, waits=[t8])
            t10 = vtt(r12, r12, r1, ALU.mult, waits=[t9])
            tok = t10
        if not rev:
            return mu, r1, (r12 if want_r2 else None), tok
        mur = rowb_t[0:1, 1024:1024 + w]
        r1r = rowb_t[0:1, 1280:1280 + w]
        c1 = vcopy(mur, mu[0:1, w - 1::-1], waits=[tok])
        c2 = vcopy(r1r, r1[0:1, w - 1::-1], waits=[c1])
        tok = c2
        if want_r2:
            r12r = rowb_t[0:1, 1536:1536 + w]
            tok = vcopy(r12r, r12[0:1, w - 1::-1], waits=[c2])
            return mur, r1r, r12r, tok
        return mur, r1r, None, tok

    bc_last = {"mub": [], "r12b": []}

    def bcast_row(row_ap, w, deps, dst=None):
        ps, pw, pdone = pp.get()
        t1 = mm(ps[:, 0:w], ones1_t[0:1, :], row_ap, True, True, list(deps) + pw, "t")
        if dst is None:
            tile, tw, tdone = sp.get()
            t2 = act(tile[:, 0:w], ps[:, 0:w], AF.Copy, waits=[t1] + tw)
            pdone(t2)
            return tile[:, 0:w], t2, tdone
        tile = {"mub": mub_t, "r12b": r12b_t}[dst]
        t2 = act(tile[:, 0:w], ps[:, 0:w], AF.Copy, waits=[t1] + bc_last[dst])
        pdone(t2)

        def tdone(tok, dst=dst):
            bc_last[dst] = [tok]

        return tile[:, 0:w], t2, tdone

    class StripMgr:
        def __init__(self):
            self.bufs = [strip_t, stripb_t]
            self.cons = [[], []]
            self.i = 0

        def load(self, src_ap, cols, extra=(), eng="sync", rows=128):
            i = self.i % 2
            self.i += 1
            buf = self.bufs[i]
            dst = buf[0:rows, 0:cols]
            if src_ap.ndim == 3:
                dst = dst.rearrange("p (kb j) -> p kb j", j=src_ap.shape[2])
            tok = b.dma(eng, dst, src_ap,
                        waits=list(extra) + self.cons[i],
                        pub="dw" if i == 0 else "dz")
            self.cons[i] = []

            def use(t, i=i):
                self.cons[i].append(t)

            return buf, tok, use

    smgr = StripMgr()

    def strip_dma(buf, src_ap, deps):
        return b.dma("sync", buf, src_ap, waits=deps, pub="dw")

    last = {"strip": [], "bcB": [], "bcC": [], "xdbl": [],
            "XS": [[] for _ in range(8)], "UU": [[] for _ in range(8)],
            "DU": [[] for _ in range(8)], "YY": [[] for _ in range(8)],
            "DL": [[] for _ in range(8)], "HS": [[], []], "stash": [],
            "exch": [], "state": [[], []], "sca": [[], []], "scb": [[], []], "sch": [[], []], "htm": [], "fs2": [], "fs2w": [],
            "scn": [], "qxf": [], "qhf": [], "tmrd": [], "xf": [],
            "hseg": [[], []], "w2s": [[], []], "ngb": [],
            "acc1": {i: [] for i in range(8)}}

    def hs_build(layer, out_ap, src_ap, mub, r12b, row_tok, w, deps):
        t1 = vstt(out_ap, src_ap, 1.0, mub, ALU.mult, ALU.subtract,
                  waits=list(deps) + [row_tok])
        t2 = vtt(out_ap, out_ap, r12b, ALU.mult, waits=[t1])
        t3 = vts2(out_ap, out_ap, SM_LNG[layer][:, 0:1].broadcast_to([128, 1]),
                  SM_LNB[layer][:, 0:1], ALU.mult, ALU.add, waits=[t2])
        return t3

    def in_sweep(layer, ph, colrange, w, rev, mub, r12b, row_tok, zpart, out_cb):
        ipw = IN["l0_ipw"] if layer == 0 else IN["l1a_win" if ph == 0 else "l1b_win"]
        c0 = 1024 if zpart else 0
        banks = [pp.get() for _ in range(8)]
        lnga = SM_LNG[layer]
        lnba = SM_LNB[layer]
        mmtoks = [None] * 8
        for k in range(8):
            sbuf_w, sd, suse = smgr.load(ipw[k * 128:(k + 1) * 128, c0:c0 + 1024], 1024)
            hs = HS[k % 2]
            src = xstr[k][:, colrange[0]:colrange[0] + w]
            if rev:
                src = src[:, w - 1::-1]
            hw = list(last["HS"][k % 2])
            t1 = vstt(hs[:, 0:w], src, 1.0, mub, ALU.mult, ALU.subtract,
                      waits=hw + [row_tok])
            t2 = vtt(hs[:, 0:w], hs[:, 0:w], r12b, ALU.mult, waits=[t1])
            t3 = vts2(hs[:, 0:w], hs[:, 0:w], lnga[:, k:k + 1], lnba[:, k:k + 1],
                      ALU.mult, ALU.add, waits=[t2])
            if DBG is not None and DBG[3] == "hs2" and zpart and k < 2:
                t3 = b.dma("gpsimd", yout[k * 128:(k + 1) * 128, 0:w], hs[:, 0:w],
                           waits=[t3], pub="do")
            for m in range(8):
                tile, pw, pdone = banks[m]
                wts = [t3, sd] + (pw if k == 0 else [])
                tok = mm(tile[:, 0:w], sbuf_w[:, m * 128:(m + 1) * 128],
                         hs[:, 0:w], k == 0, k == 7, wts)
                if k == 7:
                    mmtoks[m] = tok
                if m == 7:
                    last["HS"][k % 2] = [tok]
                    suse(tok)
        for m in range(8):
            tile, pw, pdone = banks[m]
            out_cb(m, tile, mmtoks[m], pdone)

    def attn_chunk(layer, ph, ci, stok):
        rev = ph == 1
        sc0 = ci * CH
        dtk = 64 if layer == 0 else 32
        xw = 96 if layer == 0 else 64
        st_t = stA_t if ph == 0 else stB_t
        mu, r1, r12, rtok = ln_rows(ci * 2, 2, [stok], rev, True)
        mub, mtk, mubd = bcast_row(mu, CH, [rtok], dst="mub")
        r12b, rtk, r12bd = bcast_row(r12, CH, [rtok], dst="r12b")
        row_tok = rtk if rtk[1] > mtk[1] else mtk

        def xcb(m, tile, mmtok, pdone):
            e = act(XS[m][:, 3:3 + CH], tile[:, 0:CH], AF.Copy,
                    waits=[mmtok] + last["XS"][m])
            last["XS"][m] = []
            pdone(e)
            h1 = vcopy(XS[m][:, 0:3], stash_t[:, m * 4:m * 4 + 3],
                       waits=last["stash"] + [e])
            h2 = vcopy(stash_t[:, m * 4:m * 4 + 3], XS[m][:, CH:CH + 3], waits=[e, h1])
            cw = SM_CONVW(ph)
            c = vts1(UU[m][:, 0:CH], XS[m][:, 0:CH], cw[:, m * 4:m * 4 + 1], ALU.mult,
                     waits=[h1] + last["UU"][m])
            last["UU"][m] = []
            for kk in range(1, 4):
                c = vstt(UU[m][:, 0:CH], XS[m][:, kk:kk + CH], cw[:, m * 4 + kk:m * 4 + kk + 1],
                         UU[m][:, 0:CH], ALU.mult, ALU.add, waits=[c])
            s = act(UU[m][:, 0:CH], UU[m][:, 0:CH], AF.Silu,
                    bias=SM_CONVB(ph)[:, m:m + 1], waits=[c])
            last["XS"][m] = [c]
            last["stash"] = last["stash"] + [h2]
            last["UU"][m] = [s]

        in_sweep(layer, ph, (sc0,), CH, rev, mub, r12b, row_tok, False, xcb)
        u_toks = [last["UU"][m][0] for m in range(8)]

        xpw = IN[("l0" if layer == 0 else "l1") + ("a" if ph == 0 else "b") + "_xpw"]
        sbw, sd, suse = smgr.load(xpw.rearrange("(kb p) j -> p kb j", p=128), 8 * xw)
        psd, pwd, pdd = pp.get()
        psb, pwb, pdb = pp.get()
        psc, pwc, pdc = pp.get()
        tD = tB = tC = None
        for kb in range(8):
            base = kb * xw
            wts = [sd] + u_toks
            tD = mm(psd[0:dtk, 0:CH], sbw[:, base:base + dtk], UU[kb][:, 0:CH],
                    kb == 0, kb == 7, wts + (pwd if kb == 0 else []))
            tB = mm(psb[0:16, 0:CH], sbw[:, base + dtk:base + dtk + 16],
                    UU[kb][:, 0:CH], kb == 0, kb == 7, pwb if kb == 0 else ())
            tC = mm(psc[0:16, 0:CH], sbw[:, base + dtk + 16:base + dtk + 32],
                    UU[kb][:, 0:CH], kb == 0, kb == 7, pwc if kb == 0 else ())
        suse(tD)
        suse(tB)
        suse(tC)
        eD = act(xdbl_t[0:dtk, 0:CH], psd[0:dtk, 0:CH], AF.Copy,
                 waits=[tD] + last["xdbl"])
        last["xdbl"] = []
        eB = act(bcB_t[0:16, 0:CH], psb[0:16, 0:CH], AF.Copy, waits=[tB] + last["bcB"])
        last["bcB"] = []
        eC = act(bcC_t[0:16, 0:CH], psc[0:16, 0:CH], AF.Copy, waits=[tC] + last["bcC"])
        last["bcC"] = []
        pdd(eD)
        pdb(eB)
        pdc(eC)

        dtw = IN[("l0" if layer == 0 else "l1") + ("a" if ph == 0 else "b") + "_dtw"]
        sbw2, sd2, suse2 = smgr.load(dtw, 1024, rows=dtk)
        dtmm = []
        for m in range(8):
            tile, pw, pdone = pp.get()
            tmk = mm(tile[0:128, 0:CH], sbw2[0:dtk, m * 128:(m + 1) * 128],
                     xdbl_t[0:dtk, 0:CH], True, True, [sd2, eD] + pw, "t")
            tmp, tw, tdone = sp.get()
            a1 = act(tmp[:, 0:CH], tile[:, 0:CH], AF.Exp,
                     bias=SM_DTB(ph)[:, m:m + 1], waits=[tmk] + tw)
            pdone(a1)
            a2 = act(DL[m][:, 0:CH], tmp[:, 0:CH], AF.Ln, bias=1.0,
                     waits=[a1] + last["DL"][m])
            last["DL"][m] = []
            tdone(a2)
            dtmm.append(tmk)
            d1 = vtt(DU[m][:, 0:CH], DL[m][:, 0:CH], UU[m][:, 0:CH], ALU.mult,
                     waits=[a2] + last["DU"][m])
            y1 = vts1(YY[m][:, 0:CH], UU[m][:, 0:CH], SM_DP(ph)[:, m:m + 1], ALU.mult,
                      waits=last["YY"][m])
            last["DU"][m] = [d1]
            last["YY"][m] = [y1]
            last["DL"][m] = [d1]
        suse2(dtmm[-1])
        last["xdbl"] = [dtmm[-1]]

        amat = SM_AMAT(ph)
        exrow = rowb_t[0:1, 768:768 + CH]
        exrow2 = rowb_t[0:1, 1024 + 768 - 256:1024 + 768] if False else None
        exlast = []
        for n in range(16):
            pB, pwB, pdB = pp.get()
            tx = mm(pB[0:1, 0:CH], coh_t[0:16, n:n + 1], bcB_t[0:16, 0:CH],
                    True, True, [eB] + pwB, "t")
            ex1 = act(exrow, pB[0:1, 0:CH], AF.Copy, waits=[tx] + exlast)
            pdB(ex1)
            pB2, pwB2, pdB2 = pp.get()
            t1 = mm(pB2[:, 0:CH], ones1_t[0:1, :], exrow, True, True, [ex1] + pwB2, "t")
            Bt, Bw, Bd = sp.get()
            e1 = act(Bt[:, 0:CH], pB2[:, 0:CH], AF.Copy, waits=[t1] + Bw)
            pdB2(e1)
            exlast = [t1]
            pC, pwC, pdC = pp.get()
            tx = mm(pC[0:1, 0:CH], coh_t[0:16, n:n + 1], bcC_t[0:16, 0:CH],
                    True, True, [eC] + pwC, "t")
            ex2 = act(exrow, pC[0:1, 0:CH], AF.Copy, waits=[tx] + exlast)
            pdC(ex2)
            pC2, pwC2, pdC2 = pp.get()
            t2 = mm(pC2[:, 0:CH], ones1_t[0:1, :], exrow, True, True, [ex2] + pwC2, "t")
            Ct, Cw, Cd = sp.get()
            e2 = act(Ct[:, 0:CH], pC2[:, 0:CH], AF.Copy, waits=[t2] + Cw)
            pdC2(e2)
            exlast = [t2]
            lb = lc = None
            for m in range(8):
                col = m * 16 + n
                par = (n * 8 + m) % 2
                sca, scb, sch = SCA[par], SCB[par], SCH[par]
                ea = act(sca, DL[m][:, 0:CH], AF.Exp, scale=amat[:, col:col + 1],
                         waits=last["DL"][m] + last["sca"][par])
                bb = gtt(scb, DU[m][:, 0:CH], Bt[:, 0:CH], ALU.mult,
                         waits=[e1] + last["DU"][m] + last["scb"][par])
                sc = vscan(sch, sca, scb, st_t[:, col:col + 1],
                           waits=[ea, bb] + last["state"][ph] + last["sch"][par])
                last["state"][ph] = []
                stc = vcopy(st_t[:, col:col + 1], sch[:, CH - 1:CH], waits=[sc])
                tm2 = vtt(sca, sch, Ct[:, 0:CH], ALU.mult, waits=[sc, e2])
                ya = gtt(YY[m][:, 0:CH], YY[m][:, 0:CH], sca, ALU.add,
                         waits=[tm2] + last["YY"][m])
                last["YY"][m] = [ya]
                last["DL"][m] = [ea]
                last["DU"][m] = [bb]
                last["sca"][par] = [ya]
                last["scb"][par] = [sc]
                last["sch"][par] = [tm2, stc]
                lb, lc = ya, stc
            Bd(lb)
            Cd(lb)
            last["state"][ph].append(lc)

        if DBG is not None and DBG[3] == "y0" and layer == DBG[0] and ph == DBG[1] and ci == DBG[2]:
            for m in range(8):
                dt_ = b.dma("gpsimd", yout[m * 128:(m + 1) * 128, 0:CH], YY[m][:, 0:CH],
                            waits=last["YY"][m], pub="do")
                last["YY"][m] = last["YY"][m] + [dt_]
            b.barrier_wait("gpsimd", [("do", b.cnt["do"])])

        def zcb(m, tile, mmtok, pdone):
            szt, zw, zd = sp.get()
            s = act(szt[:, 0:CH], tile[:, 0:CH], AF.Silu, waits=[mmtok] + zw)
            pdone(s)
            if DBG is not None and DBG[3] in ("sz", "szx") and (layer, ph, ci) == DBG[:3]:
                s = b.dma("gpsimd", yout[m * 128:(m + 1) * 128, 0:CH], szt[:, 0:CH],
                          waits=[s], pub="do")
            g = vtt(YY[m][:, 0:CH], YY[m][:, 0:CH], szt[:, 0:CH], ALU.mult,
                    waits=[s] + last["YY"][m])
            zd(g)
            last["YY"][m] = [g]

        zp_flag = not (DBG is not None and DBG[3] == "szx")
        in_sweep(layer, ph, (sc0,), CH, rev, mub, r12b, row_tok, zp_flag, zcb)
        y_toks = [last["YY"][m][0] for m in range(8)]
        mubd(y_toks[7])
        r12bd(y_toks[7])

        if DBG is not None and DBG[3] in ("u", "dl", "du", "y", "xs") and layer == DBG[0] and ph == DBG[1] and ci == DBG[2]:
            srcs = {"u": UU, "dl": DL, "du": DU, "y": YY, "xs": None}[DBG[3]]
            for m in range(8):
                if DBG[3] == "xs":
                    ap = XS[m][:, 3:3 + CH]
                else:
                    ap = srcs[m][:, 0:CH]
                dt_ = b.dma("gpsimd", yout[m * 128:(m + 1) * 128, 0:CH], ap,
                            waits=[y_toks[m]], pub="do")
                key = "UU" if DBG[3] == "u" else ("DL" if DBG[3] == "dl" else
                      ("DU" if DBG[3] == "du" else ("YY" if DBG[3] == "y" else "XS")))
                last[key][m] = last[key][m] + [dt_]
            b.barrier_wait("gpsimd", [("do", b.cnt["do"])])
        return y_toks

    def l0_outproj(ph, ci, y_toks):
        sc0 = ci * CH
        opm = []
        for m in range(8):
            sbw, sdw, suse = smgr.load(IN["l0_opw"][:, m * 128:(m + 1) * 128]
                                       .rearrange("(kb p) j -> p kb j", p=128), 1024)
            tile, pw, pdone = pp.get()
            tk = None
            for kb in range(8):
                tk = mm(tile[:, 0:CH], sbw[:, kb * 128:(kb + 1) * 128],
                        YY[kb][:, 0:CH], kb == 0, kb == 7,
                        ([sdw] + y_toks + pw) if kb == 0 else ())
            e = act(UU[m][:, 0:CH], tile[:, 0:CH], AF.Copy, waits=[tk] + last["UU"][m])
            pdone(e)
            suse(tk)
            last["UU"][m] = [e]
            opm.append(e)
        for m in range(8):
            sbw, sdw, suse = smgr.load(IN["l0_fcw"][:, m * 128:(m + 1) * 128]
                                       .rearrange("(kb p) j -> p kb j", p=128), 1024)
            tile, pw, pdone = pp.get()
            tk = None
            for kb in range(8):
                tk = mm(tile[:, 0:CH], sbw[:, kb * 128:(kb + 1) * 128],
                        UU[kb][:, 0:CH], kb == 0, kb == 7,
                        ([sdw] + opm + pw) if kb == 0 else ())
            acc_sl = accv[m][:, sc0:sc0 + CH]
            if ph == 0:
                e = act(acc_sl, tile[:, 0:CH], AF.Copy, waits=[tk])
                pdone(e)
            else:
                tmp, tw, tdone = sp.get()
                e0 = act(tmp[:, 0:CH], tile[:, 0:CH], AF.Copy, waits=[tk] + tw)
                pdone(e0)
                e = vtt(acc_sl, acc_sl, tmp[:, CH - 1::-1], ALU.add, waits=[e0])
                tdone(e)
            suse(tk)
            last["UU"][m].append(tk)

    def halo_mini(layer, src24, src_tok):
        ps, pw, pdone = pp.get()
        mt = None
        sq, sw, sqd = sp.get()
        sq_toks = []
        for db in range(8):
            src = src24[:, db * 3:db * 3 + 3]
            mt = mm(ps[0:128, 0:1][0:3, 0:1] if False else ps[0:3, 0:1],
                    src, w1024_t[:, 0:1], db == 0, db == 7,
                    ([src_tok] + pw) if db == 0 else (), "t" if db == 7 else None)
            sq_toks.append(act(sq[:, db * 4:db * 4 + 3], src, AF.Square,
                               waits=[src_tok] + (sw if db == 0 else [])))
        ps2, pw2, pdone2 = pp.get()
        qt = None
        for db in range(8):
            qt = mm(ps2[0:3, 0:1], sq[:, db * 4:db * 4 + 3], w1024_t[:, 0:1],
                    db == 0, db == 7, [sq_toks[db]] + (pw2 if db == 0 else []),
                    "t" if db == 7 else None)
        sqd(qt)
        ps3, pw3, pdone3 = pp.get()
        t1 = tr(ps3[0:1, 0:3], ps[0:3, 0:1], [mt] + pw3, idn=ident_t[0:3, 0:3])
        return None

    def halo_mini2(layer, ph, src24, src_tok):
        mcol, mw, mdone = sp.get()
        ps, pw, pdone = pp.get()
        mt = None
        sq_toks = []
        sq, sw, sqd = sp.get()
        for db in range(8):
            src = src24[:, db * 3:db * 3 + 3]
            mt = mm(ps[0:3, 0:1], src, w1024_t[:, 0:1], db == 0, db == 7,
                    ([src_tok] + pw) if db == 0 else (), "t" if db == 7 else None)
            sq_toks.append(act(sq[:, db * 4:db * 4 + 3], src, AF.Square,
                               waits=[src_tok] + (sw if db == 0 else [])))
        e1 = act(mcol[0:3, 0:1], ps[0:3, 0:1], AF.Copy, waits=[mt] + mw)
        pdone(e1)
        ps2, pw2, pdone2 = pp.get()
        qt = None
        for db in range(8):
            qt = mm(ps2[0:3, 0:1], sq[:, db * 4:db * 4 + 3], w1024_t[:, 0:1],
                    db == 0, db == 7, [sq_toks[db]] + (pw2 if db == 0 else []),
                    "t" if db == 7 else None)
        e2 = act(mcol[0:3, 1:2], ps2[0:3, 0:1], AF.Copy, waits=[qt])
        pdone2(e2)
        sqd(qt)
        ps3, pw3, pdone3 = pp.get()
        t1 = tr(ps3[0:1, 0:3], mcol[0:3, 0:1], [e1] + pw3, idn=ident_t[0:3, 0:3])
        r1 = act(hrow_t[0:1, 0:3], ps3[0:1, 0:3], AF.Copy, waits=[t1])
        pdone3(r1)
        ps4, pw4, pdone4 = pp.get()
        t2 = tr(ps4[0:1, 0:3], mcol[0:3, 1:2], [e2] + pw4, idn=ident_t[0:3, 0:3])
        r2 = act(hrow_t[0:1, 4:7], ps4[0:1, 0:3], AF.Copy, waits=[t2])
        pdone4(r2)
        mdone(t2)
        mu3 = hrow_t[0:1, 0:3]
        q3 = hrow_t[0:1, 4:7]
        v3 = hrow_t[0:1, 8:11]
        r12_3 = hrow_t[0:1, 12:15]
        x1 = vtt(v3, mu3, mu3, ALU.mult, waits=[r1, r2])
        x2 = vtt(q3, q3, v3, ALU.subtract, waits=[x1])
        x3 = vcopy(v3, q3, waits=[x2])
        x4 = act(q3, q3, AF.Sqrt, bias=EPS, waits=[x3])
        x5 = vrecip(q3, q3, waits=[x4])
        x6 = vtt(r12_3, q3, q3, ALU.mult, waits=[x5])
        x7 = vtt(r12_3, r12_3, v3, ALU.mult, waits=[x6])
        x8 = act(r12_3, r12_3, AF.Sqrt, bias=EPS, waits=[x7])
        x9 = vrecip(r12_3, r12_3, waits=[x8])
        x10 = vtt(r12_3, r12_3, q3, ALU.mult, waits=[x9])
        mb3, mb3t, mb3d = bcast_row(mu3, 3, [x10], dst="mub")
        rb3, rb3t, rb3d = bcast_row(r12_3, 3, [x10], dst="r12b")
        ipw = IN["l0_ipw"] if layer == 0 else IN["l1b_win"]
        hst, hw2, hsd = sp.get()
        banks = [pp.get() for _ in range(8)]
        mmt = [None] * 8
        hst_readers = list(hw2)
        for k in range(8):
            sbw, sd, suse = smgr.load(ipw[k * 128:(k + 1) * 128, 0:1024], 1024)
            h1 = vstt(hst[:, 0:3],
                      src24[:, k * 3:k * 3 + 3], 1.0, mb3, ALU.mult, ALU.subtract,
                      waits=[mb3t, src_tok] + hst_readers)
            hst_readers = []
            h2 = vtt(hst[:, 0:3], hst[:, 0:3], rb3, ALU.mult, waits=[h1, rb3t])
            h3 = vts2(hst[:, 0:3], hst[:, 0:3], SM_LNG[layer][:, k:k + 1],
                      SM_LNB[layer][:, k:k + 1], ALU.mult, ALU.add, waits=[h2])
            for m in range(8):
                tile, pw, pdone = banks[m]
                tok = mm(tile[:, 0:3], sbw[:, m * 128:(m + 1) * 128],
                         hst[:, 0:3], k == 0, k == 7,
                         ([h3, sd] + pw) if k == 0 else [h3, sd])
                if k == 7:
                    mmt[m] = tok
                if m == 7:
                    suse(tok)
                    hst_readers = [tok]
        hsd(mmt[7])
        mb3d(mmt[7])
        rb3d(mmt[7])
        sttoks = []
        for m in range(8):
            tile, pw, pdone = banks[m]
            e = act(stash_t[:, m * 4:m * 4 + 3], tile[:, 0:3], AF.Copy,
                    waits=[mmt[m]] + last["stash"])
            pdone(e)
            sttoks.append(e)
        last["stash"] = sttoks

    def exchange(nm, payload_ap, pay_w, pay_tok, out_ap):
        ain, aout = AR[nm]
        t0 = vts1(exch_t[:, 0:pay_w], payload_ap, pe_t[:, 0:1], ALU.mult,
                  waits=[pay_tok] + last["exch"] + last["sca"][0] + last["scb"][0]
                  + last["sch"][0])
        t1 = vts1(exch_t[:, 256:256 + pay_w], payload_ap, po_t[:, 0:1], ALU.mult,
                  waits=[pay_tok] + last["scb"][0] + last["sch"][0])
        d0 = b.dma("gpsimd", ain[:, 0:pay_w], exch_t[:, 0:pay_w], waits=[t0], pub="de")
        d1 = b.dma("gpsimd", ain[:, pay_w:2 * pay_w], exch_t[:, 256:256 + pay_w],
                   waits=[t1], pub="de")
        cc = b.emit("gpsimd",
                    lambda h, i=ain, o=aout: h.collective_compute(
                        "AllReduce", ALU.add, replica_groups=PAIRS,
                        ins=[i.opt()], outs=[o.opt()]),
                    [d0, d1], "cc", 1)
        d2 = b.dma("gpsimd", exch_t[:, 0:pay_w], aout[:, 0:pay_w], waits=[cc], pub="de")
        d3 = b.dma("gpsimd", exch_t[:, 256:256 + pay_w], aout[:, pay_w:2 * pay_w],
                   waits=[cc], pub="de")
        c0 = vts1(out_ap, exch_t[:, 0:pay_w], po_t[:, 0:1], ALU.mult, waits=[d2])
        c1 = vstt(out_ap, exch_t[:, 256:256 + pay_w], pe_t[:, 0:1], out_ap,
                  ALU.mult, ALU.add, waits=[d3, c0])
        last["exch"] = [c1]
        for k2 in ("sca", "scb", "sch"):
            last[k2][0] = last[k2][0] + [c1]
        return c1

    def stage_barrier():
        toks = [(s, b.cnt[s]) for s in SEM_NAMES if b.cnt[s] > 0 and s != "cc"]
        toks += [(s, b.cnt[s]) for s in ("cc",) if b.cnt[s] > 0]
        for e in ENGS:
            b.barrier_wait(e, toks)
        return toks

    def quant128(tm_ap, w, deps):
        nb = w // 16
        t_abs = act(FS1[:, 0:w], tm_ap, AF.Abs, waits=deps)
        scl = scn_t[:, 0:nb]
        scl2 = scn_t[:, 256:256 + nb]
        rscl = scn_t[:, 512:512 + nb]
        t1 = vreduce(scl.rearrange("p (nb one) -> p nb one", one=1),
                     tm_ap.rearrange("p (nb w2) -> p nb w2", w2=16),
                     ALU.max, absval=True, waits=deps + last["scn"])
        t2 = vts2(scl2, scl, 1.0 / 6.0, 1e-12, ALU.mult, ALU.max, waits=[t1])
        t3 = vrecip(rscl, scl2, waits=[t2])
        rexp = rscl.unsqueeze(2).broadcast_to([128, nb, 16])
        sexp = scl2.unsqueeze(2).broadcast_to([128, nb, 16])
        f3 = FS1[:, 0:w].rearrange("p (nb w2) -> p nb w2", w2=16)
        t4 = vtt(f3, f3, rexp, ALU.mult, waits=[t3, t_abs])
        xn = FS1[:, 0:w]
        a0 = vts1(FS2[:, 0:w], xn, 2.5, ALU.is_gt, waits=[t4] + last["fs2"])
        a1 = vstt(FS2[:, 0:w], xn, 3.5, FS2[:, 0:w], ALU.is_gt, ALU.add, waits=[a0])
        a2 = vstt(FS2[:, 0:w], xn, 5.0, FS2[:, 0:w], ALU.is_gt, ALU.add, waits=[a1])
        a3 = vstt(FS2[:, 0:w], xn, 5.0, FS2[:, 0:w], ALU.is_gt, ALU.add, waits=[a2])
        a4 = vts1(FS2[:, 0:w], FS2[:, 0:w], 2.0, ALU.mult, waits=[a3])
        a5 = vstt(FS2[:, 0:w], xn, 0.25, FS2[:, 0:w], ALU.is_gt, ALU.add, waits=[a4])
        a6 = vstt(FS2[:, 0:w], xn, 0.75, FS2[:, 0:w], ALU.is_gt, ALU.add, waits=[a5])
        a7 = vstt(FS2[:, 0:w], xn, 1.25, FS2[:, 0:w], ALU.is_gt, ALU.add, waits=[a6])
        a8 = vstt(FS2[:, 0:w], xn, 1.75, FS2[:, 0:w], ALU.is_gt, ALU.add, waits=[a7])
        a9 = vts1(FS2[:, 0:w], FS2[:, 0:w], 0.5, ALU.mult, waits=[a8])
        sg = act(FS1[:, 0:w], tm_ap, AF.Sign, waits=[a8])
        q1 = vtt(FS2[:, 0:w], FS2[:, 0:w], FS1[:, 0:w], ALU.mult, waits=[a9, sg])
        f2 = FS2[:, 0:w].rearrange("p (nb w2) -> p nb w2", w2=16)
        q2 = vtt(f2, f2, sexp, ALU.mult, waits=[q1])
        last["scn"] = [q2]
        last["fs2"] = []
        last["tmrd"] = [q2, sg]
        return q2

    def wq_pass():
        specs = [(IN["f0_gw"], qgw, 22, 1024), (IN["f0_uw"], quw, 22, 1024),
                 (IN["f0_dw"], qdw, 8, 2816)]
        stage = FS1[:, 0:128]
        for wap, qap, nrt, C in specs:
            for r in range(nrt):
                d0 = b.dma("sync", HTM[:, 0:C], wap[r * 128:(r + 1) * 128, :],
                           waits=last["htm"], pub="dw")
                last["htm"] = []
                qtoks = []
                for s0 in range(0, C, 1024):
                    w = min(1024, C - s0)
                    qt = quant128(HTM[:, s0:s0 + w], w, [d0] + last["fs2w"])
                    for c in range(w // 128):
                        ps, pw, pdone = pp.get()
                        tt_ = tr(ps[:, 0:128], FS2[:, c * 128:(c + 1) * 128], [qt] + pw)
                        ek = act(r12b_t[:, 0:128], ps[:, 0:128], AF.Copy,
                                 waits=[tt_] + bc_last["r12b"])
                        pdone(ek)
                        bc_last["r12b"] = [ek]
                        dk = b.dma("gpsimd", qap[s0 + c * 128:s0 + (c + 1) * 128,
                                               r * 128:(r + 1) * 128],
                                   r12b_t[:, 0:128], waits=[ek], pub="d4")
                        bc_last["r12b"] = [ek, dk]
                        qtoks.append(dk)
                    last["fs2w"] = [qtoks[-1]] if qtoks else []
                last["htm"] = qtoks
        last["fs2w"] = []

    def f0_ffn(bar):
        sttoks = stats_pass(xstr, bar)
        stok = max(sttoks, key=lambda t: t[1])
        for ci in range(16):
            mu, r1, _, rtok = ln_rows(ci, 1, [stok], False, False)
            mub, mtk, mubd = bcast_row(mu, 128, [rtok], dst="mub")
            r1b, rtk, r1bd = bcast_row(r1, 128, [rtok], dst="r12b")
            rowt = rtk if rtk[1] > mtk[1] else mtk
            c0 = ci * 128
            etoks = []
            for k in range(8):
                lt, lw, ld_ = sp.get()
                t1 = vstt(lt[:, 0:128], xstr[k][:, c0:c0 + 128], 1.0, mub,
                          ALU.mult, ALU.subtract, waits=[rowt] + lw)
                t2 = vtt(lt[:, 0:128], lt[:, 0:128], r1b, ALU.mult, waits=[t1])
                ps, pw, pdone = pp.get()
                t3 = tr(ps[:, 0:128], lt[:, 0:128], [t2] + pw)
                ld_(t3)
                e = act(HTM[:, k * 128:(k + 1) * 128], ps[:, 0:128], AF.Copy,
                        waits=[t3] + last["htm"])
                last["htm"] = []
                pdone(e)
                etoks.append(e)
            qt = quant128(HTM[:, 0:1024], 1024, etoks)
            if DBG is not None and DBG[3] == "f0x" and ci == 0:
                d1 = b.dma("gpsimd", yout[0:128, 0:1024], HTM[:, 0:1024],
                           waits=etoks, pub="do")
                d2 = b.dma("gpsimd", yout[128:256, 0:1024], FS2[:, 0:1024],
                           waits=[qt], pub="do")
                last["htm"] = last["htm"] + [d1]
                last["tmrd"] = last["tmrd"] + [d1]
                last["fs2"] = last["fs2"] + [d2]
            qxtoks = []
            for k in range(8):
                ps, pw, pdone = pp.get()
                t3 = tr(ps[:, 0:128], FS2[:, k * 128:(k + 1) * 128], [qt] + pw)
                e = act(QXF[k][:, 0:128], ps[:, 0:128], AF.Copy,
                        waits=[t3] + last["qxf"])
                pdone(e)
                qxtoks.append(e)
            last["qxf"] = []
            last["fs2"] = qxtoks
            htoks = []
            for m in range(22):
                sb1, sd1, su1 = smgr.load(qgw[:, m * 128:(m + 1) * 128]
                                          .rearrange("(kb p) j -> p kb j", p=128), 1024)
                pg, pwg, pdg = pp.get()
                tg = None
                for k in range(8):
                    tg = mm(pg[:, 0:128], sb1[:, k * 128:(k + 1) * 128],
                            QXF[k][:, 0:128], k == 0, k == 7,
                            ([sd1] + qxtoks + pwg) if k == 0 else ())
                su1(tg)
                sb2, sd2, su2 = smgr.load(quw[:, m * 128:(m + 1) * 128]
                                          .rearrange("(kb p) j -> p kb j", p=128), 1024)
                pu, pwu, pdu = pp.get()
                tu = None
                for k in range(8):
                    tu = mm(pu[:, 0:128], sb2[:, k * 128:(k + 1) * 128],
                            QXF[k][:, 0:128], k == 0, k == 7,
                            ([sd2] + pwu) if k == 0 else ())
                su2(tu)
                g1, gw1, gd1 = sp.get()
                sg_ = act(g1[:, 0:128], pg[:, 0:128], AF.Silu, waits=[tg] + gw1)
                pdg(sg_)
                g2, gw2, gd2 = sp.get()
                su_ = act(g2[:, 0:128], pu[:, 0:128], AF.Copy, waits=[tu] + gw2)
                pdu(su_)
                hv = vtt(g1[:, 0:128], g1[:, 0:128], g2[:, 0:128], ALU.mult,
                         waits=[sg_, su_])
                gd2(hv)
                ps, pw, pdone = pp.get()
                t3 = tr(ps[:, 0:128], g1[:, 0:128], [hv] + pw)
                gd1(t3)
                e = act(HTM[:, m * 128:(m + 1) * 128], ps[:, 0:128], AF.Copy,
                        waits=[t3] + last["tmrd"])
                pdone(e)
                htoks.append(e)
            if DBG is not None and DBG[3] == "f0h" and ci == 0:
                for j in range(3):
                    w = 1024 if j < 2 else 768
                    dj = b.dma("gpsimd", yout[j * 128:(j + 1) * 128, 0:w],
                               HTM[:, j * 1024:j * 1024 + w], waits=htoks, pub="do")
                    htoks = htoks + [dj]
            qhtoks = []
            for s0, w in ((0, 1024), (1024, 1024), (2048, 768)):
                qt = quant128(HTM[:, s0:s0 + w], w, htoks)
                for c in range(w // 128):
                    mi = (s0 // 128) + c
                    ps, pw, pdone = pp.get()
                    t3 = tr(ps[:, 0:128], FS2[:, c * 128:(c + 1) * 128], [qt] + pw)
                    e = act(QHF[mi][:, 0:128], ps[:, 0:128], AF.Copy,
                            waits=[t3] + last["qhf"])
                    pdone(e)
                    qhtoks.append(e)
                last["fs2"] = qhtoks[-8:]
            last["qhf"] = []
            dtoks = []
            for og in range(8):
                po_, pwo, pdo = pp.get()
                to = None
                for kg0, kgn in ((0, 8), (8, 8), (16, 6)):
                    sbd, sd, sud = smgr.load(
                        qdw[kg0 * 128:(kg0 + kgn) * 128, og * 128:(og + 1) * 128]
                        .rearrange("(kb p) j -> p kb j", p=128), kgn * 128)
                    for kk in range(kgn):
                        mi = kg0 + kk
                        to = mm(po_[:, 0:128], sbd[:, kk * 128:(kk + 1) * 128],
                                QHF[mi][:, 0:128], mi == 0, mi == 21,
                                ([sd] + qhtoks + pwo) if mi == 0 else [sd])
                    sud(to)
                x_sl = xstr[og][:, c0:c0 + 128]
                e = vstt(x_sl, po_[:, 0:128], 1.0, x_sl, ALU.mult, ALU.add,
                         waits=[to])
                pdo(e)
                dtoks.append(e)
            last["qhf"] = qhtoks
            last["qxf"] = qxtoks
            last["htm"] = htoks + dtoks
            mubd(dtoks[-1])
            r1bd(dtoks[-1])
        return ("v", b.cnt["v"])

    def l0_attn():
        smtok = load_branch_smalls(0, [XTOK])
        sttoks = stats_pass(xstr, [XTOK] + smtok)
        stok = max(sttoks, key=lambda t: t[1])
        z0 = vmemset(stash_t[:, :], 0.0, waits=last["stash"])
        z1 = vmemset(stA_t[:, :], 0.0, waits=last["state"][0])
        last["stash"] = [z0]
        last["state"][0] = [z1]
        for ci in range(NCH):
            yt = attn_chunk(0, 0, ci, stok)
            l0_outproj(0, ci, yt)
        exch_tok = exchange("ar1", stA_t[:, :], 128,
                            last["state"][0][-1] if last["state"][0] else z1,
                            stB_t[:, :])
        last["state"][1] = [exch_tok]
        halo_mini2(0, 1, xh_t[:, :], XTOK)
        for ci in reversed(range(NCH)):
            yt = attn_chunk(0, 1, ci, stok)
            l0_outproj(1, ci, yt)
        for ci in range(NCH):
            sc0 = ci * CH
            mu, r1, _, rtok = ln_rows(ci * 2, 2, [stok], False, False)
            mub, mtk, mubd = bcast_row(mu, CH, [rtok], dst="mub")
            r1b, rtk, r1bd = bcast_row(r1, CH, [rtok], dst="r12b")
            lt, lw, ld = sp.get()
            e = None
            for m in range(8):
                x_sl = xstr[m][:, sc0:sc0 + CH]
                acc_sl = accv[m][:, sc0:sc0 + CH]
                t1 = vstt(lt[:, 0:CH], x_sl, 1.0, mub, ALU.mult, ALU.subtract,
                          waits=[mtk, rtk] + (lw if m == 0 else []))
                t2 = vtt(lt[:, 0:CH], lt[:, 0:CH], r1b, ALU.mult, waits=[t1])
                t3 = vstt(acc_sl, acc_sl, SM_FCB[:, m:m + 1], lt[:, 0:CH],
                          ALU.add, ALU.add, waits=[t2])
                e = vtt(x_sl, x_sl, acc_sl, ALU.add, waits=[t3])
            ld(e)
            mubd(e)
            r1bd(e)
        return ("v", b.cnt["v"])

    def l1_wout(ph, ci, y_toks):
        sc0 = ci * CH
        wout = IN["l1a_wout" if ph == 0 else "l1b_wout"]
        for mi in range(4):
            sbw, sdw, suse = smgr.load(wout[:, mi * 128:(mi + 1) * 128]
                                       .rearrange("(kb p) j -> p kb j", p=128), 1024)
            tile, pw, pdone = pp.get()
            tk = None
            for kb in range(8):
                tk = mm(tile[:, 0:CH], sbw[:, kb * 128:(kb + 1) * 128],
                        YY[kb][:, 0:CH], kb == 0, kb == 7,
                        ([sdw] + y_toks + pw) if kb == 0 else ())
            suse(tk)
            tmp, tw, tdone = sp.get()
            e0 = act(tmp[:, 0:CH], tile[:, 0:CH], AF.Copy, waits=[tk] + tw)
            pdone(e0)
            acc_sl = accv[mi][:, sc0:sc0 + CH]
            stg, sw2, sd2 = sp.get()
            if ph == 0:
                t1 = vts1(acc_sl, tmp[:, 0:CH], pe_t[:, 0:1], ALU.mult, waits=[e0])
                t2 = vts1(stg[:, 0:CH], tmp[:, 0:CH], po_t[:, 0:1], ALU.mult,
                          waits=[e0] + sw2)
                dk = b.dma("gpsimd", AR["ar4"][0][:, mi * 2048 + sc0:mi * 2048 + sc0 + CH],
                           stg[:, 0:CH], waits=[t2], pub="d4")
            else:
                t1 = vstt(acc_sl, tmp[:, CH - 1::-1], po_t[:, 0:1], acc_sl,
                          ALU.mult, ALU.add, waits=[e0])
                t2 = vts1(stg[:, 0:CH], tmp[:, CH - 1::-1], pe_t[:, 0:1], ALU.mult,
                          waits=[e0] + sw2)
                dk = b.dma("gpsimd",
                           AR["ar4"][0][:, 8192 + mi * 2048 + sc0:8192 + mi * 2048 + sc0 + CH],
                           stg[:, 0:CH], waits=[t2], pub="d4")
            tdone(t2)
            sd2(dk)

    def l1_attn(bar):
        smtok = load_branch_smalls(1, bar)
        sttoks = stats_pass(xstr, bar + smtok)
        stok = max(sttoks, key=lambda t: t[1])
        z0 = vmemset(stash_t[:, :], 0.0, waits=last["stash"] + bar)
        z1 = vmemset(stA_t[:, :], 0.0, waits=last["state"][0] + bar)
        last["stash"] = [z0]
        last["state"][0] = [z1]
        for ci in range(NCH):
            yt = attn_chunk(1, 0, ci, stok)
            l1_wout(0, ci, yt)
        exch_tok = exchange("ar3", stA_t[:, :], 128, z1 if not last["state"][0]
                            else last["state"][0][-1], stB_t[:, :])
        last["state"][1] = [exch_tok]
        htoks = []
        for db in range(8):
            htoks.append(vcopy(xh_t[:, db * 3:db * 3 + 3],
                               xstr[db][:, T_HALF - 3:T_HALF], waits=bar))
        h_ex = exchange("ar2", xh_t[:, 0:24], 24, htoks[-1], xh_t[:, 0:24])
        halo_mini2(1, 1, xh_t[:, :], h_ex)
        for ci in reversed(range(NCH)):
            yt = attn_chunk(1, 1, ci, stok)
            l1_wout(1, ci, yt)
        bar2 = stage_barrier()
        cc4 = b.emit("gpsimd",
                     lambda h: h.collective_compute(
                         "AllReduce", ALU.add, replica_groups=PAIRS,
                         ins=[AR["ar4"][0].opt()], outs=[AR["ar4"][1].opt()]),
                     bar2, "cc", 1)
        for mi in range(4):
            for g in range(4):
                c0 = g * 512
                t0t, w0, d0_ = sp.get()
                dk0 = b.dma("gpsimd", t0t[:, 0:512],
                            AR["ar4"][1][:, mi * 2048 + c0:mi * 2048 + c0 + 512],
                            waits=[cc4] + w0, pub="de")
                acc_sl = accv[4 + mi][:, c0:c0 + 512]
                e1 = vts1(acc_sl, t0t[:, 0:512], pe_t[:, 0:1], ALU.mult, waits=[dk0])
                d0_(e1)
                t1t, w1, d1_ = sp.get()
                dk1 = b.dma("gpsimd", t1t[:, 0:512],
                            AR["ar4"][1][:, 8192 + mi * 2048 + c0:8192 + mi * 2048 + c0 + 512],
                            waits=[cc4] + w1, pub="de")
                e2 = vstt(acc_sl, t1t[:, 0:512], po_t[:, 0:1], acc_sl,
                          ALU.mult, ALU.add, waits=[dk1, e1])
                d1_(e2)
        for ci in range(NCH):
            sc0 = ci * CH
            mu, r1, _, rtok = ln_rows(ci * 2, 2, [stok], False, False)
            mub, mtk, mubd = bcast_row(mu, CH, [rtok], dst="mub")
            r1b, rtk, r1bd = bcast_row(r1, CH, [rtok], dst="r12b")
            e = None
            for m in range(8):
                x_sl = xstr[m][:, sc0:sc0 + CH]
                acc_sl = accv[m][:, sc0:sc0 + CH]
                lt, lw, ld_ = sp.get()
                t1 = vstt(lt[:, 0:CH], x_sl, 1.0, mub, ALU.mult, ALU.subtract,
                          waits=[mtk, rtk, ("v", b.cnt["v"])] + lw)
                t2 = vtt(lt[:, 0:CH], lt[:, 0:CH], r1b, ALU.mult, waits=[t1])
                t3 = vtt(lt[:, 0:CH], lt[:, 0:CH], acc_sl, ALU.add, waits=[t2])
                t4 = act(lt[:, 0:CH], lt[:, 0:CH], AF.Gelu, waits=[t3])
                e = vtt(x_sl, x_sl, lt[:, 0:CH], ALU.add, waits=[t4])
                ld_(e)
            mubd(e)
            r1bd(e)
        return ("v", b.cnt["v"])

    b1c_t = wv(9856, 128)

    def IN_B1_COL(m1):
        return b1c_t[:, m1:m1 + 1]

    def f1_ffn(bar):
        sttoks = stats_pass(xstr, bar)
        stok = max(sttoks, key=lambda t: t[1])
        b2buf, b2d, b2use = smgr.load(IN["f1_b2"], 1024, extra=bar, rows=8)
        lg = [b2d,
              b.dma("sync", b1c_t, IN["f1_b1"], waits=bar, pub="dw"),
              b.dma("sync", GWS[:, 0:64].rearrange("p (kb j) -> p kb j", j=8),
                    IN["f1_gw"].rearrange("(kb p) j -> p kb j", p=128),
                    waits=bar, pub="dw"),
              b.dma("sync", hrow_t[0:1, 16:24], IN["f1_gb"], waits=bar, pub="dw"),
              b.dma("sync", hrow_t[0:1, 24:32], IN["f1_rms"], waits=bar, pub="dw")]
        b2toks = []
        for m in range(8):
            ps, pw, pdone = pp.get()
            t1 = mm(ps[0:128, 0:1], b2buf[0:8, m * 128:(m + 1) * 128],
                    cs_t[0:8, 1:2], True, True, lg + pw)
            e = act(SM_B2S[:, m:m + 1], ps[:, 0:1], AF.Copy, waits=[t1])
            pdone(e)
            b2toks.append(e)
            b2use(t1)
        gbb = sm[:, 416:424]
        rmb = sm[:, 424:432]
        ps, pw, pdone = pp.get()
        t1 = mm(ps[:, 0:8], ones1_t[0:1, :], hrow_t[0:1, 16:24], True, True, lg + pw)
        e1 = act(gbb, ps[:, 0:8], AF.Copy, waits=[t1])
        pdone(e1)
        ps, pw, pdone = pp.get()
        t1 = mm(ps[:, 0:8], ones1_t[0:1, :], hrow_t[0:1, 24:32], True, True, lg + pw)
        e2 = act(rmb, ps[:, 0:8], AF.Copy, waits=[t1])
        pdone(e2)
        ng_done = []
        for g in range(16):
            c0 = g * 128
            mu, r1, _, rtok = ln_rows(g, 1, [stok], False, False)
            mub, mtk, mubd = bcast_row(mu, 128, [rtok], dst="mub")
            r1b, rtk, r1bd = bcast_row(r1, 128, [rtok], dst="r12b")
            ps, pw, pdone = pp.get()
            tk = None
            lts = []
            for k in range(8):
                lt, lw, ld_ = sp.get()
                t1 = vstt(lt[:, 0:128], xstr[k][:, c0:c0 + 128], 1.0, mub,
                          ALU.mult, ALU.subtract, waits=[mtk, rtk] + lw)
                t2 = vtt(lt[:, 0:128], lt[:, 0:128], r1b, ALU.mult, waits=[t1])
                tk = mm(ps[:, 0:8], lt[:, 0:128],
                        GWS[:, 0:64].rearrange("p (kb j) -> p kb j", j=8)[:, k, :],
                        k == 0, k == 7, ([t2] + lg + pw) if k == 0 else [t2])
                ld_(tk)
            gt, gw_, gd = sp.get()
            q0 = b.emit("vector", lambda h, o=gt[:, 0:8], p=ps[:, 0:8], bb=gbb:
                        h.tensor_tensor(o, p, bb, ALU.add), [tk, e1] + gw_, "v", 1)
            pdone(q0)
            q1 = vtt(gt[:, 8:16], gt[:, 0:8], gt[:, 0:8], ALU.mult, waits=[q0])
            q2 = vreduce(gt[:, 16:17].rearrange("p (a b) -> p a b", b=1),
                         gt[:, 8:16].rearrange("p (a b) -> p a b", b=8),
                         ALU.add, waits=[q1])
            q3 = act(gt[:, 17:18], gt[:, 16:17], AF.Sqrt, bias=F32EPS, scale=0.125,
                     waits=[q2])
            q4 = vrecip(gt[:, 18:19], gt[:, 17:18], waits=[q3])
            q5 = b.emit("vector", lambda h, o=gt[:, 0:8], i=gt[:, 0:8], s=gt[:, 18:19]:
                        h.tensor_single_scalar(o, i, s, ALU.mult), [q4], "v", 1)
            q6 = vtt(gt[:, 0:8], gt[:, 0:8], rmb, ALU.mult, waits=[q5, e2])
            q7 = vreduce(gt[:, 16:17].rearrange("p (a b) -> p a b", b=1),
                         gt[:, 0:8].rearrange("p (a b) -> p a b", b=8),
                         ALU.max, waits=[q6])
            q8 = b.emit("vector", lambda h, o=gt[:, 0:8], i=gt[:, 0:8], s=gt[:, 16:17]:
                        h.tensor_single_scalar(o, i, s, ALU.subtract), [q7], "v", 1)
            q9 = act(gt[:, 0:8], gt[:, 0:8], AF.Exp, scale=2.0, waits=[q8])
            q10 = vreduce(gt[:, 16:17].rearrange("p (a b) -> p a b", b=1),
                          gt[:, 0:8].rearrange("p (a b) -> p a b", b=8),
                          ALU.add, waits=[q9])
            q11 = vrecip(gt[:, 17:18], gt[:, 16:17], waits=[q10])
            q12 = vts1(gt[:, 18:19], gt[:, 17:18], 1e-6, ALU.add, waits=[q11])
            q13 = vrecip(gt[:, 19:20], gt[:, 18:19], waits=[q12])
            q14 = vtt(gt[:, 20:21], gt[:, 17:18], gt[:, 19:20], ALU.mult, waits=[q13])
            ps2, pw2, pdone2 = pp.get()
            tt_ = tr(ps2[0:1, 0:128], gt[:, 20:21], [q14] + pw2)
            ek = act(ROWG[0:1, c0:c0 + 128], ps2[0:1, 0:128], AF.Copy,
                     waits=[tt_] + ng_done)
            pdone2(ek)
            gd(ek)
            ng_done = ng_done + [ek]
            mubd(ek)
            r1bd(ek)
        ngtok = ng_done[-1]
        for tc in range(4):
            sc0 = tc * 512
            xftoks = []
            for half in range(2):
                c0 = sc0 + half * 256
                mu, r1, _, rtok = ln_rows(tc * 4 + half * 2, 2, [stok], False, False)
                mub, mtk, mubd = bcast_row(mu, 256, [rtok], dst="mub")
                r1b, rtk, r1bd = bcast_row(r1, 256, [rtok], dst="r12b")
                e = None
                for k in range(8):
                    dst = XF[k][:, half * 256:half * 256 + 256]
                    t1 = vstt(dst, xstr[k][:, c0:c0 + 256], 1.0, mub,
                              ALU.mult, ALU.subtract,
                              waits=[mtk, rtk] + last["xf"] if k == 0 else [mtk, rtk])
                    e = vtt(dst, dst, r1b, ALU.mult, waits=[t1])
                    xftoks.append(e)
                last["xf"] = []
                mubd(e)
                r1bd(e)
            for seg in range(64):
                htoks = []
                w2toks = []
                for j in range(2):
                    m1 = seg * 2 + j
                    sb1, sd1, su1 = smgr.load(IN["f1_w1"][:, m1 * 128:(m1 + 1) * 128]
                                              .rearrange("(kb p) j2 -> p kb j2", p=128),
                                              1024)
                    ph_, pwh, pdh = pp.get()
                    tk = None
                    for k in range(8):
                        tk = mm(ph_[:, 0:512], sb1[:, k * 128:(k + 1) * 128],
                                XF[k][:, 0:512], k == 0, k == 7,
                                ([sd1] + xftoks + pwh) if k == 0 else ())
                    su1(tk)
                    e = act(HSEG[j][:, 0:512], ph_[:, 0:512], AF.Gelu,
                            bias=IN_B1_COL(m1), waits=[tk] + last["hseg"][j])
                    last["hseg"][j] = []
                    pdh(e)
                    htoks.append(e)
                    w2toks.append(b.dma("sync", W2S[j][:, 0:1024],
                                        IN["f1_w2"][m1 * 128:(m1 + 1) * 128, :],
                                        waits=last["w2s"][j],
                                        pub="dw" if j == 0 else "dz"))
                    last["w2s"][j] = []
                for og in range(8):
                    po_, pwo, pdo = pp.get()
                    tk = None
                    for j in range(2):
                        tk = mm(po_[:, 0:512], W2S[j][:, og * 128:(og + 1) * 128],
                                HSEG[j][:, 0:512], j == 0, j == 1,
                                ([w2toks[0], w2toks[1]] + htoks + pwo) if j == 0 else ())
                        last["w2s"][j] = last["w2s"][j] + [tk]
                        last["hseg"][j] = last["hseg"][j] + [tk]
                    acc_sl = accv[og][:, sc0:sc0 + 512]
                    if seg == 0:
                        e = act(acc_sl, po_[:, 0:512], AF.Copy, waits=[tk])
                    else:
                        e = vstt(acc_sl, po_[:, 0:512], 1.0, acc_sl, ALU.mult, ALU.add,
                                 waits=[tk])
                    pdo(e)
                    last["acc1"][og] = [e]
            ps, pw, pdone = pp.get()
            t1 = mm(ps[:, 0:512], ones1_t[0:1, :], ROWG[0:1, sc0:sc0 + 512], True, True,
                    [ngtok] + pw)
            en = act(NGB[:, 0:512], ps[:, 0:512], AF.Copy, waits=[t1] + last["ngb"])
            pdone(en)
            ftoks = []
            for m in range(8):
                x_sl = xstr[m][:, sc0:sc0 + 512]
                acc_sl = accv[m][:, sc0:sc0 + 512]
                lt, lw, ld_ = sp.get()
                t1 = vts1(lt[:, 0:512], acc_sl, SM_B2S[:, m:m + 1], ALU.add,
                          waits=last["acc1"][m] + b2toks + lw)
                t2 = vtt(lt[:, 0:512], lt[:, 0:512], NGB[:, 0:512], ALU.mult,
                         waits=[t1, en])
                e = vtt(x_sl, x_sl, lt[:, 0:512], ALU.add, waits=[t2])
                ld_(e)
                ftoks.append(e)
            last["ngb"] = ftoks
            last["xf"] = ftoks
        return ("v", b.cnt["v"])

    def store_out(final_tok):
        st_last = [[], []]
        for tt in range(16):
            tmt = TMT if tt % 2 == 0 else QAT
            wts = st_last[tt % 2]
            st_last[tt % 2] = []
            etk = []
            for db in range(8):
                ps, pw, pdone = pp.get()
                ttk = tr(ps[:, 0:128], xstr[db][:, tt * 128:(tt + 1) * 128],
                         [final_tok] + pw + (wts if db == 0 else []))
                ek = act(tmt[:, db * 128:(db + 1) * 128], ps[:, 0:128], AF.Copy,
                         waits=[ttk] + (wts if db == 0 else []))
                pdone(ek)
                etk.append(ek)
            dtok = b.dma("gpsimd", yout[tt * 128:(tt + 1) * 128, :], tmt[:, 0:1024],
                         waits=etk, pub="do")
            st_last[tt % 2] = [dtok]
        b.barrier_wait("gpsimd", [("do", b.cnt["do"])])

    if STAGES >= 2:
        stage_barrier()
        wq_pass()
        stage_barrier()
        if DBG is not None and DBG[3] == "wq":
            b.dma("gpsimd", yout[0:1024, :], qgw[0:1024, 1024:2048], pub="do")
            b.dma("gpsimd", yout[1024:2048, :], qgw[0:1024, 1792:2816], pub="do")
            b.barrier_wait("gpsimd", [("do", b.cnt["do"])])
    ftok = l0_attn()
    if STAGES >= 2:
        bar = stage_barrier()
        ftok = f0_ffn(bar)
    if STAGES >= 3:
        bar = stage_barrier()
        ftok = l1_attn(bar)
    if STAGES >= 4:
        bar = stage_barrier()
        ftok = f1_ffn(bar)
    stage_barrier()
    if DBG is None:
        store_out(ftok)
    else:
        b.barrier_wait("gpsimd", [("do", b.cnt["do"]), ftok])

    b.realize(block)
    nc._ctx_holder = ctx
    return nc


_NC_CACHE = {}


def kernel(**inputs):
    in_maps = host_prepare(inputs)
    nc = build_nc()
    res = run_bass_kernel_spmd(nc, in_maps, core_ids=list(range(8)))
    return host_gather(res.results)


# version 06
# speedup vs baseline: 1.3298x; 1.0196x over previous
import contextlib
import numpy as np

import concourse.bass as bass
import concourse.mybir as mybir
from concourse.bass_utils import run_bass_kernel_spmd

F32 = mybir.dt.float32
AF = mybir.ActivationFunctionType
ALU = mybir.AluOpType

T_HALF = 2048
CH = 256
NCH = T_HALF // CH
NSUB = 16
EPS = 1e-5
F32EPS = float(np.finfo(np.float32).eps)
STAGES = 4
DBG = None
PAIRS = [[0, 1], [2, 3], [4, 5], [6, 7]]

ENGS = ("tensor", "vector", "scalar", "gpsimd", "sync")
SEM_OWNER = {"t": "tensor", "v": "vector", "a": "scalar", "g": "gpsimd"}
SEM_NAMES = ("t", "v", "a", "g", "cc", "di", "dw", "dz", "dx0", "dx1", "do", "de", "d4")
DMA_SEMS = ("di", "dw", "dz", "dx0", "dx1", "do", "de", "d4")


class Bld:
    def __init__(self, nc, sems):
        self.nc = nc
        self.sems = sems
        self.cnt = {k: 0 for k in sems}
        self.stream = {e: [] for e in ENGS}

    def emit(self, eng, fn, waits=(), pub=None, amt=1):
        toks = []
        for tk in waits:
            if tk is None:
                continue
            s, v = tk
            if s in DMA_SEMS:
                v = self.cnt[s]
            toks.append((s, v))
        tok = None
        if pub is not None:
            self.cnt[pub] += amt
            tok = (pub, self.cnt[pub])
        self.stream[eng].append((toks, fn, pub, amt))
        return tok

    def dma(self, eng, out, in_, waits=(), pub="di"):
        waits = list(waits) + [(pub, self.cnt[pub])]
        return self.emit(eng, lambda h, o=out, i=in_: h.dma_start(out=o, in_=i),
                         waits, pub, 16)

    def barrier_wait(self, eng, waits):
        self.emit(eng, lambda h: None, waits)

    def realize(self, block):
        for eng in ENGS:
            items = self.stream[eng]
            sems = self.sems

            def body(h, items=items):
                hw = {}
                for toks, fn, pub, amt in items:
                    need = {}
                    for s, v in toks:
                        if hw.get(s, 0) < v and need.get(s, 0) < v:
                            need[s] = v
                    for s, v in need.items():
                        h.wait_ge(sems[s], v)
                        hw[s] = v
                    inst = fn(h)
                    if pub is not None:
                        inst.then_inc(sems[pub], amt)

            getattr(block, eng)(body)


class Pool:
    def __init__(self, tiles):
        self.tiles = tiles
        self.rd = [None] * len(tiles)
        self.i = 0

    def get(self):
        i = self.i % len(self.tiles)
        self.i += 1
        w = self.rd[i]
        self.rd[i] = None

        def done(tok, i=i):
            self.rd[i] = tok

        return self.tiles[i], ([w] if w is not None else []), done


def _f32(a):
    return np.ascontiguousarray(a, dtype=np.float32)


def _col_pack(v):
    return _f32(np.transpose(np.asarray(v).reshape(-1, 128)))


def _colmat_pack(m, inner):
    m = np.asarray(m)
    nblk = m.shape[0] // 128
    return _f32(np.transpose(m.reshape(nblk, 128, inner), (1, 0, 2)).reshape(128, nblk * inner))


def host_prepare(inputs):
    x = np.asarray(inputs["L0_attn_hidden_states"])

    def wT(name):
        return _f32(np.asarray(inputs[name]).T)

    def arr(name):
        return _f32(np.asarray(inputs[name]))

    shared = {
        "l0_ipw": wT("L0_attn_in_proj_w"),
        "l0_opw": wT("L0_attn_out_proj_w"),
        "l0_fcw": wT("L0_attn_fc_w"),
        "l0_fcb": _col_pack(inputs["L0_attn_fc_b"]),
        "l0_lng": _col_pack(inputs["L0_attn_ln_g"]),
        "l0_lnb": _col_pack(inputs["L0_attn_ln_b"]),
        "f0_gw": arr("L0_ffn_w_gate"),
        "f0_uw": arr("L0_ffn_w_up"),
        "f0_dw": arr("L0_ffn_w_down"),
        "l1_lng": _col_pack(inputs["L1_attn_ln_g"]),
        "l1_lnb": _col_pack(inputs["L1_attn_ln_b"]),
        "f1_w1": _f32(np.transpose(np.asarray(inputs["L1_ffn_w1"]), (1, 0, 2)).reshape(1024, 16384)),
        "f1_w2": _f32(np.asarray(inputs["L1_ffn_w2"]).reshape(16384, 1024)),
        "f1_b1": _f32(np.transpose(np.asarray(inputs["L1_ffn_b1"]).reshape(128, 128))),
        "f1_b2": arr("L1_ffn_b2"),
        "f1_gw": wT("L1_ffn_gate_w"),
        "f1_gb": _f32(np.asarray(inputs["L1_ffn_gate_b"]).reshape(1, 8)),
        "f1_rms": _f32(np.asarray(inputs["L1_ffn_rms_w"]).reshape(1, 8)),
        "c_ident": _f32(np.eye(128)),
        "c_scal": _f32(np.tile(np.array([1e-5, 1.0, 2.0, float(np.finfo(np.float32).eps),
                                         1e-12, 1.0 / 6.0, 1e-6, 0.5], dtype=np.float32),
                               (128, 1))),
        "c_ones1": _f32(np.ones((1, 128))),
        "c_w1024": _f32(np.full((128, 1), 1.0 / 1024.0)),
    }
    shared["c_oh16"] = _f32(np.eye(16))

    l0_br = {}
    for tag, sfx in (("f", ""), ("b", "_b")):
        l0_br[tag] = {
            "convw": _colmat_pack(inputs["L0_attn_conv_w" + sfx], 4),
            "convb": _col_pack(inputs["L0_attn_conv_b" + sfx]),
            "xpw": wT("L0_attn_x_proj_w" + sfx),
            "dtw": wT("L0_attn_dt_proj_w" + sfx),
            "dtb": _col_pack(inputs["L0_attn_dt_proj_b" + sfx]),
            "alog": _colmat_pack(inputs["L0_attn_A_log" + sfx], 16),
            "dp": _col_pack(inputs["L0_attn_Dp" + sfx]),
        }
    l1_br = {}
    for tag in ("m1", "m2"):
        p = "L1_attn_" + tag + "_"
        l1_br[tag] = {
            "win": arr(p + "Win"),
            "convw": _colmat_pack(inputs[p + "Wconv"], 4),
            "convb": _col_pack(inputs[p + "bconv"]),
            "xpw": arr(p + "Wx"),
            "dtw": arr(p + "Wdt"),
            "dtb": _col_pack(inputs[p + "bdt"]),
            "alog": _colmat_pack(inputs[p + "Alog"], 16),
            "dp": _col_pack(inputs[p + "Dp"]),
            "wout": arr(p + "Wout"),
        }

    in_maps = []
    for c in range(8):
        bi, par = c // 2, c % 2
        m = dict(shared)
        if par == 0:
            sl = x[bi, 0:T_HALF]
            halo = x[bi, T_HALF + 2:T_HALF - 1:-1]
        else:
            sl = x[bi, T_HALF:2 * T_HALF][::-1]
            halo = x[bi, T_HALF - 3:T_HALF]
        xin = np.zeros((2176, 1024), dtype=np.float32)
        xin[:T_HALF] = sl
        xin[T_HALF:T_HALF + 3] = halo
        m["xin"] = xin
        pe = np.full((128, 1), 1.0 if par == 0 else 0.0, dtype=np.float32)
        m["c_pe"] = pe
        m["c_po"] = _f32(1.0 - pe)
        A, B = (l0_br["f"], l0_br["b"]) if par == 0 else (l0_br["b"], l0_br["f"])
        for k, v in A.items():
            m["l0a_" + k] = v
        for k, v in B.items():
            m["l0b_" + k] = v
        A, B = (l1_br["m1"], l1_br["m2"]) if par == 0 else (l1_br["m2"], l1_br["m1"])
        for k, v in A.items():
            m["l1a_" + k] = v
        for k, v in B.items():
            m["l1b_" + k] = v
        in_maps.append(m)
    return in_maps


def host_gather(results):
    out = np.empty((4, 4096, 1024), dtype=np.float32)
    for c in range(8):
        bi, par = c // 2, c % 2
        y = results[c]["yout"]
        if par == 0:
            out[bi, 0:T_HALF] = y
        else:
            out[bi, T_HALF:2 * T_HALF] = y[::-1]
    return out


def build_nc():
    nc = bass.Bass("TRN2", target_bir_lowering=False)
    ctx = contextlib.ExitStack()

    IN = {}
    in_specs = [
        ("xin", [2176, 1024]),
        ("l0_ipw", [1024, 2048]), ("l0_opw", [1024, 1024]), ("l0_fcw", [1024, 1024]),
        ("l0_fcb", [128, 8]), ("l0_lng", [128, 8]), ("l0_lnb", [128, 8]),
        ("f0_gw", [2816, 1024]), ("f0_uw", [2816, 1024]), ("f0_dw", [1024, 2816]),
        ("l1_lng", [128, 8]), ("l1_lnb", [128, 8]),
        ("f1_w1", [1024, 16384]), ("f1_w2", [16384, 1024]),
        ("f1_b1", [128, 128]), ("f1_b2", [8, 1024]),
        ("f1_gw", [1024, 8]), ("f1_gb", [1, 8]), ("f1_rms", [1, 8]),
        ("c_ident", [128, 128]), ("c_ones1", [1, 128]), ("c_w1024", [128, 1]),
        ("c_scal", [128, 8]),
        ("c_oh16", [16, 16]), ("c_pe", [128, 1]), ("c_po", [128, 1]),
    ]
    for ph in ("a", "b"):
        in_specs += [
            ("l0%s_convw" % ph, [128, 32]), ("l0%s_convb" % ph, [128, 8]),
            ("l0%s_xpw" % ph, [1024, 96]), ("l0%s_dtw" % ph, [64, 1024]),
            ("l0%s_dtb" % ph, [128, 8]), ("l0%s_alog" % ph, [128, 128]),
            ("l0%s_dp" % ph, [128, 8]),
            ("l1%s_win" % ph, [1024, 2048]), ("l1%s_convw" % ph, [128, 32]),
            ("l1%s_convb" % ph, [128, 8]), ("l1%s_xpw" % ph, [1024, 64]),
            ("l1%s_dtw" % ph, [32, 1024]), ("l1%s_dtb" % ph, [128, 8]),
            ("l1%s_alog" % ph, [128, 128]), ("l1%s_dp" % ph, [128, 8]),
            ("l1%s_wout" % ph, [1024, 512]),
        ]
    for name, shape in in_specs:
        IN[name] = nc.dram_tensor(name, shape, F32, kind="ExternalInput").ap()
    yout = nc.dram_tensor("yout", [T_HALF, 1024], F32, kind="ExternalOutput").ap()

    qgw = nc.dram_tensor("qgw", [1024, 2816], F32).ap()
    quw = nc.dram_tensor("quw", [1024, 2816], F32).ap()
    qdw = nc.dram_tensor("qdw", [2816, 1024], F32).ap()
    AR = {}
    for nm, shape in (("ar1", [128, 256]), ("ar2", [128, 48]),
                      ("ar3", [128, 256]), ("ar4", [128, 16384])):
        AR[nm] = (nc.dram_tensor(nm + "_in", shape, F32).ap(),
                  nc.dram_tensor(nm + "_out", shape, F32).ap())

    def sbuf(name, shape, dt=F32):
        return ctx.enter_context(nc.sbuf_tensor(name, shape, dt))

    xstr_t = sbuf("xstr", [128, 16384])
    acc_t = sbuf("acc", [128, 16384])
    wsa_t = sbuf("wsa", [128, 10776])
    s512_t = [sbuf("s5_%d" % i, [128, 512]) for i in range(3)]
    scn_t = sbuf("scn", [128, 768])
    coh_t = sbuf("coh", [16, 16])
    strip_t = sbuf("strip", [128, 1024])
    stripb_t = sbuf("stripb", [128, 1024])
    scn2_t = sbuf("scn2", [128, 768])
    bcB_t = sbuf("bcB", [16, 256])
    bcC_t = sbuf("bcC", [16, 256])
    xdbl_t = sbuf("xdbl", [64, 256])
    statc_t = sbuf("statc", [128, 32])
    mub_t = sbuf("mub", [128, 256])
    r12b_t = sbuf("r12b", [128, 256])
    rowb_t = sbuf("rowb", [1, 1792])
    hrow_t = sbuf("hrow", [1, 32])
    ident_t = sbuf("ident", [128, 128])
    ones1_t = sbuf("ones1", [1, 128])
    w1024_t = sbuf("w1024", [128, 1])
    cs_t = sbuf("cscal", [128, 8])
    pe_t = sbuf("pe", [128, 1])
    po_t = sbuf("po", [128, 1])
    smalls_t = sbuf("smalls", [128, 440])
    stash_t = sbuf("stash", [128, 24])
    stA_t = sbuf("stA", [128, 128])
    stB_t = sbuf("stB", [128, 128])
    xh_t = sbuf("xhalo", [128, 24])

    psum_t = [ctx.enter_context(nc.psum_tensor("ps%d" % i, [128, 512], F32))
              for i in range(8)]
    sems = {k: ctx.enter_context(nc.semaphore("sem_" + k)) for k in SEM_NAMES}
    block = ctx.enter_context(nc.Block())
    b = Bld(nc, sems)
    pp = Pool(psum_t)
    sp = Pool([t[:, :] for t in s512_t])

    def wv(off, w):
        return wsa_t[:, off:off + w]

    HS = [wv(0, 256), wv(256, 256)]
    XS = [wv(512 + i * 259, 259) for i in range(8)]
    UU = [wv(2584 + i * 256, 256) for i in range(8)]
    DU = [wv(4632 + i * 256, 256) for i in range(8)]
    YY = [wv(6680 + i * 256, 256) for i in range(8)]
    DL = [wv(8728 + i * 256, 256) for i in range(8)]
    HTM = wv(0, 2816)
    QHTM = wv(2816, 2816)
    QHF = [wv(5632 + i * 128, 128) for i in range(22)]
    QXF = [wv(8448 + i * 128, 128) for i in range(8)]
    TMT = wv(5632, 1024)
    QAT = wv(6656, 1024)
    FS1 = wv(9472, 1024)
    FS2 = wv(10496 - 1024 + 1024, 1024) if False else None
    FS2 = wv(2816, 1024)
    XF = [wv(i * 512, 512) for i in range(8)]
    HSEG = [wv(4096 + i * 512, 512) for i in range(2)]
    W2S = [wv(5120 + i * 1024, 1024) for i in range(2)]
    ROWG = wsa_t[0:1, 7168:9216]
    NGB = wv(9216, 512)
    GWS = wv(9728, 64)

    exch_t = scn_t
    SCA = [scn_t[:, 0:256], scn2_t[:, 0:256]]
    SCB = [scn_t[:, 256:512], scn2_t[:, 256:512]]
    SCH = [scn_t[:, 512:768], scn2_t[:, 512:768]]

    xstr = [xstr_t[:, i * 2048:(i + 1) * 2048] for i in range(8)]
    accv = [acc_t[:, i * 2048:(i + 1) * 2048] for i in range(8)]

    def mm(psum_ap, lhsT, rhs, start, stop, waits=(), pub="t"):
        return b.emit("tensor",
                      lambda h, p=psum_ap, l=lhsT, r=rhs, s=start, t=stop:
                      h.matmul(p, l, r, start=s, stop=t),
                      waits, "t", 1)

    def tr(psum_ap, in_ap, waits, idn=None):
        if idn is None:
            idn = ident_t[:, :]
        return b.emit("tensor",
                      lambda h, p=psum_ap, i=in_ap, d=idn: h.transpose(p, i, d),
                      waits, "t", 1)

    def act(out, in_, func, bias=0.0, scale=1.0, waits=()):
        return b.emit("scalar",
                      lambda h, o=out, i=in_, f=func, bi=bias, sc=scale:
                      h.activation(o, i, f, bias=bi, scale=sc),
                      waits, "a", 1)

    def vcopy(out, in_, waits=()):
        return b.emit("vector", lambda h, o=out, i=in_: h.tensor_copy(o, i),
                      waits, "v", 1)

    def vtt(out, a0, a1, op, waits=()):
        return b.emit("vector", lambda h, o=out, x=a0, y=a1, p=op:
                      h.tensor_tensor(o, x, y, p), waits, "v", 1)

    def gtt(out, a0, a1, op, waits=()):
        return b.emit("gpsimd", lambda h, o=out, x=a0, y=a1, p=op:
                      h.tensor_tensor(o, x, y, p), waits, "g", 1)

    def vts1(out, in0, s1, op, waits=()):
        return b.emit("vector", lambda h, o=out, i=in0, a=s1, p=op:
                      h.tensor_single_scalar(o, i, a, p), waits, "v", 1)

    def vts2(out, in0, s1, s2, op0, op1, waits=()):
        return b.emit("vector", lambda h, o=out, i=in0, a=s1, c=s2, p0=op0, p1=op1:
                      h.tensor_scalar(o, i, a, c, p0, p1), waits, "v", 1)

    def vstt(out, in0, s, in1, op0, op1, waits=()):
        return b.emit("vector", lambda h, o=out, i=in0, a=s, j=in1, p0=op0, p1=op1:
                      h.scalar_tensor_tensor(o, i, a, j, p0, p1), waits, "v", 1)

    def vscan(out, d0, d1, init, waits=()):
        return b.emit("vector", lambda h, o=out, x=d0, y=d1, i=init:
                      h.tensor_tensor_scan(o, x, y, i, ALU.mult, ALU.add),
                      waits, "v", 1)

    def vrecip(out, in_, waits=()):
        return b.emit("vector", lambda h, o=out, i=in_: h.reciprocal(o, i),
                      waits, "v", 1)

    def vmemset(ap, val, waits=()):
        return b.emit("vector", lambda h, a=ap, v=val: h.memset(a, v),
                      waits, "v", 1)

    def vreduce(out, in_, op, absval=False, waits=()):
        return b.emit("vector",
                      lambda h, o=out, i=in_, p=op, av=absval:
                      h.tensor_reduce(o, i, mybir.AxisListType.X, p,
                                      apply_absolute_value=av),
                      waits, "v", 1)

    ld0 = []
    for src, dst in ((IN["c_ident"], ident_t[:, :]), (IN["c_ones1"], ones1_t[:, :]),
                     (IN["c_w1024"], w1024_t[:, :]), (IN["c_oh16"], coh_t[:, :]),
                     (IN["c_pe"], pe_t[:, :]), (IN["c_po"], po_t[:, :]),
                     (IN["c_scal"], cs_t[:, :])):
        ld0.append(b.dma("sync", dst, src))
    for i, v in enumerate([1e-5, 1.0, 2.0, F32EPS, 1e-12, 1.0 / 6.0, 1e-6, 0.5]):
        nc.const_aps.register(F32, float(v), cs_t[:, i:i + 1])
    sm = smalls_t

    def lsm(c0, src, w):
        return b.dma("sync", sm[:, c0:c0 + w], src)

    ld0 += [lsm(0, IN["l0_fcb"], 8), lsm(8, IN["l0_lng"], 8), lsm(16, IN["l0_lnb"], 8),
            lsm(24, IN["l1_lng"], 8), lsm(32, IN["l1_lnb"], 8)]
    SM_FCB = sm[:, 0:8]
    SM_LNG = {0: sm[:, 8:16], 1: sm[:, 24:32]}
    SM_LNB = {0: sm[:, 16:24], 1: sm[:, 32:40]}
    SM_B2S = sm[:, 40:48]

    def SM_CONVB(ph):
        return sm[:, 48 + 24 * ph:56 + 24 * ph]

    def SM_DTB(ph):
        return sm[:, 56 + 24 * ph:64 + 24 * ph]

    def SM_DP(ph):
        return sm[:, 64 + 24 * ph:72 + 24 * ph]

    def SM_CONVW(ph):
        return sm[:, 96 + 32 * ph:128 + 32 * ph]

    def SM_AMAT(ph):
        return sm[:, 160 + 128 * ph:288 + 128 * ph]

    def load_branch_smalls(layer, deps):
        pre = "l0" if layer == 0 else "l1"
        toks = list(deps)
        for ph, pch in ((0, "a"), (1, "b")):
            toks.append(lsm(48 + 24 * ph, IN[pre + pch + "_convb"], 8))
            toks.append(lsm(56 + 24 * ph, IN[pre + pch + "_dtb"], 8))
            toks.append(lsm(64 + 24 * ph, IN[pre + pch + "_dp"], 8))
            toks.append(lsm(96 + 32 * ph, IN[pre + pch + "_convw"], 32))
            toks.append(lsm(160 + 128 * ph, IN[pre + pch + "_alog"], 128))
        out = []
        for ph in (0, 1):
            a0 = 160 + 128 * ph
            t1 = act(sm[:, a0:a0 + 128], sm[:, a0:a0 + 128], AF.Exp, waits=toks)
            out.append(vts1(sm[:, a0:a0 + 128], sm[:, a0:a0 + 128], -1.0, ALU.mult,
                            waits=[t1]))
        return out

    xload_toks = []
    xld_last = [[], []]
    for tt in range(17):
        tmt = TMT if tt % 2 == 0 else QAT
        wts = xld_last[tt % 2]
        xld_last[tt % 2] = []
        dtok = b.dma("sync", tmt[:, 0:1024], IN["xin"][tt * 128:(tt + 1) * 128, :],
                     waits=wts + (ld0 if tt == 0 else []), pub="dx0" if tt % 2 == 0 else "dx1")
        last = None
        for db in range(8):
            ps, pw, pdone = pp.get()
            ttk = tr(ps[:, 0:128], tmt[:, db * 128:(db + 1) * 128], [dtok] + pw)
            if tt < 16:
                etk = act(xstr[db][:, tt * 128:(tt + 1) * 128], ps[:, 0:128], AF.Copy,
                          waits=[ttk])
            else:
                etk = act(xh_t[:, db * 3:db * 3 + 3], ps[:, 0:3], AF.Copy, waits=[ttk])
            pdone(etk)
            last = etk
        xld_last[tt % 2] = [last]
        xload_toks.append(last)
    XTOK = ("a", b.cnt["a"])

    def stats_pass(src_tiles, deps):
        sq1, w1, d1 = sp.get()
        sq2, w2, d2 = sp.get()
        toks = []
        prev_q = w1 + w2
        for s in range(NSUB):
            ps, pw, pdone = pp.get()
            mtk = None
            sqtoks = []
            for db in range(8):
                src = src_tiles[db][:, s * 128:(s + 1) * 128]
                w0 = list(deps) + (pw if db == 0 else [])
                mtk = mm(ps[:, 0:1], src, w1024_t[:, 0:1], db == 0, db == 7,
                         w0, "t" if db == 7 else None)
                sqt = (sq1 if db < 4 else sq2)[:, (db % 4) * 128:(db % 4 + 1) * 128]
                sqtoks.append(act(sqt, src, AF.Square, waits=w0 + prev_q))
            e1 = act(statc_t[:, s:s + 1], ps[:, 0:1], AF.Copy, waits=[mtk])
            pdone(e1)
            ps2, pw2, pdone2 = pp.get()
            qtk = None
            for db in range(8):
                sqt = (sq1 if db < 4 else sq2)[:, (db % 4) * 128:(db % 4 + 1) * 128]
                qtk = mm(ps2[:, 0:1], sqt, w1024_t[:, 0:1], db == 0, db == 7,
                         [sqtoks[db]] + (pw2 if db == 0 else []),
                         "t" if db == 7 else None)
            prev_q = [qtk]
            e2 = act(statc_t[:, 16 + s:16 + s + 1], ps2[:, 0:1], AF.Copy, waits=[qtk])
            pdone2(e2)
            toks += [e1, e2]
        d1(prev_q[0])
        d2(prev_q[0])
        return toks

    def ln_rows(s0, ns, deps, rev, want_r2):
        w = ns * 128
        tts = []
        for j, stat in ((0, 0), (1, 16)):
            for s in range(ns):
                ps, pw, pdone = pp.get()
                tk = tr(ps[0:1, 0:128], statc_t[:, stat + s0 + s:stat + s0 + s + 1],
                        list(deps) + pw)
                ek = act(rowb_t[0:1, j * 256 + s * 128:j * 256 + (s + 1) * 128],
                         ps[0:1, 0:128], AF.Copy, waits=[tk])
                pdone(ek)
                tts.append(ek)
        mu = rowb_t[0:1, 0:w]
        r1 = rowb_t[0:1, 256:256 + w]
        r12 = rowb_t[0:1, 512:512 + w]
        tmp = rowb_t[0:1, 768:768 + w]
        t1 = vtt(tmp, mu, mu, ALU.mult, waits=tts)
        t2 = vtt(r1, r1, tmp, ALU.subtract, waits=[t1])
        tok = t2
        if want_r2:
            t3 = vcopy(tmp, r1, waits=[t2])
        t4 = act(r1, r1, AF.Sqrt, bias=EPS, waits=[tok if not want_r2 else t3])
        t5 = vrecip(r1, r1, waits=[t4])
        tok = t5
        if want_r2:
            t6 = vtt(r12, r1, r1, ALU.mult, waits=[t5])
            t7 = vtt(r12, r12, tmp, ALU.mult, waits=[t6])
            t8 = act(r12, r12, AF.Sqrt, bias=EPS, waits=[t7])
            t9 = vrecip(r12, r12, waits=[t8])
            t10 = vtt(r12, r12, r1, ALU.mult, waits=[t9])
            tok = t10
        if not rev:
            return mu, r1, (r12 if want_r2 else None), tok
        mur = rowb_t[0:1, 1024:1024 + w]
        r1r = rowb_t[0:1, 1280:1280 + w]
        c1 = vcopy(mur, mu[0:1, w - 1::-1], waits=[tok])
        c2 = vcopy(r1r, r1[0:1, w - 1::-1], waits=[c1])
        tok = c2
        if want_r2:
            r12r = rowb_t[0:1, 1536:1536 + w]
            tok = vcopy(r12r, r12[0:1, w - 1::-1], waits=[c2])
            return mur, r1r, r12r, tok
        return mur, r1r, None, tok

    bc_last = {"mub": [], "r12b": []}

    def bcast_row(row_ap, w, deps, dst=None):
        ps, pw, pdone = pp.get()
        t1 = mm(ps[:, 0:w], ones1_t[0:1, :], row_ap, True, True, list(deps) + pw, "t")
        if dst is None:
            tile, tw, tdone = sp.get()
            t2 = act(tile[:, 0:w], ps[:, 0:w], AF.Copy, waits=[t1] + tw)
            pdone(t2)
            return tile[:, 0:w], t2, tdone
        tile = {"mub": mub_t, "r12b": r12b_t}[dst]
        t2 = act(tile[:, 0:w], ps[:, 0:w], AF.Copy, waits=[t1] + bc_last[dst])
        pdone(t2)

        def tdone(tok, dst=dst):
            bc_last[dst] = [tok]

        return tile[:, 0:w], t2, tdone

    class StripMgr:
        def __init__(self):
            self.bufs = [strip_t, stripb_t]
            self.cons = [[], []]
            self.i = 0

        def load(self, src_ap, cols, extra=(), eng="sync", rows=128):
            i = self.i % 2
            self.i += 1
            buf = self.bufs[i]
            dst = buf[0:rows, 0:cols]
            if src_ap.ndim == 3:
                dst = dst.rearrange("p (kb j) -> p kb j", j=src_ap.shape[2])
            tok = b.dma(eng, dst, src_ap,
                        waits=list(extra) + self.cons[i],
                        pub="dw" if i == 0 else "dz")
            self.cons[i] = []

            def use(t, i=i):
                self.cons[i].append(t)

            return buf, tok, use

    smgr = StripMgr()

    def strip_dma(buf, src_ap, deps):
        return b.dma("sync", buf, src_ap, waits=deps, pub="dw")

    last = {"strip": [], "bcB": [], "bcC": [], "xdbl": [],
            "XS": [[] for _ in range(8)], "UU": [[] for _ in range(8)],
            "DU": [[] for _ in range(8)], "YY": [[] for _ in range(8)],
            "DL": [[] for _ in range(8)], "HS": [[], []], "stash": [],
            "exch": [], "state": [[], []], "sca": [[], []], "scb": [[], []], "sch": [[], []], "htm": [], "fs2": [], "fs2w": [],
            "scn": [], "qxf": [], "qhf": [], "tmrd": [], "xf": [],
            "hseg": [[], []], "w2s": [[], []], "ngb": [],
            "acc1": {i: [] for i in range(8)}}

    def hs_build(layer, out_ap, src_ap, mub, r12b, row_tok, w, deps):
        t1 = vstt(out_ap, src_ap, 1.0, mub, ALU.mult, ALU.subtract,
                  waits=list(deps) + [row_tok])
        t2 = vtt(out_ap, out_ap, r12b, ALU.mult, waits=[t1])
        t3 = vts2(out_ap, out_ap, SM_LNG[layer][:, 0:1].broadcast_to([128, 1]),
                  SM_LNB[layer][:, 0:1], ALU.mult, ALU.add, waits=[t2])
        return t3

    def in_sweep(layer, ph, colrange, w, rev, mub, r12b, row_tok, zpart, out_cb):
        ipw = IN["l0_ipw"] if layer == 0 else IN["l1a_win" if ph == 0 else "l1b_win"]
        c0 = 1024 if zpart else 0
        banks = [pp.get() for _ in range(8)]
        lnga = SM_LNG[layer]
        lnba = SM_LNB[layer]
        mmtoks = [None] * 8
        for k in range(8):
            sbuf_w, sd, suse = smgr.load(ipw[k * 128:(k + 1) * 128, c0:c0 + 1024], 1024)
            hs = HS[k % 2]
            src = xstr[k][:, colrange[0]:colrange[0] + w]
            if rev:
                src = src[:, w - 1::-1]
            hw = list(last["HS"][k % 2])
            t1 = vstt(hs[:, 0:w], src, 1.0, mub, ALU.mult, ALU.subtract,
                      waits=hw + [row_tok])
            t2 = vtt(hs[:, 0:w], hs[:, 0:w], r12b, ALU.mult, waits=[t1])
            t3 = vts2(hs[:, 0:w], hs[:, 0:w], lnga[:, k:k + 1], lnba[:, k:k + 1],
                      ALU.mult, ALU.add, waits=[t2])
            if DBG is not None and DBG[3] == "hs2" and zpart and k < 2:
                t3 = b.dma("gpsimd", yout[k * 128:(k + 1) * 128, 0:w], hs[:, 0:w],
                           waits=[t3], pub="do")
            for m in range(8):
                tile, pw, pdone = banks[m]
                wts = [t3, sd] + (pw if k == 0 else [])
                tok = mm(tile[:, 0:w], sbuf_w[:, m * 128:(m + 1) * 128],
                         hs[:, 0:w], k == 0, k == 7, wts)
                if k == 7:
                    mmtoks[m] = tok
                if m == 7:
                    last["HS"][k % 2] = [tok]
                    suse(tok)
        for m in range(8):
            tile, pw, pdone = banks[m]
            out_cb(m, tile, mmtoks[m], pdone)

    def attn_chunk(layer, ph, ci, stok):
        rev = ph == 1
        sc0 = ci * CH
        dtk = 64 if layer == 0 else 32
        xw = 96 if layer == 0 else 64
        st_t = stA_t if ph == 0 else stB_t
        mu, r1, r12, rtok = ln_rows(ci * 2, 2, [stok], rev, True)
        mub, mtk, mubd = bcast_row(mu, CH, [rtok], dst="mub")
        r12b, rtk, r12bd = bcast_row(r12, CH, [rtok], dst="r12b")
        row_tok = rtk if rtk[1] > mtk[1] else mtk

        def xcb(m, tile, mmtok, pdone):
            e = act(XS[m][:, 3:3 + CH], tile[:, 0:CH], AF.Copy,
                    waits=[mmtok] + last["XS"][m])
            last["XS"][m] = []
            pdone(e)
            h1 = vcopy(XS[m][:, 0:3], stash_t[:, m * 3:m * 3 + 3],
                       waits=last["stash"] + [e])
            h2 = vcopy(stash_t[:, m * 3:m * 3 + 3], XS[m][:, CH:CH + 3], waits=[e, h1])
            cw = SM_CONVW(ph)
            c = vts1(UU[m][:, 0:CH], XS[m][:, 0:CH], cw[:, m * 4:m * 4 + 1], ALU.mult,
                     waits=[h1] + last["UU"][m])
            last["UU"][m] = []
            for kk in range(1, 4):
                c = vstt(UU[m][:, 0:CH], XS[m][:, kk:kk + CH], cw[:, m * 4 + kk:m * 4 + kk + 1],
                         UU[m][:, 0:CH], ALU.mult, ALU.add, waits=[c])
            s = act(UU[m][:, 0:CH], UU[m][:, 0:CH], AF.Silu,
                    bias=SM_CONVB(ph)[:, m:m + 1], waits=[c])
            last["XS"][m] = [c]
            last["stash"] = last["stash"] + [h2]
            last["UU"][m] = [s]

        in_sweep(layer, ph, (sc0,), CH, rev, mub, r12b, row_tok, False, xcb)
        u_toks = [last["UU"][m][0] for m in range(8)]

        xpw = IN[("l0" if layer == 0 else "l1") + ("a" if ph == 0 else "b") + "_xpw"]
        sbw, sd, suse = smgr.load(xpw.rearrange("(kb p) j -> p kb j", p=128), 8 * xw)
        psd, pwd, pdd = pp.get()
        psb, pwb, pdb = pp.get()
        psc, pwc, pdc = pp.get()
        tD = tB = tC = None
        for kb in range(8):
            base = kb * xw
            wts = [sd] + u_toks
            tD = mm(psd[0:dtk, 0:CH], sbw[:, base:base + dtk], UU[kb][:, 0:CH],
                    kb == 0, kb == 7, wts + (pwd if kb == 0 else []))
            tB = mm(psb[0:16, 0:CH], sbw[:, base + dtk:base + dtk + 16],
                    UU[kb][:, 0:CH], kb == 0, kb == 7, pwb if kb == 0 else ())
            tC = mm(psc[0:16, 0:CH], sbw[:, base + dtk + 16:base + dtk + 32],
                    UU[kb][:, 0:CH], kb == 0, kb == 7, pwc if kb == 0 else ())
        suse(tD)
        suse(tB)
        suse(tC)
        eD = act(xdbl_t[0:dtk, 0:CH], psd[0:dtk, 0:CH], AF.Copy,
                 waits=[tD] + last["xdbl"])
        last["xdbl"] = []
        eB = act(bcB_t[0:16, 0:CH], psb[0:16, 0:CH], AF.Copy, waits=[tB] + last["bcB"])
        last["bcB"] = []
        eC = act(bcC_t[0:16, 0:CH], psc[0:16, 0:CH], AF.Copy, waits=[tC] + last["bcC"])
        last["bcC"] = []
        pdd(eD)
        pdb(eB)
        pdc(eC)

        dtw = IN[("l0" if layer == 0 else "l1") + ("a" if ph == 0 else "b") + "_dtw"]
        sbw2, sd2, suse2 = smgr.load(dtw, 1024, rows=dtk)
        dtmm = []
        for m in range(8):
            tile, pw, pdone = pp.get()
            tmk = mm(tile[0:128, 0:CH], sbw2[0:dtk, m * 128:(m + 1) * 128],
                     xdbl_t[0:dtk, 0:CH], True, True, [sd2, eD] + pw, "t")
            tmp, tw, tdone = sp.get()
            a1 = act(tmp[:, 0:CH], tile[:, 0:CH], AF.Exp,
                     bias=SM_DTB(ph)[:, m:m + 1], waits=[tmk] + tw)
            pdone(a1)
            a2 = act(DL[m][:, 0:CH], tmp[:, 0:CH], AF.Ln, bias=1.0,
                     waits=[a1] + last["DL"][m])
            last["DL"][m] = []
            tdone(a2)
            dtmm.append(tmk)
            d1 = vtt(DU[m][:, 0:CH], DL[m][:, 0:CH], UU[m][:, 0:CH], ALU.mult,
                     waits=[a2] + last["DU"][m])
            y1 = vts1(YY[m][:, 0:CH], UU[m][:, 0:CH], SM_DP(ph)[:, m:m + 1], ALU.mult,
                      waits=last["YY"][m])
            last["DU"][m] = [d1]
            last["YY"][m] = [y1]
            last["DL"][m] = [d1]
        suse2(dtmm[-1])
        last["xdbl"] = [dtmm[-1]]

        amat = SM_AMAT(ph)
        exrow = rowb_t[0:1, 768:768 + CH]
        exrow2 = rowb_t[0:1, 1024 + 768 - 256:1024 + 768] if False else None
        exlast = []
        for n in range(16):
            pB, pwB, pdB = pp.get()
            tx = mm(pB[0:1, 0:CH], coh_t[0:16, n:n + 1], bcB_t[0:16, 0:CH],
                    True, True, [eB] + pwB, "t")
            ex1 = act(exrow, pB[0:1, 0:CH], AF.Copy, waits=[tx] + exlast)
            pdB(ex1)
            pB2, pwB2, pdB2 = pp.get()
            t1 = mm(pB2[:, 0:CH], ones1_t[0:1, :], exrow, True, True, [ex1] + pwB2, "t")
            Bt, Bw, Bd = sp.get()
            e1 = act(Bt[:, 0:CH], pB2[:, 0:CH], AF.Copy, waits=[t1] + Bw)
            pdB2(e1)
            exlast = [t1]
            pC, pwC, pdC = pp.get()
            tx = mm(pC[0:1, 0:CH], coh_t[0:16, n:n + 1], bcC_t[0:16, 0:CH],
                    True, True, [eC] + pwC, "t")
            ex2 = act(exrow, pC[0:1, 0:CH], AF.Copy, waits=[tx] + exlast)
            pdC(ex2)
            pC2, pwC2, pdC2 = pp.get()
            t2 = mm(pC2[:, 0:CH], ones1_t[0:1, :], exrow, True, True, [ex2] + pwC2, "t")
            Ct, Cw, Cd = sp.get()
            e2 = act(Ct[:, 0:CH], pC2[:, 0:CH], AF.Copy, waits=[t2] + Cw)
            pdC2(e2)
            exlast = [t2]
            lb = lc = None
            for m in range(8):
                col = m * 16 + n
                par = (n * 8 + m) % 2
                sca, scb, sch = SCA[par], SCB[par], SCH[par]
                ea = act(sca, DL[m][:, 0:CH], AF.Exp, scale=amat[:, col:col + 1],
                         waits=last["DL"][m] + last["sca"][par])
                bb = gtt(scb, DU[m][:, 0:CH], Bt[:, 0:CH], ALU.mult,
                         waits=[e1] + last["DU"][m] + last["scb"][par])
                sc = vscan(sch, sca, scb, st_t[:, col:col + 1],
                           waits=[ea, bb] + last["state"][ph] + last["sch"][par])
                last["state"][ph] = []
                stc = vcopy(st_t[:, col:col + 1], sch[:, CH - 1:CH], waits=[sc])
                tm2 = vtt(sca, sch, Ct[:, 0:CH], ALU.mult, waits=[sc, e2])
                ya = gtt(YY[m][:, 0:CH], YY[m][:, 0:CH], sca, ALU.add,
                         waits=[tm2] + last["YY"][m])
                last["YY"][m] = [ya]
                last["DL"][m] = [ea]
                last["DU"][m] = [bb]
                last["sca"][par] = [ya]
                last["scb"][par] = [sc]
                last["sch"][par] = [tm2, stc]
                lb, lc = ya, stc
            Bd(lb)
            Cd(lb)
            last["state"][ph].append(lc)

        if DBG is not None and DBG[3] == "y0" and layer == DBG[0] and ph == DBG[1] and ci == DBG[2]:
            for m in range(8):
                dt_ = b.dma("gpsimd", yout[m * 128:(m + 1) * 128, 0:CH], YY[m][:, 0:CH],
                            waits=last["YY"][m], pub="do")
                last["YY"][m] = last["YY"][m] + [dt_]
            b.barrier_wait("gpsimd", [("do", b.cnt["do"])])

        def zcb(m, tile, mmtok, pdone):
            szt, zw, zd = sp.get()
            s = act(szt[:, 0:CH], tile[:, 0:CH], AF.Silu, waits=[mmtok] + zw)
            pdone(s)
            if DBG is not None and DBG[3] in ("sz", "szx") and (layer, ph, ci) == DBG[:3]:
                s = b.dma("gpsimd", yout[m * 128:(m + 1) * 128, 0:CH], szt[:, 0:CH],
                          waits=[s], pub="do")
            g = vtt(YY[m][:, 0:CH], YY[m][:, 0:CH], szt[:, 0:CH], ALU.mult,
                    waits=[s] + last["YY"][m])
            zd(g)
            last["YY"][m] = [g]

        zp_flag = not (DBG is not None and DBG[3] == "szx")
        in_sweep(layer, ph, (sc0,), CH, rev, mub, r12b, row_tok, zp_flag, zcb)
        y_toks = [last["YY"][m][0] for m in range(8)]
        mubd(y_toks[7])
        r12bd(y_toks[7])

        if DBG is not None and DBG[3] in ("u", "dl", "du", "y", "xs") and layer == DBG[0] and ph == DBG[1] and ci == DBG[2]:
            srcs = {"u": UU, "dl": DL, "du": DU, "y": YY, "xs": None}[DBG[3]]
            for m in range(8):
                if DBG[3] == "xs":
                    ap = XS[m][:, 3:3 + CH]
                else:
                    ap = srcs[m][:, 0:CH]
                dt_ = b.dma("gpsimd", yout[m * 128:(m + 1) * 128, 0:CH], ap,
                            waits=[y_toks[m]], pub="do")
                key = "UU" if DBG[3] == "u" else ("DL" if DBG[3] == "dl" else
                      ("DU" if DBG[3] == "du" else ("YY" if DBG[3] == "y" else "XS")))
                last[key][m] = last[key][m] + [dt_]
            b.barrier_wait("gpsimd", [("do", b.cnt["do"])])
        return y_toks

    def l0_outproj(ph, ci, y_toks):
        sc0 = ci * CH
        opm = []
        for m in range(8):
            sbw, sdw, suse = smgr.load(IN["l0_opw"][:, m * 128:(m + 1) * 128]
                                       .rearrange("(kb p) j -> p kb j", p=128), 1024)
            tile, pw, pdone = pp.get()
            tk = None
            for kb in range(8):
                tk = mm(tile[:, 0:CH], sbw[:, kb * 128:(kb + 1) * 128],
                        YY[kb][:, 0:CH], kb == 0, kb == 7,
                        ([sdw] + y_toks + pw) if kb == 0 else ())
            e = act(UU[m][:, 0:CH], tile[:, 0:CH], AF.Copy, waits=[tk] + last["UU"][m])
            pdone(e)
            suse(tk)
            last["UU"][m] = [e]
            opm.append(e)
        for m in range(8):
            sbw, sdw, suse = smgr.load(IN["l0_fcw"][:, m * 128:(m + 1) * 128]
                                       .rearrange("(kb p) j -> p kb j", p=128), 1024)
            tile, pw, pdone = pp.get()
            tk = None
            for kb in range(8):
                tk = mm(tile[:, 0:CH], sbw[:, kb * 128:(kb + 1) * 128],
                        UU[kb][:, 0:CH], kb == 0, kb == 7,
                        ([sdw] + opm + pw) if kb == 0 else ())
            acc_sl = accv[m][:, sc0:sc0 + CH]
            if ph == 0:
                e = act(acc_sl, tile[:, 0:CH], AF.Copy, waits=[tk])
                pdone(e)
            else:
                tmp, tw, tdone = sp.get()
                e0 = act(tmp[:, 0:CH], tile[:, 0:CH], AF.Copy, waits=[tk] + tw)
                pdone(e0)
                e = vtt(acc_sl, acc_sl, tmp[:, CH - 1::-1], ALU.add, waits=[e0])
                tdone(e)
            suse(tk)
            last["UU"][m].append(tk)

    def halo_mini(layer, src24, src_tok):
        ps, pw, pdone = pp.get()
        mt = None
        sq, sw, sqd = sp.get()
        sq_toks = []
        for db in range(8):
            src = src24[:, db * 3:db * 3 + 3]
            mt = mm(ps[0:128, 0:1][0:3, 0:1] if False else ps[0:3, 0:1],
                    src, w1024_t[:, 0:1], db == 0, db == 7,
                    ([src_tok] + pw) if db == 0 else (), "t" if db == 7 else None)
            sq_toks.append(act(sq[:, db * 4:db * 4 + 3], src, AF.Square,
                               waits=[src_tok] + (sw if db == 0 else [])))
        ps2, pw2, pdone2 = pp.get()
        qt = None
        for db in range(8):
            qt = mm(ps2[0:3, 0:1], sq[:, db * 4:db * 4 + 3], w1024_t[:, 0:1],
                    db == 0, db == 7, [sq_toks[db]] + (pw2 if db == 0 else []),
                    "t" if db == 7 else None)
        sqd(qt)
        ps3, pw3, pdone3 = pp.get()
        t1 = tr(ps3[0:1, 0:3], ps[0:3, 0:1], [mt] + pw3, idn=ident_t[0:3, 0:3])
        return None

    def halo_mini2(layer, ph, src24, src_tok):
        mcol, mw, mdone = sp.get()
        ps, pw, pdone = pp.get()
        mt = None
        sq_toks = []
        sq, sw, sqd = sp.get()
        for db in range(8):
            src = src24[:, db * 3:db * 3 + 3]
            mt = mm(ps[0:3, 0:1], src, w1024_t[:, 0:1], db == 0, db == 7,
                    ([src_tok] + pw) if db == 0 else (), "t" if db == 7 else None)
            sq_toks.append(act(sq[:, db * 4:db * 4 + 3], src, AF.Square,
                               waits=[src_tok] + (sw if db == 0 else [])))
        e1 = act(mcol[0:3, 0:1], ps[0:3, 0:1], AF.Copy, waits=[mt] + mw)
        pdone(e1)
        ps2, pw2, pdone2 = pp.get()
        qt = None
        for db in range(8):
            qt = mm(ps2[0:3, 0:1], sq[:, db * 4:db * 4 + 3], w1024_t[:, 0:1],
                    db == 0, db == 7, [sq_toks[db]] + (pw2 if db == 0 else []),
                    "t" if db == 7 else None)
        e2 = act(mcol[0:3, 1:2], ps2[0:3, 0:1], AF.Copy, waits=[qt])
        pdone2(e2)
        sqd(qt)
        ps3, pw3, pdone3 = pp.get()
        t1 = tr(ps3[0:1, 0:3], mcol[0:3, 0:1], [e1] + pw3, idn=ident_t[0:3, 0:3])
        r1 = act(hrow_t[0:1, 0:3], ps3[0:1, 0:3], AF.Copy, waits=[t1])
        pdone3(r1)
        ps4, pw4, pdone4 = pp.get()
        t2 = tr(ps4[0:1, 0:3], mcol[0:3, 1:2], [e2] + pw4, idn=ident_t[0:3, 0:3])
        r2 = act(hrow_t[0:1, 4:7], ps4[0:1, 0:3], AF.Copy, waits=[t2])
        pdone4(r2)
        mdone(t2)
        mu3 = hrow_t[0:1, 0:3]
        q3 = hrow_t[0:1, 4:7]
        v3 = hrow_t[0:1, 8:11]
        r12_3 = hrow_t[0:1, 12:15]
        x1 = vtt(v3, mu3, mu3, ALU.mult, waits=[r1, r2])
        x2 = vtt(q3, q3, v3, ALU.subtract, waits=[x1])
        x3 = vcopy(v3, q3, waits=[x2])
        x4 = act(q3, q3, AF.Sqrt, bias=EPS, waits=[x3])
        x5 = vrecip(q3, q3, waits=[x4])
        x6 = vtt(r12_3, q3, q3, ALU.mult, waits=[x5])
        x7 = vtt(r12_3, r12_3, v3, ALU.mult, waits=[x6])
        x8 = act(r12_3, r12_3, AF.Sqrt, bias=EPS, waits=[x7])
        x9 = vrecip(r12_3, r12_3, waits=[x8])
        x10 = vtt(r12_3, r12_3, q3, ALU.mult, waits=[x9])
        mb3, mb3t, mb3d = bcast_row(mu3, 3, [x10], dst="mub")
        rb3, rb3t, rb3d = bcast_row(r12_3, 3, [x10], dst="r12b")
        ipw = IN["l0_ipw"] if layer == 0 else IN["l1b_win"]
        hst, hw2, hsd = sp.get()
        banks = [pp.get() for _ in range(8)]
        mmt = [None] * 8
        hst_readers = list(hw2)
        for k in range(8):
            sbw, sd, suse = smgr.load(ipw[k * 128:(k + 1) * 128, 0:1024], 1024)
            h1 = vstt(hst[:, 0:3],
                      src24[:, k * 3:k * 3 + 3], 1.0, mb3, ALU.mult, ALU.subtract,
                      waits=[mb3t, src_tok] + hst_readers)
            hst_readers = []
            h2 = vtt(hst[:, 0:3], hst[:, 0:3], rb3, ALU.mult, waits=[h1, rb3t])
            h3 = vts2(hst[:, 0:3], hst[:, 0:3], SM_LNG[layer][:, k:k + 1],
                      SM_LNB[layer][:, k:k + 1], ALU.mult, ALU.add, waits=[h2])
            for m in range(8):
                tile, pw, pdone = banks[m]
                tok = mm(tile[:, 0:3], sbw[:, m * 128:(m + 1) * 128],
                         hst[:, 0:3], k == 0, k == 7,
                         ([h3, sd] + pw) if k == 0 else [h3, sd])
                if k == 7:
                    mmt[m] = tok
                if m == 7:
                    suse(tok)
                    hst_readers = [tok]
        hsd(mmt[7])
        mb3d(mmt[7])
        rb3d(mmt[7])
        sttoks = []
        for m in range(8):
            tile, pw, pdone = banks[m]
            e = act(stash_t[:, m * 3:m * 3 + 3], tile[:, 0:3], AF.Copy,
                    waits=[mmt[m]] + last["stash"])
            pdone(e)
            sttoks.append(e)
        last["stash"] = sttoks

    def exchange(nm, payload_ap, pay_w, pay_tok, out_ap):
        ain, aout = AR[nm]
        t0 = vts1(exch_t[:, 0:pay_w], payload_ap, pe_t[:, 0:1], ALU.mult,
                  waits=[pay_tok] + last["exch"] + last["sca"][0] + last["scb"][0]
                  + last["sch"][0])
        t1 = vts1(exch_t[:, 256:256 + pay_w], payload_ap, po_t[:, 0:1], ALU.mult,
                  waits=[pay_tok] + last["scb"][0] + last["sch"][0])
        d0 = b.dma("gpsimd", ain[:, 0:pay_w], exch_t[:, 0:pay_w], waits=[t0], pub="de")
        d1 = b.dma("gpsimd", ain[:, pay_w:2 * pay_w], exch_t[:, 256:256 + pay_w],
                   waits=[t1], pub="de")
        cc = b.emit("gpsimd",
                    lambda h, i=ain, o=aout: h.collective_compute(
                        "AllReduce", ALU.add, replica_groups=PAIRS,
                        ins=[i.opt()], outs=[o.opt()]),
                    [d0, d1], "cc", 1)
        d2 = b.dma("gpsimd", exch_t[:, 0:pay_w], aout[:, 0:pay_w], waits=[cc], pub="de")
        d3 = b.dma("gpsimd", exch_t[:, 256:256 + pay_w], aout[:, pay_w:2 * pay_w],
                   waits=[cc], pub="de")
        c0 = vts1(out_ap, exch_t[:, 0:pay_w], po_t[:, 0:1], ALU.mult, waits=[d2])
        c1 = vstt(out_ap, exch_t[:, 256:256 + pay_w], pe_t[:, 0:1], out_ap,
                  ALU.mult, ALU.add, waits=[d3, c0])
        last["exch"] = [c1]
        for k2 in ("sca", "scb", "sch"):
            last[k2][0] = last[k2][0] + [c1]
        return c1

    def stage_barrier():
        toks = [(s, b.cnt[s]) for s in SEM_NAMES if b.cnt[s] > 0 and s != "cc"]
        toks += [(s, b.cnt[s]) for s in ("cc",) if b.cnt[s] > 0]
        for e in ENGS:
            b.barrier_wait(e, toks)
        return toks

    def quant128(tm_ap, w, deps):
        nb = w // 16
        t_abs = act(FS1[:, 0:w], tm_ap, AF.Abs, waits=deps)
        scl = scn_t[:, 0:nb]
        scl2 = scn_t[:, 256:256 + nb]
        rscl = scn_t[:, 512:512 + nb]
        t1 = vreduce(scl.rearrange("p (nb one) -> p nb one", one=1),
                     tm_ap.rearrange("p (nb w2) -> p nb w2", w2=16),
                     ALU.max, absval=True, waits=deps + last["scn"])
        t2 = vts2(scl2, scl, 1.0 / 6.0, 1e-12, ALU.mult, ALU.max, waits=[t1])
        t3 = vrecip(rscl, scl2, waits=[t2])
        rexp = rscl.unsqueeze(2).broadcast_to([128, nb, 16])
        sexp = scl2.unsqueeze(2).broadcast_to([128, nb, 16])
        f3 = FS1[:, 0:w].rearrange("p (nb w2) -> p nb w2", w2=16)
        t4 = vtt(f3, f3, rexp, ALU.mult, waits=[t3, t_abs])
        xn = FS1[:, 0:w]
        a0 = vts1(FS2[:, 0:w], xn, 2.5, ALU.is_gt, waits=[t4] + last["fs2"])
        a1 = vstt(FS2[:, 0:w], xn, 3.5, FS2[:, 0:w], ALU.is_gt, ALU.add, waits=[a0])
        a2 = vstt(FS2[:, 0:w], xn, 5.0, FS2[:, 0:w], ALU.is_gt, ALU.add, waits=[a1])
        a3 = vstt(FS2[:, 0:w], xn, 5.0, FS2[:, 0:w], ALU.is_gt, ALU.add, waits=[a2])
        a4 = vts1(FS2[:, 0:w], FS2[:, 0:w], 2.0, ALU.mult, waits=[a3])
        a5 = vstt(FS2[:, 0:w], xn, 0.25, FS2[:, 0:w], ALU.is_gt, ALU.add, waits=[a4])
        a6 = vstt(FS2[:, 0:w], xn, 0.75, FS2[:, 0:w], ALU.is_gt, ALU.add, waits=[a5])
        a7 = vstt(FS2[:, 0:w], xn, 1.25, FS2[:, 0:w], ALU.is_gt, ALU.add, waits=[a6])
        a8 = vstt(FS2[:, 0:w], xn, 1.75, FS2[:, 0:w], ALU.is_gt, ALU.add, waits=[a7])
        a9 = vts1(FS2[:, 0:w], FS2[:, 0:w], 0.5, ALU.mult, waits=[a8])
        sg = act(FS1[:, 0:w], tm_ap, AF.Sign, waits=[a8])
        q1 = vtt(FS2[:, 0:w], FS2[:, 0:w], FS1[:, 0:w], ALU.mult, waits=[a9, sg])
        f2 = FS2[:, 0:w].rearrange("p (nb w2) -> p nb w2", w2=16)
        q2 = vtt(f2, f2, sexp, ALU.mult, waits=[q1])
        last["scn"] = [q2]
        last["fs2"] = []
        last["tmrd"] = [q2, sg]
        return q2

    def wq_pass():
        specs = [(IN["f0_gw"], qgw, 22, 1024), (IN["f0_uw"], quw, 22, 1024),
                 (IN["f0_dw"], qdw, 8, 2816)]
        stage = FS1[:, 0:128]
        for wap, qap, nrt, C in specs:
            for r in range(nrt):
                d0 = b.dma("sync", HTM[:, 0:C], wap[r * 128:(r + 1) * 128, :],
                           waits=last["htm"], pub="dw")
                last["htm"] = []
                qtoks = []
                for s0 in range(0, C, 1024):
                    w = min(1024, C - s0)
                    qt = quant128(HTM[:, s0:s0 + w], w, [d0] + last["fs2w"])
                    for c in range(w // 128):
                        ps, pw, pdone = pp.get()
                        tt_ = tr(ps[:, 0:128], FS2[:, c * 128:(c + 1) * 128], [qt] + pw)
                        ek = act(r12b_t[:, 0:128], ps[:, 0:128], AF.Copy,
                                 waits=[tt_] + bc_last["r12b"])
                        pdone(ek)
                        bc_last["r12b"] = [ek]
                        dk = b.dma("gpsimd", qap[s0 + c * 128:s0 + (c + 1) * 128,
                                               r * 128:(r + 1) * 128],
                                   r12b_t[:, 0:128], waits=[ek], pub="d4")
                        bc_last["r12b"] = [ek, dk]
                        qtoks.append(dk)
                    last["fs2w"] = [qtoks[-1]] if qtoks else []
                last["htm"] = qtoks
        last["fs2w"] = []

    def f0_ffn(bar):
        sttoks = stats_pass(xstr, bar)
        stok = max(sttoks, key=lambda t: t[1])
        for ci in range(16):
            mu, r1, _, rtok = ln_rows(ci, 1, [stok], False, False)
            mub, mtk, mubd = bcast_row(mu, 128, [rtok], dst="mub")
            r1b, rtk, r1bd = bcast_row(r1, 128, [rtok], dst="r12b")
            rowt = rtk if rtk[1] > mtk[1] else mtk
            c0 = ci * 128
            etoks = []
            for k in range(8):
                lt, lw, ld_ = sp.get()
                t1 = vstt(lt[:, 0:128], xstr[k][:, c0:c0 + 128], 1.0, mub,
                          ALU.mult, ALU.subtract, waits=[rowt] + lw)
                t2 = vtt(lt[:, 0:128], lt[:, 0:128], r1b, ALU.mult, waits=[t1])
                ps, pw, pdone = pp.get()
                t3 = tr(ps[:, 0:128], lt[:, 0:128], [t2] + pw)
                ld_(t3)
                e = act(HTM[:, k * 128:(k + 1) * 128], ps[:, 0:128], AF.Copy,
                        waits=[t3] + last["htm"])
                last["htm"] = []
                pdone(e)
                etoks.append(e)
            qt = quant128(HTM[:, 0:1024], 1024, etoks)
            if DBG is not None and DBG[3] == "f0x" and ci == 0:
                d1 = b.dma("gpsimd", yout[0:128, 0:1024], HTM[:, 0:1024],
                           waits=etoks, pub="do")
                d2 = b.dma("gpsimd", yout[128:256, 0:1024], FS2[:, 0:1024],
                           waits=[qt], pub="do")
                last["htm"] = last["htm"] + [d1]
                last["tmrd"] = last["tmrd"] + [d1]
                last["fs2"] = last["fs2"] + [d2]
            qxtoks = []
            for k in range(8):
                ps, pw, pdone = pp.get()
                t3 = tr(ps[:, 0:128], FS2[:, k * 128:(k + 1) * 128], [qt] + pw)
                e = act(QXF[k][:, 0:128], ps[:, 0:128], AF.Copy,
                        waits=[t3] + last["qxf"])
                pdone(e)
                qxtoks.append(e)
            last["qxf"] = []
            last["fs2"] = qxtoks
            htoks = []
            for m in range(22):
                sb1, sd1, su1 = smgr.load(qgw[:, m * 128:(m + 1) * 128]
                                          .rearrange("(kb p) j -> p kb j", p=128), 1024)
                pg, pwg, pdg = pp.get()
                tg = None
                for k in range(8):
                    tg = mm(pg[:, 0:128], sb1[:, k * 128:(k + 1) * 128],
                            QXF[k][:, 0:128], k == 0, k == 7,
                            ([sd1] + qxtoks + pwg) if k == 0 else ())
                su1(tg)
                sb2, sd2, su2 = smgr.load(quw[:, m * 128:(m + 1) * 128]
                                          .rearrange("(kb p) j -> p kb j", p=128), 1024)
                pu, pwu, pdu = pp.get()
                tu = None
                for k in range(8):
                    tu = mm(pu[:, 0:128], sb2[:, k * 128:(k + 1) * 128],
                            QXF[k][:, 0:128], k == 0, k == 7,
                            ([sd2] + pwu) if k == 0 else ())
                su2(tu)
                g1, gw1, gd1 = sp.get()
                sg_ = act(g1[:, 0:128], pg[:, 0:128], AF.Silu, waits=[tg] + gw1)
                pdg(sg_)
                g2, gw2, gd2 = sp.get()
                su_ = act(g2[:, 0:128], pu[:, 0:128], AF.Copy, waits=[tu] + gw2)
                pdu(su_)
                hv = vtt(g1[:, 0:128], g1[:, 0:128], g2[:, 0:128], ALU.mult,
                         waits=[sg_, su_])
                gd2(hv)
                ps, pw, pdone = pp.get()
                t3 = tr(ps[:, 0:128], g1[:, 0:128], [hv] + pw)
                gd1(t3)
                e = act(HTM[:, m * 128:(m + 1) * 128], ps[:, 0:128], AF.Copy,
                        waits=[t3] + last["tmrd"])
                pdone(e)
                htoks.append(e)
            if DBG is not None and DBG[3] == "f0h" and ci == 0:
                for j in range(3):
                    w = 1024 if j < 2 else 768
                    dj = b.dma("gpsimd", yout[j * 128:(j + 1) * 128, 0:w],
                               HTM[:, j * 1024:j * 1024 + w], waits=htoks, pub="do")
                    htoks = htoks + [dj]
            qhtoks = []
            for s0, w in ((0, 1024), (1024, 1024), (2048, 768)):
                qt = quant128(HTM[:, s0:s0 + w], w, htoks)
                for c in range(w // 128):
                    mi = (s0 // 128) + c
                    ps, pw, pdone = pp.get()
                    t3 = tr(ps[:, 0:128], FS2[:, c * 128:(c + 1) * 128], [qt] + pw)
                    e = act(QHF[mi][:, 0:128], ps[:, 0:128], AF.Copy,
                            waits=[t3] + last["qhf"])
                    pdone(e)
                    qhtoks.append(e)
                last["fs2"] = qhtoks[-8:]
            last["qhf"] = []
            dtoks = []
            for og in range(8):
                po_, pwo, pdo = pp.get()
                to = None
                for kg0, kgn in ((0, 8), (8, 8), (16, 6)):
                    sbd, sd, sud = smgr.load(
                        qdw[kg0 * 128:(kg0 + kgn) * 128, og * 128:(og + 1) * 128]
                        .rearrange("(kb p) j -> p kb j", p=128), kgn * 128)
                    for kk in range(kgn):
                        mi = kg0 + kk
                        to = mm(po_[:, 0:128], sbd[:, kk * 128:(kk + 1) * 128],
                                QHF[mi][:, 0:128], mi == 0, mi == 21,
                                ([sd] + qhtoks + pwo) if mi == 0 else [sd])
                    sud(to)
                x_sl = xstr[og][:, c0:c0 + 128]
                e = vstt(x_sl, po_[:, 0:128], 1.0, x_sl, ALU.mult, ALU.add,
                         waits=[to])
                pdo(e)
                dtoks.append(e)
            last["qhf"] = qhtoks
            last["qxf"] = qxtoks
            last["htm"] = htoks + dtoks
            mubd(dtoks[-1])
            r1bd(dtoks[-1])
        return ("v", b.cnt["v"])

    def l0_attn():
        smtok = load_branch_smalls(0, [XTOK])
        sttoks = stats_pass(xstr, [XTOK] + smtok)
        stok = max(sttoks, key=lambda t: t[1])
        z0 = vmemset(stash_t[:, :], 0.0, waits=last["stash"])
        z1 = vmemset(stA_t[:, :], 0.0, waits=last["state"][0])
        last["stash"] = [z0]
        last["state"][0] = [z1]
        for ci in range(NCH):
            yt = attn_chunk(0, 0, ci, stok)
            l0_outproj(0, ci, yt)
        exch_tok = exchange("ar1", stA_t[:, :], 128,
                            last["state"][0][-1] if last["state"][0] else z1,
                            stB_t[:, :])
        last["state"][1] = [exch_tok]
        halo_mini2(0, 1, xh_t[:, :], XTOK)
        for ci in reversed(range(NCH)):
            yt = attn_chunk(0, 1, ci, stok)
            l0_outproj(1, ci, yt)
        for ci in range(NCH):
            sc0 = ci * CH
            mu, r1, _, rtok = ln_rows(ci * 2, 2, [stok], False, False)
            mub, mtk, mubd = bcast_row(mu, CH, [rtok], dst="mub")
            r1b, rtk, r1bd = bcast_row(r1, CH, [rtok], dst="r12b")
            lt, lw, ld = sp.get()
            e = None
            for m in range(8):
                x_sl = xstr[m][:, sc0:sc0 + CH]
                acc_sl = accv[m][:, sc0:sc0 + CH]
                t1 = vstt(lt[:, 0:CH], x_sl, 1.0, mub, ALU.mult, ALU.subtract,
                          waits=[mtk, rtk] + (lw if m == 0 else []))
                t2 = vtt(lt[:, 0:CH], lt[:, 0:CH], r1b, ALU.mult, waits=[t1])
                t3 = vstt(acc_sl, acc_sl, SM_FCB[:, m:m + 1], lt[:, 0:CH],
                          ALU.add, ALU.add, waits=[t2])
                e = vtt(x_sl, x_sl, acc_sl, ALU.add, waits=[t3])
            ld(e)
            mubd(e)
            r1bd(e)
        return ("v", b.cnt["v"])

    def l1_wout(ph, ci, y_toks):
        sc0 = ci * CH
        wout = IN["l1a_wout" if ph == 0 else "l1b_wout"]
        for mi in range(4):
            sbw, sdw, suse = smgr.load(wout[:, mi * 128:(mi + 1) * 128]
                                       .rearrange("(kb p) j -> p kb j", p=128), 1024)
            tile, pw, pdone = pp.get()
            tk = None
            for kb in range(8):
                tk = mm(tile[:, 0:CH], sbw[:, kb * 128:(kb + 1) * 128],
                        YY[kb][:, 0:CH], kb == 0, kb == 7,
                        ([sdw] + y_toks + pw) if kb == 0 else ())
            suse(tk)
            tmp, tw, tdone = sp.get()
            e0 = act(tmp[:, 0:CH], tile[:, 0:CH], AF.Copy, waits=[tk] + tw)
            pdone(e0)
            acc_sl = accv[mi][:, sc0:sc0 + CH]
            stg, sw2, sd2 = sp.get()
            if ph == 0:
                t1 = vts1(acc_sl, tmp[:, 0:CH], pe_t[:, 0:1], ALU.mult, waits=[e0])
                t2 = vts1(stg[:, 0:CH], tmp[:, 0:CH], po_t[:, 0:1], ALU.mult,
                          waits=[e0] + sw2)
                dk = b.dma("gpsimd", AR["ar4"][0][:, mi * 2048 + sc0:mi * 2048 + sc0 + CH],
                           stg[:, 0:CH], waits=[t2], pub="d4")
            else:
                t1 = vstt(acc_sl, tmp[:, CH - 1::-1], po_t[:, 0:1], acc_sl,
                          ALU.mult, ALU.add, waits=[e0])
                t2 = vts1(stg[:, 0:CH], tmp[:, CH - 1::-1], pe_t[:, 0:1], ALU.mult,
                          waits=[e0] + sw2)
                dk = b.dma("gpsimd",
                           AR["ar4"][0][:, 8192 + mi * 2048 + sc0:8192 + mi * 2048 + sc0 + CH],
                           stg[:, 0:CH], waits=[t2], pub="d4")
            tdone(t2)
            sd2(dk)

    def l1_attn(bar):
        smtok = load_branch_smalls(1, bar)
        sttoks = stats_pass(xstr, bar + smtok)
        stok = max(sttoks, key=lambda t: t[1])
        z0 = vmemset(stash_t[:, :], 0.0, waits=last["stash"] + bar)
        z1 = vmemset(stA_t[:, :], 0.0, waits=last["state"][0] + bar)
        last["stash"] = [z0]
        last["state"][0] = [z1]
        for ci in range(NCH):
            yt = attn_chunk(1, 0, ci, stok)
            l1_wout(0, ci, yt)
        exch_tok = exchange("ar3", stA_t[:, :], 128, z1 if not last["state"][0]
                            else last["state"][0][-1], stB_t[:, :])
        last["state"][1] = [exch_tok]
        htoks = []
        for db in range(8):
            htoks.append(vcopy(xh_t[:, db * 3:db * 3 + 3],
                               xstr[db][:, T_HALF - 3:T_HALF], waits=bar))
        h_ex = exchange("ar2", xh_t[:, 0:24], 24, htoks[-1], xh_t[:, 0:24])
        halo_mini2(1, 1, xh_t[:, :], h_ex)
        for ci in reversed(range(NCH)):
            yt = attn_chunk(1, 1, ci, stok)
            l1_wout(1, ci, yt)
        bar2 = stage_barrier()
        cc4 = b.emit("gpsimd",
                     lambda h: h.collective_compute(
                         "AllReduce", ALU.add, replica_groups=PAIRS,
                         ins=[AR["ar4"][0].opt()], outs=[AR["ar4"][1].opt()]),
                     bar2, "cc", 1)
        for mi in range(4):
            for g in range(4):
                c0 = g * 512
                t0t, w0, d0_ = sp.get()
                dk0 = b.dma("gpsimd", t0t[:, 0:512],
                            AR["ar4"][1][:, mi * 2048 + c0:mi * 2048 + c0 + 512],
                            waits=[cc4] + w0, pub="de")
                acc_sl = accv[4 + mi][:, c0:c0 + 512]
                e1 = vts1(acc_sl, t0t[:, 0:512], pe_t[:, 0:1], ALU.mult, waits=[dk0])
                d0_(e1)
                t1t, w1, d1_ = sp.get()
                dk1 = b.dma("gpsimd", t1t[:, 0:512],
                            AR["ar4"][1][:, 8192 + mi * 2048 + c0:8192 + mi * 2048 + c0 + 512],
                            waits=[cc4] + w1, pub="de")
                e2 = vstt(acc_sl, t1t[:, 0:512], po_t[:, 0:1], acc_sl,
                          ALU.mult, ALU.add, waits=[dk1, e1])
                d1_(e2)
        for ci in range(NCH):
            sc0 = ci * CH
            mu, r1, _, rtok = ln_rows(ci * 2, 2, [stok], False, False)
            mub, mtk, mubd = bcast_row(mu, CH, [rtok], dst="mub")
            r1b, rtk, r1bd = bcast_row(r1, CH, [rtok], dst="r12b")
            e = None
            for m in range(8):
                x_sl = xstr[m][:, sc0:sc0 + CH]
                acc_sl = accv[m][:, sc0:sc0 + CH]
                lt, lw, ld_ = sp.get()
                t1 = vstt(lt[:, 0:CH], x_sl, 1.0, mub, ALU.mult, ALU.subtract,
                          waits=[mtk, rtk, ("v", b.cnt["v"])] + lw)
                t2 = vtt(lt[:, 0:CH], lt[:, 0:CH], r1b, ALU.mult, waits=[t1])
                t3 = vtt(lt[:, 0:CH], lt[:, 0:CH], acc_sl, ALU.add, waits=[t2])
                t4 = act(lt[:, 0:CH], lt[:, 0:CH], AF.Gelu, waits=[t3])
                e = vtt(x_sl, x_sl, lt[:, 0:CH], ALU.add, waits=[t4])
                ld_(e)
            mubd(e)
            r1bd(e)
        return ("v", b.cnt["v"])

    b1c_t = wv(9856, 128)

    def IN_B1_COL(m1):
        return b1c_t[:, m1:m1 + 1]

    def f1_ffn(bar):
        sttoks = stats_pass(xstr, bar)
        stok = max(sttoks, key=lambda t: t[1])
        b2buf, b2d, b2use = smgr.load(IN["f1_b2"], 1024, extra=bar, rows=8)
        lg = [b2d,
              b.dma("sync", b1c_t, IN["f1_b1"], waits=bar, pub="dw"),
              b.dma("sync", GWS[:, 0:64].rearrange("p (kb j) -> p kb j", j=8),
                    IN["f1_gw"].rearrange("(kb p) j -> p kb j", p=128),
                    waits=bar, pub="dw"),
              b.dma("sync", hrow_t[0:1, 16:24], IN["f1_gb"], waits=bar, pub="dw"),
              b.dma("sync", hrow_t[0:1, 24:32], IN["f1_rms"], waits=bar, pub="dw")]
        b2toks = []
        for m in range(8):
            ps, pw, pdone = pp.get()
            t1 = mm(ps[0:128, 0:1], b2buf[0:8, m * 128:(m + 1) * 128],
                    cs_t[0:8, 1:2], True, True, lg + pw)
            e = act(SM_B2S[:, m:m + 1], ps[:, 0:1], AF.Copy, waits=[t1])
            pdone(e)
            b2toks.append(e)
            b2use(t1)
        gbb = sm[:, 416:424]
        rmb = sm[:, 424:432]
        ps, pw, pdone = pp.get()
        t1 = mm(ps[:, 0:8], ones1_t[0:1, :], hrow_t[0:1, 16:24], True, True, lg + pw)
        e1 = act(gbb, ps[:, 0:8], AF.Copy, waits=[t1])
        pdone(e1)
        ps, pw, pdone = pp.get()
        t1 = mm(ps[:, 0:8], ones1_t[0:1, :], hrow_t[0:1, 24:32], True, True, lg + pw)
        e2 = act(rmb, ps[:, 0:8], AF.Copy, waits=[t1])
        pdone(e2)
        ng_done = []
        for g in range(16):
            c0 = g * 128
            mu, r1, _, rtok = ln_rows(g, 1, [stok], False, False)
            mub, mtk, mubd = bcast_row(mu, 128, [rtok], dst="mub")
            r1b, rtk, r1bd = bcast_row(r1, 128, [rtok], dst="r12b")
            ps, pw, pdone = pp.get()
            tk = None
            lts = []
            for k in range(8):
                lt, lw, ld_ = sp.get()
                t1 = vstt(lt[:, 0:128], xstr[k][:, c0:c0 + 128], 1.0, mub,
                          ALU.mult, ALU.subtract, waits=[mtk, rtk] + lw)
                t2 = vtt(lt[:, 0:128], lt[:, 0:128], r1b, ALU.mult, waits=[t1])
                tk = mm(ps[:, 0:8], lt[:, 0:128],
                        GWS[:, 0:64].rearrange("p (kb j) -> p kb j", j=8)[:, k, :],
                        k == 0, k == 7, ([t2] + lg + pw) if k == 0 else [t2])
                ld_(tk)
            gt, gw_, gd = sp.get()
            q0 = b.emit("vector", lambda h, o=gt[:, 0:8], p=ps[:, 0:8], bb=gbb:
                        h.tensor_tensor(o, p, bb, ALU.add), [tk, e1] + gw_, "v", 1)
            pdone(q0)
            q1 = vtt(gt[:, 8:16], gt[:, 0:8], gt[:, 0:8], ALU.mult, waits=[q0])
            q2 = vreduce(gt[:, 16:17].rearrange("p (a b) -> p a b", b=1),
                         gt[:, 8:16].rearrange("p (a b) -> p a b", b=8),
                         ALU.add, waits=[q1])
            q3 = act(gt[:, 17:18], gt[:, 16:17], AF.Sqrt, bias=F32EPS, scale=0.125,
                     waits=[q2])
            q4 = vrecip(gt[:, 18:19], gt[:, 17:18], waits=[q3])
            q5 = b.emit("vector", lambda h, o=gt[:, 0:8], i=gt[:, 0:8], s=gt[:, 18:19]:
                        h.tensor_single_scalar(o, i, s, ALU.mult), [q4], "v", 1)
            q6 = vtt(gt[:, 0:8], gt[:, 0:8], rmb, ALU.mult, waits=[q5, e2])
            q7 = vreduce(gt[:, 16:17].rearrange("p (a b) -> p a b", b=1),
                         gt[:, 0:8].rearrange("p (a b) -> p a b", b=8),
                         ALU.max, waits=[q6])
            q8 = b.emit("vector", lambda h, o=gt[:, 0:8], i=gt[:, 0:8], s=gt[:, 16:17]:
                        h.tensor_single_scalar(o, i, s, ALU.subtract), [q7], "v", 1)
            q9 = act(gt[:, 0:8], gt[:, 0:8], AF.Exp, scale=2.0, waits=[q8])
            q10 = vreduce(gt[:, 16:17].rearrange("p (a b) -> p a b", b=1),
                          gt[:, 0:8].rearrange("p (a b) -> p a b", b=8),
                          ALU.add, waits=[q9])
            q11 = vrecip(gt[:, 17:18], gt[:, 16:17], waits=[q10])
            q12 = vts1(gt[:, 18:19], gt[:, 17:18], 1e-6, ALU.add, waits=[q11])
            q13 = vrecip(gt[:, 19:20], gt[:, 18:19], waits=[q12])
            q14 = vtt(gt[:, 20:21], gt[:, 17:18], gt[:, 19:20], ALU.mult, waits=[q13])
            ps2, pw2, pdone2 = pp.get()
            tt_ = tr(ps2[0:1, 0:128], gt[:, 20:21], [q14] + pw2)
            ek = act(ROWG[0:1, c0:c0 + 128], ps2[0:1, 0:128], AF.Copy,
                     waits=[tt_] + ng_done)
            pdone2(ek)
            gd(ek)
            ng_done = ng_done + [ek]
            mubd(ek)
            r1bd(ek)
        ngtok = ng_done[-1]
        for tc in range(4):
            sc0 = tc * 512
            xftoks = []
            for half in range(2):
                c0 = sc0 + half * 256
                mu, r1, _, rtok = ln_rows(tc * 4 + half * 2, 2, [stok], False, False)
                mub, mtk, mubd = bcast_row(mu, 256, [rtok], dst="mub")
                r1b, rtk, r1bd = bcast_row(r1, 256, [rtok], dst="r12b")
                e = None
                for k in range(8):
                    dst = XF[k][:, half * 256:half * 256 + 256]
                    t1 = vstt(dst, xstr[k][:, c0:c0 + 256], 1.0, mub,
                              ALU.mult, ALU.subtract,
                              waits=[mtk, rtk] + last["xf"] if k == 0 else [mtk, rtk])
                    e = vtt(dst, dst, r1b, ALU.mult, waits=[t1])
                    xftoks.append(e)
                last["xf"] = []
                mubd(e)
                r1bd(e)
            for seg in range(64):
                htoks = []
                w2toks = []
                for j in range(2):
                    m1 = seg * 2 + j
                    sb1, sd1, su1 = smgr.load(IN["f1_w1"][:, m1 * 128:(m1 + 1) * 128]
                                              .rearrange("(kb p) j2 -> p kb j2", p=128),
                                              1024)
                    ph_, pwh, pdh = pp.get()
                    tk = None
                    for k in range(8):
                        tk = mm(ph_[:, 0:512], sb1[:, k * 128:(k + 1) * 128],
                                XF[k][:, 0:512], k == 0, k == 7,
                                ([sd1] + xftoks + pwh) if k == 0 else ())
                    su1(tk)
                    e = act(HSEG[j][:, 0:512], ph_[:, 0:512], AF.Gelu,
                            bias=IN_B1_COL(m1), waits=[tk] + last["hseg"][j])
                    last["hseg"][j] = []
                    pdh(e)
                    htoks.append(e)
                    w2toks.append(b.dma("sync", W2S[j][:, 0:1024],
                                        IN["f1_w2"][m1 * 128:(m1 + 1) * 128, :],
                                        waits=last["w2s"][j],
                                        pub="dw" if j == 0 else "dz"))
                    last["w2s"][j] = []
                for og in range(8):
                    po_, pwo, pdo = pp.get()
                    tk = None
                    for j in range(2):
                        tk = mm(po_[:, 0:512], W2S[j][:, og * 128:(og + 1) * 128],
                                HSEG[j][:, 0:512], j == 0, j == 1,
                                ([w2toks[0], w2toks[1]] + htoks + pwo) if j == 0 else ())
                        last["w2s"][j] = last["w2s"][j] + [tk]
                        last["hseg"][j] = last["hseg"][j] + [tk]
                    acc_sl = accv[og][:, sc0:sc0 + 512]
                    if seg == 0:
                        e = act(acc_sl, po_[:, 0:512], AF.Copy, waits=[tk])
                    else:
                        e = vstt(acc_sl, po_[:, 0:512], 1.0, acc_sl, ALU.mult, ALU.add,
                                 waits=[tk])
                    pdo(e)
                    last["acc1"][og] = [e]
            ps, pw, pdone = pp.get()
            t1 = mm(ps[:, 0:512], ones1_t[0:1, :], ROWG[0:1, sc0:sc0 + 512], True, True,
                    [ngtok] + pw)
            en = act(NGB[:, 0:512], ps[:, 0:512], AF.Copy, waits=[t1] + last["ngb"])
            pdone(en)
            ftoks = []
            for m in range(8):
                x_sl = xstr[m][:, sc0:sc0 + 512]
                acc_sl = accv[m][:, sc0:sc0 + 512]
                lt, lw, ld_ = sp.get()
                t1 = vts1(lt[:, 0:512], acc_sl, SM_B2S[:, m:m + 1], ALU.add,
                          waits=last["acc1"][m] + b2toks + lw)
                t2 = vtt(lt[:, 0:512], lt[:, 0:512], NGB[:, 0:512], ALU.mult,
                         waits=[t1, en])
                e = vtt(x_sl, x_sl, lt[:, 0:512], ALU.add, waits=[t2])
                ld_(e)
                ftoks.append(e)
            last["ngb"] = ftoks
            last["xf"] = ftoks
        return ("v", b.cnt["v"])

    def store_out(final_tok):
        st_last = [[], []]
        for tt in range(16):
            tmt = TMT if tt % 2 == 0 else QAT
            wts = st_last[tt % 2]
            st_last[tt % 2] = []
            etk = []
            for db in range(8):
                ps, pw, pdone = pp.get()
                ttk = tr(ps[:, 0:128], xstr[db][:, tt * 128:(tt + 1) * 128],
                         [final_tok] + pw + (wts if db == 0 else []))
                ek = act(tmt[:, db * 128:(db + 1) * 128], ps[:, 0:128], AF.Copy,
                         waits=[ttk] + (wts if db == 0 else []))
                pdone(ek)
                etk.append(ek)
            dtok = b.dma("gpsimd", yout[tt * 128:(tt + 1) * 128, :], tmt[:, 0:1024],
                         waits=etk, pub="do")
            st_last[tt % 2] = [dtok]
        b.barrier_wait("gpsimd", [("do", b.cnt["do"])])

    if STAGES >= 2:
        stage_barrier()
        wq_pass()
        stage_barrier()
        if DBG is not None and DBG[3] == "wq":
            b.dma("gpsimd", yout[0:1024, :], qgw[0:1024, 1024:2048], pub="do")
            b.dma("gpsimd", yout[1024:2048, :], qgw[0:1024, 1792:2816], pub="do")
            b.barrier_wait("gpsimd", [("do", b.cnt["do"])])
    ftok = l0_attn()
    if STAGES >= 2:
        bar = stage_barrier()
        ftok = f0_ffn(bar)
    if STAGES >= 3:
        bar = stage_barrier()
        ftok = l1_attn(bar)
    if STAGES >= 4:
        bar = stage_barrier()
        ftok = f1_ffn(bar)
    stage_barrier()
    if DBG is None:
        store_out(ftok)
    else:
        b.barrier_wait("gpsimd", [("do", b.cnt["do"]), ftok])

    b.realize(block)
    nc._ctx_holder = ctx
    return nc


_NC_CACHE = {}


def kernel(**inputs):
    in_maps = host_prepare(inputs)
    nc = build_nc()
    res = run_bass_kernel_spmd(nc, in_maps, core_ids=list(range(8)))
    return host_gather(res.results)


# version 07
# speedup vs baseline: 1.3892x; 1.0447x over previous
import contextlib
import numpy as np

import concourse.bass as bass
import concourse.mybir as mybir
from concourse.bass_utils import run_bass_kernel_spmd

F32 = mybir.dt.float32
AF = mybir.ActivationFunctionType
ALU = mybir.AluOpType

T_HALF = 2048
CH = 256
NCH = T_HALF // CH
NSUB = 16
EPS = 1e-5
F32EPS = float(np.finfo(np.float32).eps)
STAGES = 4
DBG = None
PAIRS = [[0, 1], [2, 3], [4, 5], [6, 7]]

ENGS = ("tensor", "vector", "scalar", "gpsimd", "sync")
SEM_OWNER = {"t": "tensor", "v": "vector", "a": "scalar", "g": "gpsimd"}
SEM_NAMES = ("t", "v", "a", "g", "cc", "di", "dw", "dz", "dx0", "dx1", "do", "de", "d4")
DMA_SEMS = ("di", "dw", "dz", "dx0", "dx1", "do", "de", "d4")


class Bld:
    def __init__(self, nc, sems):
        self.nc = nc
        self.sems = sems
        self.cnt = {k: 0 for k in sems}
        self.stream = {e: [] for e in ENGS}

    def emit(self, eng, fn, waits=(), pub=None, amt=1):
        toks = []
        for tk in waits:
            if tk is None:
                continue
            s, v = tk
            if s in DMA_SEMS:
                v = self.cnt[s]
            toks.append((s, v))
        tok = None
        if pub is not None:
            self.cnt[pub] += amt
            tok = (pub, self.cnt[pub])
        self.stream[eng].append((toks, fn, pub, amt))
        return tok

    def dma(self, eng, out, in_, waits=(), pub="di"):
        waits = list(waits) + [(pub, self.cnt[pub])]
        return self.emit(eng, lambda h, o=out, i=in_: h.dma_start(out=o, in_=i),
                         waits, pub, 16)

    def barrier_wait(self, eng, waits):
        self.emit(eng, lambda h: None, waits)

    def realize(self, block):
        for eng in ENGS:
            items = self.stream[eng]
            sems = self.sems

            def body(h, items=items):
                hw = {}
                for toks, fn, pub, amt in items:
                    need = {}
                    for s, v in toks:
                        if hw.get(s, 0) < v and need.get(s, 0) < v:
                            need[s] = v
                    for s, v in need.items():
                        h.wait_ge(sems[s], v)
                        hw[s] = v
                    inst = fn(h)
                    if pub is not None:
                        inst.then_inc(sems[pub], amt)

            getattr(block, eng)(body)


class Pool:
    def __init__(self, tiles):
        self.tiles = tiles
        self.rd = [None] * len(tiles)
        self.i = 0

    def get(self):
        i = self.i % len(self.tiles)
        self.i += 1
        w = self.rd[i]
        self.rd[i] = None

        def done(tok, i=i):
            self.rd[i] = tok

        return self.tiles[i], ([w] if w is not None else []), done


def _f32(a):
    return np.ascontiguousarray(a, dtype=np.float32)


def _col_pack(v):
    return _f32(np.transpose(np.asarray(v).reshape(-1, 128)))


def _colmat_pack(m, inner):
    m = np.asarray(m)
    nblk = m.shape[0] // 128
    return _f32(np.transpose(m.reshape(nblk, 128, inner), (1, 0, 2)).reshape(128, nblk * inner))


def host_prepare(inputs):
    x = np.asarray(inputs["L0_attn_hidden_states"])

    def wT(name):
        return _f32(np.asarray(inputs[name]).T)

    def arr(name):
        return _f32(np.asarray(inputs[name]))

    shared = {
        "l0_ipw": wT("L0_attn_in_proj_w"),
        "l0_opw": wT("L0_attn_out_proj_w"),
        "l0_fcw": wT("L0_attn_fc_w"),
        "l0_fcb": _col_pack(inputs["L0_attn_fc_b"]),
        "l0_lng": _col_pack(inputs["L0_attn_ln_g"]),
        "l0_lnb": _col_pack(inputs["L0_attn_ln_b"]),
        "f0_gw": arr("L0_ffn_w_gate"),
        "f0_uw": arr("L0_ffn_w_up"),
        "f0_dw": arr("L0_ffn_w_down"),
        "l1_lng": _col_pack(inputs["L1_attn_ln_g"]),
        "l1_lnb": _col_pack(inputs["L1_attn_ln_b"]),
        "f1_w1": _f32(np.transpose(np.asarray(inputs["L1_ffn_w1"]), (1, 0, 2)).reshape(1024, 16384)),
        "f1_w2": _f32(np.asarray(inputs["L1_ffn_w2"]).reshape(16384, 1024)),
        "f1_b1": _f32(np.transpose(np.asarray(inputs["L1_ffn_b1"]).reshape(128, 128))),
        "f1_b2": arr("L1_ffn_b2"),
        "f1_gw": wT("L1_ffn_gate_w"),
        "f1_gb": _f32(np.asarray(inputs["L1_ffn_gate_b"]).reshape(1, 8)),
        "f1_rms": _f32(np.asarray(inputs["L1_ffn_rms_w"]).reshape(1, 8)),
        "c_ident": _f32(np.eye(128)),
        "c_scal": _f32(np.tile(np.array([1e-5, 1.0, 2.0, float(np.finfo(np.float32).eps),
                                         1e-12, 1.0 / 6.0, 1e-6, 0.5], dtype=np.float32),
                               (128, 1))),
        "c_ones1": _f32(np.ones((1, 128))),
        "c_w1024": _f32(np.full((128, 1), 1.0 / 1024.0)),
    }
    shared["c_oh16"] = _f32(np.eye(16))

    l0_br = {}
    for tag, sfx in (("f", ""), ("b", "_b")):
        l0_br[tag] = {
            "convw": _colmat_pack(inputs["L0_attn_conv_w" + sfx], 4),
            "convb": _col_pack(inputs["L0_attn_conv_b" + sfx]),
            "xpw": wT("L0_attn_x_proj_w" + sfx),
            "dtw": wT("L0_attn_dt_proj_w" + sfx),
            "dtb": _col_pack(inputs["L0_attn_dt_proj_b" + sfx]),
            "alog": _colmat_pack(inputs["L0_attn_A_log" + sfx], 16),
            "dp": _col_pack(inputs["L0_attn_Dp" + sfx]),
        }
    l1_br = {}
    for tag in ("m1", "m2"):
        p = "L1_attn_" + tag + "_"
        l1_br[tag] = {
            "win": arr(p + "Win"),
            "convw": _colmat_pack(inputs[p + "Wconv"], 4),
            "convb": _col_pack(inputs[p + "bconv"]),
            "xpw": arr(p + "Wx"),
            "dtw": arr(p + "Wdt"),
            "dtb": _col_pack(inputs[p + "bdt"]),
            "alog": _colmat_pack(inputs[p + "Alog"], 16),
            "dp": _col_pack(inputs[p + "Dp"]),
            "wout": arr(p + "Wout"),
        }

    in_maps = []
    for c in range(8):
        bi, par = c // 2, c % 2
        m = dict(shared)
        if par == 0:
            sl = x[bi, 0:T_HALF]
            halo = x[bi, T_HALF + 2:T_HALF - 1:-1]
        else:
            sl = x[bi, T_HALF:2 * T_HALF][::-1]
            halo = x[bi, T_HALF - 3:T_HALF]
        xin = np.zeros((2176, 1024), dtype=np.float32)
        xin[:T_HALF] = sl
        xin[T_HALF:T_HALF + 3] = halo
        m["xin"] = xin
        pe = np.full((128, 1), 1.0 if par == 0 else 0.0, dtype=np.float32)
        m["c_pe"] = pe
        m["c_po"] = _f32(1.0 - pe)
        A, B = (l0_br["f"], l0_br["b"]) if par == 0 else (l0_br["b"], l0_br["f"])
        for k, v in A.items():
            m["l0a_" + k] = v
        for k, v in B.items():
            m["l0b_" + k] = v
        A, B = (l1_br["m1"], l1_br["m2"]) if par == 0 else (l1_br["m2"], l1_br["m1"])
        for k, v in A.items():
            m["l1a_" + k] = v
        for k, v in B.items():
            m["l1b_" + k] = v
        in_maps.append(m)
    return in_maps


def host_gather(results):
    out = np.empty((4, 4096, 1024), dtype=np.float32)
    for c in range(8):
        bi, par = c // 2, c % 2
        y = results[c]["yout"]
        if par == 0:
            out[bi, 0:T_HALF] = y
        else:
            out[bi, T_HALF:2 * T_HALF] = y[::-1]
    return out


def build_nc():
    nc = bass.Bass("TRN2", target_bir_lowering=False)
    ctx = contextlib.ExitStack()

    IN = {}
    in_specs = [
        ("xin", [2176, 1024]),
        ("l0_ipw", [1024, 2048]), ("l0_opw", [1024, 1024]), ("l0_fcw", [1024, 1024]),
        ("l0_fcb", [128, 8]), ("l0_lng", [128, 8]), ("l0_lnb", [128, 8]),
        ("f0_gw", [2816, 1024]), ("f0_uw", [2816, 1024]), ("f0_dw", [1024, 2816]),
        ("l1_lng", [128, 8]), ("l1_lnb", [128, 8]),
        ("f1_w1", [1024, 16384]), ("f1_w2", [16384, 1024]),
        ("f1_b1", [128, 128]), ("f1_b2", [8, 1024]),
        ("f1_gw", [1024, 8]), ("f1_gb", [1, 8]), ("f1_rms", [1, 8]),
        ("c_ident", [128, 128]), ("c_ones1", [1, 128]), ("c_w1024", [128, 1]),
        ("c_scal", [128, 8]),
        ("c_oh16", [16, 16]), ("c_pe", [128, 1]), ("c_po", [128, 1]),
    ]
    for ph in ("a", "b"):
        in_specs += [
            ("l0%s_convw" % ph, [128, 32]), ("l0%s_convb" % ph, [128, 8]),
            ("l0%s_xpw" % ph, [1024, 96]), ("l0%s_dtw" % ph, [64, 1024]),
            ("l0%s_dtb" % ph, [128, 8]), ("l0%s_alog" % ph, [128, 128]),
            ("l0%s_dp" % ph, [128, 8]),
            ("l1%s_win" % ph, [1024, 2048]), ("l1%s_convw" % ph, [128, 32]),
            ("l1%s_convb" % ph, [128, 8]), ("l1%s_xpw" % ph, [1024, 64]),
            ("l1%s_dtw" % ph, [32, 1024]), ("l1%s_dtb" % ph, [128, 8]),
            ("l1%s_alog" % ph, [128, 128]), ("l1%s_dp" % ph, [128, 8]),
            ("l1%s_wout" % ph, [1024, 512]),
        ]
    for name, shape in in_specs:
        IN[name] = nc.dram_tensor(name, shape, F32, kind="ExternalInput").ap()
    yout = nc.dram_tensor("yout", [T_HALF, 1024], F32, kind="ExternalOutput").ap()

    qgw = nc.dram_tensor("qgw", [1024, 2816], F32).ap()
    quw = nc.dram_tensor("quw", [1024, 2816], F32).ap()
    qdw = nc.dram_tensor("qdw", [2816, 1024], F32).ap()
    AR = {}
    for nm, shape in (("ar1", [128, 256]), ("ar2", [128, 48]),
                      ("ar3", [128, 256]), ("ar4", [128, 16384])):
        AR[nm] = (nc.dram_tensor(nm + "_in", shape, F32).ap(),
                  nc.dram_tensor(nm + "_out", shape, F32).ap())

    def sbuf(name, shape, dt=F32):
        return ctx.enter_context(nc.sbuf_tensor(name, shape, dt))

    xstr_t = sbuf("xstr", [128, 16384])
    acc_t = sbuf("acc", [128, 16384])
    wsa_t = sbuf("wsa", [128, 10776])
    s512_t = [sbuf("s5_%d" % i, [128, 512]) for i in range(3)]
    scn_t = sbuf("scn", [128, 768])
    coh_t = sbuf("coh", [16, 16])
    strip_t = sbuf("strip", [128, 1024])
    stripb_t = sbuf("stripb", [128, 1024])
    scn2_t = sbuf("scn2", [128, 768])
    bcB_t = sbuf("bcB", [16, 256])
    bcC_t = sbuf("bcC", [16, 256])
    xdbl_t = sbuf("xdbl", [64, 256])
    statc_t = sbuf("statc", [128, 32])
    mub_t = sbuf("mub", [128, 256])
    r12b_t = sbuf("r12b", [128, 256])
    rowb_t = sbuf("rowb", [1, 1792])
    hrow_t = sbuf("hrow", [1, 32])
    ident_t = sbuf("ident", [128, 128])
    ones1_t = sbuf("ones1", [1, 128])
    w1024_t = sbuf("w1024", [128, 1])
    cs_t = sbuf("cscal", [128, 8])
    pe_t = sbuf("pe", [128, 1])
    po_t = sbuf("po", [128, 1])
    smalls_t = sbuf("smalls", [128, 440])
    stash_t = sbuf("stash", [128, 24])
    stA_t = sbuf("stA", [128, 128])
    stB_t = sbuf("stB", [128, 128])
    xh_t = sbuf("xhalo", [128, 24])

    psum_t = [ctx.enter_context(nc.psum_tensor("ps%d" % i, [128, 512], F32))
              for i in range(8)]
    sems = {k: ctx.enter_context(nc.semaphore("sem_" + k)) for k in SEM_NAMES}
    block = ctx.enter_context(nc.Block())
    b = Bld(nc, sems)
    pp = Pool(psum_t)
    sp = Pool([t[:, :] for t in s512_t])

    def wv(off, w):
        return wsa_t[:, off:off + w]

    HS = [wv(0, 256), wv(256, 256)]
    XS = [wv(512 + i * 259, 259) for i in range(8)]
    UU = [wv(2584 + i * 256, 256) for i in range(8)]
    DU = [wv(4632 + i * 256, 256) for i in range(8)]
    YY = [wv(6680 + i * 256, 256) for i in range(8)]
    DL = [wv(8728 + i * 256, 256) for i in range(8)]
    HTM = wv(0, 2816)
    QHTM = wv(2816, 2816)
    QHF = [wv(5632 + i * 128, 128) for i in range(22)]
    QXF = [wv(8448 + i * 128, 128) for i in range(8)]
    TMT = wv(5632, 1024)
    QAT = wv(6656, 1024)
    FS1 = wv(9472, 1024)
    FS2 = wv(10496 - 1024 + 1024, 1024) if False else None
    FS2 = wv(2816, 1024)
    XF = [wv(i * 512, 512) for i in range(8)]
    HSEG = [wv(4096 + i * 512, 512) for i in range(2)]
    W2S = [wv(5120 + i * 1024, 1024) for i in range(2)]
    ROWG = wsa_t[0:1, 7168:9216]
    NGB = wv(9216, 512)
    GWS = wv(9728, 64)

    exch_t = scn_t
    SCA = [scn_t[:, 0:256], scn2_t[:, 0:256]]
    SCB = [scn_t[:, 256:512], scn2_t[:, 256:512]]
    SCH = [scn_t[:, 512:768], scn2_t[:, 512:768]]

    xstr = [xstr_t[:, i * 2048:(i + 1) * 2048] for i in range(8)]
    accv = [acc_t[:, i * 2048:(i + 1) * 2048] for i in range(8)]

    def mm(psum_ap, lhsT, rhs, start, stop, waits=(), pub="t"):
        return b.emit("tensor",
                      lambda h, p=psum_ap, l=lhsT, r=rhs, s=start, t=stop:
                      h.matmul(p, l, r, start=s, stop=t),
                      waits, "t", 1)

    def tr(psum_ap, in_ap, waits, idn=None):
        if idn is None:
            idn = ident_t[:, :]
        return b.emit("tensor",
                      lambda h, p=psum_ap, i=in_ap, d=idn: h.transpose(p, i, d),
                      waits, "t", 1)

    def act(out, in_, func, bias=0.0, scale=1.0, waits=()):
        return b.emit("scalar",
                      lambda h, o=out, i=in_, f=func, bi=bias, sc=scale:
                      h.activation(o, i, f, bias=bi, scale=sc),
                      waits, "a", 1)

    def vcopy(out, in_, waits=()):
        return b.emit("vector", lambda h, o=out, i=in_: h.tensor_copy(o, i),
                      waits, "v", 1)

    def vtt(out, a0, a1, op, waits=()):
        return b.emit("vector", lambda h, o=out, x=a0, y=a1, p=op:
                      h.tensor_tensor(o, x, y, p), waits, "v", 1)

    def gtt(out, a0, a1, op, waits=()):
        return b.emit("gpsimd", lambda h, o=out, x=a0, y=a1, p=op:
                      h.tensor_tensor(o, x, y, p), waits, "g", 1)

    def vts1(out, in0, s1, op, waits=()):
        return b.emit("vector", lambda h, o=out, i=in0, a=s1, p=op:
                      h.tensor_single_scalar(o, i, a, p), waits, "v", 1)

    def vts2(out, in0, s1, s2, op0, op1, waits=()):
        return b.emit("vector", lambda h, o=out, i=in0, a=s1, c=s2, p0=op0, p1=op1:
                      h.tensor_scalar(o, i, a, c, p0, p1), waits, "v", 1)

    def vstt(out, in0, s, in1, op0, op1, waits=()):
        return b.emit("vector", lambda h, o=out, i=in0, a=s, j=in1, p0=op0, p1=op1:
                      h.scalar_tensor_tensor(o, i, a, j, p0, p1), waits, "v", 1)

    def vscan(out, d0, d1, init, waits=()):
        return b.emit("vector", lambda h, o=out, x=d0, y=d1, i=init:
                      h.tensor_tensor_scan(o, x, y, i, ALU.mult, ALU.add),
                      waits, "v", 1)

    def vrecip(out, in_, waits=()):
        return b.emit("vector", lambda h, o=out, i=in_: h.reciprocal(o, i),
                      waits, "v", 1)

    def vmemset(ap, val, waits=()):
        return b.emit("vector", lambda h, a=ap, v=val: h.memset(a, v),
                      waits, "v", 1)

    def vreduce(out, in_, op, absval=False, waits=()):
        return b.emit("vector",
                      lambda h, o=out, i=in_, p=op, av=absval:
                      h.tensor_reduce(o, i, mybir.AxisListType.X, p,
                                      apply_absolute_value=av),
                      waits, "v", 1)

    ld0 = []
    for src, dst in ((IN["c_ident"], ident_t[:, :]), (IN["c_ones1"], ones1_t[:, :]),
                     (IN["c_w1024"], w1024_t[:, :]), (IN["c_oh16"], coh_t[:, :]),
                     (IN["c_pe"], pe_t[:, :]), (IN["c_po"], po_t[:, :]),
                     (IN["c_scal"], cs_t[:, :])):
        ld0.append(b.dma("sync", dst, src))
    for i, v in enumerate([1e-5, 1.0, 2.0, F32EPS, 1e-12, 1.0 / 6.0, 1e-6, 0.5]):
        nc.const_aps.register(F32, float(v), cs_t[:, i:i + 1])
    sm = smalls_t

    def lsm(c0, src, w):
        return b.dma("sync", sm[:, c0:c0 + w], src)

    ld0 += [lsm(0, IN["l0_fcb"], 8), lsm(8, IN["l0_lng"], 8), lsm(16, IN["l0_lnb"], 8),
            lsm(24, IN["l1_lng"], 8), lsm(32, IN["l1_lnb"], 8)]
    SM_FCB = sm[:, 0:8]
    SM_LNG = {0: sm[:, 8:16], 1: sm[:, 24:32]}
    SM_LNB = {0: sm[:, 16:24], 1: sm[:, 32:40]}
    SM_B2S = sm[:, 40:48]

    def SM_CONVB(ph):
        return sm[:, 48 + 24 * ph:56 + 24 * ph]

    def SM_DTB(ph):
        return sm[:, 56 + 24 * ph:64 + 24 * ph]

    def SM_DP(ph):
        return sm[:, 64 + 24 * ph:72 + 24 * ph]

    def SM_CONVW(ph):
        return sm[:, 96 + 32 * ph:128 + 32 * ph]

    def SM_AMAT(ph):
        return sm[:, 160 + 128 * ph:288 + 128 * ph]

    def load_branch_smalls(layer, deps):
        pre = "l0" if layer == 0 else "l1"
        toks = list(deps)
        for ph, pch in ((0, "a"), (1, "b")):
            toks.append(lsm(48 + 24 * ph, IN[pre + pch + "_convb"], 8))
            toks.append(lsm(56 + 24 * ph, IN[pre + pch + "_dtb"], 8))
            toks.append(lsm(64 + 24 * ph, IN[pre + pch + "_dp"], 8))
            toks.append(lsm(96 + 32 * ph, IN[pre + pch + "_convw"], 32))
            toks.append(lsm(160 + 128 * ph, IN[pre + pch + "_alog"], 128))
        out = []
        for ph in (0, 1):
            a0 = 160 + 128 * ph
            t1 = act(sm[:, a0:a0 + 128], sm[:, a0:a0 + 128], AF.Exp, waits=toks)
            out.append(vts1(sm[:, a0:a0 + 128], sm[:, a0:a0 + 128], -1.0, ALU.mult,
                            waits=[t1]))
        return out

    xload_toks = []
    xld_last = [[], []]
    for tt in range(17):
        tmt = TMT if tt % 2 == 0 else QAT
        wts = xld_last[tt % 2]
        xld_last[tt % 2] = []
        dtok = b.dma("sync", tmt[:, 0:1024], IN["xin"][tt * 128:(tt + 1) * 128, :],
                     waits=wts + (ld0 if tt == 0 else []), pub="dx0" if tt % 2 == 0 else "dx1")
        last = None
        for db in range(8):
            ps, pw, pdone = pp.get()
            ttk = tr(ps[:, 0:128], tmt[:, db * 128:(db + 1) * 128], [dtok] + pw)
            if tt < 16:
                etk = act(xstr[db][:, tt * 128:(tt + 1) * 128], ps[:, 0:128], AF.Copy,
                          waits=[ttk])
            else:
                etk = act(xh_t[:, db * 3:db * 3 + 3], ps[:, 0:3], AF.Copy, waits=[ttk])
            pdone(etk)
            last = etk
        xld_last[tt % 2] = [last]
        xload_toks.append(last)
    XTOK = ("a", b.cnt["a"])

    def stats_pass(src_tiles, deps):
        sq1, w1, d1 = sp.get()
        sq2, w2, d2 = sp.get()
        toks = []
        prev_q = w1 + w2
        for s in range(NSUB):
            ps, pw, pdone = pp.get()
            mtk = None
            sqtoks = []
            for db in range(8):
                src = src_tiles[db][:, s * 128:(s + 1) * 128]
                w0 = list(deps) + (pw if db == 0 else [])
                mtk = mm(ps[:, 0:1], src, w1024_t[:, 0:1], db == 0, db == 7,
                         w0, "t" if db == 7 else None)
                sqt = (sq1 if db < 4 else sq2)[:, (db % 4) * 128:(db % 4 + 1) * 128]
                sqtoks.append(act(sqt, src, AF.Square, waits=w0 + prev_q))
            e1 = act(statc_t[:, s:s + 1], ps[:, 0:1], AF.Copy, waits=[mtk])
            pdone(e1)
            ps2, pw2, pdone2 = pp.get()
            qtk = None
            for db in range(8):
                sqt = (sq1 if db < 4 else sq2)[:, (db % 4) * 128:(db % 4 + 1) * 128]
                qtk = mm(ps2[:, 0:1], sqt, w1024_t[:, 0:1], db == 0, db == 7,
                         [sqtoks[db]] + (pw2 if db == 0 else []),
                         "t" if db == 7 else None)
            prev_q = [qtk]
            e2 = act(statc_t[:, 16 + s:16 + s + 1], ps2[:, 0:1], AF.Copy, waits=[qtk])
            pdone2(e2)
            toks += [e1, e2]
        d1(prev_q[0])
        d2(prev_q[0])
        return toks

    def ln_rows(s0, ns, deps, rev, want_r2):
        w = ns * 128
        tts = []
        for j, stat in ((0, 0), (1, 16)):
            for s in range(ns):
                ps, pw, pdone = pp.get()
                tk = tr(ps[0:1, 0:128], statc_t[:, stat + s0 + s:stat + s0 + s + 1],
                        list(deps) + pw)
                ek = act(rowb_t[0:1, j * 256 + s * 128:j * 256 + (s + 1) * 128],
                         ps[0:1, 0:128], AF.Copy, waits=[tk])
                pdone(ek)
                tts.append(ek)
        mu = rowb_t[0:1, 0:w]
        r1 = rowb_t[0:1, 256:256 + w]
        r12 = rowb_t[0:1, 512:512 + w]
        tmp = rowb_t[0:1, 768:768 + w]
        t1 = vtt(tmp, mu, mu, ALU.mult, waits=tts)
        t2 = vtt(r1, r1, tmp, ALU.subtract, waits=[t1])
        tok = t2
        if want_r2:
            t3 = vcopy(tmp, r1, waits=[t2])
        t4 = act(r1, r1, AF.Sqrt, bias=EPS, waits=[tok if not want_r2 else t3])
        t5 = vrecip(r1, r1, waits=[t4])
        tok = t5
        if want_r2:
            t6 = vtt(r12, r1, r1, ALU.mult, waits=[t5])
            t7 = vtt(r12, r12, tmp, ALU.mult, waits=[t6])
            t8 = act(r12, r12, AF.Sqrt, bias=EPS, waits=[t7])
            t9 = vrecip(r12, r12, waits=[t8])
            t10 = vtt(r12, r12, r1, ALU.mult, waits=[t9])
            tok = t10
        if not rev:
            return mu, r1, (r12 if want_r2 else None), tok
        mur = rowb_t[0:1, 1024:1024 + w]
        r1r = rowb_t[0:1, 1280:1280 + w]
        c1 = vcopy(mur, mu[0:1, w - 1::-1], waits=[tok])
        c2 = vcopy(r1r, r1[0:1, w - 1::-1], waits=[c1])
        tok = c2
        if want_r2:
            r12r = rowb_t[0:1, 1536:1536 + w]
            tok = vcopy(r12r, r12[0:1, w - 1::-1], waits=[c2])
            return mur, r1r, r12r, tok
        return mur, r1r, None, tok

    bc_last = {"mub": [], "r12b": []}

    def bcast_row(row_ap, w, deps, dst=None):
        ps, pw, pdone = pp.get()
        t1 = mm(ps[:, 0:w], ones1_t[0:1, :], row_ap, True, True, list(deps) + pw, "t")
        if dst is None:
            tile, tw, tdone = sp.get()
            t2 = act(tile[:, 0:w], ps[:, 0:w], AF.Copy, waits=[t1] + tw)
            pdone(t2)
            return tile[:, 0:w], t2, tdone
        tile = {"mub": mub_t, "r12b": r12b_t}[dst]
        t2 = act(tile[:, 0:w], ps[:, 0:w], AF.Copy, waits=[t1] + bc_last[dst])
        pdone(t2)

        def tdone(tok, dst=dst):
            bc_last[dst] = [tok]

        return tile[:, 0:w], t2, tdone

    class StripMgr:
        def __init__(self):
            self.bufs = [strip_t, stripb_t]
            self.cons = [[], []]
            self.i = 0

        def load(self, src_ap, cols, extra=(), eng="sync", rows=128):
            i = self.i % 2
            self.i += 1
            buf = self.bufs[i]
            dst = buf[0:rows, 0:cols]
            if src_ap.ndim == 3:
                dst = dst.rearrange("p (kb j) -> p kb j", j=src_ap.shape[2])
            tok = b.dma(eng, dst, src_ap,
                        waits=list(extra) + self.cons[i],
                        pub="dw" if i == 0 else "dz")
            self.cons[i] = []

            def use(t, i=i):
                self.cons[i].append(t)

            return buf, tok, use

    smgr = StripMgr()

    def strip_dma(buf, src_ap, deps):
        return b.dma("sync", buf, src_ap, waits=deps, pub="dw")

    last = {"strip": [], "bcB": [], "bcC": [], "xdbl": [],
            "XS": [[] for _ in range(8)], "UU": [[] for _ in range(8)],
            "DU": [[] for _ in range(8)], "YY": [[] for _ in range(8)],
            "DL": [[] for _ in range(8)], "HS": [[], []], "stash": [],
            "exch": [], "state": [[], []], "sca": [[], []], "scb": [[], []], "sch": [[], []], "htm": [], "fs2": [], "fs2w": [],
            "scn": [], "qxf": [], "qhf": [], "tmrd": [], "xf": [],
            "hseg": [[], []], "w2s": [[], []], "ngb": [],
            "acc1": {i: [] for i in range(8)}}

    def hs_build(layer, out_ap, src_ap, mub, r12b, row_tok, w, deps):
        t1 = vstt(out_ap, src_ap, 1.0, mub, ALU.mult, ALU.subtract,
                  waits=list(deps) + [row_tok])
        t2 = vtt(out_ap, out_ap, r12b, ALU.mult, waits=[t1])
        t3 = vts2(out_ap, out_ap, SM_LNG[layer][:, 0:1].broadcast_to([128, 1]),
                  SM_LNB[layer][:, 0:1], ALU.mult, ALU.add, waits=[t2])
        return t3

    def in_sweep(layer, ph, colrange, w, rev, mub, r12b, row_tok, zpart, out_cb):
        ipw = IN["l0_ipw"] if layer == 0 else IN["l1a_win" if ph == 0 else "l1b_win"]
        c0 = 1024 if zpart else 0
        banks = [pp.get() for _ in range(8)]
        lnga = SM_LNG[layer]
        lnba = SM_LNB[layer]
        mmtoks = [None] * 8
        for k in range(8):
            sbuf_w, sd, suse = smgr.load(ipw[k * 128:(k + 1) * 128, c0:c0 + 1024], 1024)
            hs = HS[k % 2]
            src = xstr[k][:, colrange[0]:colrange[0] + w]
            if rev:
                src = src[:, w - 1::-1]
            hw = list(last["HS"][k % 2])
            t1 = vstt(hs[:, 0:w], src, 1.0, mub, ALU.mult, ALU.subtract,
                      waits=hw + [row_tok])
            t2 = vtt(hs[:, 0:w], hs[:, 0:w], r12b, ALU.mult, waits=[t1])
            t3 = vts2(hs[:, 0:w], hs[:, 0:w], lnga[:, k:k + 1], lnba[:, k:k + 1],
                      ALU.mult, ALU.add, waits=[t2])
            if DBG is not None and DBG[3] == "hs2" and zpart and k < 2:
                t3 = b.dma("gpsimd", yout[k * 128:(k + 1) * 128, 0:w], hs[:, 0:w],
                           waits=[t3], pub="do")
            for m in range(8):
                tile, pw, pdone = banks[m]
                wts = [t3, sd] + (pw if k == 0 else [])
                tok = mm(tile[:, 0:w], sbuf_w[:, m * 128:(m + 1) * 128],
                         hs[:, 0:w], k == 0, k == 7, wts)
                if k == 7:
                    mmtoks[m] = tok
                if m == 7:
                    last["HS"][k % 2] = [tok]
                    suse(tok)
        for m in range(8):
            tile, pw, pdone = banks[m]
            out_cb(m, tile, mmtoks[m], pdone)

    def attn_chunk(layer, ph, ci, stok):
        rev = ph == 1
        sc0 = ci * CH
        dtk = 64 if layer == 0 else 32
        xw = 96 if layer == 0 else 64
        st_t = stA_t if ph == 0 else stB_t
        mu, r1, r12, rtok = ln_rows(ci * 2, 2, [stok], rev, True)
        mub, mtk, mubd = bcast_row(mu, CH, [rtok], dst="mub")
        r12b, rtk, r12bd = bcast_row(r12, CH, [rtok], dst="r12b")
        row_tok = rtk if rtk[1] > mtk[1] else mtk

        def xcb(m, tile, mmtok, pdone):
            e = act(XS[m][:, 3:3 + CH], tile[:, 0:CH], AF.Copy,
                    waits=[mmtok] + last["XS"][m])
            last["XS"][m] = []
            pdone(e)
            h1 = vcopy(XS[m][:, 0:3], stash_t[:, m * 3:m * 3 + 3],
                       waits=last["stash"] + [e])
            h2 = vcopy(stash_t[:, m * 3:m * 3 + 3], XS[m][:, CH:CH + 3], waits=[e, h1])
            cw = SM_CONVW(ph)
            c = vts1(UU[m][:, 0:CH], XS[m][:, 0:CH], cw[:, m * 4:m * 4 + 1], ALU.mult,
                     waits=[h1] + last["UU"][m])
            last["UU"][m] = []
            for kk in range(1, 4):
                c = vstt(UU[m][:, 0:CH], XS[m][:, kk:kk + CH], cw[:, m * 4 + kk:m * 4 + kk + 1],
                         UU[m][:, 0:CH], ALU.mult, ALU.add, waits=[c])
            s = act(UU[m][:, 0:CH], UU[m][:, 0:CH], AF.Silu,
                    bias=SM_CONVB(ph)[:, m:m + 1], waits=[c])
            last["XS"][m] = [c]
            last["stash"] = last["stash"] + [h2]
            last["UU"][m] = [s]

        in_sweep(layer, ph, (sc0,), CH, rev, mub, r12b, row_tok, False, xcb)
        u_toks = [last["UU"][m][0] for m in range(8)]

        xpw = IN[("l0" if layer == 0 else "l1") + ("a" if ph == 0 else "b") + "_xpw"]
        sbw, sd, suse = smgr.load(xpw.rearrange("(kb p) j -> p kb j", p=128), 8 * xw)
        psd, pwd, pdd = pp.get()
        psb, pwb, pdb = pp.get()
        psc, pwc, pdc = pp.get()
        tD = tB = tC = None
        for kb in range(8):
            base = kb * xw
            wts = [sd] + u_toks
            tD = mm(psd[0:dtk, 0:CH], sbw[:, base:base + dtk], UU[kb][:, 0:CH],
                    kb == 0, kb == 7, wts + (pwd if kb == 0 else []))
            tB = mm(psb[0:16, 0:CH], sbw[:, base + dtk:base + dtk + 16],
                    UU[kb][:, 0:CH], kb == 0, kb == 7, pwb if kb == 0 else ())
            tC = mm(psc[0:16, 0:CH], sbw[:, base + dtk + 16:base + dtk + 32],
                    UU[kb][:, 0:CH], kb == 0, kb == 7, pwc if kb == 0 else ())
        suse(tD)
        suse(tB)
        suse(tC)
        eD = act(xdbl_t[0:dtk, 0:CH], psd[0:dtk, 0:CH], AF.Copy,
                 waits=[tD] + last["xdbl"])
        last["xdbl"] = []
        eB = act(bcB_t[0:16, 0:CH], psb[0:16, 0:CH], AF.Copy, waits=[tB] + last["bcB"])
        last["bcB"] = []
        eC = act(bcC_t[0:16, 0:CH], psc[0:16, 0:CH], AF.Copy, waits=[tC] + last["bcC"])
        last["bcC"] = []
        pdd(eD)
        pdb(eB)
        pdc(eC)

        dtw = IN[("l0" if layer == 0 else "l1") + ("a" if ph == 0 else "b") + "_dtw"]
        sbw2, sd2, suse2 = smgr.load(dtw, 1024, rows=dtk)
        dtmm = []
        for m in range(8):
            tile, pw, pdone = pp.get()
            tmk = mm(tile[0:128, 0:CH], sbw2[0:dtk, m * 128:(m + 1) * 128],
                     xdbl_t[0:dtk, 0:CH], True, True, [sd2, eD] + pw, "t")
            tmp, tw, tdone = sp.get()
            a1 = act(tmp[:, 0:CH], tile[:, 0:CH], AF.Exp,
                     bias=SM_DTB(ph)[:, m:m + 1], waits=[tmk] + tw)
            pdone(a1)
            a2 = act(DL[m][:, 0:CH], tmp[:, 0:CH], AF.Ln, bias=1.0,
                     waits=[a1] + last["DL"][m])
            last["DL"][m] = []
            tdone(a2)
            dtmm.append(tmk)
            d1 = vtt(DU[m][:, 0:CH], DL[m][:, 0:CH], UU[m][:, 0:CH], ALU.mult,
                     waits=[a2] + last["DU"][m])
            y1 = vts1(YY[m][:, 0:CH], UU[m][:, 0:CH], SM_DP(ph)[:, m:m + 1], ALU.mult,
                      waits=last["YY"][m])
            last["DU"][m] = [d1]
            last["YY"][m] = [y1]
            last["DL"][m] = [d1]
        suse2(dtmm[-1])
        last["xdbl"] = [dtmm[-1]]

        amat = SM_AMAT(ph)
        exrow = rowb_t[0:1, 768:768 + CH]
        exrow2 = rowb_t[0:1, 1024 + 768 - 256:1024 + 768] if False else None
        exlast = []
        for n in range(16):
            pB, pwB, pdB = pp.get()
            tx = mm(pB[0:1, 0:CH], coh_t[0:16, n:n + 1], bcB_t[0:16, 0:CH],
                    True, True, [eB] + pwB, "t")
            ex1 = act(exrow, pB[0:1, 0:CH], AF.Copy, waits=[tx] + exlast)
            pdB(ex1)
            pB2, pwB2, pdB2 = pp.get()
            t1 = mm(pB2[:, 0:CH], ones1_t[0:1, :], exrow, True, True, [ex1] + pwB2, "t")
            Bt, Bw, Bd = sp.get()
            e1 = act(Bt[:, 0:CH], pB2[:, 0:CH], AF.Copy, waits=[t1] + Bw)
            pdB2(e1)
            exlast = [t1]
            pC, pwC, pdC = pp.get()
            tx = mm(pC[0:1, 0:CH], coh_t[0:16, n:n + 1], bcC_t[0:16, 0:CH],
                    True, True, [eC] + pwC, "t")
            ex2 = act(exrow, pC[0:1, 0:CH], AF.Copy, waits=[tx] + exlast)
            pdC(ex2)
            pC2, pwC2, pdC2 = pp.get()
            t2 = mm(pC2[:, 0:CH], ones1_t[0:1, :], exrow, True, True, [ex2] + pwC2, "t")
            Ct, Cw, Cd = sp.get()
            e2 = act(Ct[:, 0:CH], pC2[:, 0:CH], AF.Copy, waits=[t2] + Cw)
            pdC2(e2)
            exlast = [t2]
            lb = lc = None
            for m in range(8):
                col = m * 16 + n
                par = (n * 8 + m) % 2
                sca, scb, sch = SCA[par], SCB[par], SCH[par]
                ea = act(sca, DL[m][:, 0:CH], AF.Exp, scale=amat[:, col:col + 1],
                         waits=last["DL"][m] + last["sca"][par])
                bb = gtt(scb, DU[m][:, 0:CH], Bt[:, 0:CH], ALU.mult,
                         waits=[e1] + last["DU"][m] + last["scb"][par])
                sc = vscan(sch, sca, scb, st_t[:, col:col + 1],
                           waits=[ea, bb] + last["state"][ph] + last["sch"][par])
                last["state"][ph] = []
                stc = vcopy(st_t[:, col:col + 1], sch[:, CH - 1:CH], waits=[sc])
                tm2 = vtt(sca, sch, Ct[:, 0:CH], ALU.mult, waits=[sc, e2])
                ya = vtt(YY[m][:, 0:CH], YY[m][:, 0:CH], sca, ALU.add,
                         waits=[tm2] + last["YY"][m])
                last["YY"][m] = [ya]
                last["DL"][m] = [ea]
                last["DU"][m] = [bb]
                last["sca"][par] = [ya]
                last["scb"][par] = [sc]
                last["sch"][par] = [tm2, stc]
                lb, lc = ya, stc
            Bd(lb)
            Cd(lb)
            last["state"][ph].append(lc)

        if DBG is not None and DBG[3] == "y0" and layer == DBG[0] and ph == DBG[1] and ci == DBG[2]:
            for m in range(8):
                dt_ = b.dma("gpsimd", yout[m * 128:(m + 1) * 128, 0:CH], YY[m][:, 0:CH],
                            waits=last["YY"][m], pub="do")
                last["YY"][m] = last["YY"][m] + [dt_]
            b.barrier_wait("gpsimd", [("do", b.cnt["do"])])

        def zcb(m, tile, mmtok, pdone):
            szt, zw, zd = sp.get()
            s = act(szt[:, 0:CH], tile[:, 0:CH], AF.Silu, waits=[mmtok] + zw)
            pdone(s)
            if DBG is not None and DBG[3] in ("sz", "szx") and (layer, ph, ci) == DBG[:3]:
                s = b.dma("gpsimd", yout[m * 128:(m + 1) * 128, 0:CH], szt[:, 0:CH],
                          waits=[s], pub="do")
            g = vtt(YY[m][:, 0:CH], YY[m][:, 0:CH], szt[:, 0:CH], ALU.mult,
                    waits=[s] + last["YY"][m])
            zd(g)
            last["YY"][m] = [g]

        zp_flag = not (DBG is not None and DBG[3] == "szx")
        in_sweep(layer, ph, (sc0,), CH, rev, mub, r12b, row_tok, zp_flag, zcb)
        y_toks = [last["YY"][m][0] for m in range(8)]
        mubd(y_toks[7])
        r12bd(y_toks[7])

        if DBG is not None and DBG[3] in ("u", "dl", "du", "y", "xs") and layer == DBG[0] and ph == DBG[1] and ci == DBG[2]:
            srcs = {"u": UU, "dl": DL, "du": DU, "y": YY, "xs": None}[DBG[3]]
            for m in range(8):
                if DBG[3] == "xs":
                    ap = XS[m][:, 3:3 + CH]
                else:
                    ap = srcs[m][:, 0:CH]
                dt_ = b.dma("gpsimd", yout[m * 128:(m + 1) * 128, 0:CH], ap,
                            waits=[y_toks[m]], pub="do")
                key = "UU" if DBG[3] == "u" else ("DL" if DBG[3] == "dl" else
                      ("DU" if DBG[3] == "du" else ("YY" if DBG[3] == "y" else "XS")))
                last[key][m] = last[key][m] + [dt_]
            b.barrier_wait("gpsimd", [("do", b.cnt["do"])])
        return y_toks

    def l0_outproj(ph, ci, y_toks):
        sc0 = ci * CH
        opm = []
        for m in range(8):
            sbw, sdw, suse = smgr.load(IN["l0_opw"][:, m * 128:(m + 1) * 128]
                                       .rearrange("(kb p) j -> p kb j", p=128), 1024)
            tile, pw, pdone = pp.get()
            tk = None
            for kb in range(8):
                tk = mm(tile[:, 0:CH], sbw[:, kb * 128:(kb + 1) * 128],
                        YY[kb][:, 0:CH], kb == 0, kb == 7,
                        ([sdw] + y_toks + pw) if kb == 0 else ())
            e = act(UU[m][:, 0:CH], tile[:, 0:CH], AF.Copy, waits=[tk] + last["UU"][m])
            pdone(e)
            suse(tk)
            last["UU"][m] = [e]
            opm.append(e)
        for m in range(8):
            sbw, sdw, suse = smgr.load(IN["l0_fcw"][:, m * 128:(m + 1) * 128]
                                       .rearrange("(kb p) j -> p kb j", p=128), 1024)
            tile, pw, pdone = pp.get()
            tk = None
            for kb in range(8):
                tk = mm(tile[:, 0:CH], sbw[:, kb * 128:(kb + 1) * 128],
                        UU[kb][:, 0:CH], kb == 0, kb == 7,
                        ([sdw] + opm + pw) if kb == 0 else ())
            acc_sl = accv[m][:, sc0:sc0 + CH]
            if ph == 0:
                e = act(acc_sl, tile[:, 0:CH], AF.Copy, waits=[tk])
                pdone(e)
            else:
                tmp, tw, tdone = sp.get()
                e0 = act(tmp[:, 0:CH], tile[:, 0:CH], AF.Copy, waits=[tk] + tw)
                pdone(e0)
                e = vtt(acc_sl, acc_sl, tmp[:, CH - 1::-1], ALU.add, waits=[e0])
                tdone(e)
            suse(tk)
            last["UU"][m].append(tk)

    def halo_mini(layer, src24, src_tok):
        ps, pw, pdone = pp.get()
        mt = None
        sq, sw, sqd = sp.get()
        sq_toks = []
        for db in range(8):
            src = src24[:, db * 3:db * 3 + 3]
            mt = mm(ps[0:128, 0:1][0:3, 0:1] if False else ps[0:3, 0:1],
                    src, w1024_t[:, 0:1], db == 0, db == 7,
                    ([src_tok] + pw) if db == 0 else (), "t" if db == 7 else None)
            sq_toks.append(act(sq[:, db * 4:db * 4 + 3], src, AF.Square,
                               waits=[src_tok] + (sw if db == 0 else [])))
        ps2, pw2, pdone2 = pp.get()
        qt = None
        for db in range(8):
            qt = mm(ps2[0:3, 0:1], sq[:, db * 4:db * 4 + 3], w1024_t[:, 0:1],
                    db == 0, db == 7, [sq_toks[db]] + (pw2 if db == 0 else []),
                    "t" if db == 7 else None)
        sqd(qt)
        ps3, pw3, pdone3 = pp.get()
        t1 = tr(ps3[0:1, 0:3], ps[0:3, 0:1], [mt] + pw3, idn=ident_t[0:3, 0:3])
        return None

    def halo_mini2(layer, ph, src24, src_tok):
        mcol, mw, mdone = sp.get()
        ps, pw, pdone = pp.get()
        mt = None
        sq_toks = []
        sq, sw, sqd = sp.get()
        for db in range(8):
            src = src24[:, db * 3:db * 3 + 3]
            mt = mm(ps[0:3, 0:1], src, w1024_t[:, 0:1], db == 0, db == 7,
                    ([src_tok] + pw) if db == 0 else (), "t" if db == 7 else None)
            sq_toks.append(act(sq[:, db * 4:db * 4 + 3], src, AF.Square,
                               waits=[src_tok] + (sw if db == 0 else [])))
        e1 = act(mcol[0:3, 0:1], ps[0:3, 0:1], AF.Copy, waits=[mt] + mw)
        pdone(e1)
        ps2, pw2, pdone2 = pp.get()
        qt = None
        for db in range(8):
            qt = mm(ps2[0:3, 0:1], sq[:, db * 4:db * 4 + 3], w1024_t[:, 0:1],
                    db == 0, db == 7, [sq_toks[db]] + (pw2 if db == 0 else []),
                    "t" if db == 7 else None)
        e2 = act(mcol[0:3, 1:2], ps2[0:3, 0:1], AF.Copy, waits=[qt])
        pdone2(e2)
        sqd(qt)
        ps3, pw3, pdone3 = pp.get()
        t1 = tr(ps3[0:1, 0:3], mcol[0:3, 0:1], [e1] + pw3, idn=ident_t[0:3, 0:3])
        r1 = act(hrow_t[0:1, 0:3], ps3[0:1, 0:3], AF.Copy, waits=[t1])
        pdone3(r1)
        ps4, pw4, pdone4 = pp.get()
        t2 = tr(ps4[0:1, 0:3], mcol[0:3, 1:2], [e2] + pw4, idn=ident_t[0:3, 0:3])
        r2 = act(hrow_t[0:1, 4:7], ps4[0:1, 0:3], AF.Copy, waits=[t2])
        pdone4(r2)
        mdone(t2)
        mu3 = hrow_t[0:1, 0:3]
        q3 = hrow_t[0:1, 4:7]
        v3 = hrow_t[0:1, 8:11]
        r12_3 = hrow_t[0:1, 12:15]
        x1 = vtt(v3, mu3, mu3, ALU.mult, waits=[r1, r2])
        x2 = vtt(q3, q3, v3, ALU.subtract, waits=[x1])
        x3 = vcopy(v3, q3, waits=[x2])
        x4 = act(q3, q3, AF.Sqrt, bias=EPS, waits=[x3])
        x5 = vrecip(q3, q3, waits=[x4])
        x6 = vtt(r12_3, q3, q3, ALU.mult, waits=[x5])
        x7 = vtt(r12_3, r12_3, v3, ALU.mult, waits=[x6])
        x8 = act(r12_3, r12_3, AF.Sqrt, bias=EPS, waits=[x7])
        x9 = vrecip(r12_3, r12_3, waits=[x8])
        x10 = vtt(r12_3, r12_3, q3, ALU.mult, waits=[x9])
        mb3, mb3t, mb3d = bcast_row(mu3, 3, [x10], dst="mub")
        rb3, rb3t, rb3d = bcast_row(r12_3, 3, [x10], dst="r12b")
        ipw = IN["l0_ipw"] if layer == 0 else IN["l1b_win"]
        hst, hw2, hsd = sp.get()
        banks = [pp.get() for _ in range(8)]
        mmt = [None] * 8
        hst_readers = list(hw2)
        for k in range(8):
            sbw, sd, suse = smgr.load(ipw[k * 128:(k + 1) * 128, 0:1024], 1024)
            h1 = vstt(hst[:, 0:3],
                      src24[:, k * 3:k * 3 + 3], 1.0, mb3, ALU.mult, ALU.subtract,
                      waits=[mb3t, src_tok] + hst_readers)
            hst_readers = []
            h2 = vtt(hst[:, 0:3], hst[:, 0:3], rb3, ALU.mult, waits=[h1, rb3t])
            h3 = vts2(hst[:, 0:3], hst[:, 0:3], SM_LNG[layer][:, k:k + 1],
                      SM_LNB[layer][:, k:k + 1], ALU.mult, ALU.add, waits=[h2])
            for m in range(8):
                tile, pw, pdone = banks[m]
                tok = mm(tile[:, 0:3], sbw[:, m * 128:(m + 1) * 128],
                         hst[:, 0:3], k == 0, k == 7,
                         ([h3, sd] + pw) if k == 0 else [h3, sd])
                if k == 7:
                    mmt[m] = tok
                if m == 7:
                    suse(tok)
                    hst_readers = [tok]
        hsd(mmt[7])
        mb3d(mmt[7])
        rb3d(mmt[7])
        sttoks = []
        for m in range(8):
            tile, pw, pdone = banks[m]
            e = act(stash_t[:, m * 3:m * 3 + 3], tile[:, 0:3], AF.Copy,
                    waits=[mmt[m]] + last["stash"])
            pdone(e)
            sttoks.append(e)
        last["stash"] = sttoks

    def exchange(nm, payload_ap, pay_w, pay_tok, out_ap):
        ain, aout = AR[nm]
        t0 = vts1(exch_t[:, 0:pay_w], payload_ap, pe_t[:, 0:1], ALU.mult,
                  waits=[pay_tok] + last["exch"] + last["sca"][0] + last["scb"][0]
                  + last["sch"][0])
        t1 = vts1(exch_t[:, 256:256 + pay_w], payload_ap, po_t[:, 0:1], ALU.mult,
                  waits=[pay_tok] + last["scb"][0] + last["sch"][0])
        d0 = b.dma("gpsimd", ain[:, 0:pay_w], exch_t[:, 0:pay_w], waits=[t0], pub="de")
        d1 = b.dma("gpsimd", ain[:, pay_w:2 * pay_w], exch_t[:, 256:256 + pay_w],
                   waits=[t1], pub="de")
        cc = b.emit("gpsimd",
                    lambda h, i=ain, o=aout: h.collective_compute(
                        "AllReduce", ALU.add, replica_groups=PAIRS,
                        ins=[i.opt()], outs=[o.opt()]),
                    [d0, d1], "cc", 1)
        d2 = b.dma("gpsimd", exch_t[:, 0:pay_w], aout[:, 0:pay_w], waits=[cc], pub="de")
        d3 = b.dma("gpsimd", exch_t[:, 256:256 + pay_w], aout[:, pay_w:2 * pay_w],
                   waits=[cc], pub="de")
        c0 = vts1(out_ap, exch_t[:, 0:pay_w], po_t[:, 0:1], ALU.mult, waits=[d2])
        c1 = vstt(out_ap, exch_t[:, 256:256 + pay_w], pe_t[:, 0:1], out_ap,
                  ALU.mult, ALU.add, waits=[d3, c0])
        last["exch"] = [c1]
        for k2 in ("sca", "scb", "sch"):
            last[k2][0] = last[k2][0] + [c1]
        return c1

    def stage_barrier():
        toks = [(s, b.cnt[s]) for s in SEM_NAMES if b.cnt[s] > 0 and s != "cc"]
        toks += [(s, b.cnt[s]) for s in ("cc",) if b.cnt[s] > 0]
        for e in ENGS:
            b.barrier_wait(e, toks)
        return toks

    def quant128(tm_ap, w, deps):
        nb = w // 16
        t_abs = act(FS1[:, 0:w], tm_ap, AF.Abs, waits=deps)
        scl = scn_t[:, 0:nb]
        scl2 = scn_t[:, 256:256 + nb]
        rscl = scn_t[:, 512:512 + nb]
        t1 = vreduce(scl.rearrange("p (nb one) -> p nb one", one=1),
                     tm_ap.rearrange("p (nb w2) -> p nb w2", w2=16),
                     ALU.max, absval=True, waits=deps + last["scn"])
        t2 = vts2(scl2, scl, 1.0 / 6.0, 1e-12, ALU.mult, ALU.max, waits=[t1])
        t3 = vrecip(rscl, scl2, waits=[t2])
        rexp = rscl.unsqueeze(2).broadcast_to([128, nb, 16])
        sexp = scl2.unsqueeze(2).broadcast_to([128, nb, 16])
        f3 = FS1[:, 0:w].rearrange("p (nb w2) -> p nb w2", w2=16)
        t4 = vtt(f3, f3, rexp, ALU.mult, waits=[t3, t_abs])
        xn = FS1[:, 0:w]
        a0 = vts1(FS2[:, 0:w], xn, 2.5, ALU.is_gt, waits=[t4] + last["fs2"])
        a1 = vstt(FS2[:, 0:w], xn, 3.5, FS2[:, 0:w], ALU.is_gt, ALU.add, waits=[a0])
        a2 = vstt(FS2[:, 0:w], xn, 5.0, FS2[:, 0:w], ALU.is_gt, ALU.add, waits=[a1])
        a3 = vstt(FS2[:, 0:w], xn, 5.0, FS2[:, 0:w], ALU.is_gt, ALU.add, waits=[a2])
        a4 = vts1(FS2[:, 0:w], FS2[:, 0:w], 2.0, ALU.mult, waits=[a3])
        a5 = vstt(FS2[:, 0:w], xn, 0.25, FS2[:, 0:w], ALU.is_gt, ALU.add, waits=[a4])
        a6 = vstt(FS2[:, 0:w], xn, 0.75, FS2[:, 0:w], ALU.is_gt, ALU.add, waits=[a5])
        a7 = vstt(FS2[:, 0:w], xn, 1.25, FS2[:, 0:w], ALU.is_gt, ALU.add, waits=[a6])
        a8 = vstt(FS2[:, 0:w], xn, 1.75, FS2[:, 0:w], ALU.is_gt, ALU.add, waits=[a7])
        a9 = vts1(FS2[:, 0:w], FS2[:, 0:w], 0.5, ALU.mult, waits=[a8])
        sg = act(FS1[:, 0:w], tm_ap, AF.Sign, waits=[a8])
        q1 = vtt(FS2[:, 0:w], FS2[:, 0:w], FS1[:, 0:w], ALU.mult, waits=[a9, sg])
        f2 = FS2[:, 0:w].rearrange("p (nb w2) -> p nb w2", w2=16)
        q2 = vtt(f2, f2, sexp, ALU.mult, waits=[q1])
        last["scn"] = [q2]
        last["fs2"] = []
        last["tmrd"] = [q2, sg]
        return q2

    def wq_pass():
        specs = [(IN["f0_gw"], qgw, 22, 1024), (IN["f0_uw"], quw, 22, 1024),
                 (IN["f0_dw"], qdw, 8, 2816)]
        stage = FS1[:, 0:128]
        for wap, qap, nrt, C in specs:
            for r in range(nrt):
                d0 = b.dma("sync", HTM[:, 0:C], wap[r * 128:(r + 1) * 128, :],
                           waits=last["htm"], pub="dw")
                last["htm"] = []
                qtoks = []
                for s0 in range(0, C, 1024):
                    w = min(1024, C - s0)
                    qt = quant128(HTM[:, s0:s0 + w], w, [d0] + last["fs2w"])
                    for c in range(w // 128):
                        ps, pw, pdone = pp.get()
                        tt_ = tr(ps[:, 0:128], FS2[:, c * 128:(c + 1) * 128], [qt] + pw)
                        ek = act(r12b_t[:, 0:128], ps[:, 0:128], AF.Copy,
                                 waits=[tt_] + bc_last["r12b"])
                        pdone(ek)
                        bc_last["r12b"] = [ek]
                        dk = b.dma("gpsimd", qap[s0 + c * 128:s0 + (c + 1) * 128,
                                               r * 128:(r + 1) * 128],
                                   r12b_t[:, 0:128], waits=[ek], pub="d4")
                        bc_last["r12b"] = [ek, dk]
                        qtoks.append(dk)
                    last["fs2w"] = [qtoks[-1]] if qtoks else []
                last["htm"] = qtoks
        last["fs2w"] = []

    def f0_ffn(bar):
        sttoks = stats_pass(xstr, bar)
        stok = max(sttoks, key=lambda t: t[1])
        for ci in range(16):
            mu, r1, _, rtok = ln_rows(ci, 1, [stok], False, False)
            mub, mtk, mubd = bcast_row(mu, 128, [rtok], dst="mub")
            r1b, rtk, r1bd = bcast_row(r1, 128, [rtok], dst="r12b")
            rowt = rtk if rtk[1] > mtk[1] else mtk
            c0 = ci * 128
            etoks = []
            for k in range(8):
                lt, lw, ld_ = sp.get()
                t1 = vstt(lt[:, 0:128], xstr[k][:, c0:c0 + 128], 1.0, mub,
                          ALU.mult, ALU.subtract, waits=[rowt] + lw)
                t2 = vtt(lt[:, 0:128], lt[:, 0:128], r1b, ALU.mult, waits=[t1])
                ps, pw, pdone = pp.get()
                t3 = tr(ps[:, 0:128], lt[:, 0:128], [t2] + pw)
                ld_(t3)
                e = act(HTM[:, k * 128:(k + 1) * 128], ps[:, 0:128], AF.Copy,
                        waits=[t3] + last["htm"])
                last["htm"] = []
                pdone(e)
                etoks.append(e)
            qt = quant128(HTM[:, 0:1024], 1024, etoks)
            if DBG is not None and DBG[3] == "f0x" and ci == 0:
                d1 = b.dma("gpsimd", yout[0:128, 0:1024], HTM[:, 0:1024],
                           waits=etoks, pub="do")
                d2 = b.dma("gpsimd", yout[128:256, 0:1024], FS2[:, 0:1024],
                           waits=[qt], pub="do")
                last["htm"] = last["htm"] + [d1]
                last["tmrd"] = last["tmrd"] + [d1]
                last["fs2"] = last["fs2"] + [d2]
            qxtoks = []
            for k in range(8):
                ps, pw, pdone = pp.get()
                t3 = tr(ps[:, 0:128], FS2[:, k * 128:(k + 1) * 128], [qt] + pw)
                e = act(QXF[k][:, 0:128], ps[:, 0:128], AF.Copy,
                        waits=[t3] + last["qxf"])
                pdone(e)
                qxtoks.append(e)
            last["qxf"] = []
            last["fs2"] = qxtoks
            htoks = []
            for m in range(22):
                sb1, sd1, su1 = smgr.load(qgw[:, m * 128:(m + 1) * 128]
                                          .rearrange("(kb p) j -> p kb j", p=128), 1024)
                pg, pwg, pdg = pp.get()
                tg = None
                for k in range(8):
                    tg = mm(pg[:, 0:128], sb1[:, k * 128:(k + 1) * 128],
                            QXF[k][:, 0:128], k == 0, k == 7,
                            ([sd1] + qxtoks + pwg) if k == 0 else ())
                su1(tg)
                sb2, sd2, su2 = smgr.load(quw[:, m * 128:(m + 1) * 128]
                                          .rearrange("(kb p) j -> p kb j", p=128), 1024)
                pu, pwu, pdu = pp.get()
                tu = None
                for k in range(8):
                    tu = mm(pu[:, 0:128], sb2[:, k * 128:(k + 1) * 128],
                            QXF[k][:, 0:128], k == 0, k == 7,
                            ([sd2] + pwu) if k == 0 else ())
                su2(tu)
                g1, gw1, gd1 = sp.get()
                sg_ = act(g1[:, 0:128], pg[:, 0:128], AF.Silu, waits=[tg] + gw1)
                pdg(sg_)
                g2, gw2, gd2 = sp.get()
                su_ = act(g2[:, 0:128], pu[:, 0:128], AF.Copy, waits=[tu] + gw2)
                pdu(su_)
                hv = vtt(g1[:, 0:128], g1[:, 0:128], g2[:, 0:128], ALU.mult,
                         waits=[sg_, su_])
                gd2(hv)
                ps, pw, pdone = pp.get()
                t3 = tr(ps[:, 0:128], g1[:, 0:128], [hv] + pw)
                gd1(t3)
                e = act(HTM[:, m * 128:(m + 1) * 128], ps[:, 0:128], AF.Copy,
                        waits=[t3] + last["tmrd"])
                pdone(e)
                htoks.append(e)
            if DBG is not None and DBG[3] == "f0h" and ci == 0:
                for j in range(3):
                    w = 1024 if j < 2 else 768
                    dj = b.dma("gpsimd", yout[j * 128:(j + 1) * 128, 0:w],
                               HTM[:, j * 1024:j * 1024 + w], waits=htoks, pub="do")
                    htoks = htoks + [dj]
            qhtoks = []
            for s0, w in ((0, 1024), (1024, 1024), (2048, 768)):
                qt = quant128(HTM[:, s0:s0 + w], w, htoks)
                for c in range(w // 128):
                    mi = (s0 // 128) + c
                    ps, pw, pdone = pp.get()
                    t3 = tr(ps[:, 0:128], FS2[:, c * 128:(c + 1) * 128], [qt] + pw)
                    e = act(QHF[mi][:, 0:128], ps[:, 0:128], AF.Copy,
                            waits=[t3] + last["qhf"])
                    pdone(e)
                    qhtoks.append(e)
                last["fs2"] = qhtoks[-8:]
            last["qhf"] = []
            dtoks = []
            for og in range(8):
                po_, pwo, pdo = pp.get()
                to = None
                for kg0, kgn in ((0, 8), (8, 8), (16, 6)):
                    sbd, sd, sud = smgr.load(
                        qdw[kg0 * 128:(kg0 + kgn) * 128, og * 128:(og + 1) * 128]
                        .rearrange("(kb p) j -> p kb j", p=128), kgn * 128)
                    for kk in range(kgn):
                        mi = kg0 + kk
                        to = mm(po_[:, 0:128], sbd[:, kk * 128:(kk + 1) * 128],
                                QHF[mi][:, 0:128], mi == 0, mi == 21,
                                ([sd] + qhtoks + pwo) if mi == 0 else [sd])
                    sud(to)
                x_sl = xstr[og][:, c0:c0 + 128]
                e = vstt(x_sl, po_[:, 0:128], 1.0, x_sl, ALU.mult, ALU.add,
                         waits=[to])
                pdo(e)
                dtoks.append(e)
            last["qhf"] = qhtoks
            last["qxf"] = qxtoks
            last["htm"] = htoks + dtoks
            mubd(dtoks[-1])
            r1bd(dtoks[-1])
        return ("v", b.cnt["v"])

    def l0_attn():
        smtok = load_branch_smalls(0, [XTOK])
        sttoks = stats_pass(xstr, [XTOK] + smtok)
        stok = max(sttoks, key=lambda t: t[1])
        z0 = vmemset(stash_t[:, :], 0.0, waits=last["stash"])
        z1 = vmemset(stA_t[:, :], 0.0, waits=last["state"][0])
        last["stash"] = [z0]
        last["state"][0] = [z1]
        for ci in range(NCH):
            yt = attn_chunk(0, 0, ci, stok)
            l0_outproj(0, ci, yt)
        exch_tok = exchange("ar1", stA_t[:, :], 128,
                            last["state"][0][-1] if last["state"][0] else z1,
                            stB_t[:, :])
        last["state"][1] = [exch_tok]
        halo_mini2(0, 1, xh_t[:, :], XTOK)
        for ci in reversed(range(NCH)):
            yt = attn_chunk(0, 1, ci, stok)
            l0_outproj(1, ci, yt)
        for ci in range(NCH):
            sc0 = ci * CH
            mu, r1, _, rtok = ln_rows(ci * 2, 2, [stok], False, False)
            mub, mtk, mubd = bcast_row(mu, CH, [rtok], dst="mub")
            r1b, rtk, r1bd = bcast_row(r1, CH, [rtok], dst="r12b")
            lt, lw, ld = sp.get()
            e = None
            for m in range(8):
                x_sl = xstr[m][:, sc0:sc0 + CH]
                acc_sl = accv[m][:, sc0:sc0 + CH]
                t1 = vstt(lt[:, 0:CH], x_sl, 1.0, mub, ALU.mult, ALU.subtract,
                          waits=[mtk, rtk] + (lw if m == 0 else []))
                t2 = vtt(lt[:, 0:CH], lt[:, 0:CH], r1b, ALU.mult, waits=[t1])
                t3 = vstt(acc_sl, acc_sl, SM_FCB[:, m:m + 1], lt[:, 0:CH],
                          ALU.add, ALU.add, waits=[t2])
                e = vtt(x_sl, x_sl, acc_sl, ALU.add, waits=[t3])
            ld(e)
            mubd(e)
            r1bd(e)
        return ("v", b.cnt["v"])

    def l1_wout(ph, ci, y_toks):
        sc0 = ci * CH
        wout = IN["l1a_wout" if ph == 0 else "l1b_wout"]
        for mi in range(4):
            sbw, sdw, suse = smgr.load(wout[:, mi * 128:(mi + 1) * 128]
                                       .rearrange("(kb p) j -> p kb j", p=128), 1024)
            tile, pw, pdone = pp.get()
            tk = None
            for kb in range(8):
                tk = mm(tile[:, 0:CH], sbw[:, kb * 128:(kb + 1) * 128],
                        YY[kb][:, 0:CH], kb == 0, kb == 7,
                        ([sdw] + y_toks + pw) if kb == 0 else ())
            suse(tk)
            tmp, tw, tdone = sp.get()
            e0 = act(tmp[:, 0:CH], tile[:, 0:CH], AF.Copy, waits=[tk] + tw)
            pdone(e0)
            acc_sl = accv[mi][:, sc0:sc0 + CH]
            stg, sw2, sd2 = sp.get()
            if ph == 0:
                t1 = vts1(acc_sl, tmp[:, 0:CH], pe_t[:, 0:1], ALU.mult, waits=[e0])
                t2 = vts1(stg[:, 0:CH], tmp[:, 0:CH], po_t[:, 0:1], ALU.mult,
                          waits=[e0] + sw2)
                dk = b.dma("gpsimd", AR["ar4"][0][:, mi * 2048 + sc0:mi * 2048 + sc0 + CH],
                           stg[:, 0:CH], waits=[t2], pub="d4")
            else:
                t1 = vstt(acc_sl, tmp[:, CH - 1::-1], po_t[:, 0:1], acc_sl,
                          ALU.mult, ALU.add, waits=[e0])
                t2 = vts1(stg[:, 0:CH], tmp[:, CH - 1::-1], pe_t[:, 0:1], ALU.mult,
                          waits=[e0] + sw2)
                dk = b.dma("gpsimd",
                           AR["ar4"][0][:, 8192 + mi * 2048 + sc0:8192 + mi * 2048 + sc0 + CH],
                           stg[:, 0:CH], waits=[t2], pub="d4")
            tdone(t2)
            sd2(dk)

    def l1_attn(bar):
        smtok = load_branch_smalls(1, bar)
        sttoks = stats_pass(xstr, bar + smtok)
        stok = max(sttoks, key=lambda t: t[1])
        z0 = vmemset(stash_t[:, :], 0.0, waits=last["stash"] + bar)
        z1 = vmemset(stA_t[:, :], 0.0, waits=last["state"][0] + bar)
        last["stash"] = [z0]
        last["state"][0] = [z1]
        for ci in range(NCH):
            yt = attn_chunk(1, 0, ci, stok)
            l1_wout(0, ci, yt)
        exch_tok = exchange("ar3", stA_t[:, :], 128, z1 if not last["state"][0]
                            else last["state"][0][-1], stB_t[:, :])
        last["state"][1] = [exch_tok]
        htoks = []
        for db in range(8):
            htoks.append(vcopy(xh_t[:, db * 3:db * 3 + 3],
                               xstr[db][:, T_HALF - 3:T_HALF], waits=bar))
        h_ex = exchange("ar2", xh_t[:, 0:24], 24, htoks[-1], xh_t[:, 0:24])
        halo_mini2(1, 1, xh_t[:, :], h_ex)
        for ci in reversed(range(NCH)):
            yt = attn_chunk(1, 1, ci, stok)
            l1_wout(1, ci, yt)
        bar2 = stage_barrier()
        cc4 = b.emit("gpsimd",
                     lambda h: h.collective_compute(
                         "AllReduce", ALU.add, replica_groups=PAIRS,
                         ins=[AR["ar4"][0].opt()], outs=[AR["ar4"][1].opt()]),
                     bar2, "cc", 1)
        for mi in range(4):
            for g in range(4):
                c0 = g * 512
                t0t, w0, d0_ = sp.get()
                dk0 = b.dma("gpsimd", t0t[:, 0:512],
                            AR["ar4"][1][:, mi * 2048 + c0:mi * 2048 + c0 + 512],
                            waits=[cc4] + w0, pub="de")
                acc_sl = accv[4 + mi][:, c0:c0 + 512]
                e1 = vts1(acc_sl, t0t[:, 0:512], pe_t[:, 0:1], ALU.mult, waits=[dk0])
                d0_(e1)
                t1t, w1, d1_ = sp.get()
                dk1 = b.dma("gpsimd", t1t[:, 0:512],
                            AR["ar4"][1][:, 8192 + mi * 2048 + c0:8192 + mi * 2048 + c0 + 512],
                            waits=[cc4] + w1, pub="de")
                e2 = vstt(acc_sl, t1t[:, 0:512], po_t[:, 0:1], acc_sl,
                          ALU.mult, ALU.add, waits=[dk1, e1])
                d1_(e2)
        for ci in range(NCH):
            sc0 = ci * CH
            mu, r1, _, rtok = ln_rows(ci * 2, 2, [stok], False, False)
            mub, mtk, mubd = bcast_row(mu, CH, [rtok], dst="mub")
            r1b, rtk, r1bd = bcast_row(r1, CH, [rtok], dst="r12b")
            e = None
            for m in range(8):
                x_sl = xstr[m][:, sc0:sc0 + CH]
                acc_sl = accv[m][:, sc0:sc0 + CH]
                lt, lw, ld_ = sp.get()
                t1 = vstt(lt[:, 0:CH], x_sl, 1.0, mub, ALU.mult, ALU.subtract,
                          waits=[mtk, rtk, ("v", b.cnt["v"])] + lw)
                t2 = vtt(lt[:, 0:CH], lt[:, 0:CH], r1b, ALU.mult, waits=[t1])
                t3 = vtt(lt[:, 0:CH], lt[:, 0:CH], acc_sl, ALU.add, waits=[t2])
                t4 = act(lt[:, 0:CH], lt[:, 0:CH], AF.Gelu, waits=[t3])
                e = vtt(x_sl, x_sl, lt[:, 0:CH], ALU.add, waits=[t4])
                ld_(e)
            mubd(e)
            r1bd(e)
        return ("v", b.cnt["v"])

    b1c_t = wv(9856, 128)

    def IN_B1_COL(m1):
        return b1c_t[:, m1:m1 + 1]

    def f1_ffn(bar):
        sttoks = stats_pass(xstr, bar)
        stok = max(sttoks, key=lambda t: t[1])
        b2buf, b2d, b2use = smgr.load(IN["f1_b2"], 1024, extra=bar, rows=8)
        lg = [b2d,
              b.dma("sync", b1c_t, IN["f1_b1"], waits=bar, pub="dw"),
              b.dma("sync", GWS[:, 0:64].rearrange("p (kb j) -> p kb j", j=8),
                    IN["f1_gw"].rearrange("(kb p) j -> p kb j", p=128),
                    waits=bar, pub="dw"),
              b.dma("sync", hrow_t[0:1, 16:24], IN["f1_gb"], waits=bar, pub="dw"),
              b.dma("sync", hrow_t[0:1, 24:32], IN["f1_rms"], waits=bar, pub="dw")]
        b2toks = []
        for m in range(8):
            ps, pw, pdone = pp.get()
            t1 = mm(ps[0:128, 0:1], b2buf[0:8, m * 128:(m + 1) * 128],
                    cs_t[0:8, 1:2], True, True, lg + pw)
            e = act(SM_B2S[:, m:m + 1], ps[:, 0:1], AF.Copy, waits=[t1])
            pdone(e)
            b2toks.append(e)
            b2use(t1)
        gbb = sm[:, 416:424]
        rmb = sm[:, 424:432]
        ps, pw, pdone = pp.get()
        t1 = mm(ps[:, 0:8], ones1_t[0:1, :], hrow_t[0:1, 16:24], True, True, lg + pw)
        e1 = act(gbb, ps[:, 0:8], AF.Copy, waits=[t1])
        pdone(e1)
        ps, pw, pdone = pp.get()
        t1 = mm(ps[:, 0:8], ones1_t[0:1, :], hrow_t[0:1, 24:32], True, True, lg + pw)
        e2 = act(rmb, ps[:, 0:8], AF.Copy, waits=[t1])
        pdone(e2)
        ng_done = []
        for g in range(16):
            c0 = g * 128
            mu, r1, _, rtok = ln_rows(g, 1, [stok], False, False)
            mub, mtk, mubd = bcast_row(mu, 128, [rtok], dst="mub")
            r1b, rtk, r1bd = bcast_row(r1, 128, [rtok], dst="r12b")
            ps, pw, pdone = pp.get()
            tk = None
            lts = []
            for k in range(8):
                lt, lw, ld_ = sp.get()
                t1 = vstt(lt[:, 0:128], xstr[k][:, c0:c0 + 128], 1.0, mub,
                          ALU.mult, ALU.subtract, waits=[mtk, rtk] + lw)
                t2 = vtt(lt[:, 0:128], lt[:, 0:128], r1b, ALU.mult, waits=[t1])
                tk = mm(ps[:, 0:8], lt[:, 0:128],
                        GWS[:, 0:64].rearrange("p (kb j) -> p kb j", j=8)[:, k, :],
                        k == 0, k == 7, ([t2] + lg + pw) if k == 0 else [t2])
                ld_(tk)
            gt, gw_, gd = sp.get()
            q0 = b.emit("vector", lambda h, o=gt[:, 0:8], p=ps[:, 0:8], bb=gbb:
                        h.tensor_tensor(o, p, bb, ALU.add), [tk, e1] + gw_, "v", 1)
            pdone(q0)
            q1 = vtt(gt[:, 8:16], gt[:, 0:8], gt[:, 0:8], ALU.mult, waits=[q0])
            q2 = vreduce(gt[:, 16:17].rearrange("p (a b) -> p a b", b=1),
                         gt[:, 8:16].rearrange("p (a b) -> p a b", b=8),
                         ALU.add, waits=[q1])
            q3 = act(gt[:, 17:18], gt[:, 16:17], AF.Sqrt, bias=F32EPS, scale=0.125,
                     waits=[q2])
            q4 = vrecip(gt[:, 18:19], gt[:, 17:18], waits=[q3])
            q5 = b.emit("vector", lambda h, o=gt[:, 0:8], i=gt[:, 0:8], s=gt[:, 18:19]:
                        h.tensor_single_scalar(o, i, s, ALU.mult), [q4], "v", 1)
            q6 = vtt(gt[:, 0:8], gt[:, 0:8], rmb, ALU.mult, waits=[q5, e2])
            q7 = vreduce(gt[:, 16:17].rearrange("p (a b) -> p a b", b=1),
                         gt[:, 0:8].rearrange("p (a b) -> p a b", b=8),
                         ALU.max, waits=[q6])
            q8 = b.emit("vector", lambda h, o=gt[:, 0:8], i=gt[:, 0:8], s=gt[:, 16:17]:
                        h.tensor_single_scalar(o, i, s, ALU.subtract), [q7], "v", 1)
            q9 = act(gt[:, 0:8], gt[:, 0:8], AF.Exp, scale=2.0, waits=[q8])
            q10 = vreduce(gt[:, 16:17].rearrange("p (a b) -> p a b", b=1),
                          gt[:, 0:8].rearrange("p (a b) -> p a b", b=8),
                          ALU.add, waits=[q9])
            q11 = vrecip(gt[:, 17:18], gt[:, 16:17], waits=[q10])
            q12 = vts1(gt[:, 18:19], gt[:, 17:18], 1e-6, ALU.add, waits=[q11])
            q13 = vrecip(gt[:, 19:20], gt[:, 18:19], waits=[q12])
            q14 = vtt(gt[:, 20:21], gt[:, 17:18], gt[:, 19:20], ALU.mult, waits=[q13])
            ps2, pw2, pdone2 = pp.get()
            tt_ = tr(ps2[0:1, 0:128], gt[:, 20:21], [q14] + pw2)
            ek = act(ROWG[0:1, c0:c0 + 128], ps2[0:1, 0:128], AF.Copy,
                     waits=[tt_] + ng_done)
            pdone2(ek)
            gd(ek)
            ng_done = ng_done + [ek]
            mubd(ek)
            r1bd(ek)
        ngtok = ng_done[-1]
        for tc in range(4):
            sc0 = tc * 512
            xftoks = []
            for half in range(2):
                c0 = sc0 + half * 256
                mu, r1, _, rtok = ln_rows(tc * 4 + half * 2, 2, [stok], False, False)
                mub, mtk, mubd = bcast_row(mu, 256, [rtok], dst="mub")
                r1b, rtk, r1bd = bcast_row(r1, 256, [rtok], dst="r12b")
                e = None
                for k in range(8):
                    dst = XF[k][:, half * 256:half * 256 + 256]
                    t1 = vstt(dst, xstr[k][:, c0:c0 + 256], 1.0, mub,
                              ALU.mult, ALU.subtract,
                              waits=[mtk, rtk] + last["xf"] if k == 0 else [mtk, rtk])
                    e = vtt(dst, dst, r1b, ALU.mult, waits=[t1])
                    xftoks.append(e)
                last["xf"] = []
                mubd(e)
                r1bd(e)
            for seg in range(64):
                htoks = []
                w2toks = []
                for j in range(2):
                    m1 = seg * 2 + j
                    sb1, sd1, su1 = smgr.load(IN["f1_w1"][:, m1 * 128:(m1 + 1) * 128]
                                              .rearrange("(kb p) j2 -> p kb j2", p=128),
                                              1024)
                    ph_, pwh, pdh = pp.get()
                    tk = None
                    for k in range(8):
                        tk = mm(ph_[:, 0:512], sb1[:, k * 128:(k + 1) * 128],
                                XF[k][:, 0:512], k == 0, k == 7,
                                ([sd1] + xftoks + pwh) if k == 0 else ())
                    su1(tk)
                    e = act(HSEG[j][:, 0:512], ph_[:, 0:512], AF.Gelu,
                            bias=IN_B1_COL(m1), waits=[tk] + last["hseg"][j])
                    last["hseg"][j] = []
                    pdh(e)
                    htoks.append(e)
                    w2toks.append(b.dma("sync", W2S[j][:, 0:1024],
                                        IN["f1_w2"][m1 * 128:(m1 + 1) * 128, :],
                                        waits=last["w2s"][j],
                                        pub="dw" if j == 0 else "dz"))
                    last["w2s"][j] = []
                for og in range(8):
                    po_, pwo, pdo = pp.get()
                    tk = None
                    for j in range(2):
                        tk = mm(po_[:, 0:512], W2S[j][:, og * 128:(og + 1) * 128],
                                HSEG[j][:, 0:512], j == 0, j == 1,
                                ([w2toks[0], w2toks[1]] + htoks + pwo) if j == 0 else ())
                        last["w2s"][j] = last["w2s"][j] + [tk]
                        last["hseg"][j] = last["hseg"][j] + [tk]
                    acc_sl = accv[og][:, sc0:sc0 + 512]
                    if seg == 0:
                        e = act(acc_sl, po_[:, 0:512], AF.Copy, waits=[tk])
                    else:
                        e = vstt(acc_sl, po_[:, 0:512], 1.0, acc_sl, ALU.mult, ALU.add,
                                 waits=[tk])
                    pdo(e)
                    last["acc1"][og] = [e]
            ps, pw, pdone = pp.get()
            t1 = mm(ps[:, 0:512], ones1_t[0:1, :], ROWG[0:1, sc0:sc0 + 512], True, True,
                    [ngtok] + pw)
            en = act(NGB[:, 0:512], ps[:, 0:512], AF.Copy, waits=[t1] + last["ngb"])
            pdone(en)
            ftoks = []
            for m in range(8):
                x_sl = xstr[m][:, sc0:sc0 + 512]
                acc_sl = accv[m][:, sc0:sc0 + 512]
                lt, lw, ld_ = sp.get()
                t1 = vts1(lt[:, 0:512], acc_sl, SM_B2S[:, m:m + 1], ALU.add,
                          waits=last["acc1"][m] + b2toks + lw)
                t2 = vtt(lt[:, 0:512], lt[:, 0:512], NGB[:, 0:512], ALU.mult,
                         waits=[t1, en])
                e = vtt(x_sl, x_sl, lt[:, 0:512], ALU.add, waits=[t2])
                ld_(e)
                ftoks.append(e)
            last["ngb"] = ftoks
            last["xf"] = ftoks
        return ("v", b.cnt["v"])

    def store_out(final_tok):
        st_last = [[], []]
        for tt in range(16):
            tmt = TMT if tt % 2 == 0 else QAT
            wts = st_last[tt % 2]
            st_last[tt % 2] = []
            etk = []
            for db in range(8):
                ps, pw, pdone = pp.get()
                ttk = tr(ps[:, 0:128], xstr[db][:, tt * 128:(tt + 1) * 128],
                         [final_tok] + pw + (wts if db == 0 else []))
                ek = act(tmt[:, db * 128:(db + 1) * 128], ps[:, 0:128], AF.Copy,
                         waits=[ttk] + (wts if db == 0 else []))
                pdone(ek)
                etk.append(ek)
            dtok = b.dma("gpsimd", yout[tt * 128:(tt + 1) * 128, :], tmt[:, 0:1024],
                         waits=etk, pub="do")
            st_last[tt % 2] = [dtok]
        b.barrier_wait("gpsimd", [("do", b.cnt["do"])])

    if STAGES >= 2:
        stage_barrier()
        wq_pass()
        stage_barrier()
        if DBG is not None and DBG[3] == "wq":
            b.dma("gpsimd", yout[0:1024, :], qgw[0:1024, 1024:2048], pub="do")
            b.dma("gpsimd", yout[1024:2048, :], qgw[0:1024, 1792:2816], pub="do")
            b.barrier_wait("gpsimd", [("do", b.cnt["do"])])
    ftok = l0_attn()
    if STAGES >= 2:
        bar = stage_barrier()
        ftok = f0_ffn(bar)
    if STAGES >= 3:
        bar = stage_barrier()
        ftok = l1_attn(bar)
    if STAGES >= 4:
        bar = stage_barrier()
        ftok = f1_ffn(bar)
    stage_barrier()
    if DBG is None:
        store_out(ftok)
    else:
        b.barrier_wait("gpsimd", [("do", b.cnt["do"]), ftok])

    b.realize(block)
    nc._ctx_holder = ctx
    return nc


_NC_CACHE = {}


def kernel(**inputs):
    in_maps = host_prepare(inputs)
    nc = build_nc()
    res = run_bass_kernel_spmd(nc, in_maps, core_ids=list(range(8)))
    return host_gather(res.results)
